# version 14
# baseline (speedup 1.0000x reference)
import numpy as np
import concourse.bass as bass
import concourse.mybir as mybir
from concourse.bass import IndirectOffsetOnAxis
from concourse.bass_utils import run_bass_kernel_spmd

F32 = mybir.dt.float32
U32 = mybir.dt.uint32
AF = mybir.ActivationFunctionType
ALU = mybir.AluOpType

T = 2048
D = 1024
DIN = 2048
S = 16
NE = 16
M = 512
NCHUNK = 4
CAP = 384
TC = T // NCHUNK
FD = D // 128
FDI = DIN // 128
LN_EPS = 1e-5

ENGINES = ("tensor", "vector", "scalar", "gpsimd", "sync")
SEMS = ("tensor", "vector", "scalar", "gpsimd", "sync_q", "scalar_q", "gpsimd_q")
ENG_SEM = {"tensor": "tensor", "vector": "vector", "scalar": "scalar",
           "gpsimd": "gpsimd", "sync": None}


class Buf:
    __slots__ = ("ap", "wr", "rd", "name")

    def __init__(self, ap, name=""):
        self.ap = ap
        self.wr = None
        self.rd = []
        self.name = name


class Prog:
    def __init__(self, nc, sems):
        self.nc = nc
        self.sems = sems
        self.cnt = {s: 0 for s in SEMS}
        self.wm = {}
        self.streams = {e: [] for e in ENGINES}
        self.nops = 0
        self.reducy = set()
        self.dram = {}

    def dbuf(self, name, c=None):
        key = (name, c)
        if key not in self.dram:
            self.dram[key] = Buf(None, f"dram_{name}_{c}")
        return self.dram[key]

    def _emit(self, eng, fn, deps, sem_name, inc, soft_deps=()):
        waits = []
        own = ENG_SEM[eng]
        for hard, lst in ((True, deps), (False, soft_deps)):
            for s, v in lst:
                if v <= 0:
                    continue
                if not hard and s == own:
                    continue
                key = (eng, s)
                if self.wm.get(key, 0) >= v:
                    continue
                self.wm[key] = v
                waits.append((s, v))
        self.cnt[sem_name] += inc
        tok = (sem_name, self.cnt[sem_name])
        self.streams[eng].append((waits, fn, sem_name, inc))
        self.nops += 1
        return tok

    def op(self, eng, fn, ins=(), outs=(), extra_deps=(), reducy=False):
        deps = list(extra_deps)
        soft = []
        for b in ins:
            if b.wr is not None:
                deps.append(b.wr)
        for b in outs:
            if b.wr is not None:
                (deps if b.wr in self.reducy else soft).append(b.wr)
            soft.extend(b.rd)
        tok = self._emit(eng, fn, deps, ENG_SEM[eng], 1, soft_deps=soft)
        if reducy:
            self.reducy.add(tok)
        for b in ins:
            b.rd.append(tok)
        for b in outs:
            b.wr = tok
            b.rd = []
        return tok

    def dma_nofence(self, queue, fn, ins=(), outs=(), extra_deps=()):
        eng = {"sync_q": "sync", "scalar_q": "scalar", "gpsimd_q": "gpsimd"}[queue]
        deps = list(extra_deps)
        for b in ins:
            if b is not None and b.wr is not None:
                deps.append(b.wr)
        for b in outs:
            if b is None:
                continue
            if b.wr is not None:
                deps.append(b.wr)
            deps.extend(b.rd)
        tok = self._emit(eng, fn, deps, queue, 16)
        for b in ins:
            if b is not None:
                b.rd.append(tok)
        for b in outs:
            if b is not None:
                b.wr = tok
                b.rd = []
        return tok

    def dma(self, queue, fn, ins=(), outs=(), extra_deps=()):
        eng = {"sync_q": "sync", "scalar_q": "scalar", "gpsimd_q": "gpsimd"}[queue]
        deps = list(extra_deps)
        for b in ins:
            if b is not None and b.wr is not None:
                deps.append(b.wr)
        for b in outs:
            if b is None:
                continue
            if b.wr is not None:
                deps.append(b.wr)
            deps.extend(b.rd)
        tok = self._emit(eng, fn, deps, queue, 16)
        self._emit(eng, None, [tok], queue, 0)
        for b in ins:
            if b is not None:
                b.rd.append(tok)
        for b in outs:
            if b is not None:
                b.wr = tok
                b.rd = []
        return tok

    def qfence(self, queue):
        eng = {"sync_q": "sync", "scalar_q": "scalar", "gpsimd_q": "gpsimd"}[queue]
        self._emit(eng, None, [(queue, self.cnt[queue])], queue, 0)

    def wait_all(self, eng):
        deps = [(s, self.cnt[s]) for s in SEMS]
        self._emit(eng, None, deps, ENG_SEM[eng] or "sync_q", 0)

    def replay(self, block):
        sems = self.sems

        def mk(eng_name):
            def run(eng):
                for waits, fn, sem_name, inc in self.streams[eng_name]:
                    for s, v in waits:
                        eng.wait_ge(sems[s], v)
                    if fn is not None:
                        instr = fn(eng)
                        if inc:
                            instr.then_inc(sems[sem_name], inc)
            return run

        block.tensor(mk("tensor"))
        block.vector(mk("vector"))
        block.scalar(mk("scalar"))
        block.gpsimd(mk("gpsimd"))
        block.sync(mk("sync"))


class Arena:

    def __init__(self, handle, ncols):
        self.h = handle
        self.n = ncols
        self.free = [(0, ncols, [])]
        self.live = {}
        self.peak = 0

    def alloc(self, cols, name):
        cols = int(cols)
        for i, (st, ln, haz) in enumerate(self.free):
            if ln >= cols:
                rest = (st + cols, ln - cols, list(haz))
                self.free[i:i + 1] = [rest] if rest[1] > 0 else []
                self.live[name] = (st, cols)
                self.peak = max(self.peak, max((s + l for s, l, _ in ([(st, cols, 0)])), default=0),
                                st + cols)
                b = Buf(self.h[:, st:st + cols], name)
                if haz:
                    b.rd = list(haz)
                return b
        raise MemoryError(f"SBUF arena: cannot alloc {cols} cols for {name}; "
                          f"free={[(s, l) for s, l, _ in self.free]}")

    def freebuf(self, b):
        st, ln = self.live.pop(b.name)
        haz = list(b.rd)
        if b.wr is not None:
            haz.append(b.wr)
        self.free.append((st, ln, haz))
        self.free.sort(key=lambda x: x[0])
        merged = []
        for seg in self.free:
            if merged and merged[-1][0] + merged[-1][1] == seg[0]:
                s0, l0, h0 = merged.pop()
                merged.append((s0, l0 + seg[1], h0 + list(seg[2])))
            else:
                merged.append(list(seg) if isinstance(seg, tuple) else seg)
        self.free = [tuple(s) for s in merged]


def host_inputs(inputs, core):
    x = np.asarray(inputs["L0_attn_x"])[core]
    W_in = np.asarray(inputs["L0_attn_W_in"])
    m = {}
    m["x"] = np.ascontiguousarray(x)
    m["w_in_x_T"] = np.ascontiguousarray(W_in[:DIN].T)
    m["w_in_z_T"] = np.ascontiguousarray(W_in[DIN:].T)
    m["w_dt_T"] = np.ascontiguousarray(np.asarray(inputs["L0_attn_W_dt"]).T)
    wxT = np.asarray(inputs["L0_attn_W_x"]).T
    perm = np.empty(32, np.int64)
    perm[0::2] = np.arange(16)
    perm[1::2] = np.arange(16) + 16
    m["w_x_T"] = np.ascontiguousarray(wxT[:, perm])
    m["w_out_T"] = np.ascontiguousarray(np.asarray(inputs["L0_attn_W_out"]).T)
    m["gate_w_T"] = np.ascontiguousarray(np.asarray(inputs["L0_ffn_gate_w"]).T)
    m["w_gate"] = np.ascontiguousarray(np.asarray(inputs["L0_ffn_w_gate"]))
    m["w_up"] = np.ascontiguousarray(np.asarray(inputs["L0_ffn_w_up"]))
    m["w_down"] = np.ascontiguousarray(np.asarray(inputs["L0_ffn_w_down"]))
    m["ws_gate"] = np.ascontiguousarray(np.asarray(inputs["L0_ffn_ws_gate"]))
    m["ws_up"] = np.ascontiguousarray(np.asarray(inputs["L0_ffn_ws_up"]))
    m["ws_down"] = np.ascontiguousarray(np.asarray(inputs["L0_ffn_ws_down"]))

    def pc(v):
        return np.ascontiguousarray(np.asarray(v).reshape(FDI, 128).T)
    m["b_dt_pc"] = pc(inputs["L0_attn_b_dt"])
    m["conv_b_pc"] = pc(inputs["L0_attn_conv_b"])
    m["d_param_pc"] = pc(inputs["L0_attn_D_param"])
    m["ln_g_pc"] = pc(inputs["L0_attn_ln_g"])
    m["ln_b_pc"] = pc(inputs["L0_attn_ln_b"])
    cw = np.asarray(inputs["L0_attn_conv_w"]).reshape(FDI, 128, 4)
    m["conv_w_pc"] = np.ascontiguousarray(cw.transpose(1, 0, 2).reshape(128, FDI * 4))
    m["a_log"] = np.ascontiguousarray(np.asarray(inputs["L0_attn_A_log"]).reshape(1, S))
    m["identity"] = np.eye(128, dtype=np.float32)
    ck = np.zeros((128, 4), dtype=np.float32)
    ck[:, 0] = LN_EPS
    ck[:, 1] = 1.0
    m["constk"] = ck
    m["ones_row"] = np.ones((1, 128), dtype=np.float32)
    m["iota16"] = np.tile(np.arange(16, dtype=np.float32), (128, 1))
    sel = np.zeros((32, 32 * 128), dtype=np.float32)
    for j in range(32):
        sel[j, j * 128:(j + 1) * 128] = 1.0
    m["sel32"] = sel
    return m


DBG_SHAPES = {"xn": [D, T], "xc": [DIN, T], "dt": [DIN, T], "bp": [S, T],
              "cp": [S, T], "y": [DIN, T], "yg": [DIN, T], "s_out": [D, T],
              "hacc": [D, T]}


def build(debug_taps=(), stages=99):
    dbg = set(debug_taps)
    nc = bass.Bass("TRN2", target_bir_lowering=False)

    dr = {}
    dr["x"] = nc.dram_tensor("x", [T, D], F32, kind="ExternalInput").ap()
    for nm, shp in [("w_in_x_T", [D, DIN]), ("w_in_z_T", [D, DIN]),
                    ("w_dt_T", [DIN, DIN]), ("w_x_T", [DIN, 2 * S]),
                    ("w_out_T", [DIN, D]), ("gate_w_T", [D, NE]),
                    ("w_gate", [NE, D, M]), ("w_up", [NE, D, M]),
                    ("w_down", [NE, M, D]),
                    ("ws_gate", [D, M]), ("ws_up", [D, M]), ("ws_down", [M, D]),
                    ("b_dt_pc", [128, FDI]), ("conv_b_pc", [128, FDI]),
                    ("d_param_pc", [128, FDI]), ("ln_g_pc", [128, FDI]),
                    ("ln_b_pc", [128, FDI]), ("conv_w_pc", [128, FDI * 4]),
                    ("a_log", [1, S]), ("identity", [128, 128]),
                    ("ones_row", [1, 128]), ("iota16", [128, 16]),
                    ("constk", [128, 4]), ("sel32", [32, 32 * 128])]:
        dr[nm] = nc.dram_tensor(nm, shp, F32, kind="ExternalInput").ap()
    dr["out"] = nc.dram_tensor("out", [T, D], F32, kind="ExternalOutput").ap()
    dr["xn_s"] = nc.dram_tensor("xn_s", [D, T], F32, kind="Internal").ap()
    dr["xc_s"] = nc.dram_tensor("xc_s", [DIN, T], F32, kind="Internal").ap()
    dr["dt_s"] = nc.dram_tensor("dt_s", [DIN, T], F32, kind="Internal").ap()
    dr["y_s"] = nc.dram_tensor("y_s", [DIN, T], F32, kind="Internal").ap()
    dr["yg_s"] = nc.dram_tensor("yg_s", [DIN, T], F32, kind="Internal").ap()
    dr["hacc_s"] = nc.dram_tensor("hacc_s", [D, T], F32, kind="Internal").ap()
    dr["xn_tok_s"] = nc.dram_tensor("xn_tok_s", [T, D], F32, kind="Internal").ap()
    dr["bpcp_s"] = nc.dram_tensor("bpcp_s", [2 * S, T], F32, kind="Internal").ap()
    dr["zs_s"] = nc.dram_tensor("zs_s", [DIN, T], F32, kind="Internal").ap()
    dr["glist"] = nc.dram_tensor("glist", [NE * CAP, 1], U32, kind="Internal").ap()
    dr["dlist"] = nc.dram_tensor("dlist", [NE * CAP, 1], U32, kind="Internal").ap()
    dr["wlist"] = nc.dram_tensor("wlist", [NE * CAP, 1], F32, kind="Internal").ap()
    dr["moebuf"] = nc.dram_tensor("moebuf", [2 * T + NE * CAP, D], F32, kind="Internal").ap()
    for nm in dbg:
        if nm == "lists":
            dr["dbg_lists"] = nc.dram_tensor("dbg_lists", [128, 144], F32,
                                             kind="ExternalOutput").ap()
            continue
        if nm == "moebuf":
            dr["dbg_moebuf"] = nc.dram_tensor("dbg_moebuf", [2 * T, D], F32,
                                              kind="ExternalOutput").ap()
            continue
        dr["dbg_" + nm] = nc.dram_tensor("dbg_" + nm, DBG_SHAPES[nm], F32,
                                         kind="ExternalOutput").ap()

    import contextlib
    with contextlib.ExitStack() as stack:
        ncols = 53150
        sbh = stack.enter_context(nc.sbuf_tensor("arena", [128, ncols], F32))
        psh = [stack.enter_context(nc.psum_tensor(f"ps{i}", [128, 512], F32))
               for i in range(8)]
        sems = {s: stack.enter_context(nc.semaphore(s)) for s in SEMS}
        p = Prog(nc, sems)
        ar = Arena(sbh, ncols)
        pbk = [Buf(psh[i], f"ps{i}") for i in range(8)]
        build_body(p, ar, pbk, dr, dbg, stages)
        with nc.Block() as block:
            p.replay(block)
    return nc


def build_body(p, ar, pbk, dr, dbg, stages):
    nc = p.nc

    ident = ar.alloc(128, "ident")
    ones_row = ar.alloc(128, "ones_row")
    tabs = {}
    for nm in ("b_dt_pc", "conv_b_pc", "d_param_pc", "ln_g_pc", "ln_b_pc"):
        tabs[nm] = ar.alloc(FDI, nm)
        p.dma("sync_q", lambda e, nm=nm, b=tabs[nm]: e.dma_start(out=b.ap, in_=dr[nm]),
              outs=[tabs[nm]])
    conv_w = ar.alloc(FDI * 4, "conv_w_pc")
    a_log = ar.alloc(S, "a_log")
    iota16 = ar.alloc(S, "iota16")
    constk = ar.alloc(4, "constk")
    p.dma("sync_q", lambda e: e.dma_start(out=constk.ap, in_=dr["constk"]), outs=[constk])
    p.dma("sync_q", lambda e: e.dma_start(out=ident.ap, in_=dr["identity"]), outs=[ident])
    p.dma("sync_q", lambda e: e.dma_start(out=ones_row.ap[0:1, :], in_=dr["ones_row"]),
          outs=[ones_row])
    p.dma("sync_q", lambda e: e.dma_start(out=conv_w.ap, in_=dr["conv_w_pc"]), outs=[conv_w])
    p.dma("sync_q", lambda e: e.dma_start(out=a_log.ap[0:1, :], in_=dr["a_log"]), outs=[a_log])
    p.dma("sync_q", lambda e: e.dma_start(out=iota16.ap, in_=dr["iota16"]), outs=[iota16])

    a_row = ar.alloc(S, "a_row")
    p.op("scalar", lambda e: e.activation(a_row.ap[0:1, :], a_log.ap[0:1, :], AF.Exp),
         ins=[a_log], outs=[a_row])
    p.op("vector", lambda e: e.tensor_scalar(a_row.ap[0:1, :], a_row.ap[0:1, :],
                                             -1.0, None, ALU.mult),
         ins=[a_row], outs=[a_row])
    a_bc = ar.alloc(S, "a_bc")
    p.op("tensor", lambda e: e.matmul(pbk[7].ap[:, 0:S], ones_row.ap[0:1, :],
                                      a_row.ap[0:1, :], start=True, stop=True),
         ins=[ones_row, a_row], outs=[pbk[7]])
    p.op("vector", lambda e: e.tensor_copy(a_bc.ap, pbk[7].ap[:, 0:S]),
         ins=[pbk[7]], outs=[a_bc])

    ctx = dict(ident=ident, ones_row=ones_row, tabs=tabs, conv_w=conv_w,
               a_bc=a_bc, iota16=iota16, constk=constk)

    ones_col = ar.alloc(1, "onec")
    p.op("vector", lambda e: e.memset(ones_col.ap, 1.0), outs=[ones_col])
    ctx["ones_col"] = ones_col
    pass1(p, ar, pbk, dr, dbg, ctx)
    pass2(p, ar, pbk, dr, dbg, ctx)
    rgen = routing_sparse_v3(p, ar, pbk, dr, ctx)

    def rfeed(n):
        for _ in range(n):
            if next(rgen, "END") == "END":
                break
    bpcp = pass3(p, ar, pbk, dr, dbg, ctx, feed=rfeed)
    pass3_store_bpcp(p, ar, dr, bpcp)
    rfeed(10 ** 9)
    sc_toks = ctx["sc_toks"]
    ms = {}
    ms["wh"] = [ar.alloc(4 * M, "ms_wh0"), ar.alloc(4 * M, "ms_wh1")]
    ms["wi"] = 0
    ms["xe"] = ar.alloc(FD * TC, "ms_xe")
    ms["xg"] = [ar.alloc(D, f"ms_xg{i}") for i in range(CAP // 128)]
    ms["gs"] = ar.alloc(4 * TC, "ms_gs")
    ms["ub"] = ar.alloc(4 * TC, "ms_ub")
    ms["rb"] = ar.alloc(CAP, "ms_rb")
    ms["idxg"] = ar.alloc(2 * (CAP // 128), "ms_idxg")
    gd = [sc_toks[-1], p.dbuf("glist").wr, p.dbuf("dlist").wr,
          p.dbuf("wlist").wr]
    gd += [tk for tk in (p.dbuf("xn_tok_s", cc).wr for cc in range(NCHUNK)) if tk]
    ms["gather_dep"] = gd
    ms["first"] = True
    state = ar.alloc(S * FDI, "scan_state")
    p.op("vector", lambda e: e.memset(state.ap, 0.0), outs=[state])
    ctx["gate_bufs"] = dict(
        yq=ar.alloc(4 * TC, "g_yq"), sq=ar.alloc(TC, "g_sq"),
        stat=ar.alloc(3 * TC, "g_st"), xk=ar.alloc(2 * TC, "g_xk"),
        zk=ar.alloc(2 * TC, "g_zk"))
    xn_c = ar.alloc(FD * TC, "blk_xn")
    for e_i in range(4):
        for _ in expert_emit_v4(p, ar, pbk, dr, ctx, ms, e_i):
            pass
    import itertools
    for c in range(NCHUNK):
        sb4 = pass4v4_alloc(p, ar, dr, c)
        load_fm(p, dr["xn_s"], xn_c, FD, c, name="xn_s")
        gens = [pass6az_v3(p, ar, pbk, dr, dbg, ctx, c, xn_c),
                 shared_gen_v4(p, ar, pbk, dr, ctx, ms, c, xn_c)]
        if c < NCHUNK - 1:
            for e_i in range(4 * (c + 1), 4 * (c + 1) + 4):
                gens.append(expert_emit_v4(p, ar, pbk, dr, ctx, ms, e_i))
        chain = itertools.chain(*gens)

        def feed(n, chain=chain):
            for _ in range(n):
                if next(chain, "END") == "END":
                    break
        pass4v4(p, ar, pbk, dr, dbg, ctx, c, state, sb4, feed)
        feed(10 ** 9)
        if c > 0:
            for _ in pass6gate_v4(p, ar, pbk, dr, dbg, ctx, c - 1):
                pass
    for b in ms["wh"] + ms["xg"] + [ms["xe"], ms["gs"], ms["ub"], ms["rb"],
                                    ms["idxg"], state, xn_c]:
        ar.freebuf(b)
    pass6b(p, ar, pbk, dr, dbg, ctx, chunks=[0])
    for _ in pass6gate_v4(p, ar, pbk, dr, dbg, ctx, NCHUNK - 1):
        pass
    for b in ctx["gate_bufs"].values():
        ar.freebuf(b)
    pass6b(p, ar, pbk, dr, dbg, ctx, chunks=[1, 2, 3], wo=ctx["p7_wo"])
    for b in ctx["p7_wo"]:
        ar.freebuf(b)
    if "lists" in dbg:
        lb = ar.alloc(3 * 48, "dbg_lb")
        NG = NE * CAP // 128
        mt = [ctx["moe_sc"][-1]] if "moe_sc" in ctx else []
        for j, nm in enumerate(("glist", "dlist", "wlist")):
            src_ap = dr[nm]
            if nm != "wlist":
                p.dma("sync_q", lambda e, j=j, src_ap=src_ap, lb=lb, NG=NG:
                      e.dma_start(out=lb.ap[:, j * NG:(j + 1) * NG].bitcast(U32),
                                  in_=src_ap), outs=[lb], extra_deps=mt)
            else:
                p.dma("sync_q", lambda e, j=j, src_ap=src_ap, lb=lb, NG=NG:
                      e.dma_start(out=lb.ap[:, j * NG:(j + 1) * NG],
                                  in_=src_ap), outs=[lb], extra_deps=mt)
        cv = ar.alloc(3 * 48, "dbg_cv")
        NGc = NE * CAP // 128
        for j in range(2):
            p.op("vector", lambda e, j=j, NGc=NGc: e.tensor_copy(
                cv.ap[:, j * NGc:(j + 1) * NGc],
                lb.ap[:, j * NGc:(j + 1) * NGc].bitcast(U32)),
                ins=[lb], outs=[cv])
        p.op("vector", lambda e, NGc=NGc: e.tensor_copy(
            cv.ap[:, 2 * NGc:3 * NGc], lb.ap[:, 2 * NGc:3 * NGc]),
            ins=[lb], outs=[cv])
        p.dma("gpsimd_q", lambda e, cv=cv: e.dma_start(
            out=dr["dbg_lists"], in_=cv.ap), ins=[cv])
    if "moebuf" in dbg:
        mb = ar.alloc(FD * TC, "dbg_mb")
        mt = [ctx["moe_sc"][-1]] if "moe_sc" in ctx else []
        for cc in range(8):
            p.dma("sync_q", lambda e, cc=cc, mb=mb: e.dma_start(
                out=mb.ap, in_=dr["moebuf"][cc * 512:(cc + 1) * 512, :].rearrange(
                    "(b q) d -> q b d", q=128)), outs=[mb], extra_deps=mt)
            p.dma("gpsimd_q", lambda e, cc=cc, mb=mb: e.dma_start(
                out=dr["dbg_moebuf"][cc * 512:(cc + 1) * 512, :].rearrange(
                    "(b q) d -> q b d", q=128), in_=mb.ap), ins=[mb])
        ar.freebuf(mb)
    pass6b(p, ar, pbk, dr, dbg, ctx)

    p.wait_all("sync")


def pass1(p, ar, pbk, dr, dbg, ctx):
    ident = ctx["ident"]
    constk = ctx["constk"]
    sq = ar.alloc(D, "p1_sq")
    for c in range(NCHUNK):
        xts = [ar.alloc(D, f"p1_x_{c}_{i}") for i in range(4)]
        mv = ar.alloc(8, f"p1_mv_{c}")
        xn_cb = ar.alloc(FD * TC, f"p1_xn_{c}")
        for pt in range(4):
            xt = xts[pt]
            t0 = c * TC + pt * 128
            p.dma("sync_q", lambda e, xt=xt, t0=t0: e.dma_start(
                out=xt.ap, in_=dr["x"][t0:t0 + 128, :]), outs=[xt])
            p.op("vector", lambda e, xt=xt, mv=mv: e.tensor_reduce(
                mv.ap[:, 0:1], xt.ap, mybir.AxisListType.X, ALU.add),
                ins=[xt], outs=[mv])
            p.op("scalar", lambda e, xt=xt, mv=mv: e.activation(
                sq.ap, xt.ap, AF.Square, accum_out=mv.ap[:, 1:2]),
                ins=[xt], outs=[sq, mv])
            p.op("vector", lambda e, mv=mv: e.tensor_scalar(
                mv.ap[:, 2:4], mv.ap[:, 0:2], 1.0 / D, None, ALU.mult),
                ins=[mv], outs=[mv])
            p.op("vector", lambda e, mv=mv: e.tensor_tensor(
                mv.ap[:, 4:5], mv.ap[:, 2:3], mv.ap[:, 2:3], ALU.mult),
                ins=[mv], outs=[mv])
            p.op("vector", lambda e, mv=mv: e.tensor_tensor(
                mv.ap[:, 5:6], mv.ap[:, 3:4], mv.ap[:, 4:5], ALU.subtract),
                ins=[mv], outs=[mv])
            p.op("scalar", lambda e, mv=mv: e.activation(
                mv.ap[:, 6:7], mv.ap[:, 5:6], AF.Sqrt, bias=constk.ap[:, 0:1]),
                ins=[mv, constk], outs=[mv])
            p.op("vector", lambda e, mv=mv: e.reciprocal(
                mv.ap[:, 7:8], mv.ap[:, 6:7]), ins=[mv], outs=[mv])
            p.op("vector", lambda e, xt=xt, mv=mv: e.tensor_scalar(
                xt.ap, xt.ap, mv.ap[:, 2:3], mv.ap[:, 7:8],
                ALU.subtract, ALU.mult), ins=[xt, mv], outs=[xt])
        for pt in range(4):
            t0 = c * TC + pt * 128
            p.dma("gpsimd_q", lambda e, xt=xts[pt], t0=t0: e.dma_start(
                out=dr["xn_tok_s"][t0:t0 + 128, :], in_=xt.ap),
                ins=[xts[pt]], outs=[p.dbuf("xn_tok_s", c)])
        for f in range(FD):
            bank = pbk[f % 4]
            for pt in range(4):
                xt = xts[pt]
                p.op("tensor", lambda e, xt=xt, f=f, pt=pt, bank=bank: e.transpose(
                    bank.ap[:, pt * 128:(pt + 1) * 128],
                    xt.ap[:, f * 128:(f + 1) * 128], ident.ap),
                    ins=[xt, ident], outs=[bank])
            p.op("vector", lambda e, bank=bank, d=xn_cb, f=f: e.tensor_copy(
                d.ap[:, f * TC:(f + 1) * TC], bank.ap[:, 0:512]),
                ins=[bank], outs=[xn_cb])
        store_fm(p, dr["xn_s"], xn_cb, FD, c, name="xn_s")
        if "xn" in dbg:
            store_fm(p, dr["dbg_xn"], xn_cb, FD, c)
        for b in xts + [mv, xn_cb]:
            ar.freebuf(b)
    ar.freebuf(sq)
    p.qfence("gpsimd_q")


def store_fm(p, dram_ap, buf, nblk, c, q="gpsimd_q", rows=128, name=None):
    d3 = dram_ap.rearrange("(b q) t -> q b t", q=128)[0:rows, :, c * TC:(c + 1) * TC]
    s3 = buf.ap[0:rows, 0:nblk * TC].rearrange("q (b t) -> q b t", t=TC)
    outs = [p.dbuf(name, c)] if name else []
    return p.dma(q, lambda e: e.dma_start(out=d3, in_=s3), ins=[buf], outs=outs)


def load_fm(p, dram_ap, buf, nblk, c, q="sync_q", rows=128, blk0=0, name=None):
    d3 = dram_ap.rearrange("(b q) t -> q b t", q=128)[0:rows, blk0:blk0 + nblk,
                                                      c * TC:(c + 1) * TC]
    s3 = buf.ap[0:rows, 0:nblk * TC].rearrange("q (b t) -> q b t", t=TC)
    ins = [p.dbuf(name, c)] if name else []
    return p.dma(q, lambda e: e.dma_start(out=s3, in_=d3), outs=[buf], ins=ins)


def mm_acc(p, bank, w_tiles, x_buf, nk, n=TC, bank_cols=None):
    ins_bufs = None
    for k in range(nk):
        lhsT = w_tiles(k)
        rhs = x_buf(k)
        p.op("tensor", lambda e, lhsT=lhsT, rhs=rhs, k=k: e.matmul(
            bank.ap[:, 0:n] if bank_cols is None else bank.ap[:, bank_cols[0]:bank_cols[1]],
            lhsT, rhs, start=(k == 0), stop=(k == nk - 1)),
            ins=[], outs=[bank], extra_deps=[])


def pass2(p, ar, pbk, dr, dbg, ctx):
    conv_w = ctx["conv_w"]
    tabs = ctx["tabs"]
    wb = [ar.alloc(DIN, f"p2_w_{k}") for k in range(FD)]
    for k in range(FD):
        p.dma("sync_q", lambda e, k=k, b=wb[k]: e.dma_start(
            out=b.ap, in_=dr["w_in_x_T"][k * 128:(k + 1) * 128, :]), outs=[wb[k]])
    tail = ar.alloc(3 * FDI, "p2_tail")
    p.op("vector", lambda e: e.memset(tail.ap, 0.0), outs=[tail])
    for c in range(NCHUNK):
        xn_c = ar.alloc(FD * TC, f"p2_xn_{c}")
        load_fm(p, dr["xn_s"], xn_c, FD, c, name="xn_s")
        xp = [ar.alloc(TC + 3, f"p2_xp_{c}_{i}") for i in range(FDI)]
        xc = ar.alloc(FDI * TC, f"p2_xc_{c}")
        acc = ar.alloc(TC, f"p2_acc_{c}")
        for mi in range(FDI):
            bank = pbk[mi % 2]
            p.op("vector", lambda e, mi=mi, b=xp[mi]: e.tensor_copy(
                b.ap[:, 0:3], tail.ap[:, mi * 3:mi * 3 + 3]),
                ins=[tail], outs=[xp[mi]])
            for k in range(FD):
                p.op("tensor", lambda e, k=k, mi=mi, bank=bank, xn_c=xn_c: e.matmul(
                    bank.ap[:, 0:TC], wb[k].ap[:, mi * 128:(mi + 1) * 128],
                    xn_c.ap[:, k * TC:(k + 1) * TC],
                    start=(k == 0), stop=(k == FD - 1)),
                    ins=[wb[k], xn_c], outs=[bank])
            p.op("scalar", lambda e, mi=mi, bank=bank, b=xp[mi]: e.copy(
                b.ap[:, 3:TC + 3], bank.ap[:, 0:TC]), ins=[bank], outs=[xp[mi]])
            p.op("vector", lambda e, mi=mi, b=xp[mi]: e.tensor_copy(
                tail.ap[:, mi * 3:mi * 3 + 3], b.ap[:, TC:TC + 3]),
                ins=[xp[mi]], outs=[tail])
            p.op("vector", lambda e, mi=mi, b=xp[mi], acc=acc: e.tensor_scalar(
                acc.ap, b.ap[:, 0:TC], conv_w.ap[:, mi * 4:mi * 4 + 1], None,
                ALU.mult), ins=[xp[mi], conv_w], outs=[acc])
            for kk in range(1, 4):
                p.op("vector", lambda e, mi=mi, kk=kk, b=xp[mi], acc=acc:
                     e.scalar_tensor_tensor(
                         acc.ap, b.ap[:, kk:kk + TC],
                         conv_w.ap[:, mi * 4 + kk:mi * 4 + kk + 1], acc.ap,
                         ALU.mult, ALU.add),
                     ins=[xp[mi], conv_w, acc], outs=[acc])
            p.op("scalar", lambda e, mi=mi, acc=acc, xc=xc: e.activation(
                xc.ap[:, mi * TC:(mi + 1) * TC], acc.ap, AF.Silu,
                bias=tabs["conv_b_pc"].ap[:, mi:mi + 1]),
                ins=[acc, tabs["conv_b_pc"]], outs=[xc])
        store_fm(p, dr["xc_s"], xc, FDI, c, name="xc_s")
        if "xc" in dbg:
            store_fm(p, dr["dbg_xc"], xc, FDI, c)
        for b in [xn_c, xc, acc] + xp:
            ar.freebuf(b)
    for b in wb + [tail]:
        ar.freebuf(b)
    p.qfence("gpsimd_q")


def pass3(p, ar, pbk, dr, dbg, ctx, feed=None):
    tabs = ctx["tabs"]
    constk = ctx["constk"]
    bpcp = ar.alloc(T, "bpcp")
    wx = ar.alloc(FDI * 2 * S, "p3_wx")
    p.dma("sync_q", lambda e: e.dma_start(
        out=wx.ap.rearrange("q (b s) -> q b s", s=2 * S),
        in_=dr["w_x_T"].rearrange("(b q) s -> q b s", q=128)), outs=[wx])
    tmp = [ar.alloc(TC, f"p3_t{i}") for i in range(4)]
    for mh in range(2):
        wdt = [ar.alloc(1024, f"p3_w_{mh}_{k}") for k in range(FDI)]
        for k in range(FDI):
            p.dma("sync_q", lambda e, k=k, mh=mh, b=wdt[k]: e.dma_start(
                out=b.ap, in_=dr["w_dt_T"][k * 128:(k + 1) * 128,
                                           mh * 1024:(mh + 1) * 1024]),
                outs=[wdt[k]])
        for c in range(NCHUNK):
            xc_c = ar.alloc(FDI * TC, f"p3_xc_{mh}_{c}")
            load_fm(p, dr["xc_s"], xc_c, FDI, c, name="xc_s")
            dtb = ar.alloc(8 * TC, f"p3_dt_{mh}_{c}")
            for mo in range(8):
                mi = mh * 8 + mo
                bank = pbk[mo % 2]
                for k in range(FDI):
                    p.op("tensor", lambda e, k=k, mo=mo, bank=bank, xc_c=xc_c,
                         wdt=wdt: e.matmul(
                             bank.ap[:, 0:TC], wdt[k].ap[:, mo * 128:(mo + 1) * 128],
                             xc_c.ap[:, k * TC:(k + 1) * TC],
                             start=(k == 0), stop=(k == FDI - 1)),
                         ins=[wdt[k], xc_c], outs=[bank])
                v, r, m, l = tmp
                p.op("scalar", lambda e, bank=bank, mi=mi: e.activation(
                    v.ap, bank.ap[:, 0:TC], AF.Identity,
                    bias=tabs["b_dt_pc"].ap[:, mi:mi + 1]),
                    ins=[bank, tabs["b_dt_pc"]], outs=[v])
                p.op("scalar", lambda e: e.activation(r.ap, v.ap, AF.Relu),
                     ins=[v], outs=[r])
                p.op("vector", lambda e: e.scalar_tensor_tensor(
                    m.ap, r.ap, -2.0, v.ap, ALU.mult, ALU.add),
                    ins=[r, v], outs=[m])
                p.op("scalar", lambda e: e.activation(m.ap, m.ap, AF.Exp),
                     ins=[m], outs=[m])
                p.op("scalar", lambda e: e.activation(
                    l.ap, m.ap, AF.Ln, bias=constk.ap[:, 1:2]),
                    ins=[m, constk], outs=[l])
                p.op("vector", lambda e, mo=mo, dtb=dtb: e.tensor_tensor(
                    dtb.ap[:, mo * TC:(mo + 1) * TC], r.ap, l.ap, ALU.add),
                    ins=[r, l], outs=[dtb])
                if feed is not None:
                    feed(1)
            d3 = dr["dt_s"].rearrange("(b q) t -> q b t", q=128)[
                :, mh * 8:(mh + 1) * 8, c * TC:(c + 1) * TC]
            s3 = dtb.ap.rearrange("q (b t) -> q b t", t=TC)
            p.dma("gpsimd_q", lambda e, d3=d3, s3=s3: e.dma_start(out=d3, in_=s3),
                  ins=[dtb], outs=[p.dbuf("dt_s", c)])
            if "dt" in dbg:
                d3b = dr["dbg_dt"].rearrange("(b q) t -> q b t", q=128)[
                    :, mh * 8:(mh + 1) * 8, c * TC:(c + 1) * TC]
                p.dma("gpsimd_q", lambda e, d3b=d3b, s3=s3: e.dma_start(
                    out=d3b, in_=s3), ins=[dtb])
            if mh == 0:
                bank = pbk[2]
                for k in range(FDI):
                    p.op("tensor", lambda e, k=k, bank=bank, xc_c=xc_c: e.matmul(
                        bank.ap[0:32, 0:TC], wx.ap[:, k * 2 * S:(k + 1) * 2 * S],
                        xc_c.ap[:, k * TC:(k + 1) * TC],
                        start=(k == 0), stop=(k == FDI - 1)),
                        ins=[wx, xc_c], outs=[bank])
                p.op("vector", lambda e, c=c, bank=bank: e.tensor_copy(
                    bpcp.ap[0:32, c * TC:(c + 1) * TC], bank.ap[0:32, 0:TC]),
                    ins=[bank], outs=[bpcp])
            ar.freebuf(xc_c)
            ar.freebuf(dtb)
        for b in wdt:
            ar.freebuf(b)
    for b in tmp + [wx]:
        ar.freebuf(b)
    if "bp" in dbg:
        p.dma("gpsimd_q", lambda e: e.dma_start(
            out=dr["dbg_bp"], in_=bpcp.ap[0:S, :]), ins=[bpcp])
    if "cp" in dbg:
        p.dma("gpsimd_q", lambda e: e.dma_start(
            out=dr["dbg_cp"], in_=bpcp.ap[S:2 * S, :]), ins=[bpcp])
    return bpcp


def pass4(p, ar, pbk, dr, dbg, ctx):
    a_bc = ctx["a_bc"]
    bpcp = ctx["bpcp"]
    sel = ar.alloc(32 * 128, "p4_sel")
    p.dma("sync_q", lambda e: e.dma_start(out=sel.ap[0:32, :], in_=dr["sel32"]),
          outs=[sel])
    state = ar.alloc(S * FDI, "p4_state")
    p.op("vector", lambda e: e.memset(state.ap, 0.0), outs=[state])
    W = FDI * TC
    for c in range(NCHUNK):
        dt_c = ar.alloc(W, f"p4_dt_{c}")
        xc_c = ar.alloc(W, f"p4_xc_{c}")
        load_fm(p, dr["dt_s"], dt_c, FDI, c, name="dt_s")
        load_fm(p, dr["xc_s"], xc_c, FDI, c, name="xc_s")
        dtx = ar.alloc(W, f"p4_dtx_{c}")
        p.op("vector", lambda e, a=dt_c, b=xc_c, o=dtx: e.tensor_tensor(
            o.ap, a.ap, b.ap, ALU.mult), ins=[dt_c, xc_c], outs=[dtx])
        ar.freebuf(xc_c)
        dA = ar.alloc(W, f"p4_dA_{c}")
        uh = ar.alloc(W, f"p4_uh_{c}")
        y = ar.alloc(W, f"p4_y_{c}")
        for s in range(S):
            bB, bC = pbk[4], pbk[5]
            p.op("tensor", lambda e, s=s, c=c, bB=bB: e.matmul(
                bB.ap[:, 0:TC], sel.ap[0:32, s * 128:(s + 1) * 128],
                bpcp.ap[0:32, c * TC:(c + 1) * TC], start=True, stop=True),
                ins=[sel, bpcp], outs=[bB])
            p.op("tensor", lambda e, s=s, c=c, bC=bC: e.matmul(
                bC.ap[:, 0:TC], sel.ap[0:32, (16 + s) * 128:(17 + s) * 128],
                bpcp.ap[0:32, c * TC:(c + 1) * TC], start=True, stop=True),
                ins=[sel, bpcp], outs=[bC])
            p.op("scalar", lambda e, s=s, dt_c=dt_c, dA=dA: e.activation(
                dA.ap, dt_c.ap, AF.Exp, scale=a_bc.ap[:, s:s + 1]),
                ins=[dt_c, a_bc], outs=[dA])
            p.op("vector", lambda e, dtx=dtx, bB=bB, uh=uh: e.tensor_tensor(
                uh.ap.rearrange("q (b t) -> q b t", t=TC), dtx.ap.rearrange(
                    "q (b t) -> q b t", t=TC),
                bB.ap[:, 0:TC].unsqueeze(1).broadcast_to((128, FDI, TC)),
                ALU.mult), ins=[dtx, bB], outs=[uh])
            for dc in range(FDI):
                col = s * FDI + dc
                p.op("vector", lambda e, dc=dc, col=col, dA=dA, uh=uh, st=state:
                     e.tensor_tensor_scan(
                         uh.ap[:, dc * TC:(dc + 1) * TC],
                         dA.ap[:, dc * TC:(dc + 1) * TC],
                         uh.ap[:, dc * TC:(dc + 1) * TC],
                         st.ap[:, col:col + 1], ALU.mult, ALU.add),
                     ins=[dA, uh, state], outs=[uh])
            p.op("vector", lambda e, s=s, uh=uh, st=state: e.tensor_copy(
                st.ap[:, s * FDI:(s + 1) * FDI],
                uh.ap.rearrange("q (b t) -> q b t", t=TC)[:, :, TC - 1:TC].squeeze(2)),
                ins=[uh], outs=[state])
            if s == 0:
                p.op("vector", lambda e, uh=uh, bC=bC, y=y: e.tensor_tensor(
                    y.ap.rearrange("q (b t) -> q b t", t=TC),
                    uh.ap.rearrange("q (b t) -> q b t", t=TC),
                    bC.ap[:, 0:TC].unsqueeze(1).broadcast_to((128, FDI, TC)),
                    ALU.mult), ins=[uh, bC], outs=[y])
            else:
                p.op("vector", lambda e, uh=uh, bC=bC, dA=dA: e.tensor_tensor(
                    dA.ap.rearrange("q (b t) -> q b t", t=TC),
                    uh.ap.rearrange("q (b t) -> q b t", t=TC),
                    bC.ap[:, 0:TC].unsqueeze(1).broadcast_to((128, FDI, TC)),
                    ALU.mult), ins=[uh, bC], outs=[dA])
                p.op("vector", lambda e, y=y, dA=dA: e.tensor_tensor(
                    y.ap, y.ap, dA.ap, ALU.add), ins=[dA, y], outs=[y])
        store_fm(p, dr["y_s"], y, FDI, c, name="y_s")
        if "y" in dbg:
            store_fm(p, dr["dbg_y"], y, FDI, c)
        for b in (dt_c, dtx, dA, uh, y):
            ar.freebuf(b)
    for b in (sel, state):
        ar.freebuf(b)


def routing(p, ar, pbk, dr, ctx, xn_full):
    iota16 = ctx["iota16"]
    sel = ctx["sel"]
    gw = ar.alloc(FD * NE, "p5_gw")
    p.dma("sync_q", lambda e: e.dma_start(
        out=gw.ap.rearrange("q (b s) -> q b s", s=NE),
        in_=dr["gate_w_T"].rearrange("(b q) s -> q b s", q=128)), outs=[gw])
    sc_fm = ar.alloc(T, "p5_scfm")
    for c in range(NCHUNK):
        bank = pbk[0]
        for k in range(FD):
            p.op("tensor", lambda e, k=k, c=c, bank=bank: e.matmul(
                bank.ap[0:16, 0:TC], gw.ap[:, k * NE:(k + 1) * NE],
                xn_full.ap[:, k * T + c * TC:k * T + (c + 1) * TC],
                start=(k == 0), stop=(k == FD - 1)),
                ins=[gw, xn_full], outs=[bank])
        p.op("scalar", lambda e, c=c, bank=bank: e.activation(
            sc_fm.ap[0:16, c * TC:(c + 1) * TC], bank.ap[0:16, 0:TC],
            AF.Sigmoid), ins=[bank], outs=[sc_fm])
    sc_tok = ar.alloc(16 * NE, "p5_sctok")
    rt_tok = ar.alloc(16 * NE, "p5_rttok")
    mx8 = ar.alloc(8, "p5_mx8")
    idx = ar.alloc(8, "p5_idx")
    w12 = ar.alloc(6, "p5_w12")
    oh = ar.alloc(NE, "p5_oh")
    rT = ar.alloc(T, "p5_rT")
    for tt in range(16):
        bank = pbk[1]
        p.op("tensor", lambda e, tt=tt, bank=bank: e.transpose(
            bank.ap[:, 0:16], sc_fm.ap[0:16, tt * 128:(tt + 1) * 128],
            ctx["ident"].ap[0:16, 0:16]), ins=[sc_fm, ctx["ident"]], outs=[bank])
        p.op("vector", lambda e, tt=tt, bank=bank: e.tensor_copy(
            sc_tok.ap[:, tt * NE:(tt + 1) * NE], bank.ap[:, 0:16]),
            ins=[bank], outs=[sc_tok])
        st = sc_tok.ap[:, tt * NE:(tt + 1) * NE]
        p.op("vector", lambda e, st=st: e.max(mx8.ap, st),
             ins=[sc_tok], outs=[mx8], reducy=True)
        p.op("vector", lambda e, st=st: e.max_index(
            idx.ap.bitcast(U32), mx8.ap, st),
            ins=[sc_tok, mx8], outs=[idx], reducy=True)
        p.op("vector", lambda e: e.tensor_copy(
            w12.ap[:, 0:2], idx.ap[:, 0:2].bitcast(U32)),
            ins=[idx], outs=[w12])
        p.op("vector", lambda e: e.tensor_tensor(
            w12.ap[:, 2:3], mx8.ap[:, 0:1], mx8.ap[:, 1:2], ALU.add),
            ins=[mx8], outs=[w12])
        p.op("vector", lambda e: e.tensor_scalar(
            w12.ap[:, 2:3], w12.ap[:, 2:3], 1e-20, None, ALU.add),
            ins=[w12], outs=[w12])
        p.op("vector", lambda e: e.reciprocal(w12.ap[:, 3:4], w12.ap[:, 2:3]),
             ins=[w12], outs=[w12])
        p.op("vector", lambda e: e.tensor_scalar(
            w12.ap[:, 4:5], mx8.ap[:, 0:1], w12.ap[:, 3:4], 2.5,
            ALU.mult, ALU.mult), ins=[mx8, w12], outs=[w12])
        p.op("vector", lambda e: e.tensor_scalar(
            w12.ap[:, 5:6], mx8.ap[:, 1:2], w12.ap[:, 3:4], 2.5,
            ALU.mult, ALU.mult), ins=[mx8, w12], outs=[w12])
        rts = rt_tok.ap[:, tt * NE:(tt + 1) * NE]
        p.op("vector", lambda e: e.tensor_scalar(
            oh.ap, iota16.ap, w12.ap[:, 0:1], None, ALU.is_equal),
            ins=[iota16, w12], outs=[oh])
        p.op("vector", lambda e, rts=rts: e.tensor_scalar(
            rts, oh.ap, w12.ap[:, 4:5], None, ALU.mult),
            ins=[oh, w12], outs=[rt_tok])
        p.op("vector", lambda e: e.tensor_scalar(
            oh.ap, iota16.ap, w12.ap[:, 1:2], None, ALU.is_equal),
            ins=[iota16, w12], outs=[oh])
        p.op("vector", lambda e, rts=rts: e.scalar_tensor_tensor(
            rts, oh.ap, w12.ap[:, 5:6], rts, ALU.mult, ALU.add),
            ins=[oh, w12, rt_tok], outs=[rt_tok])
        bank2 = pbk[2]
        p.op("tensor", lambda e, rts=rts, bank2=bank2: e.transpose(
            bank2.ap[0:16, 0:128], rts, ctx["ident"].ap),
            ins=[rt_tok, ctx["ident"]], outs=[bank2])
        p.op("vector", lambda e, tt=tt, bank2=bank2: e.tensor_copy(
            rT.ap[0:16, tt * 128:(tt + 1) * 128], bank2.ap[0:16, 0:128]),
            ins=[bank2], outs=[rT])
    for b in (gw, sc_fm, sc_tok, rt_tok, mx8, idx, w12, oh):
        ar.freebuf(b)
    return rT


def pass5(p, ar, pbk, dr, dbg, ctx):
    sel = ctx["sel"] = ar.alloc(32 * 128, "p5_sel")
    p.dma("sync_q", lambda e: e.dma_start(out=sel.ap[0:32, :], in_=dr["sel32"]),
          outs=[sel])
    xn_full = ar.alloc(FD * T, "p5_xn")
    p.dma("sync_q", lambda e: e.dma_start(
        out=xn_full.ap.rearrange("q (b t) -> q b t", t=T),
        in_=dr["xn_s"].rearrange("(b q) t -> q b t", q=128)),
        outs=[xn_full], extra_deps=[p.dbuf("xn_s", c).wr for c in range(NCHUNK)
                                    if p.dbuf("xn_s", c).wr])
    rT = routing(p, ar, pbk, dr, ctx, xn_full)
    wgb = ar.alloc(FD * M, "p5_wg")
    wub = ar.alloc(FD * M, "p5_wu")
    wdb = ar.alloc(4 * D, "p5_wd")
    gs = ar.alloc(4 * TC, "p5_gs")
    ub = ar.alloc(4 * TC, "p5_ub")
    eo = ar.alloc(FD * TC, "p5_eo")
    for e_i in range(NE + 1):
        shared = e_i == NE
        wg_src = dr["ws_gate"] if shared else dr["w_gate"][e_i]
        wu_src = dr["ws_up"] if shared else dr["w_up"][e_i]
        wd_src = dr["ws_down"] if shared else dr["w_down"][e_i]
        p.dma("sync_q", lambda e, s=wg_src: e.dma_start(
            out=wgb.ap.rearrange("q (b m) -> q b m", m=M),
            in_=s.rearrange("(b q) m -> q b m", q=128)), outs=[wgb])
        p.dma("sync_q", lambda e, s=wu_src: e.dma_start(
            out=wub.ap.rearrange("q (b m) -> q b m", m=M),
            in_=s.rearrange("(b q) m -> q b m", q=128)), outs=[wub])
        p.dma("sync_q", lambda e, s=wd_src: e.dma_start(
            out=wdb.ap.rearrange("q (b m) -> q b m", m=D),
            in_=s.rearrange("(b q) m -> q b m", q=128)), outs=[wdb])
        for c in range(NCHUNK):
            for mg in range(4):
                bank = pbk[mg % 2]
                for k in range(FD):
                    p.op("tensor", lambda e, k=k, mg=mg, c=c, bank=bank: e.matmul(
                        bank.ap[:, 0:TC],
                        wgb.ap[:, k * M + mg * 128:k * M + (mg + 1) * 128],
                        xn_full.ap[:, k * T + c * TC:k * T + (c + 1) * TC],
                        start=(k == 0), stop=(k == FD - 1)),
                        ins=[wgb, xn_full], outs=[bank])
                p.op("scalar", lambda e, mg=mg, bank=bank: e.activation(
                    gs.ap[:, mg * TC:(mg + 1) * TC], bank.ap[:, 0:TC], AF.Silu),
                    ins=[bank], outs=[gs])
            for mg in range(4):
                bank = pbk[mg % 2]
                for k in range(FD):
                    p.op("tensor", lambda e, k=k, mg=mg, c=c, bank=bank: e.matmul(
                        bank.ap[:, 0:TC],
                        wub.ap[:, k * M + mg * 128:k * M + (mg + 1) * 128],
                        xn_full.ap[:, k * T + c * TC:k * T + (c + 1) * TC],
                        start=(k == 0), stop=(k == FD - 1)),
                        ins=[wub, xn_full], outs=[bank])
                p.op("scalar", lambda e, mg=mg, bank=bank: e.copy(
                    ub.ap[:, mg * TC:(mg + 1) * TC], bank.ap[:, 0:TC]),
                    ins=[bank], outs=[ub])
            p.op("vector", lambda e: e.tensor_tensor(
                gs.ap, gs.ap, ub.ap, ALU.mult), ins=[gs, ub], outs=[gs])
            if not shared:
                bank6 = pbk[6]
                p.op("tensor", lambda e, e_i=e_i, c=c, bank6=bank6: e.matmul(
                    bank6.ap[:, 0:TC], sel.ap[0:16, e_i * 128:(e_i + 1) * 128],
                    rT.ap[0:16, c * TC:(c + 1) * TC], start=True, stop=True),
                    ins=[sel, rT], outs=[bank6])
                p.op("vector", lambda e, bank6=bank6: e.tensor_tensor(
                    gs.ap.rearrange("q (b t) -> q b t", t=TC),
                    gs.ap.rearrange("q (b t) -> q b t", t=TC),
                    bank6.ap[:, 0:TC].unsqueeze(1).broadcast_to((128, 4, TC)),
                    ALU.mult), ins=[gs, bank6], outs=[gs])
            for mo in range(FD):
                bank = pbk[2 + mo % 2]
                for k in range(4):
                    p.op("tensor", lambda e, k=k, mo=mo, bank=bank: e.matmul(
                        bank.ap[:, 0:TC],
                        wdb.ap[:, k * D + mo * 128:k * D + (mo + 1) * 128],
                        gs.ap[:, k * TC:(k + 1) * TC],
                        start=(k == 0), stop=(k == 3)),
                        ins=[wdb, gs], outs=[bank])
                p.op("scalar", lambda e, mo=mo, bank=bank: e.copy(
                    eo.ap[:, mo * TC:(mo + 1) * TC], bank.ap[:, 0:TC]),
                    ins=[bank], outs=[eo])
            d3 = dr["hacc_s"].rearrange("(b q) t -> q b t", q=128)[
                :, :, c * TC:(c + 1) * TC]
            s3 = eo.ap.rearrange("q (b t) -> q b t", t=TC)
            if e_i == 0:
                p.dma("gpsimd_q", lambda e, d3=d3, s3=s3: e.dma_start(
                    out=d3, in_=s3), ins=[eo], outs=[p.dbuf("hacc_s", c)])
            else:
                p.dma("gpsimd_q", lambda e, d3=d3, s3=s3: e.dma_start(
                    out=d3, in_=s3, accum_op=ALU.add),
                    ins=[eo], outs=[p.dbuf("hacc_s", c)])
    if "hacc" in dbg:
        hb = ar.alloc(FD * TC, "p5_hdbg")
        for c in range(NCHUNK):
            load_fm(p, dr["hacc_s"], hb, FD, c, name="hacc_s")
            store_fm(p, dr["dbg_hacc"], hb, FD, c)
        ar.freebuf(hb)
    for b in (wgb, wub, wdb, gs, ub, eo, xn_full, rT, sel):
        ar.freebuf(b)


def pass6a(p, ar, pbk, dr, dbg, ctx):
    tabs = ctx["tabs"]
    constk = ctx["constk"]
    ones_row = ctx["ones_row"]
    wz = [ar.alloc(DIN, f"p6_wz_{k}") for k in range(FD)]
    for k in range(FD):
        p.dma("sync_q", lambda e, k=k, b=wz[k]: e.dma_start(
            out=b.ap, in_=dr["w_in_z_T"][k * 128:(k + 1) * 128, :]), outs=[wz[k]])
    ones_col = ar.alloc(1, "p6_onec")
    p.op("vector", lambda e: e.memset(ones_col.ap, 1.0), outs=[ones_col])
    stat = ar.alloc(6 * TC, "p6_stat")
    sq = ar.alloc(TC, "p6_sq")
    for c in range(NCHUNK):
        xn_c = ar.alloc(FD * TC, f"p6_xn_{c}")
        load_fm(p, dr["xn_s"], xn_c, FD, c, name="xn_s")
        zs = ar.alloc(FDI * TC, f"p6_zs_{c}")
        for mi in range(FDI):
            bank = pbk[mi % 2]
            for k in range(FD):
                p.op("tensor", lambda e, k=k, mi=mi, bank=bank, xn_c=xn_c: e.matmul(
                    bank.ap[:, 0:TC], wz[k].ap[:, mi * 128:(mi + 1) * 128],
                    xn_c.ap[:, k * TC:(k + 1) * TC],
                    start=(k == 0), stop=(k == FD - 1)),
                    ins=[wz[k], xn_c], outs=[bank])
            p.op("scalar", lambda e, mi=mi, bank=bank: e.activation(
                zs.ap[:, mi * TC:(mi + 1) * TC], bank.ap[:, 0:TC], AF.Silu),
                ins=[bank], outs=[zs])
        ar.freebuf(xn_c)
        y = ar.alloc(FDI * TC, f"p6_y_{c}")
        xc_c = ar.alloc(FDI * TC, f"p6_xc_{c}")
        load_fm(p, dr["y_s"], y, FDI, c, name="y_s")
        load_fm(p, dr["xc_s"], xc_c, FDI, c, name="xc_s")
        bA, bB2 = pbk[2], pbk[3]
        for k in range(FDI):
            p.op("scalar", lambda e, k=k, y=y: e.activation(
                sq.ap, y.ap[:, k * TC:(k + 1) * TC], AF.Square),
                ins=[y], outs=[sq])
            p.op("tensor", lambda e, k=k, y=y, bA=bA: e.matmul(
                bA.ap[0:1, 0:TC], ones_col.ap, y.ap[:, k * TC:(k + 1) * TC],
                start=(k == 0), stop=(k == FDI - 1)),
                ins=[ones_col, y], outs=[bA])
            p.op("tensor", lambda e, k=k, bB2=bB2: e.matmul(
                bB2.ap[0:1, 0:TC], ones_col.ap, sq.ap,
                start=(k == 0), stop=(k == FDI - 1)),
                ins=[ones_col, sq], outs=[bB2])
        r0 = stat.ap[0:1, :]
        p.op("vector", lambda e, bA=bA: e.tensor_scalar(
            r0[:, 0:TC], bA.ap[0:1, 0:TC], 1.0 / DIN, None, ALU.mult),
            ins=[bA], outs=[stat])
        p.op("vector", lambda e, bB2=bB2: e.tensor_scalar(
            r0[:, TC:2 * TC], bB2.ap[0:1, 0:TC], 1.0 / DIN, None, ALU.mult),
            ins=[bB2], outs=[stat])
        p.op("vector", lambda e: e.tensor_tensor(
            r0[:, 2 * TC:3 * TC], r0[:, 0:TC], r0[:, 0:TC], ALU.mult),
            ins=[stat], outs=[stat])
        p.op("vector", lambda e: e.tensor_tensor(
            r0[:, 2 * TC:3 * TC], r0[:, TC:2 * TC], r0[:, 2 * TC:3 * TC],
            ALU.subtract), ins=[stat], outs=[stat])
        p.op("scalar", lambda e: e.activation(
            r0[:, 3 * TC:4 * TC], r0[:, 2 * TC:3 * TC], AF.Sqrt,
            bias=constk.ap[0:1, 0:1]), ins=[stat, constk], outs=[stat])
        p.op("vector", lambda e: e.reciprocal(
            r0[:, 4 * TC:5 * TC], r0[:, 3 * TC:4 * TC]), ins=[stat], outs=[stat])
        bMu, bRs = pbk[4], pbk[5]
        p.op("tensor", lambda e, bMu=bMu: e.matmul(
            bMu.ap[:, 0:TC], ones_row.ap[0:1, 0:128], r0[:, 0:TC],
            start=True, stop=True), ins=[ones_row, stat], outs=[bMu])
        p.op("tensor", lambda e, bRs=bRs: e.matmul(
            bRs.ap[:, 0:TC], ones_row.ap[0:1, 0:128], r0[:, 4 * TC:5 * TC],
            start=True, stop=True), ins=[ones_row, stat], outs=[bRs])
        for k in range(FDI):
            yk = y.ap[:, k * TC:(k + 1) * TC]
            p.op("vector", lambda e, yk=yk, bMu=bMu: e.tensor_tensor(
                yk, yk, bMu.ap[:, 0:TC], ALU.subtract), ins=[y, bMu], outs=[y])
            p.op("vector", lambda e, yk=yk, bRs=bRs: e.tensor_tensor(
                yk, yk, bRs.ap[:, 0:TC], ALU.mult), ins=[y, bRs], outs=[y])
            p.op("vector", lambda e, yk=yk, k=k: e.tensor_scalar(
                yk, yk, tabs["ln_g_pc"].ap[:, k:k + 1],
                tabs["ln_b_pc"].ap[:, k:k + 1], ALU.mult, ALU.add),
                ins=[y, tabs["ln_g_pc"], tabs["ln_b_pc"]], outs=[y])
            p.op("vector", lambda e, yk=yk, k=k, xc_c=xc_c: e.scalar_tensor_tensor(
                yk, xc_c.ap[:, k * TC:(k + 1) * TC],
                tabs["d_param_pc"].ap[:, k:k + 1], yk, ALU.mult, ALU.add),
                ins=[xc_c, tabs["d_param_pc"], y], outs=[y])
            p.op("vector", lambda e, yk=yk, k=k, zs=zs: e.tensor_tensor(
                yk, yk, zs.ap[:, k * TC:(k + 1) * TC], ALU.mult),
                ins=[y, zs], outs=[y])
        store_fm(p, dr["yg_s"], y, FDI, c, name="yg_s")
        if "yg" in dbg:
            store_fm(p, dr["dbg_yg"], y, FDI, c)
        for b in (zs, y, xc_c):
            ar.freebuf(b)
    for b in wz + [ones_col, stat, sq]:
        ar.freebuf(b)


def pass6b(p, ar, pbk, dr, dbg, ctx, chunks=None, wo=None):
    ident = ctx["ident"]
    if wo is None:
        wo = ctx["p7_wo"] = [ar.alloc(D, f"p7_wo_{k}") for k in range(FDI)]
        for k in range(FDI):
            p.dma("sync_q", lambda e, k=k, b=wo[k]: e.dma_start(
                out=b.ap, in_=dr["w_out_T"][k * 128:(k + 1) * 128, :]),
                outs=[wo[k]])
    for c in (chunks if chunks is not None else range(NCHUNK)):
        yg = ar.alloc(FDI * TC, f"p7_yg_{c}")
        load_fm(p, dr["yg_s"], yg, FDI, c, name="yg_s")
        gfm = ar.alloc(FD * TC, f"p7_g_{c}")
        load_fm(p, dr["hacc_s"], gfm, FD, c, name="hacc_s")
        s_tok = ar.alloc(D, f"p7_s_{c}")
        g_tok = ar.alloc(D, f"p7_gt_{c}")
        x_tok = ar.alloc(D, f"p7_x_{c}")
        b1 = ar.alloc(D, f"p7_b1_{c}")
        b2 = ar.alloc(D, f"p7_b2_{c}")
        moedep = ([ctx["moe_sc"][-1]] if "moe_sc" in ctx else [])
        for tt in range(4):
            t0 = c * TC + tt * 128
            p.dma("sync_q", lambda e, t0=t0, x_tok=x_tok: e.dma_start(
                out=x_tok.ap, in_=dr["x"][t0:t0 + 128, :]), outs=[x_tok])
            p.dma("sync_q", lambda e, t0=t0, b1=b1: e.dma_start(
                out=b1.ap, in_=dr["moebuf"][t0:t0 + 128, :]), outs=[b1],
                extra_deps=moedep)
            p.dma("sync_q", lambda e, t0=t0, b2=b2: e.dma_start(
                out=b2.ap, in_=dr["moebuf"][T + t0:T + t0 + 128, :]), outs=[b2],
                extra_deps=moedep)
            bL, bH = pbk[0], pbk[1]
            for k in range(FDI):
                p.op("tensor", lambda e, k=k, tt=tt, yg=yg, bL=bL: e.matmul(
                    bL.ap[:, 0:TC],
                    yg.ap[:, k * TC + tt * 128:k * TC + (tt + 1) * 128],
                    wo[k].ap[:, 0:512],
                    start=(k == 0), stop=(k == FDI - 1)),
                    ins=[yg, wo[k]], outs=[bL])
                p.op("tensor", lambda e, k=k, tt=tt, yg=yg, bH=bH: e.matmul(
                    bH.ap[:, 0:TC],
                    yg.ap[:, k * TC + tt * 128:k * TC + (tt + 1) * 128],
                    wo[k].ap[:, 512:1024],
                    start=(k == 0), stop=(k == FDI - 1)),
                    ins=[yg, wo[k]], outs=[bH])
            p.op("vector", lambda e, bL=bL, s_tok=s_tok: e.tensor_copy(
                s_tok.ap[:, 0:512], bL.ap[:, 0:TC]), ins=[bL], outs=[s_tok])
            p.op("vector", lambda e, bH=bH, s_tok=s_tok: e.tensor_copy(
                s_tok.ap[:, 512:1024], bH.ap[:, 0:TC]), ins=[bH], outs=[s_tok])
            bT0, bT1 = pbk[2], pbk[3]
            for f in range(FD):
                bk = bT0 if f < 4 else bT1
                p.op("tensor", lambda e, f=f, tt=tt, gfm=gfm, bk=bk: e.transpose(
                    bk.ap[:, (f % 4) * 128:(f % 4 + 1) * 128],
                    gfm.ap[:, f * TC + tt * 128:f * TC + (tt + 1) * 128],
                    ident.ap), ins=[gfm, ident], outs=[bk])
            p.op("vector", lambda e, bT0=bT0, g_tok=g_tok: e.tensor_copy(
                g_tok.ap[:, 0:512], bT0.ap[:, 0:512]), ins=[bT0], outs=[g_tok])
            p.op("vector", lambda e, bT1=bT1, g_tok=g_tok: e.tensor_copy(
                g_tok.ap[:, 512:1024], bT1.ap[:, 0:512]), ins=[bT1], outs=[g_tok])
            p.op("vector", lambda e, g_tok=g_tok, b1=b1: e.tensor_tensor(
                g_tok.ap, g_tok.ap, b1.ap, ALU.add), ins=[g_tok, b1], outs=[g_tok])
            p.op("vector", lambda e, g_tok=g_tok, b2=b2: e.tensor_tensor(
                g_tok.ap, g_tok.ap, b2.ap, ALU.add), ins=[g_tok, b2], outs=[g_tok])
            p.op("scalar", lambda e, g_tok=g_tok: e.activation(
                g_tok.ap, g_tok.ap, AF.Silu), ins=[g_tok], outs=[g_tok])
            p.op("vector", lambda e, s_tok=s_tok, g_tok=g_tok: e.tensor_tensor(
                s_tok.ap, s_tok.ap, g_tok.ap, ALU.mult),
                ins=[s_tok, g_tok], outs=[s_tok])
            p.op("vector", lambda e, s_tok=s_tok, x_tok=x_tok: e.tensor_tensor(
                s_tok.ap, s_tok.ap, x_tok.ap, ALU.add),
                ins=[s_tok, x_tok], outs=[s_tok])
            p.dma("gpsimd_q", lambda e, t0=t0, s_tok=s_tok: e.dma_start(
                out=dr["out"][t0:t0 + 128, :], in_=s_tok.ap),
                ins=[s_tok], outs=[p.dbuf("out")])
        for b in (yg, gfm, s_tok, g_tok, x_tok, b1, b2):
            ar.freebuf(b)


def routing_sparse(p, ar, pbk, dr, ctx, xn_full):
    iota16 = ctx["iota16"]
    ident = ctx["ident"]
    gw = ar.alloc(FD * NE, "p5_gw")
    p.dma("sync_q", lambda e: e.dma_start(
        out=gw.ap.rearrange("q (b s) -> q b s", s=NE),
        in_=dr["gate_w_T"].rearrange("(b q) s -> q b s", q=128)), outs=[gw])
    sc_fm = ar.alloc(T, "p5_scfm")
    for c in range(NCHUNK):
        bank = pbk[0]
        for k in range(FD):
            p.op("tensor", lambda e, k=k, c=c, bank=bank: e.matmul(
                bank.ap[0:16, 0:TC], gw.ap[:, k * NE:(k + 1) * NE],
                xn_full.ap[:, k * T + c * TC:k * T + (c + 1) * TC],
                start=(k == 0), stop=(k == FD - 1)),
                ins=[gw, xn_full], outs=[bank])
        p.op("scalar", lambda e, c=c, bank=bank: e.activation(
            sc_fm.ap[0:16, c * TC:(c + 1) * TC], bank.ap[0:16, 0:TC],
            AF.Sigmoid), ins=[bank], outs=[sc_fm])
    tok0 = ar.alloc(NE, "p5_tok0")
    tok1 = ar.alloc(NE, "p5_tok1")
    p.op("gpsimd", lambda e: e.iota(tok0.ap.bitcast(U32), [[128, 16]],
                                    base=0, channel_multiplier=1), outs=[tok0])
    p.op("gpsimd", lambda e: e.iota(tok1.ap.bitcast(U32), [[128, 16]],
                                    base=T, channel_multiplier=1), outs=[tok1])
    NG = NE * CAP // 128
    zf = ar.alloc(NG, "p5_zf")
    p.op("vector", lambda e: e.memset(zf.ap, 0.0), outs=[zf])
    dumpi = ar.alloc(NG, "p5_dump")
    p.op("gpsimd", lambda e: e.iota(dumpi.ap.bitcast(U32), [[1, NG]],
                                    base=2 * T, channel_multiplier=NG), outs=[dumpi])
    for nm, buf in (("glist", zf), ("dlist", dumpi), ("wlist", zf)):
        s2 = (buf.ap[:, 0:NG].bitcast(U32) if nm != "wlist"
              else buf.ap[:, 0:NG])
        p.dma("gpsimd_q", lambda e, nm=nm, s2=s2: e.dma_start(
            out=dr[nm], in_=s2), ins=[buf], outs=[p.dbuf(nm)])
    sc_tok = ar.alloc(NE, "p5_sctok")
    oh1 = ar.alloc(16 * NE, "p5_oh1")
    oh2 = ar.alloc(16 * NE, "p5_oh2")
    idxf = ar.alloc(32, "p5_idxf")
    wp = ar.alloc(32, "p5_wp")
    msk_T = ar.alloc(T, "p5_mskT")
    mx8 = ar.alloc(8, "p5_mx8")
    idx = ar.alloc(8, "p5_idx")
    sm = ar.alloc(4, "p5_sm")
    t16 = ar.alloc(NE, "p5_t16")
    for tt in range(16):
        bank = pbk[4]
        p.op("tensor", lambda e, tt=tt, bank=bank: e.transpose(
            bank.ap[:, 0:16], sc_fm.ap[0:16, tt * 128:(tt + 1) * 128],
            ident.ap[0:16, 0:16]), ins=[sc_fm, ident], outs=[bank])
        p.op("vector", lambda e, bank=bank: e.tensor_copy(
            sc_tok.ap, bank.ap[:, 0:16]), ins=[bank], outs=[sc_tok])
        p.op("vector", lambda e: e.max(mx8.ap, sc_tok.ap),
             ins=[sc_tok], outs=[mx8], reducy=True)
        p.op("vector", lambda e: e.max_index(
            idx.ap.bitcast(U32), mx8.ap, sc_tok.ap),
            ins=[sc_tok, mx8], outs=[idx], reducy=True)
        p.op("vector", lambda e, tt=tt: e.tensor_copy(
            idxf.ap[:, 2 * tt:2 * tt + 2], idx.ap[:, 0:2].bitcast(U32)),
            ins=[idx], outs=[idxf])
        p.op("vector", lambda e: e.tensor_tensor(
            sm.ap[:, 0:1], mx8.ap[:, 0:1], mx8.ap[:, 1:2], ALU.add),
            ins=[mx8], outs=[sm])
        p.op("vector", lambda e: e.tensor_scalar(
            sm.ap[:, 0:1], sm.ap[:, 0:1], 1e-20, None, ALU.add),
            ins=[sm], outs=[sm])
        p.op("vector", lambda e: e.reciprocal(sm.ap[:, 1:2], sm.ap[:, 0:1]),
             ins=[sm], outs=[sm])
        p.op("vector", lambda e, tt=tt: e.tensor_scalar(
            wp.ap[:, 2 * tt:2 * tt + 2], mx8.ap[:, 0:2], sm.ap[:, 1:2], 2.5,
            ALU.mult, ALU.mult), ins=[mx8, sm], outs=[wp])
        o1 = oh1.ap[:, tt * NE:(tt + 1) * NE]
        o2 = oh2.ap[:, tt * NE:(tt + 1) * NE]
        p.op("vector", lambda e, o1=o1, tt=tt: e.tensor_scalar(
            o1, iota16.ap, idxf.ap[:, 2 * tt:2 * tt + 1], None, ALU.is_equal),
            ins=[iota16, idxf], outs=[oh1])
        p.op("vector", lambda e, o2=o2, tt=tt: e.tensor_scalar(
            o2, iota16.ap, idxf.ap[:, 2 * tt + 1:2 * tt + 2], None,
            ALU.is_equal), ins=[iota16, idxf], outs=[oh2])
        p.op("vector", lambda e, o1=o1, o2=o2: e.tensor_tensor(
            t16.ap, o1, o2, ALU.add), ins=[oh1, oh2], outs=[t16])
        yield
        bank3 = pbk[5]
        p.op("tensor", lambda e, bank3=bank3: e.transpose(
            bank3.ap[0:16, 0:128], t16.ap, ident.ap),
            ins=[t16, ident], outs=[bank3])
        p.op("vector", lambda e, tt=tt, bank3=bank3: e.tensor_copy(
            msk_T.ap[0:16, tt * 128:(tt + 1) * 128], bank3.ap[0:16, 0:128]),
            ins=[bank3], outs=[msk_T])
        yield
    zrow = ar.alloc(T, "p5_zrow")
    p.op("vector", lambda e: e.memset(zrow.ap[0:16, :], 0.0), outs=[zrow])
    pos_T = ar.alloc(T, "p5_posT")
    p.op("vector", lambda e: e.tensor_tensor_scan(
        pos_T.ap[0:16, :], msk_T.ap[0:16, :], zrow.ap[0:16, :], 0.0,
        ALU.add, ALU.add), ins=[msk_T, zrow], outs=[pos_T])
    pos_tok = ar.alloc(NE, "p5_postok")
    cols = ar.alloc(8, "p5_cols")
    du = ar.alloc(32, "p5_du")
    sc_toks = ctx.setdefault("sc_toks", [])
    for tt in range(16):
        bank = pbk[4]
        p.op("tensor", lambda e, tt=tt, bank=bank: e.transpose(
            bank.ap[:, 0:16], pos_T.ap[0:16, tt * 128:(tt + 1) * 128],
            ident.ap[0:16, 0:16]), ins=[pos_T, ident], outs=[bank])
        p.op("vector", lambda e, bank=bank: e.tensor_copy(
            pos_tok.ap, bank.ap[:, 0:16]), ins=[bank], outs=[pos_tok])
        for r, ohb, tokb in ((0, oh1, tok0), (1, oh2, tok1)):
            o = ohb.ap[:, tt * NE:(tt + 1) * NE]
            p.op("vector", lambda e, o=o: e.tensor_tensor(
                t16.ap, o, pos_tok.ap, ALU.mult),
                ins=[ohb, pos_tok], outs=[t16])
            p.op("vector", lambda e: e.tensor_reduce(
                cols.ap[:, 0:1], t16.ap, mybir.AxisListType.X, ALU.add),
                ins=[t16], outs=[cols], reducy=True)
            p.op("vector", lambda e, tt=tt, r=r: e.tensor_scalar(
                cols.ap[:, 1:2], idxf.ap[:, 2 * tt + r:2 * tt + r + 1],
                float(CAP), -1.0, ALU.mult, ALU.add),
                ins=[idxf], outs=[cols])
            p.op("vector", lambda e: e.tensor_tensor(
                cols.ap[:, 1:2], cols.ap[:, 1:2], cols.ap[:, 0:1], ALU.add),
                ins=[cols], outs=[cols])
            dslot = 2 * tt + r
            p.op("vector", lambda e, dslot=dslot: e.tensor_copy(
                du.ap[:, dslot:dslot + 1].bitcast(U32), cols.ap[:, 1:2]),
                ins=[cols], outs=[du])
            ioff = IndirectOffsetOnAxis(
                ap=du.ap[:, dslot:dslot + 1].bitcast(U32), axis=0)
            tk_g = tok0.ap[:, tt:tt + 1].bitcast(U32)
            tk_d = tokb.ap[:, tt:tt + 1].bitcast(U32)
            wsl = wp.ap[:, 2 * tt + r:2 * tt + r + 1]
            sc_toks.append(p.dma("gpsimd_q", lambda e, tk_g=tk_g, ioff=ioff:
                                 e.indirect_dma_start(out=dr["glist"],
                                                      out_offset=ioff,
                                                      in_=tk_g, in_offset=None),
                                 ins=[tok0, du],
                                 extra_deps=[p.dbuf("glist").wr]))
            sc_toks.append(p.dma("gpsimd_q", lambda e, tk_d=tk_d, ioff=ioff:
                                 e.indirect_dma_start(out=dr["dlist"],
                                                      out_offset=ioff,
                                                      in_=tk_d, in_offset=None),
                                 ins=[tokb, du],
                                 extra_deps=[p.dbuf("dlist").wr]))
            sc_toks.append(p.dma("gpsimd_q", lambda e, wsl=wsl, ioff=ioff:
                                 e.indirect_dma_start(out=dr["wlist"],
                                                      out_offset=ioff,
                                                      in_=wsl, in_offset=None),
                                 ins=[wp, du],
                                 extra_deps=[p.dbuf("wlist").wr]))
    for b in (gw, sc_fm, sc_tok, oh1, oh2, idxf, wp, msk_T, mx8, idx, sm,
              t16, zrow, pos_T, pos_tok, cols, du, tok0, tok1, zf, dumpi):
        ar.freebuf(b)
    return sc_toks


def pass5s(p, ar, pbk, dr, dbg, ctx):
    ones_row = ctx["ones_row"]
    ident = ctx["ident"]
    xn_full = ar.alloc(FD * T, "p5_xn")
    p.dma("sync_q", lambda e: e.dma_start(
        out=xn_full.ap.rearrange("q (b t) -> q b t", t=T),
        in_=dr["xn_s"].rearrange("(b q) t -> q b t", q=128)),
        outs=[xn_full],
        extra_deps=[tk for tk in (p.dbuf("xn_s", c).wr for c in range(NCHUNK))
                    if tk])
    sc_toks = routing_sparse(p, ar, pbk, dr, ctx, xn_full)
    wgb = ar.alloc(FD * M, "p5_wg")
    wub = ar.alloc(FD * M, "p5_wu")
    wdb = ar.alloc(4 * D, "p5_wd")
    gs = ar.alloc(4 * TC, "p5_gs")
    ub = ar.alloc(4 * TC, "p5_ub")
    eo = ar.alloc(FD * TC, "p5_eo")
    p.dma("sync_q", lambda e: e.dma_start(
        out=wgb.ap.rearrange("q (b m) -> q b m", m=M),
        in_=dr["ws_gate"].rearrange("(b q) m -> q b m", q=128)), outs=[wgb])
    p.dma("sync_q", lambda e: e.dma_start(
        out=wub.ap.rearrange("q (b m) -> q b m", m=M),
        in_=dr["ws_up"].rearrange("(b q) m -> q b m", q=128)), outs=[wub])
    p.dma("sync_q", lambda e: e.dma_start(
        out=wdb.ap.rearrange("q (b m) -> q b m", m=D),
        in_=dr["ws_down"].rearrange("(b q) m -> q b m", q=128)), outs=[wdb])
    for c in range(NCHUNK):
        for mg in range(4):
            bank = pbk[mg % 2]
            for k in range(FD):
                p.op("tensor", lambda e, k=k, mg=mg, c=c, bank=bank: e.matmul(
                    bank.ap[:, 0:TC],
                    wgb.ap[:, k * M + mg * 128:k * M + (mg + 1) * 128],
                    xn_full.ap[:, k * T + c * TC:k * T + (c + 1) * TC],
                    start=(k == 0), stop=(k == FD - 1)),
                    ins=[wgb, xn_full], outs=[bank])
            p.op("scalar", lambda e, mg=mg, bank=bank: e.activation(
                gs.ap[:, mg * TC:(mg + 1) * TC], bank.ap[:, 0:TC], AF.Silu),
                ins=[bank], outs=[gs])
        for mg in range(4):
            bank = pbk[mg % 2]
            for k in range(FD):
                p.op("tensor", lambda e, k=k, mg=mg, c=c, bank=bank: e.matmul(
                    bank.ap[:, 0:TC],
                    wub.ap[:, k * M + mg * 128:k * M + (mg + 1) * 128],
                    xn_full.ap[:, k * T + c * TC:k * T + (c + 1) * TC],
                    start=(k == 0), stop=(k == FD - 1)),
                    ins=[wub, xn_full], outs=[bank])
            p.op("scalar", lambda e, mg=mg, bank=bank: e.copy(
                ub.ap[:, mg * TC:(mg + 1) * TC], bank.ap[:, 0:TC]),
                ins=[bank], outs=[ub])
        p.op("vector", lambda e: e.tensor_tensor(
            gs.ap, gs.ap, ub.ap, ALU.mult), ins=[gs, ub], outs=[gs])
        for mo in range(FD):
            bank = pbk[2 + mo % 2]
            for k in range(4):
                p.op("tensor", lambda e, k=k, mo=mo, bank=bank: e.matmul(
                    bank.ap[:, 0:TC],
                    wdb.ap[:, k * D + mo * 128:k * D + (mo + 1) * 128],
                    gs.ap[:, k * TC:(k + 1) * TC],
                    start=(k == 0), stop=(k == 3)),
                    ins=[wdb, gs], outs=[bank])
            p.op("scalar", lambda e, mo=mo, bank=bank: e.copy(
                eo.ap[:, mo * TC:(mo + 1) * TC], bank.ap[:, 0:TC]),
                ins=[bank], outs=[eo])
        store_fm(p, dr["hacc_s"], eo, FD, c, name="hacc_s")
    ar.freebuf(xn_full)
    NP = CAP // 128
    xg_tok = [ar.alloc(D, f"p5_xg{i}") for i in range(NP)]
    xg_fm = ar.alloc(FD * CAP, "p5_xgfm")
    idxg = ar.alloc(2 * NP, "p5_idxg")
    wrow = ar.alloc(CAP, "p5_wrow")
    gs2 = ar.alloc(4 * CAP, "p5_gs2")
    ub2 = ar.alloc(4 * CAP, "p5_ub2")
    eo2 = ar.alloc(FD * CAP, "p5_eo2")
    eo_tok = [ar.alloc(D, f"p5_eot{i}") for i in range(NP)]
    gather_dep = [sc_toks[-1], p.dbuf("glist").wr, p.dbuf("dlist").wr,
                  p.dbuf("wlist").wr]
    gather_dep += [tk for tk in (p.dbuf("xn_tok_s", c).wr
                                 for c in range(NCHUNK)) if tk]
    first = True
    for e_i in range(NE):
        p.dma("sync_q", lambda e, e_i=e_i: e.dma_start(
            out=wgb.ap.rearrange("q (b m) -> q b m", m=M),
            in_=dr["w_gate"][e_i].rearrange("(b q) m -> q b m", q=128)),
            outs=[wgb])
        p.dma("sync_q", lambda e, e_i=e_i: e.dma_start(
            out=wub.ap.rearrange("q (b m) -> q b m", m=M),
            in_=dr["w_up"][e_i].rearrange("(b q) m -> q b m", q=128)),
            outs=[wub])
        p.dma("sync_q", lambda e, e_i=e_i: e.dma_start(
            out=wdb.ap.rearrange("q (b m) -> q b m", m=D),
            in_=dr["w_down"][e_i].rearrange("(b q) m -> q b m", q=128)),
            outs=[wdb])
        dep = gather_dep if first else []
        first = False
        for i in range(NP):
            r0 = e_i * CAP + i * 128
            p.dma("sync_q", lambda e, r0=r0, i=i: e.dma_start(
                out=idxg.ap[:, i:i + 1].bitcast(U32),
                in_=dr["glist"][r0:r0 + 128, :]), outs=[idxg], extra_deps=dep)
            p.dma("sync_q", lambda e, r0=r0, i=i: e.dma_start(
                out=idxg.ap[:, NP + i:NP + i + 1].bitcast(U32),
                in_=dr["dlist"][r0:r0 + 128, :]), outs=[idxg], extra_deps=dep)
        p.dma("sync_q", lambda e, e_i=e_i: e.dma_start(
            out=wrow.ap[0:1, :],
            in_=dr["wlist"][e_i * CAP:(e_i + 1) * CAP, 0:1].rearrange(
                "a one -> one a")), outs=[wrow], extra_deps=dep)
        for i in range(NP):
            ioff = IndirectOffsetOnAxis(ap=idxg.ap[:, i:i + 1].bitcast(U32),
                                        axis=0)
            p.dma("gpsimd_q", lambda e, i=i, ioff=ioff: e.indirect_dma_start(
                out=xg_tok[i].ap, out_offset=None,
                in_=dr["xn_tok_s"], in_offset=ioff),
                ins=[idxg], outs=[xg_tok[i]])
        for f in range(FD):
            bank = pbk[4 + f % 2]
            for i in range(NP):
                p.op("tensor", lambda e, f=f, i=i, bank=bank: e.transpose(
                    bank.ap[:, i * 128:(i + 1) * 128],
                    xg_tok[i].ap[:, f * 128:(f + 1) * 128], ident.ap),
                    ins=[xg_tok[i], ident], outs=[bank])
            p.op("vector", lambda e, f=f, bank=bank: e.tensor_copy(
                xg_fm.ap[:, f * CAP:(f + 1) * CAP], bank.ap[:, 0:CAP]),
                ins=[bank], outs=[xg_fm])
        for mg in range(4):
            bank = pbk[mg % 2]
            for k in range(FD):
                p.op("tensor", lambda e, k=k, mg=mg, bank=bank: e.matmul(
                    bank.ap[:, 0:CAP],
                    wgb.ap[:, k * M + mg * 128:k * M + (mg + 1) * 128],
                    xg_fm.ap[:, k * CAP:(k + 1) * CAP],
                    start=(k == 0), stop=(k == FD - 1)),
                    ins=[wgb, xg_fm], outs=[bank])
            p.op("scalar", lambda e, mg=mg, bank=bank: e.activation(
                gs2.ap[:, mg * CAP:(mg + 1) * CAP], bank.ap[:, 0:CAP], AF.Silu),
                ins=[bank], outs=[gs2])
        for mg in range(4):
            bank = pbk[mg % 2]
            for k in range(FD):
                p.op("tensor", lambda e, k=k, mg=mg, bank=bank: e.matmul(
                    bank.ap[:, 0:CAP],
                    wub.ap[:, k * M + mg * 128:k * M + (mg + 1) * 128],
                    xg_fm.ap[:, k * CAP:(k + 1) * CAP],
                    start=(k == 0), stop=(k == FD - 1)),
                    ins=[wub, xg_fm], outs=[bank])
            p.op("scalar", lambda e, mg=mg, bank=bank: e.copy(
                ub2.ap[:, mg * CAP:(mg + 1) * CAP], bank.ap[:, 0:CAP]),
                ins=[bank], outs=[ub2])
        p.op("vector", lambda e: e.tensor_tensor(
            gs2.ap, gs2.ap, ub2.ap, ALU.mult), ins=[gs2, ub2], outs=[gs2])
        bank6 = pbk[6]
        p.op("tensor", lambda e, bank6=bank6: e.matmul(
            bank6.ap[:, 0:CAP], ones_row.ap[0:1, 0:128], wrow.ap[0:1, :],
            start=True, stop=True), ins=[ones_row, wrow], outs=[bank6])
        p.op("vector", lambda e, bank6=bank6: e.tensor_tensor(
            gs2.ap.rearrange("q (b t) -> q b t", t=CAP),
            gs2.ap.rearrange("q (b t) -> q b t", t=CAP),
            bank6.ap[:, 0:CAP].unsqueeze(1).broadcast_to((128, 4, CAP)),
            ALU.mult), ins=[gs2, bank6], outs=[gs2])
        for mo in range(FD):
            bank = pbk[2 + mo % 2]
            for k in range(4):
                p.op("tensor", lambda e, k=k, mo=mo, bank=bank: e.matmul(
                    bank.ap[:, 0:CAP],
                    wdb.ap[:, k * D + mo * 128:k * D + (mo + 1) * 128],
                    gs2.ap[:, k * CAP:(k + 1) * CAP],
                    start=(k == 0), stop=(k == 3)),
                    ins=[wdb, gs2], outs=[bank])
            p.op("scalar", lambda e, mo=mo, bank=bank: e.copy(
                eo2.ap[:, mo * CAP:(mo + 1) * CAP], bank.ap[:, 0:CAP]),
                ins=[bank], outs=[eo2])
        for i in range(NP):
            bank = pbk[4 + i % 2]
            for f in range(FD):
                bk = bank if f < 4 else pbk[6]
                p.op("tensor", lambda e, f=f, i=i, bk=bk: e.transpose(
                    bk.ap[:, (f % 4) * 128:(f % 4 + 1) * 128],
                    eo2.ap[:, f * CAP + i * 128:f * CAP + (i + 1) * 128],
                    ident.ap), ins=[eo2, ident], outs=[bk])
                if f == 3:
                    p.op("vector", lambda e, i=i, bk=bk: e.tensor_copy(
                        eo_tok[i].ap[:, 0:512], bk.ap[:, 0:512]),
                        ins=[bk], outs=[eo_tok[i]])
                if f == 7:
                    p.op("vector", lambda e, i=i, bk=bk: e.tensor_copy(
                        eo_tok[i].ap[:, 512:1024], bk.ap[:, 0:512]),
                        ins=[bk], outs=[eo_tok[i]])
            soff = IndirectOffsetOnAxis(
                ap=idxg.ap[:, NP + i:NP + i + 1].bitcast(U32), axis=0)
            ctx.setdefault("moe_sc", []).append(p.dma(
                "gpsimd_q", lambda e, i=i, soff=soff: e.indirect_dma_start(
                    out=dr["moebuf"], out_offset=soff,
                    in_=eo_tok[i].ap, in_offset=None),
                ins=[eo_tok[i], idxg]))
    for b in xg_tok + eo_tok + [xg_fm, idxg, wrow, gs2, ub2, eo2,
                                wgb, wub, wdb, gs, ub, eo]:
        ar.freebuf(b)



def pass3_store_bpcp(p, ar, dr, bpcp):
    p.dma("gpsimd_q", lambda e: e.dma_start(out=dr["bpcp_s"], in_=bpcp.ap[0:32, :]),
          ins=[bpcp], outs=[p.dbuf("bpcp_s")])
    ar.freebuf(bpcp)
    p.qfence("gpsimd_q")


def routing_sparse_v3(p, ar, pbk, dr, ctx):
    xn_c = ar.alloc(FD * TC, "p5r_xn")
    gw = ar.alloc(FD * NE, "p5_gw")
    p.dma("sync_q", lambda e: e.dma_start(
        out=gw.ap.rearrange("q (b s) -> q b s", s=NE),
        in_=dr["gate_w_T"].rearrange("(b q) s -> q b s", q=128)), outs=[gw])
    sc_fm = ar.alloc(T, "p5_scfm")
    for c in range(NCHUNK):
        load_fm(p, dr["xn_s"], xn_c, FD, c, name="xn_s")
        bank = pbk[3]
        for k in range(FD):
            p.op("tensor", lambda e, k=k, c=c, bank=bank, xn_c=xn_c: e.matmul(
                bank.ap[0:16, 0:TC], gw.ap[:, k * NE:(k + 1) * NE],
                xn_c.ap[:, k * TC:(k + 1) * TC],
                start=(k == 0), stop=(k == FD - 1)),
                ins=[gw, xn_c], outs=[bank])
        p.op("scalar", lambda e, c=c, bank=bank: e.activation(
            sc_fm.ap[0:16, c * TC:(c + 1) * TC], bank.ap[0:16, 0:TC],
            AF.Sigmoid), ins=[bank], outs=[sc_fm])
        yield
    ar.freebuf(xn_c)
    ar.freebuf(gw)
    yield from routing_lists(p, ar, pbk, dr, ctx, sc_fm)


def routing_lists(p, ar, pbk, dr, ctx, sc_fm):
    iota16 = ctx["iota16"]
    ident = ctx["ident"]
    tok0 = ar.alloc(NE, "p5_tok0")
    tok1 = ar.alloc(NE, "p5_tok1")
    p.op("gpsimd", lambda e: e.iota(tok0.ap.bitcast(U32), [[128, 16]],
                                    base=0, channel_multiplier=1), outs=[tok0])
    p.op("gpsimd", lambda e: e.iota(tok1.ap.bitcast(U32), [[128, 16]],
                                    base=T, channel_multiplier=1), outs=[tok1])
    NG = NE * CAP // 128
    zf = ar.alloc(NG, "p5_zf")
    p.op("vector", lambda e: e.memset(zf.ap, 0.0), outs=[zf])
    dumpi = ar.alloc(NG, "p5_dump")
    p.op("gpsimd", lambda e: e.iota(dumpi.ap.bitcast(U32), [[1, NG]],
                                    base=2 * T, channel_multiplier=NG), outs=[dumpi])
    for nm, buf in (("glist", zf), ("dlist", dumpi), ("wlist", zf)):
        s2 = (buf.ap[:, 0:NG].bitcast(U32) if nm != "wlist"
              else buf.ap[:, 0:NG])
        p.dma("gpsimd_q", lambda e, nm=nm, s2=s2: e.dma_start(
            out=dr[nm], in_=s2), ins=[buf], outs=[p.dbuf(nm)])
    sc_tok = ar.alloc(NE, "p5_sctok")
    oh1 = ar.alloc(16 * NE, "p5_oh1")
    oh2 = ar.alloc(16 * NE, "p5_oh2")
    idxf = ar.alloc(32, "p5_idxf")
    wp = ar.alloc(32, "p5_wp")
    msk_T = ar.alloc(T, "p5_mskT")
    mx8 = ar.alloc(8, "p5_mx8")
    idx = ar.alloc(8, "p5_idx")
    sm = ar.alloc(4, "p5_sm")
    t16 = ar.alloc(NE, "p5_t16")
    for tt in range(16):
        bank = pbk[4]
        p.op("tensor", lambda e, tt=tt, bank=bank: e.transpose(
            bank.ap[:, 0:16], sc_fm.ap[0:16, tt * 128:(tt + 1) * 128],
            ident.ap[0:16, 0:16]), ins=[sc_fm, ident], outs=[bank])
        p.op("vector", lambda e, bank=bank: e.tensor_copy(
            sc_tok.ap, bank.ap[:, 0:16]), ins=[bank], outs=[sc_tok])
        p.op("vector", lambda e: e.max(mx8.ap, sc_tok.ap),
             ins=[sc_tok], outs=[mx8], reducy=True)
        p.op("vector", lambda e: e.max_index(
            idx.ap.bitcast(U32), mx8.ap, sc_tok.ap),
            ins=[sc_tok, mx8], outs=[idx], reducy=True)
        p.op("vector", lambda e, tt=tt: e.tensor_copy(
            idxf.ap[:, 2 * tt:2 * tt + 2], idx.ap[:, 0:2].bitcast(U32)),
            ins=[idx], outs=[idxf])
        p.op("vector", lambda e: e.tensor_tensor(
            sm.ap[:, 0:1], mx8.ap[:, 0:1], mx8.ap[:, 1:2], ALU.add),
            ins=[mx8], outs=[sm])
        p.op("vector", lambda e: e.tensor_scalar(
            sm.ap[:, 0:1], sm.ap[:, 0:1], 1e-20, None, ALU.add),
            ins=[sm], outs=[sm])
        p.op("vector", lambda e: e.reciprocal(sm.ap[:, 1:2], sm.ap[:, 0:1]),
             ins=[sm], outs=[sm])
        p.op("vector", lambda e, tt=tt: e.tensor_scalar(
            wp.ap[:, 2 * tt:2 * tt + 2], mx8.ap[:, 0:2], sm.ap[:, 1:2], 2.5,
            ALU.mult, ALU.mult), ins=[mx8, sm], outs=[wp])
        o1 = oh1.ap[:, tt * NE:(tt + 1) * NE]
        o2 = oh2.ap[:, tt * NE:(tt + 1) * NE]
        p.op("vector", lambda e, o1=o1, tt=tt: e.tensor_scalar(
            o1, iota16.ap, idxf.ap[:, 2 * tt:2 * tt + 1], None, ALU.is_equal),
            ins=[iota16, idxf], outs=[oh1])
        p.op("vector", lambda e, o2=o2, tt=tt: e.tensor_scalar(
            o2, iota16.ap, idxf.ap[:, 2 * tt + 1:2 * tt + 2], None,
            ALU.is_equal), ins=[iota16, idxf], outs=[oh2])
        p.op("vector", lambda e, o1=o1, o2=o2: e.tensor_tensor(
            t16.ap, o1, o2, ALU.add), ins=[oh1, oh2], outs=[t16])
        yield
        bank3 = pbk[5]
        p.op("tensor", lambda e, bank3=bank3: e.transpose(
            bank3.ap[0:16, 0:128], t16.ap, ident.ap),
            ins=[t16, ident], outs=[bank3])
        p.op("vector", lambda e, tt=tt, bank3=bank3: e.tensor_copy(
            msk_T.ap[0:16, tt * 128:(tt + 1) * 128], bank3.ap[0:16, 0:128]),
            ins=[bank3], outs=[msk_T])
        yield
    zrow = ar.alloc(T, "p5_zrow")
    p.op("vector", lambda e: e.memset(zrow.ap[0:16, :], 0.0), outs=[zrow])
    pos_T = ar.alloc(T, "p5_posT")
    p.op("vector", lambda e: e.tensor_tensor_scan(
        pos_T.ap[0:16, :], msk_T.ap[0:16, :], zrow.ap[0:16, :], 0.0,
        ALU.add, ALU.add), ins=[msk_T, zrow], outs=[pos_T])
    pos_tok = ar.alloc(NE, "p5_postok")
    cols = ar.alloc(8, "p5_cols")
    du = ar.alloc(32, "p5_du")
    sc_toks = ctx.setdefault("sc_toks", [])
    for tt in range(16):
        bank = pbk[4]
        p.op("tensor", lambda e, tt=tt, bank=bank: e.transpose(
            bank.ap[:, 0:16], pos_T.ap[0:16, tt * 128:(tt + 1) * 128],
            ident.ap[0:16, 0:16]), ins=[pos_T, ident], outs=[bank])
        p.op("vector", lambda e, bank=bank: e.tensor_copy(
            pos_tok.ap, bank.ap[:, 0:16]), ins=[bank], outs=[pos_tok])
        for r, ohb, tokb in ((0, oh1, tok0), (1, oh2, tok1)):
            o = ohb.ap[:, tt * NE:(tt + 1) * NE]
            p.op("vector", lambda e, o=o: e.tensor_tensor(
                t16.ap, o, pos_tok.ap, ALU.mult),
                ins=[ohb, pos_tok], outs=[t16])
            p.op("vector", lambda e: e.tensor_reduce(
                cols.ap[:, 0:1], t16.ap, mybir.AxisListType.X, ALU.add),
                ins=[t16], outs=[cols], reducy=True)
            p.op("vector", lambda e, tt=tt, r=r: e.tensor_scalar(
                cols.ap[:, 1:2], idxf.ap[:, 2 * tt + r:2 * tt + r + 1],
                float(CAP), -1.0, ALU.mult, ALU.add),
                ins=[idxf], outs=[cols])
            p.op("vector", lambda e: e.tensor_tensor(
                cols.ap[:, 1:2], cols.ap[:, 1:2], cols.ap[:, 0:1], ALU.add),
                ins=[cols], outs=[cols])
            dslot = 2 * tt + r
            p.op("vector", lambda e, dslot=dslot: e.tensor_copy(
                du.ap[:, dslot:dslot + 1].bitcast(U32), cols.ap[:, 1:2]),
                ins=[cols], outs=[du])
            ioff = IndirectOffsetOnAxis(
                ap=du.ap[:, dslot:dslot + 1].bitcast(U32), axis=0)
            tk_g = tok0.ap[:, tt:tt + 1].bitcast(U32)
            tk_d = tokb.ap[:, tt:tt + 1].bitcast(U32)
            wsl = wp.ap[:, 2 * tt + r:2 * tt + r + 1]
            sc_toks.append(p.dma("gpsimd_q", lambda e, tk_g=tk_g, ioff=ioff:
                                 e.indirect_dma_start(out=dr["glist"],
                                                      out_offset=ioff,
                                                      in_=tk_g, in_offset=None),
                                 ins=[tok0, du],
                                 extra_deps=[p.dbuf("glist").wr]))
            sc_toks.append(p.dma("gpsimd_q", lambda e, tk_d=tk_d, ioff=ioff:
                                 e.indirect_dma_start(out=dr["dlist"],
                                                      out_offset=ioff,
                                                      in_=tk_d, in_offset=None),
                                 ins=[tokb, du],
                                 extra_deps=[p.dbuf("dlist").wr]))
            sc_toks.append(p.dma("gpsimd_q", lambda e, wsl=wsl, ioff=ioff:
                                 e.indirect_dma_start(out=dr["wlist"],
                                                      out_offset=ioff,
                                                      in_=wsl, in_offset=None),
                                 ins=[wp, du],
                                 extra_deps=[p.dbuf("wlist").wr]))
    for b in (sc_fm, sc_tok, oh1, oh2, idxf, wp, msk_T, mx8, idx, sm,
              t16, zrow, pos_T, pos_tok, cols, du, tok0, tok1, zf, dumpi):
        ar.freebuf(b)
    p.qfence("gpsimd_q")
    return sc_toks


def moe_state(p, ar, dr, sc_toks):
    ms = {}
    ms["w"] = [ar.alloc(FD * M, "ms_w0"), ar.alloc(FD * M, "ms_w1"),
               ar.alloc(FD * M, "ms_w2")]
    ms["wi"] = 0
    ms["xe"] = ar.alloc(FD * TC, "ms_xe")
    ms["xg"] = [ar.alloc(D, f"ms_xg{i}") for i in range(CAP // 128)]
    ms["gs"] = ar.alloc(4 * TC, "ms_gs")
    ms["ub"] = ar.alloc(4 * TC, "ms_ub")
    ms["rb"] = ar.alloc(CAP, "ms_rb")
    ms["idxg"] = ar.alloc(2 * (CAP // 128), "ms_idxg")
    gd = [sc_toks[-1], p.dbuf("glist").wr, p.dbuf("dlist").wr,
          p.dbuf("wlist").wr]
    gd += [tk for tk in (p.dbuf("xn_tok_s", c).wr for c in range(NCHUNK)) if tk]
    ms["gather_dep"] = gd
    ms["first"] = True
    return ms


def next_w(p, ar, dr, ms, src_ap):
    b = ms["w"][ms["wi"] % 3]
    ms["wi"] += 1
    m2 = src_ap.shape[1]
    p.dma("sync_q", lambda e, b=b, src_ap=src_ap, m2=m2: e.dma_start(
        out=b.ap.rearrange("q (b m) -> q b m", m=m2),
        in_=src_ap.rearrange("(b q) m -> q b m", q=128)), outs=[b])
    return b


def expert_emit(p, ar, pbk, dr, ctx, ms, e_i):
    ident = ctx["ident"]
    NP = CAP // 128
    xg, xe, gs2, ub2 = ms["xg"], ms["xe"], ms["gs"], ms["ub"]
    idxg, rb = ms["idxg"], ms["rb"]
    wg = next_w(p, ar, dr, ms, dr["w_gate"][e_i])
    wu = next_w(p, ar, dr, ms, dr["w_up"][e_i])
    wd = next_w(p, ar, dr, ms, dr["w_down"][e_i])
    dep = ms["gather_dep"] if ms["first"] else []
    ms["first"] = False
    for i in range(NP):
        r0 = e_i * CAP + i * 128
        p.dma("gpsimd_q", lambda e, r0=r0, i=i: e.dma_start(
            out=idxg.ap[:, i:i + 1].bitcast(U32),
            in_=dr["glist"][r0:r0 + 128, :]), outs=[idxg], extra_deps=dep)
        p.dma("gpsimd_q", lambda e, r0=r0, i=i: e.dma_start(
            out=idxg.ap[:, NP + i:NP + i + 1].bitcast(U32),
            in_=dr["dlist"][r0:r0 + 128, :]), outs=[idxg], extra_deps=dep)
    p.dma("gpsimd_q", lambda e, e_i=e_i: e.dma_start(
        out=rb.ap, in_=dr["wlist"][e_i * CAP:(e_i + 1) * CAP, 0:1].rearrange(
            "a one -> one a").broadcast_to((128, CAP))), outs=[rb],
        extra_deps=dep)
    for i in range(NP):
        ioff = IndirectOffsetOnAxis(ap=idxg.ap[:, i:i + 1].bitcast(U32), axis=0)
        p.dma("gpsimd_q", lambda e, i=i, ioff=ioff: e.indirect_dma_start(
            out=xg[i].ap, out_offset=None,
            in_=dr["xn_tok_s"], in_offset=ioff),
            ins=[idxg], outs=[xg[i]])
    p.qfence("gpsimd_q")
    for f in range(FD):
        bank = pbk[4 + f % 2]
        for i in range(NP):
            p.op("tensor", lambda e, f=f, i=i, bank=bank: e.transpose(
                bank.ap[:, i * 128:(i + 1) * 128],
                xg[i].ap[:, f * 128:(f + 1) * 128], ident.ap),
                ins=[xg[i], ident], outs=[bank])
        p.op("scalar", lambda e, f=f, bank=bank: e.copy(
            xe.ap[:, f * CAP:(f + 1) * CAP], bank.ap[:, 0:CAP]),
            ins=[bank], outs=[xe])
    for wsrc, dst, act in ((wg, gs2, True), (wu, ub2, False)):
        for mg in range(4):
            bank = pbk[mg % 2]
            for k in range(FD):
                p.op("tensor", lambda e, k=k, mg=mg, bank=bank, wsrc=wsrc:
                     e.matmul(bank.ap[:, 0:CAP],
                              wsrc.ap[:, k * M + mg * 128:k * M + (mg + 1) * 128],
                              xe.ap[:, k * CAP:(k + 1) * CAP],
                              start=(k == 0), stop=(k == FD - 1)),
                     ins=[wsrc, xe], outs=[bank])
            if act:
                p.op("scalar", lambda e, mg=mg, bank=bank, dst=dst: e.activation(
                    dst.ap[:, mg * CAP:(mg + 1) * CAP], bank.ap[:, 0:CAP],
                    AF.Silu), ins=[bank], outs=[dst])
            else:
                p.op("scalar", lambda e, mg=mg, bank=bank, dst=dst: e.copy(
                    dst.ap[:, mg * CAP:(mg + 1) * CAP], bank.ap[:, 0:CAP]),
                    ins=[bank], outs=[dst])
    p.op("vector", lambda e: e.tensor_tensor(
        gs2.ap[:, 0:4 * CAP], gs2.ap[:, 0:4 * CAP], ub2.ap[:, 0:4 * CAP],
        ALU.mult), ins=[gs2, ub2], outs=[gs2])
    p.op("vector", lambda e: e.tensor_tensor(
        gs2.ap[:, 0:4 * CAP].rearrange("q (b t) -> q b t", t=CAP),
        gs2.ap[:, 0:4 * CAP].rearrange("q (b t) -> q b t", t=CAP),
        rb.ap.unsqueeze(1).broadcast_to((128, 4, CAP)),
        ALU.mult), ins=[gs2, rb], outs=[gs2])
    for mo in range(FD):
        bank = pbk[2 + mo % 2]
        for k in range(4):
            p.op("tensor", lambda e, k=k, mo=mo, bank=bank, wd=wd: e.matmul(
                bank.ap[:, 0:CAP],
                wd.ap[:, k * D + mo * 128:k * D + (mo + 1) * 128],
                gs2.ap[:, k * CAP:(k + 1) * CAP],
                start=(k == 0), stop=(k == 3)),
                ins=[wd, gs2], outs=[bank])
        p.op("scalar", lambda e, mo=mo, bank=bank: e.copy(
            xe.ap[:, mo * CAP:(mo + 1) * CAP], bank.ap[:, 0:CAP]),
            ins=[bank], outs=[xe])
        yield
    for i in range(NP):
        for fh in range(2):
            bk = pbk[4 + fh]
            for f4 in range(4):
                f = fh * 4 + f4
                p.op("tensor", lambda e, f=f, f4=f4, i=i, bk=bk: e.transpose(
                    bk.ap[:, f4 * 128:(f4 + 1) * 128],
                    xe.ap[:, f * CAP + i * 128:f * CAP + (i + 1) * 128],
                    ident.ap), ins=[xe, ident], outs=[bk])
            p.op("scalar", lambda e, i=i, fh=fh, bk=bk: e.copy(
                xg[i].ap[:, fh * 512:(fh + 1) * 512], bk.ap[:, 0:512]),
                ins=[bk], outs=[xg[i]])
            yield
        soff = IndirectOffsetOnAxis(
            ap=idxg.ap[:, NP + i:NP + i + 1].bitcast(U32), axis=0)
        ctx.setdefault("moe_sc", []).append(p.dma(
            "gpsimd_q", lambda e, i=i, soff=soff: e.indirect_dma_start(
                out=dr["moebuf"], out_offset=soff,
                in_=xg[i].ap, in_offset=None),
            ins=[xg[i], idxg]))


def shared_chunk_emit(p, ar, pbk, dr, ctx, ms, c, xn_c):
    gs2, ub2, xe = ms["gs"], ms["ub"], ms["xe"]
    wg = next_w(p, ar, dr, ms, dr["ws_gate"])
    wu = next_w(p, ar, dr, ms, dr["ws_up"])
    wd = next_w(p, ar, dr, ms, dr["ws_down"])
    for wsrc, dst, act in ((wg, gs2, True), (wu, ub2, False)):
        for mg in range(4):
            bank = pbk[mg % 2]
            for k in range(FD):
                p.op("tensor", lambda e, k=k, mg=mg, bank=bank, wsrc=wsrc,
                     xn_c=xn_c: e.matmul(
                         bank.ap[:, 0:TC],
                         wsrc.ap[:, k * M + mg * 128:k * M + (mg + 1) * 128],
                         xn_c.ap[:, k * TC:(k + 1) * TC],
                         start=(k == 0), stop=(k == FD - 1)),
                     ins=[wsrc, xn_c], outs=[bank])
            if act:
                p.op("scalar", lambda e, mg=mg, bank=bank, dst=dst: e.activation(
                    dst.ap[:, mg * TC:(mg + 1) * TC], bank.ap[:, 0:TC],
                    AF.Silu), ins=[bank], outs=[dst])
            else:
                p.op("scalar", lambda e, mg=mg, bank=bank, dst=dst: e.copy(
                    dst.ap[:, mg * TC:(mg + 1) * TC], bank.ap[:, 0:TC]),
                    ins=[bank], outs=[dst])
    p.op("vector", lambda e: e.tensor_tensor(
        gs2.ap, gs2.ap, ub2.ap, ALU.mult), ins=[gs2, ub2], outs=[gs2])
    for mo in range(FD):
        bank = pbk[2 + mo % 2]
        for k in range(4):
            p.op("tensor", lambda e, k=k, mo=mo, bank=bank, wd=wd: e.matmul(
                bank.ap[:, 0:TC],
                wd.ap[:, k * D + mo * 128:k * D + (mo + 1) * 128],
                gs2.ap[:, k * TC:(k + 1) * TC],
                start=(k == 0), stop=(k == 3)),
                ins=[wd, gs2], outs=[bank])
        p.op("scalar", lambda e, mo=mo, bank=bank: e.copy(
            xe.ap[:, mo * TC:(mo + 1) * TC], bank.ap[:, 0:TC]),
            ins=[bank], outs=[xe])
    store_fm(p, dr["hacc_s"], xe, FD, c, name="hacc_s")


def pass4v3(p, ar, pbk, dr, dbg, ctx, c, state):
    a_bc = ctx["a_bc"]
    HF = 8 * TC
    dt_h = ar.alloc(HF, f"s4dt_{c}")
    dtx = ar.alloc(HF, f"s4dtx_{c}")
    dA = ar.alloc(HF, f"s4dA_{c}")
    uh = ar.alloc(HF, f"s4uh_{c}")
    y_h = ar.alloc(HF, f"s4y_{c}")
    bB = [ar.alloc(TC, f"s4bB_{c}_{j}") for j in range(2)]
    bC = [ar.alloc(TC, f"s4bC_{c}_{j}") for j in range(2)]
    bp_dep = [p.dbuf("bpcp_s").wr] if p.dbuf("bpcp_s").wr else []
    for h in range(2):
        load_fm(p, dr["dt_s"], dt_h, 8, c, blk0=h * 8, name="dt_s")
        load_fm(p, dr["xc_s"], dA, 8, c, blk0=h * 8, name="xc_s")
        p.op("vector", lambda e: e.tensor_tensor(
            dtx.ap, dt_h.ap, dA.ap, ALU.mult), ins=[dt_h, dA], outs=[dtx])
        for s in range(S):
            jb = s % 2
            p.dma("sync_q", lambda e, s=s, jb=jb: e.dma_start(
                out=bB[jb].ap,
                in_=dr["bpcp_s"][s:s + 1, c * TC:(c + 1) * TC].broadcast_to(
                    (128, TC))), outs=[bB[jb]], extra_deps=bp_dep)
            p.dma("sync_q", lambda e, s=s, jb=jb: e.dma_start(
                out=bC[jb].ap,
                in_=dr["bpcp_s"][16 + s:17 + s, c * TC:(c + 1) * TC].broadcast_to(
                    (128, TC))), outs=[bC[jb]], extra_deps=bp_dep)
            p.op("scalar", lambda e, s=s: e.activation(
                dA.ap, dt_h.ap, AF.Exp, scale=a_bc.ap[:, s:s + 1]),
                ins=[dt_h, a_bc], outs=[dA])
            p.op("vector", lambda e, jb=jb: e.tensor_tensor(
                uh.ap.rearrange("q (b t) -> q b t", t=TC),
                dtx.ap.rearrange("q (b t) -> q b t", t=TC),
                bB[jb].ap.unsqueeze(1).broadcast_to((128, 8, TC)),
                ALU.mult), ins=[dtx, bB[jb]], outs=[uh])
            for dc in range(8):
                col = s * FDI + h * 8 + dc
                p.op("vector", lambda e, dc=dc, col=col:
                     e.tensor_tensor_scan(
                         uh.ap[:, dc * TC:(dc + 1) * TC],
                         dA.ap[:, dc * TC:(dc + 1) * TC],
                         uh.ap[:, dc * TC:(dc + 1) * TC],
                         state.ap[:, col:col + 1], ALU.mult, ALU.add),
                     ins=[dA, uh, state], outs=[uh])
            p.op("vector", lambda e, s=s, h=h: e.tensor_copy(
                state.ap[:, s * FDI + h * 8:s * FDI + h * 8 + 8],
                uh.ap.rearrange("q (b t) -> q b t", t=TC)[:, :, TC - 1:TC
                                                          ].squeeze(2)),
                ins=[uh], outs=[state])
            if s == 0:
                p.op("vector", lambda e, jb=jb: e.tensor_tensor(
                    y_h.ap.rearrange("q (b t) -> q b t", t=TC),
                    uh.ap.rearrange("q (b t) -> q b t", t=TC),
                    bC[jb].ap.unsqueeze(1).broadcast_to((128, 8, TC)),
                    ALU.mult), ins=[uh, bC[jb]], outs=[y_h])
            else:
                p.op("vector", lambda e, jb=jb: e.tensor_tensor(
                    dA.ap.rearrange("q (b t) -> q b t", t=TC),
                    uh.ap.rearrange("q (b t) -> q b t", t=TC),
                    bC[jb].ap.unsqueeze(1).broadcast_to((128, 8, TC)),
                    ALU.mult), ins=[uh, bC[jb]], outs=[dA])
                p.op("vector", lambda e: e.tensor_tensor(
                    y_h.ap, y_h.ap, dA.ap, ALU.add), ins=[dA, y_h], outs=[y_h])
        d3 = dr["y_s"].rearrange("(b q) t -> q b t", q=128)[
            :, h * 8:(h + 1) * 8, c * TC:(c + 1) * TC]
        s3 = y_h.ap.rearrange("q (b t) -> q b t", t=TC)
        p.dma("gpsimd_q", lambda e, d3=d3, s3=s3: e.dma_start(out=d3, in_=s3),
              ins=[y_h], outs=[p.dbuf("y_s", c)])
        if "y" in dbg:
            d3b = dr["dbg_y"].rearrange("(b q) t -> q b t", q=128)[
                :, h * 8:(h + 1) * 8, c * TC:(c + 1) * TC]
            p.dma("gpsimd_q", lambda e, d3b=d3b, s3=s3: e.dma_start(
                out=d3b, in_=s3), ins=[y_h])
    for b in [dt_h, dtx, uh, y_h] + dAb + bBC:
        ar.freebuf(b)
    p.qfence("gpsimd_q")


def pass6az_v3(p, ar, pbk, dr, dbg, ctx, c, xn_c):
    wzp = [ar.alloc(FD * 128, f"az_w_{c}_{j}") for j in range(2)]
    zt = ar.alloc(TC, f"az_z_{c}")
    for mi in range(FDI):
        wb = wzp[mi % 2]
        src = dr["w_in_z_T"][:, mi * 128:(mi + 1) * 128]
        p.dma("sync_q", lambda e, wb=wb, src=src: e.dma_start(
            out=wb.ap.rearrange("q (b m) -> q b m", m=128),
            in_=src.rearrange("(b q) m -> q b m", q=128)), outs=[wb])
        bank = pbk[6 + mi % 2]
        for k in range(FD):
            p.op("tensor", lambda e, k=k, bank=bank, wb=wb, xn_c=xn_c: e.matmul(
                bank.ap[:, 0:TC], wb.ap[:, k * 128:(k + 1) * 128],
                xn_c.ap[:, k * TC:(k + 1) * TC],
                start=(k == 0), stop=(k == FD - 1)),
                ins=[wb, xn_c], outs=[bank])
        p.op("scalar", lambda e, bank=bank, zt=zt: e.activation(
            zt.ap, bank.ap[:, 0:TC], AF.Silu), ins=[bank], outs=[zt])
        p.dma("gpsimd_q", lambda e, mi=mi, zt=zt: e.dma_start(
            out=dr["zs_s"][mi * 128:(mi + 1) * 128, c * TC:(c + 1) * TC],
            in_=zt.ap), ins=[zt], outs=[p.dbuf("zs_s", c)])
        yield
    for b in wzp + [zt]:
        ar.freebuf(b)
    p.qfence("gpsimd_q")


def pass6gate_v3(p, ar, pbk, dr, dbg, ctx, c):
    tabs = ctx["tabs"]
    constk = ctx["constk"]
    ones_row = ctx["ones_row"]
    ones_col = ctx["ones_col"]
    y0 = ar.alloc(8 * TC, f"g_y0_{c}")
    y1 = ar.alloc(8 * TC, f"g_y1_{c}")
    sq = ar.alloc(TC, f"g_sq_{c}")
    stat = ar.alloc(6 * TC, f"g_st_{c}")
    load_fm(p, dr["y_s"], y0, 8, c, blk0=0, name="y_s")
    load_fm(p, dr["y_s"], y1, 8, c, blk0=8, name="y_s")
    bA, bB2 = pbk[6], pbk[7]
    for k in range(FDI):
        yk = (y0 if k < 8 else y1).ap[:, (k % 8) * TC:(k % 8 + 1) * TC]
        ybuf = y0 if k < 8 else y1
        p.op("scalar", lambda e, yk=yk: e.activation(
            sq.ap, yk, AF.Square), ins=[ybuf], outs=[sq])
        p.op("tensor", lambda e, k=k, yk=yk, bA=bA: e.matmul(
            bA.ap[0:1, 0:TC], ones_col.ap, yk,
            start=(k == 0), stop=(k == FDI - 1)), ins=[ones_col, ybuf], outs=[bA])
        p.op("tensor", lambda e, k=k, bB2=bB2: e.matmul(
            bB2.ap[0:1, 0:TC], ones_col.ap, sq.ap,
            start=(k == 0), stop=(k == FDI - 1)), ins=[ones_col, sq], outs=[bB2])
    r0 = stat.ap[0:1, :]
    p.op("vector", lambda e, bA=bA: e.tensor_scalar(
        r0[:, 0:TC], bA.ap[0:1, 0:TC], 1.0 / DIN, None, ALU.mult),
        ins=[bA], outs=[stat])
    p.op("vector", lambda e, bB2=bB2: e.tensor_scalar(
        r0[:, TC:2 * TC], bB2.ap[0:1, 0:TC], 1.0 / DIN, None, ALU.mult),
        ins=[bB2], outs=[stat])
    p.op("vector", lambda e: e.tensor_tensor(
        r0[:, 2 * TC:3 * TC], r0[:, 0:TC], r0[:, 0:TC], ALU.mult),
        ins=[stat], outs=[stat])
    p.op("vector", lambda e: e.tensor_tensor(
        r0[:, TC:2 * TC], r0[:, TC:2 * TC], r0[:, 2 * TC:3 * TC],
        ALU.subtract), ins=[stat], outs=[stat])
    p.op("scalar", lambda e: e.activation(
        r0[:, 2 * TC:3 * TC], r0[:, TC:2 * TC], AF.Sqrt,
        bias=constk.ap[0:1, 0:1]), ins=[stat, constk], outs=[stat])
    p.op("vector", lambda e: e.reciprocal(
        r0[:, TC:2 * TC], r0[:, 2 * TC:3 * TC]), ins=[stat], outs=[stat])
    p.op("tensor", lambda e, bA=bA: e.matmul(
        bA.ap[:, 0:TC], ones_row.ap[0:1, 0:128], r0[:, 0:TC],
        start=True, stop=True), ins=[ones_row, stat], outs=[bA])
    p.op("tensor", lambda e, bB2=bB2: e.matmul(
        bB2.ap[:, 0:TC], ones_row.ap[0:1, 0:128], r0[:, TC:2 * TC],
        start=True, stop=True), ins=[ones_row, stat], outs=[bB2])
    xh = ar.alloc(8 * TC, f"g_xh_{c}")
    zh = ar.alloc(8 * TC, f"g_zh_{c}")
    for h in range(2):
        ybuf = y0 if h == 0 else y1
        load_fm(p, dr["xc_s"], xh, 8, c, blk0=h * 8, name="xc_s")
        load_fm(p, dr["zs_s"], zh, 8, c, blk0=h * 8, name="zs_s")
        for k8 in range(8):
            k = h * 8 + k8
            yk = ybuf.ap[:, k8 * TC:(k8 + 1) * TC]
            p.op("vector", lambda e, yk=yk, bA=bA: e.tensor_tensor(
                yk, yk, bA.ap[:, 0:TC], ALU.subtract), ins=[ybuf, bA], outs=[ybuf])
            p.op("vector", lambda e, yk=yk, bB2=bB2: e.tensor_tensor(
                yk, yk, bB2.ap[:, 0:TC], ALU.mult), ins=[ybuf, bB2], outs=[ybuf])
            p.op("vector", lambda e, yk=yk, k=k: e.tensor_scalar(
                yk, yk, tabs["ln_g_pc"].ap[:, k:k + 1],
                tabs["ln_b_pc"].ap[:, k:k + 1], ALU.mult, ALU.add),
                ins=[ybuf, tabs["ln_g_pc"], tabs["ln_b_pc"]], outs=[ybuf])
            p.op("vector", lambda e, yk=yk, k=k, k8=k8, xh=xh:
                 e.scalar_tensor_tensor(
                     yk, xh.ap[:, k8 * TC:(k8 + 1) * TC],
                     tabs["d_param_pc"].ap[:, k:k + 1], yk, ALU.mult, ALU.add),
                 ins=[xh, tabs["d_param_pc"], ybuf], outs=[ybuf])
            p.op("vector", lambda e, yk=yk, k8=k8, zh=zh: e.tensor_tensor(
                yk, yk, zh.ap[:, k8 * TC:(k8 + 1) * TC], ALU.mult),
                ins=[ybuf, zh], outs=[ybuf])
        d3 = dr["yg_s"].rearrange("(b q) t -> q b t", q=128)[
            :, h * 8:(h + 1) * 8, c * TC:(c + 1) * TC]
        s3 = ybuf.ap.rearrange("q (b t) -> q b t", t=TC)
        p.dma("gpsimd_q", lambda e, d3=d3, s3=s3: e.dma_start(out=d3, in_=s3),
              ins=[ybuf], outs=[p.dbuf("yg_s", c)])
        if "yg" in dbg:
            d3b = dr["dbg_yg"].rearrange("(b q) t -> q b t", q=128)[
                :, h * 8:(h + 1) * 8, c * TC:(c + 1) * TC]
            p.dma("gpsimd_q", lambda e, d3b=d3b, s3=s3: e.dma_start(
                out=d3b, in_=s3), ins=[ybuf])
    for b in (y0, y1, sq, stat, xh, zh):
        ar.freebuf(b)


def expert_emit_v4(p, ar, pbk, dr, ctx, ms, e_i):
    ident = ctx["ident"]
    NP = CAP // 128
    xg, xe, gs2, ub2 = ms["xg"], ms["xe"], ms["gs"], ms["ub"]
    idxg, rb = ms["idxg"], ms["rb"]
    wh = ms["wh"]

    def whalf(src_ap, klo):
        b = wh[ms["wi"] % 2]
        ms["wi"] += 1
        m2 = src_ap.shape[1]
        sl = src_ap[klo * 512:(klo + 4 * (m2 // 512) * 0 + 4) * 128 + klo * 0, :]
        return b

    dep = ms["gather_dep"] if ms["first"] else []
    ms["first"] = False
    for i in range(NP):
        r0 = e_i * CAP + i * 128
        p.dma("gpsimd_q", lambda e, r0=r0, i=i: e.dma_start(
            out=idxg.ap[:, i:i + 1].bitcast(U32),
            in_=dr["glist"][r0:r0 + 128, :]), outs=[idxg], extra_deps=dep)
        p.dma("gpsimd_q", lambda e, r0=r0, i=i: e.dma_start(
            out=idxg.ap[:, NP + i:NP + i + 1].bitcast(U32),
            in_=dr["dlist"][r0:r0 + 128, :]), outs=[idxg], extra_deps=dep)
    p.dma("gpsimd_q", lambda e, e_i=e_i: e.dma_start(
        out=rb.ap, in_=dr["wlist"][e_i * CAP:(e_i + 1) * CAP, 0:1].rearrange(
            "a one -> one a").broadcast_to((128, CAP))), outs=[rb],
        extra_deps=dep)
    for i in range(NP):
        ioff = IndirectOffsetOnAxis(ap=idxg.ap[:, i:i + 1].bitcast(U32), axis=0)
        p.dma("gpsimd_q", lambda e, i=i, ioff=ioff: e.indirect_dma_start(
            out=xg[i].ap, out_offset=None,
            in_=dr["xn_tok_s"], in_offset=ioff),
            ins=[idxg], outs=[xg[i]])
    p.qfence("gpsimd_q")
    for f in range(FD):
        bank = pbk[4 + f % 2]
        for i in range(NP):
            p.op("tensor", lambda e, f=f, i=i, bank=bank: e.transpose(
                bank.ap[:, i * 128:(i + 1) * 128],
                xg[i].ap[:, f * 128:(f + 1) * 128], ident.ap),
                ins=[xg[i], ident], outs=[bank])
        p.op("scalar", lambda e, f=f, bank=bank: e.copy(
            xe.ap[:, f * CAP:(f + 1) * CAP], bank.ap[:, 0:CAP]),
            ins=[bank], outs=[xe])
        yield
    for src_d, dst, act in ((dr["w_gate"][e_i], gs2, True),
                            (dr["w_up"][e_i], ub2, False)):
        halves = []
        for kh in range(2):
            b = wh[ms["wi"] % 2]
            ms["wi"] += 1
            sl = src_d[kh * 512:(kh + 1) * 512, :]
            p.dma("gpsimd_q", lambda e, b=b, sl=sl: e.dma_start(
                out=b.ap.rearrange("q (b m) -> q b m", m=M),
                in_=sl.rearrange("(b q) m -> q b m", q=128)), outs=[b])
            halves.append(b)
        for mg in range(4):
            bank = pbk[mg % 2]
            for k in range(FD):
                b = halves[k // 4]
                k4 = k % 4
                p.op("tensor", lambda e, k=k, k4=k4, mg=mg, bank=bank, b=b:
                     e.matmul(bank.ap[:, 0:CAP],
                              b.ap[:, k4 * M + mg * 128:k4 * M + (mg + 1) * 128],
                              xe.ap[:, k * CAP:(k + 1) * CAP],
                              start=(k == 0), stop=(k == FD - 1)),
                     ins=[b, xe], outs=[bank])
            if act:
                p.op("scalar", lambda e, mg=mg, bank=bank, dst=dst: e.activation(
                    dst.ap[:, mg * CAP:(mg + 1) * CAP], bank.ap[:, 0:CAP],
                    AF.Silu), ins=[bank], outs=[dst])
            else:
                p.op("scalar", lambda e, mg=mg, bank=bank, dst=dst: e.copy(
                    dst.ap[:, mg * CAP:(mg + 1) * CAP], bank.ap[:, 0:CAP]),
                    ins=[bank], outs=[dst])
            yield
    p.op("gpsimd", lambda e: e.tensor_tensor(
        gs2.ap[:, 0:4 * CAP], gs2.ap[:, 0:4 * CAP], ub2.ap[:, 0:4 * CAP],
        ALU.mult), ins=[gs2, ub2], outs=[gs2])
    p.op("gpsimd", lambda e: e.tensor_tensor(
        gs2.ap[:, 0:4 * CAP].rearrange("q (b t) -> q b t", t=CAP),
        gs2.ap[:, 0:4 * CAP].rearrange("q (b t) -> q b t", t=CAP),
        rb.ap.unsqueeze(1).broadcast_to((128, 4, CAP)),
        ALU.mult), ins=[gs2, rb], outs=[gs2])
    halves = []
    for kh in range(2):
        b = wh[ms["wi"] % 2]
        ms["wi"] += 1
        sl = dr["w_down"][e_i][kh * 256:(kh + 1) * 256, :]
        p.dma("gpsimd_q", lambda e, b=b, sl=sl: e.dma_start(
            out=b.ap.rearrange("q (b m) -> q b m", m=D),
            in_=sl.rearrange("(b q) m -> q b m", q=128)), outs=[b])
        halves.append(b)
    for mo in range(FD):
        bank = pbk[2 + mo % 2]
        for k in range(4):
            b = halves[k // 2]
            k2 = k % 2
            p.op("tensor", lambda e, k=k, k2=k2, mo=mo, bank=bank, b=b: e.matmul(
                bank.ap[:, 0:CAP],
                b.ap[:, k2 * D + mo * 128:k2 * D + (mo + 1) * 128],
                gs2.ap[:, k * CAP:(k + 1) * CAP],
                start=(k == 0), stop=(k == 3)),
                ins=[b, gs2], outs=[bank])
        p.op("scalar", lambda e, mo=mo, bank=bank: e.copy(
            xe.ap[:, mo * CAP:(mo + 1) * CAP], bank.ap[:, 0:CAP]),
            ins=[bank], outs=[xe])
        yield
    for i in range(NP):
        for fh in range(2):
            bk = pbk[4 + fh]
            for f4 in range(4):
                f = fh * 4 + f4
                p.op("tensor", lambda e, f=f, f4=f4, i=i, bk=bk: e.transpose(
                    bk.ap[:, f4 * 128:(f4 + 1) * 128],
                    xe.ap[:, f * CAP + i * 128:f * CAP + (i + 1) * 128],
                    ident.ap), ins=[xe, ident], outs=[bk])
            p.op("scalar", lambda e, i=i, fh=fh, bk=bk: e.copy(
                xg[i].ap[:, fh * 512:(fh + 1) * 512], bk.ap[:, 0:512]),
                ins=[bk], outs=[xg[i]])
            yield
        soff = IndirectOffsetOnAxis(
            ap=idxg.ap[:, NP + i:NP + i + 1].bitcast(U32), axis=0)
        ctx.setdefault("moe_sc", []).append(p.dma(
            "gpsimd_q", lambda e, i=i, soff=soff: e.indirect_dma_start(
                out=dr["moebuf"], out_offset=soff,
                in_=xg[i].ap, in_offset=None),
            ins=[xg[i], idxg]))
    p.qfence("gpsimd_q")


def shared_tail(p, ar, pbk, dr, ctx):
    wgb = ar.alloc(FD * M, "st_wg")
    wub = ar.alloc(FD * M, "st_wu")
    wdb = ar.alloc(4 * D, "st_wd")
    gs = ar.alloc(4 * TC, "st_gs")
    ub = ar.alloc(4 * TC, "st_ub")
    eo = ar.alloc(FD * TC, "st_eo")
    xn_c = ar.alloc(FD * TC, "st_xn")
    p.dma("sync_q", lambda e: e.dma_start(
        out=wgb.ap.rearrange("q (b m) -> q b m", m=M),
        in_=dr["ws_gate"].rearrange("(b q) m -> q b m", q=128)), outs=[wgb])
    p.dma("sync_q", lambda e: e.dma_start(
        out=wub.ap.rearrange("q (b m) -> q b m", m=M),
        in_=dr["ws_up"].rearrange("(b q) m -> q b m", q=128)), outs=[wub])
    p.dma("sync_q", lambda e: e.dma_start(
        out=wdb.ap.rearrange("q (b m) -> q b m", m=D),
        in_=dr["ws_down"].rearrange("(b q) m -> q b m", q=128)), outs=[wdb])
    for c in range(NCHUNK):
        load_fm(p, dr["xn_s"], xn_c, FD, c, name="xn_s")
        for mg in range(4):
            bank = pbk[mg % 2]
            for k in range(FD):
                p.op("tensor", lambda e, k=k, mg=mg, bank=bank, xn_c=xn_c:
                     e.matmul(bank.ap[:, 0:TC],
                              wgb.ap[:, k * M + mg * 128:k * M + (mg + 1) * 128],
                              xn_c.ap[:, k * TC:(k + 1) * TC],
                              start=(k == 0), stop=(k == FD - 1)),
                     ins=[wgb, xn_c], outs=[bank])
            p.op("scalar", lambda e, mg=mg, bank=bank: e.activation(
                gs.ap[:, mg * TC:(mg + 1) * TC], bank.ap[:, 0:TC], AF.Silu),
                ins=[bank], outs=[gs])
        for mg in range(4):
            bank = pbk[mg % 2]
            for k in range(FD):
                p.op("tensor", lambda e, k=k, mg=mg, bank=bank, xn_c=xn_c:
                     e.matmul(bank.ap[:, 0:TC],
                              wub.ap[:, k * M + mg * 128:k * M + (mg + 1) * 128],
                              xn_c.ap[:, k * TC:(k + 1) * TC],
                              start=(k == 0), stop=(k == FD - 1)),
                     ins=[wub, xn_c], outs=[bank])
            p.op("scalar", lambda e, mg=mg, bank=bank: e.copy(
                ub.ap[:, mg * TC:(mg + 1) * TC], bank.ap[:, 0:TC]),
                ins=[bank], outs=[ub])
        p.op("vector", lambda e: e.tensor_tensor(
            gs.ap, gs.ap, ub.ap, ALU.mult), ins=[gs, ub], outs=[gs])
        for mo in range(FD):
            bank = pbk[2 + mo % 2]
            for k in range(4):
                p.op("tensor", lambda e, k=k, mo=mo, bank=bank: e.matmul(
                    bank.ap[:, 0:TC],
                    wdb.ap[:, k * D + mo * 128:k * D + (mo + 1) * 128],
                    gs.ap[:, k * TC:(k + 1) * TC],
                    start=(k == 0), stop=(k == 3)),
                    ins=[wdb, gs], outs=[bank])
            p.op("scalar", lambda e, mo=mo, bank=bank: e.copy(
                eo.ap[:, mo * TC:(mo + 1) * TC], bank.ap[:, 0:TC]),
                ins=[bank], outs=[eo])
        store_fm(p, dr["hacc_s"], eo, FD, c, name="hacc_s")
    for b in (wgb, wub, wdb, gs, ub, eo, xn_c):
        ar.freebuf(b)
    p.qfence("gpsimd_q")


def shared_gen_v4(p, ar, pbk, dr, ctx, ms, c, xn_c):
    gs2, ub2, xe, wh = ms["gs"], ms["ub"], ms["xe"], ms["wh"]
    for src_d, dst, act in ((dr["ws_gate"], gs2, True), (dr["ws_up"], ub2, False)):
        halves = []
        for kh in range(2):
            b = wh[ms["wi"] % 2]
            ms["wi"] += 1
            sl = src_d[kh * 512:(kh + 1) * 512, :]
            p.dma("gpsimd_q", lambda e, b=b, sl=sl: e.dma_start(
                out=b.ap.rearrange("q (b m) -> q b m", m=M),
                in_=sl.rearrange("(b q) m -> q b m", q=128)), outs=[b])
            halves.append(b)
        for mg in range(4):
            bank = pbk[mg % 2]
            for k in range(FD):
                b = halves[k // 4]
                k4 = k % 4
                p.op("tensor", lambda e, k=k, k4=k4, mg=mg, bank=bank, b=b,
                     xn_c=xn_c: e.matmul(
                         bank.ap[:, 0:TC],
                         b.ap[:, k4 * M + mg * 128:k4 * M + (mg + 1) * 128],
                         xn_c.ap[:, k * TC:(k + 1) * TC],
                         start=(k == 0), stop=(k == FD - 1)),
                     ins=[b, xn_c], outs=[bank])
            if act:
                p.op("scalar", lambda e, mg=mg, bank=bank, dst=dst: e.activation(
                    dst.ap[:, mg * TC:(mg + 1) * TC], bank.ap[:, 0:TC],
                    AF.Silu), ins=[bank], outs=[dst])
            else:
                p.op("scalar", lambda e, mg=mg, bank=bank, dst=dst: e.copy(
                    dst.ap[:, mg * TC:(mg + 1) * TC], bank.ap[:, 0:TC]),
                    ins=[bank], outs=[dst])
            yield
    p.op("gpsimd", lambda e: e.tensor_tensor(
        gs2.ap, gs2.ap, ub2.ap, ALU.mult), ins=[gs2, ub2], outs=[gs2])
    halves = []
    for kh in range(2):
        b = wh[ms["wi"] % 2]
        ms["wi"] += 1
        sl = dr["ws_down"][kh * 256:(kh + 1) * 256, :]
        p.dma("gpsimd_q", lambda e, b=b, sl=sl: e.dma_start(
            out=b.ap.rearrange("q (b m) -> q b m", m=D),
            in_=sl.rearrange("(b q) m -> q b m", q=128)), outs=[b])
        halves.append(b)
    for mo in range(FD):
        bank = pbk[2 + mo % 2]
        for k in range(4):
            b = halves[k // 2]
            k2 = k % 2
            p.op("tensor", lambda e, k=k, k2=k2, mo=mo, bank=bank, b=b: e.matmul(
                bank.ap[:, 0:TC],
                b.ap[:, k2 * D + mo * 128:k2 * D + (mo + 1) * 128],
                gs2.ap[:, k * TC:(k + 1) * TC],
                start=(k == 0), stop=(k == 3)),
                ins=[b, gs2], outs=[bank])
        p.op("scalar", lambda e, mo=mo, bank=bank: e.copy(
            xe.ap[:, mo * TC:(mo + 1) * TC], bank.ap[:, 0:TC]),
            ins=[bank], outs=[xe])
        yield
    store_fm(p, dr["hacc_s"], xe, FD, c, name="hacc_s")
    p.qfence("gpsimd_q")


def pass4v4_alloc(p, ar, dr, c):
    HF = 8 * TC
    QF = 4 * TC
    bufs = dict(
        dt_h=ar.alloc(HF, f"s4dt_{c}"),
        dtx=ar.alloc(HF, f"s4dtx_{c}"),
        dAb=[ar.alloc(QF, f"s4dA_{c}_{j}") for j in range(2)],
        uh=ar.alloc(QF, f"s4uh_{c}"),
        y_h=ar.alloc(HF, f"s4y_{c}"),
        bBC=[ar.alloc(4 * TC, f"s4bBC_{c}_{j}") for j in range(2)])
    load_fm(p, dr["dt_s"], bufs["dt_h"], 8, c, blk0=0, name="dt_s")
    load_fm(p, dr["xc_s"], bufs["dtx"], 8, c, blk0=0, name="xc_s")
    return bufs


def pass4v4(p, ar, pbk, dr, dbg, ctx, c, state, bufs, feed=None):
    a_bc = ctx["a_bc"]
    HF = 8 * TC
    QF = 4 * TC
    dt_h, dtx, dAb, uh, y_h, bBC = (bufs["dt_h"], bufs["dtx"], bufs["dAb"],
                                    bufs["uh"], bufs["y_h"], bufs["bBC"])
    bp_dep = [p.dbuf("bpcp_s").wr] if p.dbuf("bpcp_s").wr else []
    for h in range(2):
        if h == 1:
            load_fm(p, dr["dt_s"], dt_h, 8, c, blk0=8, name="dt_s")
            load_fm(p, dr["xc_s"], dtx, 8, c, blk0=8, name="xc_s")
        p.op("vector", lambda e: e.tensor_tensor(
            dtx.ap, dt_h.ap, dtx.ap, ALU.mult), ins=[dt_h, dtx], outs=[dtx])
        qi = 0
        for s in range(S):
            sg, s2 = s // 2, s % 2
            grp = bBC[sg % 2]
            if s2 == 0:
                p.dma("sync_q", lambda e, sg=sg, grp=grp: e.dma_start(
                    out=grp.ap.rearrange("q (j t) -> q j t", t=TC),
                    in_=dr["bpcp_s"][4 * sg:4 * sg + 4, c * TC:(c + 1) * TC
                                     ].unsqueeze(0).broadcast_to((128, 4, TC))),
                    outs=[grp], extra_deps=bp_dep)
            bBs = grp.ap[:, (2 * s2) * TC:(2 * s2 + 1) * TC]
            bCs = grp.ap[:, (2 * s2 + 1) * TC:(2 * s2 + 2) * TC]
            for q in range(2):
                qs = q * QF
                dA = dAb[qi % 2]
                qi += 1
                p.op("scalar", lambda e, s=s, qs=qs, dA=dA: e.activation(
                    dA.ap, dt_h.ap[:, qs:qs + QF], AF.Exp,
                    scale=a_bc.ap[:, s:s + 1]), ins=[dt_h, a_bc], outs=[dA])
                p.op("vector", lambda e, bBs=bBs, qs=qs: e.tensor_tensor(
                    uh.ap.rearrange("q (b t) -> q b t", t=TC),
                    dtx.ap[:, qs:qs + QF].rearrange("q (b t) -> q b t", t=TC),
                    bBs.unsqueeze(1).broadcast_to((128, 4, TC)),
                    ALU.mult), ins=[dtx, grp], outs=[uh])
                for dc in range(4):
                    col = s * FDI + h * 8 + q * 4 + dc
                    p.op("vector", lambda e, dc=dc, col=col, dA=dA:
                         e.tensor_tensor_scan(
                             uh.ap[:, dc * TC:(dc + 1) * TC],
                             dA.ap[:, dc * TC:(dc + 1) * TC],
                             uh.ap[:, dc * TC:(dc + 1) * TC],
                             state.ap[:, col:col + 1], ALU.mult, ALU.add),
                         ins=[dA, uh, state], outs=[uh])
                p.op("vector", lambda e, s=s, h=h, q=q: e.tensor_copy(
                    state.ap[:, s * FDI + h * 8 + q * 4:s * FDI + h * 8 + q * 4 + 4],
                    uh.ap.rearrange("q (b t) -> q b t", t=TC)[:, :, TC - 1:TC
                                                              ].squeeze(2)),
                    ins=[uh], outs=[state])
                if s == 0:
                    p.op("vector", lambda e, bCs=bCs, qs=qs: e.tensor_tensor(
                        y_h.ap[:, qs:qs + QF].rearrange("q (b t) -> q b t", t=TC),
                        uh.ap.rearrange("q (b t) -> q b t", t=TC),
                        bCs.unsqueeze(1).broadcast_to((128, 4, TC)),
                        ALU.mult), ins=[uh, grp], outs=[y_h])
                else:
                    p.op("vector", lambda e, bCs=bCs, dA=dA: e.tensor_tensor(
                        dA.ap.rearrange("q (b t) -> q b t", t=TC),
                        uh.ap.rearrange("q (b t) -> q b t", t=TC),
                        bCs.unsqueeze(1).broadcast_to((128, 4, TC)),
                        ALU.mult), ins=[uh, grp], outs=[dA])
                    p.op("vector", lambda e, qs=qs, dA=dA: e.tensor_tensor(
                        y_h.ap[:, qs:qs + QF], y_h.ap[:, qs:qs + QF], dA.ap,
                        ALU.add), ins=[dA, y_h], outs=[y_h])
                if feed is not None:
                    feed(2)
        d3 = dr["y_s"].rearrange("(b q) t -> q b t", q=128)[
            :, h * 8:(h + 1) * 8, c * TC:(c + 1) * TC]
        s3 = y_h.ap.rearrange("q (b t) -> q b t", t=TC)
        p.dma("gpsimd_q", lambda e, d3=d3, s3=s3: e.dma_start(out=d3, in_=s3),
              ins=[y_h], outs=[p.dbuf("y_s", c)])
        if "y" in dbg:
            d3b = dr["dbg_y"].rearrange("(b q) t -> q b t", q=128)[
                :, h * 8:(h + 1) * 8, c * TC:(c + 1) * TC]
            p.dma("gpsimd_q", lambda e, d3b=d3b, s3=s3: e.dma_start(
                out=d3b, in_=s3), ins=[y_h])
    for b in [dt_h, dtx, uh, y_h] + dAb + bBC:
        ar.freebuf(b)
    p.qfence("gpsimd_q")


def pass6gate_v4(p, ar, pbk, dr, dbg, ctx, c):
    tabs = ctx["tabs"]
    constk = ctx["constk"]
    ones_row = ctx["ones_row"]
    ones_col = ctx["ones_col"]
    gb = ctx["gate_bufs"]
    yq, sq, stat, xk, zk = (gb["yq"], gb["sq"], gb["stat"], gb["xk"], gb["zk"])
    bA, bB2 = pbk[6], pbk[7]
    for qq in range(4):
        load_fm(p, dr["y_s"], yq, 4, c, blk0=qq * 4, name="y_s", q="scalar_q")
        p.qfence("scalar_q")
        for k4 in range(4):
            k = qq * 4 + k4
            yk = yq.ap[:, k4 * TC:(k4 + 1) * TC]
            p.op("scalar", lambda e, yk=yk: e.activation(
                sq.ap, yk, AF.Square), ins=[yq], outs=[sq])
            p.op("tensor", lambda e, k=k, yk=yk, bA=bA: e.matmul(
                bA.ap[0:1, 0:TC], ones_col.ap, yk,
                start=(k == 0), stop=(k == FDI - 1)),
                ins=[ones_col, yq], outs=[bA])
            p.op("tensor", lambda e, k=k, bB2=bB2: e.matmul(
                bB2.ap[0:1, 0:TC], ones_col.ap, sq.ap,
                start=(k == 0), stop=(k == FDI - 1)),
                ins=[ones_col, sq], outs=[bB2])
        yield
    r0 = stat.ap[0:1, :]
    p.op("vector", lambda e, bA=bA: e.tensor_scalar(
        r0[:, 0:TC], bA.ap[0:1, 0:TC], 1.0 / DIN, None, ALU.mult),
        ins=[bA], outs=[stat])
    p.op("vector", lambda e, bB2=bB2: e.tensor_scalar(
        r0[:, TC:2 * TC], bB2.ap[0:1, 0:TC], 1.0 / DIN, None, ALU.mult),
        ins=[bB2], outs=[stat])
    p.op("vector", lambda e: e.tensor_tensor(
        r0[:, 2 * TC:3 * TC], r0[:, 0:TC], r0[:, 0:TC], ALU.mult),
        ins=[stat], outs=[stat])
    p.op("vector", lambda e: e.tensor_tensor(
        r0[:, TC:2 * TC], r0[:, TC:2 * TC], r0[:, 2 * TC:3 * TC],
        ALU.subtract), ins=[stat], outs=[stat])
    p.op("scalar", lambda e: e.activation(
        r0[:, 2 * TC:3 * TC], r0[:, TC:2 * TC], AF.Sqrt,
        bias=constk.ap[0:1, 0:1]), ins=[stat, constk], outs=[stat])
    p.op("vector", lambda e: e.reciprocal(
        r0[:, TC:2 * TC], r0[:, 2 * TC:3 * TC]), ins=[stat], outs=[stat])
    p.op("tensor", lambda e, bA=bA: e.matmul(
        bA.ap[:, 0:TC], ones_row.ap[0:1, 0:128], r0[:, 0:TC],
        start=True, stop=True), ins=[ones_row, stat], outs=[bA])
    p.op("tensor", lambda e, bB2=bB2: e.matmul(
        bB2.ap[:, 0:TC], ones_row.ap[0:1, 0:128], r0[:, TC:2 * TC],
        start=True, stop=True), ins=[ones_row, stat], outs=[bB2])
    for qq in range(4):
        load_fm(p, dr["y_s"], yq, 4, c, blk0=qq * 4, name="y_s", q="scalar_q")
        p.qfence("scalar_q")
        for k2 in range(2):
            load_fm(p, dr["xc_s"], xk, 2, c, blk0=qq * 4 + k2 * 2,
                    name="xc_s", q="scalar_q")
            load_fm(p, dr["zs_s"], zk, 2, c, blk0=qq * 4 + k2 * 2,
                    name="zs_s", q="scalar_q")
            for kj in range(2):
                k4 = k2 * 2 + kj
                k = qq * 4 + k4
                yk = yq.ap[:, k4 * TC:(k4 + 1) * TC]
                p.op("vector", lambda e, yk=yk, bA=bA: e.tensor_tensor(
                    yk, yk, bA.ap[:, 0:TC], ALU.subtract), ins=[yq, bA], outs=[yq])
                p.op("vector", lambda e, yk=yk, bB2=bB2: e.tensor_tensor(
                    yk, yk, bB2.ap[:, 0:TC], ALU.mult), ins=[yq, bB2], outs=[yq])
                p.op("vector", lambda e, yk=yk, k=k: e.tensor_scalar(
                    yk, yk, tabs["ln_g_pc"].ap[:, k:k + 1],
                    tabs["ln_b_pc"].ap[:, k:k + 1], ALU.mult, ALU.add),
                    ins=[yq, tabs["ln_g_pc"], tabs["ln_b_pc"]], outs=[yq])
                p.op("vector", lambda e, yk=yk, k=k, kj=kj, xk=xk:
                     e.scalar_tensor_tensor(
                         yk, xk.ap[:, kj * TC:(kj + 1) * TC],
                         tabs["d_param_pc"].ap[:, k:k + 1], yk, ALU.mult,
                         ALU.add),
                     ins=[xk, tabs["d_param_pc"], yq], outs=[yq])
                p.op("vector", lambda e, yk=yk, kj=kj, zk=zk: e.tensor_tensor(
                    yk, yk, zk.ap[:, kj * TC:(kj + 1) * TC], ALU.mult),
                    ins=[yq, zk], outs=[yq])
                yield
        d3 = dr["yg_s"].rearrange("(b q) t -> q b t", q=128)[
            :, qq * 4:(qq + 1) * 4, c * TC:(c + 1) * TC]
        s3 = yq.ap.rearrange("q (b t) -> q b t", t=TC)
        p.dma("gpsimd_q", lambda e, d3=d3, s3=s3: e.dma_start(out=d3, in_=s3),
              ins=[yq], outs=[p.dbuf("yg_s", c)])
        if "yg" in dbg:
            d3b = dr["dbg_yg"].rearrange("(b q) t -> q b t", q=128)[
                :, qq * 4:(qq + 1) * 4, c * TC:(c + 1) * TC]
            p.dma("gpsimd_q", lambda e, d3b=d3b, s3=s3: e.dma_start(
                out=d3b, in_=s3), ins=[yq])
    p.qfence("gpsimd_q")
    p.qfence("scalar_q")


def kernel(**inputs):
    nc = build()
    in_maps = [host_inputs(inputs, core) for core in range(8)]
    res = run_bass_kernel_spmd(nc, in_maps, core_ids=list(range(8)))
    out = np.stack([res.results[i]["out"] for i in range(8)], axis=0)
    return out


# version 15
# speedup vs baseline: 1.0456x; 1.0456x over previous
import numpy as np
import concourse.bass as bass
import concourse.mybir as mybir
from concourse.bass import IndirectOffsetOnAxis
from concourse.bass_utils import run_bass_kernel_spmd

F32 = mybir.dt.float32
U32 = mybir.dt.uint32
AF = mybir.ActivationFunctionType
ALU = mybir.AluOpType

T = 2048
D = 1024
DIN = 2048
S = 16
NE = 16
M = 512
NCHUNK = 4
CAP = 384
TC = T // NCHUNK
FD = D // 128
FDI = DIN // 128
LN_EPS = 1e-5

ENGINES = ("tensor", "vector", "scalar", "gpsimd", "sync")
SEMS = ("tensor", "vector", "scalar", "gpsimd", "sync_q", "scalar_q", "gpsimd_q")
ENG_SEM = {"tensor": "tensor", "vector": "vector", "scalar": "scalar",
           "gpsimd": "gpsimd", "sync": None}


class Buf:
    __slots__ = ("ap", "wr", "rd", "name")

    def __init__(self, ap, name=""):
        self.ap = ap
        self.wr = None
        self.rd = []
        self.name = name


class Prog:
    def __init__(self, nc, sems):
        self.nc = nc
        self.sems = sems
        self.cnt = {s: 0 for s in SEMS}
        self.wm = {}
        self.streams = {e: [] for e in ENGINES}
        self.nops = 0
        self.reducy = set()
        self.dram = {}

    def dbuf(self, name, c=None):
        key = (name, c)
        if key not in self.dram:
            self.dram[key] = Buf(None, f"dram_{name}_{c}")
        return self.dram[key]

    def _emit(self, eng, fn, deps, sem_name, inc, soft_deps=()):
        waits = []
        own = ENG_SEM[eng]
        for hard, lst in ((True, deps), (False, soft_deps)):
            for s, v in lst:
                if v <= 0:
                    continue
                if not hard and s == own:
                    continue
                key = (eng, s)
                if self.wm.get(key, 0) >= v:
                    continue
                self.wm[key] = v
                waits.append((s, v))
        self.cnt[sem_name] += inc
        tok = (sem_name, self.cnt[sem_name])
        self.streams[eng].append((waits, fn, sem_name, inc))
        self.nops += 1
        return tok

    def op(self, eng, fn, ins=(), outs=(), extra_deps=(), reducy=False):
        deps = list(extra_deps)
        soft = []
        for b in ins:
            if b.wr is not None:
                deps.append(b.wr)
        for b in outs:
            if b.wr is not None:
                (deps if b.wr in self.reducy else soft).append(b.wr)
            soft.extend(b.rd)
        tok = self._emit(eng, fn, deps, ENG_SEM[eng], 1, soft_deps=soft)
        if reducy:
            self.reducy.add(tok)
        for b in ins:
            b.rd.append(tok)
        for b in outs:
            b.wr = tok
            b.rd = []
        return tok

    def dma(self, queue, fn, ins=(), outs=(), extra_deps=()):
        eng = {"sync_q": "sync", "scalar_q": "scalar", "gpsimd_q": "gpsimd"}[queue]
        deps = list(extra_deps)
        for b in ins:
            if b is not None and b.wr is not None:
                deps.append(b.wr)
        for b in outs:
            if b is None:
                continue
            if b.wr is not None:
                deps.append(b.wr)
            deps.extend(b.rd)
        tok = self._emit(eng, fn, deps, queue, 16)
        self._emit(eng, None, [tok], queue, 0)
        for b in ins:
            if b is not None:
                b.rd.append(tok)
        for b in outs:
            if b is not None:
                b.wr = tok
                b.rd = []
        return tok

    def qfence(self, queue):
        eng = {"sync_q": "sync", "scalar_q": "scalar", "gpsimd_q": "gpsimd"}[queue]
        self._emit(eng, None, [(queue, self.cnt[queue])], queue, 0)

    def wait_all(self, eng):
        deps = [(s, self.cnt[s]) for s in SEMS]
        self._emit(eng, None, deps, ENG_SEM[eng] or "sync_q", 0)

    def replay(self, block):
        sems = self.sems

        def mk(eng_name):
            def run(eng):
                for waits, fn, sem_name, inc in self.streams[eng_name]:
                    for s, v in waits:
                        eng.wait_ge(sems[s], v)
                    if fn is not None:
                        instr = fn(eng)
                        if inc:
                            instr.then_inc(sems[sem_name], inc)
            return run

        block.tensor(mk("tensor"))
        block.vector(mk("vector"))
        block.scalar(mk("scalar"))
        block.gpsimd(mk("gpsimd"))
        block.sync(mk("sync"))


class Arena:

    def __init__(self, handle, ncols):
        self.h = handle
        self.n = ncols
        self.free = [(0, ncols, [])]
        self.live = {}
        self.peak = 0

    def alloc(self, cols, name):
        cols = int(cols)
        for i, (st, ln, haz) in enumerate(self.free):
            if ln >= cols:
                rest = (st + cols, ln - cols, list(haz))
                self.free[i:i + 1] = [rest] if rest[1] > 0 else []
                self.live[name] = (st, cols)
                self.peak = max(self.peak, max((s + l for s, l, _ in ([(st, cols, 0)])), default=0),
                                st + cols)
                b = Buf(self.h[:, st:st + cols], name)
                if haz:
                    b.rd = list(haz)
                return b
        raise MemoryError(f"SBUF arena: cannot alloc {cols} cols for {name}; "
                          f"free={[(s, l) for s, l, _ in self.free]}")

    def freebuf(self, b):
        st, ln = self.live.pop(b.name)
        haz = list(b.rd)
        if b.wr is not None:
            haz.append(b.wr)
        self.free.append((st, ln, haz))
        self.free.sort(key=lambda x: x[0])
        merged = []
        for seg in self.free:
            if merged and merged[-1][0] + merged[-1][1] == seg[0]:
                s0, l0, h0 = merged.pop()
                merged.append((s0, l0 + seg[1], h0 + list(seg[2])))
            else:
                merged.append(list(seg) if isinstance(seg, tuple) else seg)
        self.free = [tuple(s) for s in merged]


def host_inputs(inputs, core):
    x = np.asarray(inputs["L0_attn_x"])[core]
    W_in = np.asarray(inputs["L0_attn_W_in"])
    m = {}
    m["x"] = np.ascontiguousarray(x)
    m["w_in_x_T"] = np.ascontiguousarray(W_in[:DIN].T)
    m["w_in_z_T"] = np.ascontiguousarray(W_in[DIN:].T)
    m["w_dt_T"] = np.ascontiguousarray(np.asarray(inputs["L0_attn_W_dt"]).T)
    wxT = np.asarray(inputs["L0_attn_W_x"]).T
    perm = np.empty(32, np.int64)
    perm[0::2] = np.arange(16)
    perm[1::2] = np.arange(16) + 16
    m["w_x_T"] = np.ascontiguousarray(wxT[:, perm])
    m["w_out_T"] = np.ascontiguousarray(np.asarray(inputs["L0_attn_W_out"]).T)
    m["gate_w_T"] = np.ascontiguousarray(np.asarray(inputs["L0_ffn_gate_w"]).T)
    m["w_gate"] = np.ascontiguousarray(np.asarray(inputs["L0_ffn_w_gate"]))
    m["w_up"] = np.ascontiguousarray(np.asarray(inputs["L0_ffn_w_up"]))
    m["w_down"] = np.ascontiguousarray(np.asarray(inputs["L0_ffn_w_down"]))
    m["ws_gate"] = np.ascontiguousarray(np.asarray(inputs["L0_ffn_ws_gate"]))
    m["ws_up"] = np.ascontiguousarray(np.asarray(inputs["L0_ffn_ws_up"]))
    m["ws_down"] = np.ascontiguousarray(np.asarray(inputs["L0_ffn_ws_down"]))

    def pc(v):
        return np.ascontiguousarray(np.asarray(v).reshape(FDI, 128).T)
    m["b_dt_pc"] = pc(inputs["L0_attn_b_dt"])
    m["conv_b_pc"] = pc(inputs["L0_attn_conv_b"])
    m["d_param_pc"] = pc(inputs["L0_attn_D_param"])
    m["ln_g_pc"] = pc(inputs["L0_attn_ln_g"])
    m["ln_b_pc"] = pc(inputs["L0_attn_ln_b"])
    cw = np.asarray(inputs["L0_attn_conv_w"]).reshape(FDI, 128, 4)
    m["conv_w_pc"] = np.ascontiguousarray(cw.transpose(1, 0, 2).reshape(128, FDI * 4))
    m["a_log"] = np.ascontiguousarray(np.asarray(inputs["L0_attn_A_log"]).reshape(1, S))
    m["identity"] = np.eye(128, dtype=np.float32)
    ck = np.zeros((128, 4), dtype=np.float32)
    ck[:, 0] = LN_EPS
    ck[:, 1] = 1.0
    m["constk"] = ck
    m["ones_row"] = np.ones((1, 128), dtype=np.float32)
    m["iota16"] = np.tile(np.arange(16, dtype=np.float32), (128, 1))
    sel = np.zeros((32, 32 * 128), dtype=np.float32)
    for j in range(32):
        sel[j, j * 128:(j + 1) * 128] = 1.0
    m["sel32"] = sel
    return m


DBG_SHAPES = {"xn": [D, T], "xc": [DIN, T], "dt": [DIN, T], "bp": [S, T],
              "cp": [S, T], "y": [DIN, T], "yg": [DIN, T], "s_out": [D, T],
              "hacc": [D, T]}


def build(debug_taps=(), stages=99):
    dbg = set(debug_taps)
    nc = bass.Bass("TRN2", target_bir_lowering=False)

    dr = {}
    dr["x"] = nc.dram_tensor("x", [T, D], F32, kind="ExternalInput").ap()
    for nm, shp in [("w_in_x_T", [D, DIN]), ("w_in_z_T", [D, DIN]),
                    ("w_dt_T", [DIN, DIN]), ("w_x_T", [DIN, 2 * S]),
                    ("w_out_T", [DIN, D]), ("gate_w_T", [D, NE]),
                    ("w_gate", [NE, D, M]), ("w_up", [NE, D, M]),
                    ("w_down", [NE, M, D]),
                    ("ws_gate", [D, M]), ("ws_up", [D, M]), ("ws_down", [M, D]),
                    ("b_dt_pc", [128, FDI]), ("conv_b_pc", [128, FDI]),
                    ("d_param_pc", [128, FDI]), ("ln_g_pc", [128, FDI]),
                    ("ln_b_pc", [128, FDI]), ("conv_w_pc", [128, FDI * 4]),
                    ("a_log", [1, S]), ("identity", [128, 128]),
                    ("ones_row", [1, 128]), ("iota16", [128, 16]),
                    ("constk", [128, 4]), ("sel32", [32, 32 * 128])]:
        dr[nm] = nc.dram_tensor(nm, shp, F32, kind="ExternalInput").ap()
    dr["out"] = nc.dram_tensor("out", [T, D], F32, kind="ExternalOutput").ap()
    dr["xn_s"] = nc.dram_tensor("xn_s", [D, T], F32, kind="Internal").ap()
    dr["xc_s"] = nc.dram_tensor("xc_s", [DIN, T], F32, kind="Internal").ap()
    dr["dt_s"] = nc.dram_tensor("dt_s", [DIN, T], F32, kind="Internal").ap()
    dr["y_s"] = nc.dram_tensor("y_s", [DIN, T], F32, kind="Internal").ap()
    dr["yg_s"] = nc.dram_tensor("yg_s", [DIN, T], F32, kind="Internal").ap()
    dr["hacc_s"] = nc.dram_tensor("hacc_s", [D, T], F32, kind="Internal").ap()
    dr["xn_tok_s"] = nc.dram_tensor("xn_tok_s", [T, D], F32, kind="Internal").ap()
    dr["bpcp_s"] = nc.dram_tensor("bpcp_s", [2 * S, T], F32, kind="Internal").ap()
    dr["zs_s"] = nc.dram_tensor("zs_s", [DIN, T], F32, kind="Internal").ap()
    dr["glist"] = nc.dram_tensor("glist", [NE * CAP, 1], U32, kind="Internal").ap()
    dr["dlist"] = nc.dram_tensor("dlist", [NE * CAP, 1], U32, kind="Internal").ap()
    dr["wlist"] = nc.dram_tensor("wlist", [NE * CAP, 1], F32, kind="Internal").ap()
    dr["moebuf"] = nc.dram_tensor("moebuf", [2 * T + NE * CAP, D], F32, kind="Internal").ap()
    for nm in dbg:
        if nm == "lists":
            dr["dbg_lists"] = nc.dram_tensor("dbg_lists", [128, 144], F32,
                                             kind="ExternalOutput").ap()
            continue
        if nm == "moebuf":
            dr["dbg_moebuf"] = nc.dram_tensor("dbg_moebuf", [2 * T, D], F32,
                                              kind="ExternalOutput").ap()
            continue
        dr["dbg_" + nm] = nc.dram_tensor("dbg_" + nm, DBG_SHAPES[nm], F32,
                                         kind="ExternalOutput").ap()

    import contextlib
    with contextlib.ExitStack() as stack:
        ncols = 53150
        sbh = stack.enter_context(nc.sbuf_tensor("arena", [128, ncols], F32))
        psh = [stack.enter_context(nc.psum_tensor(f"ps{i}", [128, 512], F32))
               for i in range(8)]
        sems = {s: stack.enter_context(nc.semaphore(s)) for s in SEMS}
        p = Prog(nc, sems)
        ar = Arena(sbh, ncols)
        pbk = [Buf(psh[i], f"ps{i}") for i in range(8)]
        build_body(p, ar, pbk, dr, dbg, stages)
        with nc.Block() as block:
            p.replay(block)
    return nc


def build_body(p, ar, pbk, dr, dbg, stages):
    nc = p.nc

    ident = ar.alloc(128, "ident")
    ones_row = ar.alloc(128, "ones_row")
    tabs = {}
    for nm in ("b_dt_pc", "conv_b_pc", "d_param_pc", "ln_g_pc", "ln_b_pc"):
        tabs[nm] = ar.alloc(FDI, nm)
        p.dma("sync_q", lambda e, nm=nm, b=tabs[nm]: e.dma_start(out=b.ap, in_=dr[nm]),
              outs=[tabs[nm]])
    conv_w = ar.alloc(FDI * 4, "conv_w_pc")
    a_log = ar.alloc(S, "a_log")
    iota16 = ar.alloc(S, "iota16")
    constk = ar.alloc(4, "constk")
    p.dma("sync_q", lambda e: e.dma_start(out=constk.ap, in_=dr["constk"]), outs=[constk])
    p.dma("sync_q", lambda e: e.dma_start(out=ident.ap, in_=dr["identity"]), outs=[ident])
    p.dma("sync_q", lambda e: e.dma_start(out=ones_row.ap[0:1, :], in_=dr["ones_row"]),
          outs=[ones_row])
    p.dma("sync_q", lambda e: e.dma_start(out=conv_w.ap, in_=dr["conv_w_pc"]), outs=[conv_w])
    p.dma("sync_q", lambda e: e.dma_start(out=a_log.ap[0:1, :], in_=dr["a_log"]), outs=[a_log])
    p.dma("sync_q", lambda e: e.dma_start(out=iota16.ap, in_=dr["iota16"]), outs=[iota16])

    a_row = ar.alloc(S, "a_row")
    p.op("scalar", lambda e: e.activation(a_row.ap[0:1, :], a_log.ap[0:1, :], AF.Exp),
         ins=[a_log], outs=[a_row])
    p.op("vector", lambda e: e.tensor_scalar(a_row.ap[0:1, :], a_row.ap[0:1, :],
                                             -1.0, None, ALU.mult),
         ins=[a_row], outs=[a_row])
    a_bc = ar.alloc(S, "a_bc")
    p.op("tensor", lambda e: e.matmul(pbk[7].ap[:, 0:S], ones_row.ap[0:1, :],
                                      a_row.ap[0:1, :], start=True, stop=True),
         ins=[ones_row, a_row], outs=[pbk[7]])
    p.op("vector", lambda e: e.tensor_copy(a_bc.ap, pbk[7].ap[:, 0:S]),
         ins=[pbk[7]], outs=[a_bc])

    ctx = dict(ident=ident, ones_row=ones_row, tabs=tabs, conv_w=conv_w,
               a_bc=a_bc, iota16=iota16, constk=constk)

    ones_col = ar.alloc(1, "onec")
    p.op("vector", lambda e: e.memset(ones_col.ap, 1.0), outs=[ones_col])
    ctx["ones_col"] = ones_col
    pass1(p, ar, pbk, dr, dbg, ctx)
    pass2(p, ar, pbk, dr, dbg, ctx)
    rgen = routing_sparse_v3(p, ar, pbk, dr, ctx)

    def rfeed(n):
        for _ in range(n):
            if next(rgen, "END") == "END":
                break
    bpcp = pass3(p, ar, pbk, dr, dbg, ctx, feed=rfeed)
    pass3_store_bpcp(p, ar, dr, bpcp)
    rfeed(10 ** 9)
    sc_toks = ctx["sc_toks"]
    ms = {}
    ms["wh"] = [ar.alloc(4 * M, "ms_wh0"), ar.alloc(4 * M, "ms_wh1")]
    ms["wi"] = 0
    ms["xe"] = ar.alloc(FD * TC, "ms_xe")
    ms["xg"] = [ar.alloc(D, f"ms_xg{i}") for i in range(CAP // 128)]
    ms["gs"] = ar.alloc(4 * TC, "ms_gs")
    ms["ub"] = ar.alloc(4 * TC, "ms_ub")
    ms["rb"] = ar.alloc(CAP, "ms_rb")
    ms["idxg"] = ar.alloc(2 * (CAP // 128), "ms_idxg")
    gd = [sc_toks[-1], p.dbuf("glist").wr, p.dbuf("dlist").wr,
          p.dbuf("wlist").wr]
    gd += [tk for tk in (p.dbuf("xn_tok_s", cc).wr for cc in range(NCHUNK)) if tk]
    ms["gather_dep"] = gd
    ms["first"] = True
    state = ar.alloc(S * FDI, "scan_state")
    p.op("vector", lambda e: e.memset(state.ap, 0.0), outs=[state])
    ctx["gate_bufs"] = dict(
        yq=ar.alloc(4 * TC, "g_yq"), sq=ar.alloc(TC, "g_sq"),
        stat=ar.alloc(3 * TC, "g_st"), xk=ar.alloc(2 * TC, "g_xk"),
        zk=ar.alloc(2 * TC, "g_zk"))
    xn_c = ar.alloc(FD * TC, "blk_xn")
    for e_i in range(4):
        for _ in expert_emit_v4(p, ar, pbk, dr, ctx, ms, e_i):
            pass
    import itertools
    for c in range(NCHUNK):
        sb4 = pass4v4_alloc(p, ar, dr, c)
        load_fm(p, dr["xn_s"], xn_c, FD, c, name="xn_s")
        gens = [pass6az_v3(p, ar, pbk, dr, dbg, ctx, c, xn_c),
                 shared_gen_v4(p, ar, pbk, dr, ctx, ms, c, xn_c)]
        if c < NCHUNK - 1:
            for e_i in range(4 * (c + 1), 4 * (c + 1) + 4):
                gens.append(expert_emit_v4(p, ar, pbk, dr, ctx, ms, e_i))
        chain = itertools.chain(*gens)

        def feed(n, chain=chain):
            for _ in range(n):
                if next(chain, "END") == "END":
                    break
        pass4v4(p, ar, pbk, dr, dbg, ctx, c, state, sb4, feed)
        feed(10 ** 9)
        if c > 0:
            for _ in pass6gate_v4(p, ar, pbk, dr, dbg, ctx, c - 1):
                pass
    for _ in pass6gate_v4(p, ar, pbk, dr, dbg, ctx, NCHUNK - 1):
        pass
    for b in ms["wh"] + ms["xg"] + [ms["xe"], ms["gs"], ms["ub"], ms["rb"],
                                    ms["idxg"], state, xn_c] + list(
                                        ctx["gate_bufs"].values()):
        ar.freebuf(b)
    if "lists" in dbg:
        lb = ar.alloc(3 * 48, "dbg_lb")
        NG = NE * CAP // 128
        mt = [ctx["moe_sc"][-1]] if "moe_sc" in ctx else []
        for j, nm in enumerate(("glist", "dlist", "wlist")):
            src_ap = dr[nm]
            if nm != "wlist":
                p.dma("sync_q", lambda e, j=j, src_ap=src_ap, lb=lb, NG=NG:
                      e.dma_start(out=lb.ap[:, j * NG:(j + 1) * NG].bitcast(U32),
                                  in_=src_ap), outs=[lb], extra_deps=mt)
            else:
                p.dma("sync_q", lambda e, j=j, src_ap=src_ap, lb=lb, NG=NG:
                      e.dma_start(out=lb.ap[:, j * NG:(j + 1) * NG],
                                  in_=src_ap), outs=[lb], extra_deps=mt)
        cv = ar.alloc(3 * 48, "dbg_cv")
        NGc = NE * CAP // 128
        for j in range(2):
            p.op("vector", lambda e, j=j, NGc=NGc: e.tensor_copy(
                cv.ap[:, j * NGc:(j + 1) * NGc],
                lb.ap[:, j * NGc:(j + 1) * NGc].bitcast(U32)),
                ins=[lb], outs=[cv])
        p.op("vector", lambda e, NGc=NGc: e.tensor_copy(
            cv.ap[:, 2 * NGc:3 * NGc], lb.ap[:, 2 * NGc:3 * NGc]),
            ins=[lb], outs=[cv])
        p.dma("gpsimd_q", lambda e, cv=cv: e.dma_start(
            out=dr["dbg_lists"], in_=cv.ap), ins=[cv])
    if "moebuf" in dbg:
        mb = ar.alloc(FD * TC, "dbg_mb")
        mt = [ctx["moe_sc"][-1]] if "moe_sc" in ctx else []
        for cc in range(8):
            p.dma("sync_q", lambda e, cc=cc, mb=mb: e.dma_start(
                out=mb.ap, in_=dr["moebuf"][cc * 512:(cc + 1) * 512, :].rearrange(
                    "(b q) d -> q b d", q=128)), outs=[mb], extra_deps=mt)
            p.dma("gpsimd_q", lambda e, cc=cc, mb=mb: e.dma_start(
                out=dr["dbg_moebuf"][cc * 512:(cc + 1) * 512, :].rearrange(
                    "(b q) d -> q b d", q=128), in_=mb.ap), ins=[mb])
        ar.freebuf(mb)
    pass6b(p, ar, pbk, dr, dbg, ctx)

    p.wait_all("sync")


def pass1(p, ar, pbk, dr, dbg, ctx):
    ident = ctx["ident"]
    constk = ctx["constk"]
    sq2 = [ar.alloc(D, f"p1_sq{j}") for j in range(2)]
    for c in range(NCHUNK):
        xts = [ar.alloc(D, f"p1_x_{c}_{i}") for i in range(4)]
        mv2 = [ar.alloc(8, f"p1_mv_{c}_{j}") for j in range(2)]
        xn_cb = ar.alloc(FD * TC, f"p1_xn_{c}")
        for pt in range(4):
            xt = xts[pt]
            mv = mv2[pt % 2]
            sq = sq2[pt % 2]
            t0 = c * TC + pt * 128
            p.dma("sync_q", lambda e, xt=xt, t0=t0: e.dma_start(
                out=xt.ap, in_=dr["x"][t0:t0 + 128, :]), outs=[xt])
            p.op("vector", lambda e, xt=xt, mv=mv: e.tensor_reduce(
                mv.ap[:, 0:1], xt.ap, mybir.AxisListType.X, ALU.add),
                ins=[xt], outs=[mv])
            p.op("scalar", lambda e, xt=xt, mv=mv, sq=sq: e.activation(
                sq.ap, xt.ap, AF.Square, accum_out=mv.ap[:, 1:2]),
                ins=[xt], outs=[sq, mv])
            p.op("vector", lambda e, mv=mv: e.tensor_scalar(
                mv.ap[:, 2:4], mv.ap[:, 0:2], 1.0 / D, None, ALU.mult),
                ins=[mv], outs=[mv])
            p.op("vector", lambda e, mv=mv: e.tensor_tensor(
                mv.ap[:, 4:5], mv.ap[:, 2:3], mv.ap[:, 2:3], ALU.mult),
                ins=[mv], outs=[mv])
            p.op("vector", lambda e, mv=mv: e.tensor_tensor(
                mv.ap[:, 5:6], mv.ap[:, 3:4], mv.ap[:, 4:5], ALU.subtract),
                ins=[mv], outs=[mv])
            p.op("scalar", lambda e, mv=mv: e.activation(
                mv.ap[:, 6:7], mv.ap[:, 5:6], AF.Sqrt, bias=constk.ap[:, 0:1]),
                ins=[mv, constk], outs=[mv])
            p.op("vector", lambda e, mv=mv: e.reciprocal(
                mv.ap[:, 7:8], mv.ap[:, 6:7]), ins=[mv], outs=[mv])
            p.op("vector", lambda e, xt=xt, mv=mv: e.tensor_scalar(
                xt.ap, xt.ap, mv.ap[:, 2:3], mv.ap[:, 7:8],
                ALU.subtract, ALU.mult), ins=[xt, mv], outs=[xt])
        for pt in range(4):
            t0 = c * TC + pt * 128
            p.dma("gpsimd_q", lambda e, xt=xts[pt], t0=t0: e.dma_start(
                out=dr["xn_tok_s"][t0:t0 + 128, :], in_=xt.ap),
                ins=[xts[pt]], outs=[p.dbuf("xn_tok_s", c)])
        for f in range(FD):
            bank = pbk[f % 4]
            for pt in range(4):
                xt = xts[pt]
                p.op("tensor", lambda e, xt=xt, f=f, pt=pt, bank=bank: e.transpose(
                    bank.ap[:, pt * 128:(pt + 1) * 128],
                    xt.ap[:, f * 128:(f + 1) * 128], ident.ap),
                    ins=[xt, ident], outs=[bank])
            p.op("vector", lambda e, bank=bank, d=xn_cb, f=f: e.tensor_copy(
                d.ap[:, f * TC:(f + 1) * TC], bank.ap[:, 0:512]),
                ins=[bank], outs=[xn_cb])
        store_fm(p, dr["xn_s"], xn_cb, FD, c, name="xn_s")
        if "xn" in dbg:
            store_fm(p, dr["dbg_xn"], xn_cb, FD, c)
        for b in xts + mv2 + [xn_cb]:
            ar.freebuf(b)
    for b in sq2:
        ar.freebuf(b)
    p.qfence("gpsimd_q")


def store_fm(p, dram_ap, buf, nblk, c, q="gpsimd_q", rows=128, name=None):
    d3 = dram_ap.rearrange("(b q) t -> q b t", q=128)[0:rows, :, c * TC:(c + 1) * TC]
    s3 = buf.ap[0:rows, 0:nblk * TC].rearrange("q (b t) -> q b t", t=TC)
    outs = [p.dbuf(name, c)] if name else []
    return p.dma(q, lambda e: e.dma_start(out=d3, in_=s3), ins=[buf], outs=outs)


def load_fm(p, dram_ap, buf, nblk, c, q="sync_q", rows=128, blk0=0, name=None):
    d3 = dram_ap.rearrange("(b q) t -> q b t", q=128)[0:rows, blk0:blk0 + nblk,
                                                      c * TC:(c + 1) * TC]
    s3 = buf.ap[0:rows, 0:nblk * TC].rearrange("q (b t) -> q b t", t=TC)
    ins = [p.dbuf(name, c)] if name else []
    return p.dma(q, lambda e: e.dma_start(out=s3, in_=d3), outs=[buf], ins=ins)


def mm_acc(p, bank, w_tiles, x_buf, nk, n=TC, bank_cols=None):
    ins_bufs = None
    for k in range(nk):
        lhsT = w_tiles(k)
        rhs = x_buf(k)
        p.op("tensor", lambda e, lhsT=lhsT, rhs=rhs, k=k: e.matmul(
            bank.ap[:, 0:n] if bank_cols is None else bank.ap[:, bank_cols[0]:bank_cols[1]],
            lhsT, rhs, start=(k == 0), stop=(k == nk - 1)),
            ins=[], outs=[bank], extra_deps=[])


def pass2(p, ar, pbk, dr, dbg, ctx):
    conv_w = ctx["conv_w"]
    tabs = ctx["tabs"]
    wb = [ar.alloc(DIN, f"p2_w_{k}") for k in range(FD)]
    for k in range(FD):
        p.dma("sync_q", lambda e, k=k, b=wb[k]: e.dma_start(
            out=b.ap, in_=dr["w_in_x_T"][k * 128:(k + 1) * 128, :]), outs=[wb[k]])
    tail = ar.alloc(3 * FDI, "p2_tail")
    p.op("vector", lambda e: e.memset(tail.ap, 0.0), outs=[tail])
    for c in range(NCHUNK):
        xn_c = ar.alloc(FD * TC, f"p2_xn_{c}")
        load_fm(p, dr["xn_s"], xn_c, FD, c, name="xn_s")
        xp = [ar.alloc(TC + 3, f"p2_xp_{c}_{i}") for i in range(FDI)]
        xc = ar.alloc(FDI * TC, f"p2_xc_{c}")
        acc = ar.alloc(TC, f"p2_acc_{c}")
        for mi in range(FDI):
            bank = pbk[mi % 2]
            p.op("vector", lambda e, mi=mi, b=xp[mi]: e.tensor_copy(
                b.ap[:, 0:3], tail.ap[:, mi * 3:mi * 3 + 3]),
                ins=[tail], outs=[xp[mi]])
            for k in range(FD):
                p.op("tensor", lambda e, k=k, mi=mi, bank=bank, xn_c=xn_c: e.matmul(
                    bank.ap[:, 0:TC], wb[k].ap[:, mi * 128:(mi + 1) * 128],
                    xn_c.ap[:, k * TC:(k + 1) * TC],
                    start=(k == 0), stop=(k == FD - 1)),
                    ins=[wb[k], xn_c], outs=[bank])
            p.op("scalar", lambda e, mi=mi, bank=bank, b=xp[mi]: e.copy(
                b.ap[:, 3:TC + 3], bank.ap[:, 0:TC]), ins=[bank], outs=[xp[mi]])
            p.op("vector", lambda e, mi=mi, b=xp[mi]: e.tensor_copy(
                tail.ap[:, mi * 3:mi * 3 + 3], b.ap[:, TC:TC + 3]),
                ins=[xp[mi]], outs=[tail])
            p.op("vector", lambda e, mi=mi, b=xp[mi], acc=acc: e.tensor_scalar(
                acc.ap, b.ap[:, 0:TC], conv_w.ap[:, mi * 4:mi * 4 + 1], None,
                ALU.mult), ins=[xp[mi], conv_w], outs=[acc])
            for kk in range(1, 4):
                p.op("vector", lambda e, mi=mi, kk=kk, b=xp[mi], acc=acc:
                     e.scalar_tensor_tensor(
                         acc.ap, b.ap[:, kk:kk + TC],
                         conv_w.ap[:, mi * 4 + kk:mi * 4 + kk + 1], acc.ap,
                         ALU.mult, ALU.add),
                     ins=[xp[mi], conv_w, acc], outs=[acc])
            p.op("scalar", lambda e, mi=mi, acc=acc, xc=xc: e.activation(
                xc.ap[:, mi * TC:(mi + 1) * TC], acc.ap, AF.Silu,
                bias=tabs["conv_b_pc"].ap[:, mi:mi + 1]),
                ins=[acc, tabs["conv_b_pc"]], outs=[xc])
        store_fm(p, dr["xc_s"], xc, FDI, c, name="xc_s")
        if "xc" in dbg:
            store_fm(p, dr["dbg_xc"], xc, FDI, c)
        for b in [xn_c, xc, acc] + xp:
            ar.freebuf(b)
    for b in wb + [tail]:
        ar.freebuf(b)
    p.qfence("gpsimd_q")


def pass3(p, ar, pbk, dr, dbg, ctx, feed=None):
    tabs = ctx["tabs"]
    constk = ctx["constk"]
    bpcp = ar.alloc(T, "bpcp")
    wx = ar.alloc(FDI * 2 * S, "p3_wx")
    p.dma("sync_q", lambda e: e.dma_start(
        out=wx.ap.rearrange("q (b s) -> q b s", s=2 * S),
        in_=dr["w_x_T"].rearrange("(b q) s -> q b s", q=128)), outs=[wx])
    tmp2 = [[ar.alloc(TC, f"p3_t{j}_{i}") for i in range(4)] for j in range(2)]
    for mh in range(2):
        wdt = [ar.alloc(1024, f"p3_w_{mh}_{k}") for k in range(FDI)]
        for k in range(FDI):
            p.dma("sync_q", lambda e, k=k, mh=mh, b=wdt[k]: e.dma_start(
                out=b.ap, in_=dr["w_dt_T"][k * 128:(k + 1) * 128,
                                           mh * 1024:(mh + 1) * 1024]),
                outs=[wdt[k]])
        for c in range(NCHUNK):
            xc_c = ar.alloc(FDI * TC, f"p3_xc_{mh}_{c}")
            load_fm(p, dr["xc_s"], xc_c, FDI, c, name="xc_s")
            dtb = ar.alloc(8 * TC, f"p3_dt_{mh}_{c}")
            for mo in range(8):
                mi = mh * 8 + mo
                bank = pbk[mo % 2]
                for k in range(FDI):
                    p.op("tensor", lambda e, k=k, mo=mo, bank=bank, xc_c=xc_c,
                         wdt=wdt: e.matmul(
                             bank.ap[:, 0:TC], wdt[k].ap[:, mo * 128:(mo + 1) * 128],
                             xc_c.ap[:, k * TC:(k + 1) * TC],
                             start=(k == 0), stop=(k == FDI - 1)),
                         ins=[wdt[k], xc_c], outs=[bank])
                v, r, m, l = tmp2[mo % 2]
                p.op("scalar", lambda e, bank=bank, mi=mi: e.activation(
                    v.ap, bank.ap[:, 0:TC], AF.Identity,
                    bias=tabs["b_dt_pc"].ap[:, mi:mi + 1]),
                    ins=[bank, tabs["b_dt_pc"]], outs=[v])
                p.op("scalar", lambda e: e.activation(r.ap, v.ap, AF.Relu),
                     ins=[v], outs=[r])
                p.op("vector", lambda e: e.scalar_tensor_tensor(
                    m.ap, r.ap, -2.0, v.ap, ALU.mult, ALU.add),
                    ins=[r, v], outs=[m])
                p.op("scalar", lambda e: e.activation(m.ap, m.ap, AF.Exp),
                     ins=[m], outs=[m])
                p.op("scalar", lambda e: e.activation(
                    l.ap, m.ap, AF.Ln, bias=constk.ap[:, 1:2]),
                    ins=[m, constk], outs=[l])
                p.op("vector", lambda e, mo=mo, dtb=dtb: e.tensor_tensor(
                    dtb.ap[:, mo * TC:(mo + 1) * TC], r.ap, l.ap, ALU.add),
                    ins=[r, l], outs=[dtb])
                if feed is not None:
                    feed(1)
            d3 = dr["dt_s"].rearrange("(b q) t -> q b t", q=128)[
                :, mh * 8:(mh + 1) * 8, c * TC:(c + 1) * TC]
            s3 = dtb.ap.rearrange("q (b t) -> q b t", t=TC)
            p.dma("gpsimd_q", lambda e, d3=d3, s3=s3: e.dma_start(out=d3, in_=s3),
                  ins=[dtb], outs=[p.dbuf("dt_s", c)])
            if "dt" in dbg:
                d3b = dr["dbg_dt"].rearrange("(b q) t -> q b t", q=128)[
                    :, mh * 8:(mh + 1) * 8, c * TC:(c + 1) * TC]
                p.dma("gpsimd_q", lambda e, d3b=d3b, s3=s3: e.dma_start(
                    out=d3b, in_=s3), ins=[dtb])
            if mh == 0:
                bank = pbk[2]
                for k in range(FDI):
                    p.op("tensor", lambda e, k=k, bank=bank, xc_c=xc_c: e.matmul(
                        bank.ap[0:32, 0:TC], wx.ap[:, k * 2 * S:(k + 1) * 2 * S],
                        xc_c.ap[:, k * TC:(k + 1) * TC],
                        start=(k == 0), stop=(k == FDI - 1)),
                        ins=[wx, xc_c], outs=[bank])
                p.op("vector", lambda e, c=c, bank=bank: e.tensor_copy(
                    bpcp.ap[0:32, c * TC:(c + 1) * TC], bank.ap[0:32, 0:TC]),
                    ins=[bank], outs=[bpcp])
            ar.freebuf(xc_c)
            ar.freebuf(dtb)
        for b in wdt:
            ar.freebuf(b)
    for b in tmp2[0] + tmp2[1] + [wx]:
        ar.freebuf(b)
    if "bp" in dbg:
        p.dma("gpsimd_q", lambda e: e.dma_start(
            out=dr["dbg_bp"], in_=bpcp.ap[0:S, :]), ins=[bpcp])
    if "cp" in dbg:
        p.dma("gpsimd_q", lambda e: e.dma_start(
            out=dr["dbg_cp"], in_=bpcp.ap[S:2 * S, :]), ins=[bpcp])
    return bpcp


def pass4(p, ar, pbk, dr, dbg, ctx):
    a_bc = ctx["a_bc"]
    bpcp = ctx["bpcp"]
    sel = ar.alloc(32 * 128, "p4_sel")
    p.dma("sync_q", lambda e: e.dma_start(out=sel.ap[0:32, :], in_=dr["sel32"]),
          outs=[sel])
    state = ar.alloc(S * FDI, "p4_state")
    p.op("vector", lambda e: e.memset(state.ap, 0.0), outs=[state])
    W = FDI * TC
    for c in range(NCHUNK):
        dt_c = ar.alloc(W, f"p4_dt_{c}")
        xc_c = ar.alloc(W, f"p4_xc_{c}")
        load_fm(p, dr["dt_s"], dt_c, FDI, c, name="dt_s")
        load_fm(p, dr["xc_s"], xc_c, FDI, c, name="xc_s")
        dtx = ar.alloc(W, f"p4_dtx_{c}")
        p.op("vector", lambda e, a=dt_c, b=xc_c, o=dtx: e.tensor_tensor(
            o.ap, a.ap, b.ap, ALU.mult), ins=[dt_c, xc_c], outs=[dtx])
        ar.freebuf(xc_c)
        dA = ar.alloc(W, f"p4_dA_{c}")
        uh = ar.alloc(W, f"p4_uh_{c}")
        y = ar.alloc(W, f"p4_y_{c}")
        for s in range(S):
            bB, bC = pbk[4], pbk[5]
            p.op("tensor", lambda e, s=s, c=c, bB=bB: e.matmul(
                bB.ap[:, 0:TC], sel.ap[0:32, s * 128:(s + 1) * 128],
                bpcp.ap[0:32, c * TC:(c + 1) * TC], start=True, stop=True),
                ins=[sel, bpcp], outs=[bB])
            p.op("tensor", lambda e, s=s, c=c, bC=bC: e.matmul(
                bC.ap[:, 0:TC], sel.ap[0:32, (16 + s) * 128:(17 + s) * 128],
                bpcp.ap[0:32, c * TC:(c + 1) * TC], start=True, stop=True),
                ins=[sel, bpcp], outs=[bC])
            p.op("scalar", lambda e, s=s, dt_c=dt_c, dA=dA: e.activation(
                dA.ap, dt_c.ap, AF.Exp, scale=a_bc.ap[:, s:s + 1]),
                ins=[dt_c, a_bc], outs=[dA])
            p.op("vector", lambda e, dtx=dtx, bB=bB, uh=uh: e.tensor_tensor(
                uh.ap.rearrange("q (b t) -> q b t", t=TC), dtx.ap.rearrange(
                    "q (b t) -> q b t", t=TC),
                bB.ap[:, 0:TC].unsqueeze(1).broadcast_to((128, FDI, TC)),
                ALU.mult), ins=[dtx, bB], outs=[uh])
            for dc in range(FDI):
                col = s * FDI + dc
                p.op("vector", lambda e, dc=dc, col=col, dA=dA, uh=uh, st=state:
                     e.tensor_tensor_scan(
                         uh.ap[:, dc * TC:(dc + 1) * TC],
                         dA.ap[:, dc * TC:(dc + 1) * TC],
                         uh.ap[:, dc * TC:(dc + 1) * TC],
                         st.ap[:, col:col + 1], ALU.mult, ALU.add),
                     ins=[dA, uh, state], outs=[uh])
            p.op("vector", lambda e, s=s, uh=uh, st=state: e.tensor_copy(
                st.ap[:, s * FDI:(s + 1) * FDI],
                uh.ap.rearrange("q (b t) -> q b t", t=TC)[:, :, TC - 1:TC].squeeze(2)),
                ins=[uh], outs=[state])
            if s == 0:
                p.op("vector", lambda e, uh=uh, bC=bC, y=y: e.tensor_tensor(
                    y.ap.rearrange("q (b t) -> q b t", t=TC),
                    uh.ap.rearrange("q (b t) -> q b t", t=TC),
                    bC.ap[:, 0:TC].unsqueeze(1).broadcast_to((128, FDI, TC)),
                    ALU.mult), ins=[uh, bC], outs=[y])
            else:
                p.op("vector", lambda e, uh=uh, bC=bC, dA=dA: e.tensor_tensor(
                    dA.ap.rearrange("q (b t) -> q b t", t=TC),
                    uh.ap.rearrange("q (b t) -> q b t", t=TC),
                    bC.ap[:, 0:TC].unsqueeze(1).broadcast_to((128, FDI, TC)),
                    ALU.mult), ins=[uh, bC], outs=[dA])
                p.op("vector", lambda e, y=y, dA=dA: e.tensor_tensor(
                    y.ap, y.ap, dA.ap, ALU.add), ins=[dA, y], outs=[y])
        store_fm(p, dr["y_s"], y, FDI, c, name="y_s")
        if "y" in dbg:
            store_fm(p, dr["dbg_y"], y, FDI, c)
        for b in (dt_c, dtx, dA, uh, y):
            ar.freebuf(b)
    for b in (sel, state):
        ar.freebuf(b)


def routing(p, ar, pbk, dr, ctx, xn_full):
    iota16 = ctx["iota16"]
    sel = ctx["sel"]
    gw = ar.alloc(FD * NE, "p5_gw")
    p.dma("sync_q", lambda e: e.dma_start(
        out=gw.ap.rearrange("q (b s) -> q b s", s=NE),
        in_=dr["gate_w_T"].rearrange("(b q) s -> q b s", q=128)), outs=[gw])
    sc_fm = ar.alloc(T, "p5_scfm")
    for c in range(NCHUNK):
        bank = pbk[0]
        for k in range(FD):
            p.op("tensor", lambda e, k=k, c=c, bank=bank: e.matmul(
                bank.ap[0:16, 0:TC], gw.ap[:, k * NE:(k + 1) * NE],
                xn_full.ap[:, k * T + c * TC:k * T + (c + 1) * TC],
                start=(k == 0), stop=(k == FD - 1)),
                ins=[gw, xn_full], outs=[bank])
        p.op("scalar", lambda e, c=c, bank=bank: e.activation(
            sc_fm.ap[0:16, c * TC:(c + 1) * TC], bank.ap[0:16, 0:TC],
            AF.Sigmoid), ins=[bank], outs=[sc_fm])
    sc_tok = ar.alloc(16 * NE, "p5_sctok")
    rt_tok = ar.alloc(16 * NE, "p5_rttok")
    mx8 = ar.alloc(8, "p5_mx8")
    idx = ar.alloc(8, "p5_idx")
    w12 = ar.alloc(6, "p5_w12")
    oh = ar.alloc(NE, "p5_oh")
    rT = ar.alloc(T, "p5_rT")
    for tt in range(16):
        bank = pbk[1]
        p.op("tensor", lambda e, tt=tt, bank=bank: e.transpose(
            bank.ap[:, 0:16], sc_fm.ap[0:16, tt * 128:(tt + 1) * 128],
            ctx["ident"].ap[0:16, 0:16]), ins=[sc_fm, ctx["ident"]], outs=[bank])
        p.op("vector", lambda e, tt=tt, bank=bank: e.tensor_copy(
            sc_tok.ap[:, tt * NE:(tt + 1) * NE], bank.ap[:, 0:16]),
            ins=[bank], outs=[sc_tok])
        st = sc_tok.ap[:, tt * NE:(tt + 1) * NE]
        p.op("vector", lambda e, st=st: e.max(mx8.ap, st),
             ins=[sc_tok], outs=[mx8], reducy=True)
        p.op("vector", lambda e, st=st: e.max_index(
            idx.ap.bitcast(U32), mx8.ap, st),
            ins=[sc_tok, mx8], outs=[idx], reducy=True)
        p.op("vector", lambda e: e.tensor_copy(
            w12.ap[:, 0:2], idx.ap[:, 0:2].bitcast(U32)),
            ins=[idx], outs=[w12])
        p.op("vector", lambda e: e.tensor_tensor(
            w12.ap[:, 2:3], mx8.ap[:, 0:1], mx8.ap[:, 1:2], ALU.add),
            ins=[mx8], outs=[w12])
        p.op("vector", lambda e: e.tensor_scalar(
            w12.ap[:, 2:3], w12.ap[:, 2:3], 1e-20, None, ALU.add),
            ins=[w12], outs=[w12])
        p.op("vector", lambda e: e.reciprocal(w12.ap[:, 3:4], w12.ap[:, 2:3]),
             ins=[w12], outs=[w12])
        p.op("vector", lambda e: e.tensor_scalar(
            w12.ap[:, 4:5], mx8.ap[:, 0:1], w12.ap[:, 3:4], 2.5,
            ALU.mult, ALU.mult), ins=[mx8, w12], outs=[w12])
        p.op("vector", lambda e: e.tensor_scalar(
            w12.ap[:, 5:6], mx8.ap[:, 1:2], w12.ap[:, 3:4], 2.5,
            ALU.mult, ALU.mult), ins=[mx8, w12], outs=[w12])
        rts = rt_tok.ap[:, tt * NE:(tt + 1) * NE]
        p.op("vector", lambda e: e.tensor_scalar(
            oh.ap, iota16.ap, w12.ap[:, 0:1], None, ALU.is_equal),
            ins=[iota16, w12], outs=[oh])
        p.op("vector", lambda e, rts=rts: e.tensor_scalar(
            rts, oh.ap, w12.ap[:, 4:5], None, ALU.mult),
            ins=[oh, w12], outs=[rt_tok])
        p.op("vector", lambda e: e.tensor_scalar(
            oh.ap, iota16.ap, w12.ap[:, 1:2], None, ALU.is_equal),
            ins=[iota16, w12], outs=[oh])
        p.op("vector", lambda e, rts=rts: e.scalar_tensor_tensor(
            rts, oh.ap, w12.ap[:, 5:6], rts, ALU.mult, ALU.add),
            ins=[oh, w12, rt_tok], outs=[rt_tok])
        bank2 = pbk[2]
        p.op("tensor", lambda e, rts=rts, bank2=bank2: e.transpose(
            bank2.ap[0:16, 0:128], rts, ctx["ident"].ap),
            ins=[rt_tok, ctx["ident"]], outs=[bank2])
        p.op("vector", lambda e, tt=tt, bank2=bank2: e.tensor_copy(
            rT.ap[0:16, tt * 128:(tt + 1) * 128], bank2.ap[0:16, 0:128]),
            ins=[bank2], outs=[rT])
    for b in (gw, sc_fm, sc_tok, rt_tok, mx8, idx, w12, oh):
        ar.freebuf(b)
    return rT


def pass5(p, ar, pbk, dr, dbg, ctx):
    sel = ctx["sel"] = ar.alloc(32 * 128, "p5_sel")
    p.dma("sync_q", lambda e: e.dma_start(out=sel.ap[0:32, :], in_=dr["sel32"]),
          outs=[sel])
    xn_full = ar.alloc(FD * T, "p5_xn")
    p.dma("sync_q", lambda e: e.dma_start(
        out=xn_full.ap.rearrange("q (b t) -> q b t", t=T),
        in_=dr["xn_s"].rearrange("(b q) t -> q b t", q=128)),
        outs=[xn_full], extra_deps=[p.dbuf("xn_s", c).wr for c in range(NCHUNK)
                                    if p.dbuf("xn_s", c).wr])
    rT = routing(p, ar, pbk, dr, ctx, xn_full)
    wgb = ar.alloc(FD * M, "p5_wg")
    wub = ar.alloc(FD * M, "p5_wu")
    wdb = ar.alloc(4 * D, "p5_wd")
    gs = ar.alloc(4 * TC, "p5_gs")
    ub = ar.alloc(4 * TC, "p5_ub")
    eo = ar.alloc(FD * TC, "p5_eo")
    for e_i in range(NE + 1):
        shared = e_i == NE
        wg_src = dr["ws_gate"] if shared else dr["w_gate"][e_i]
        wu_src = dr["ws_up"] if shared else dr["w_up"][e_i]
        wd_src = dr["ws_down"] if shared else dr["w_down"][e_i]
        p.dma("sync_q", lambda e, s=wg_src: e.dma_start(
            out=wgb.ap.rearrange("q (b m) -> q b m", m=M),
            in_=s.rearrange("(b q) m -> q b m", q=128)), outs=[wgb])
        p.dma("sync_q", lambda e, s=wu_src: e.dma_start(
            out=wub.ap.rearrange("q (b m) -> q b m", m=M),
            in_=s.rearrange("(b q) m -> q b m", q=128)), outs=[wub])
        p.dma("sync_q", lambda e, s=wd_src: e.dma_start(
            out=wdb.ap.rearrange("q (b m) -> q b m", m=D),
            in_=s.rearrange("(b q) m -> q b m", q=128)), outs=[wdb])
        for c in range(NCHUNK):
            for mg in range(4):
                bank = pbk[mg % 2]
                for k in range(FD):
                    p.op("tensor", lambda e, k=k, mg=mg, c=c, bank=bank: e.matmul(
                        bank.ap[:, 0:TC],
                        wgb.ap[:, k * M + mg * 128:k * M + (mg + 1) * 128],
                        xn_full.ap[:, k * T + c * TC:k * T + (c + 1) * TC],
                        start=(k == 0), stop=(k == FD - 1)),
                        ins=[wgb, xn_full], outs=[bank])
                p.op("scalar", lambda e, mg=mg, bank=bank: e.activation(
                    gs.ap[:, mg * TC:(mg + 1) * TC], bank.ap[:, 0:TC], AF.Silu),
                    ins=[bank], outs=[gs])
            for mg in range(4):
                bank = pbk[mg % 2]
                for k in range(FD):
                    p.op("tensor", lambda e, k=k, mg=mg, c=c, bank=bank: e.matmul(
                        bank.ap[:, 0:TC],
                        wub.ap[:, k * M + mg * 128:k * M + (mg + 1) * 128],
                        xn_full.ap[:, k * T + c * TC:k * T + (c + 1) * TC],
                        start=(k == 0), stop=(k == FD - 1)),
                        ins=[wub, xn_full], outs=[bank])
                p.op("scalar", lambda e, mg=mg, bank=bank: e.copy(
                    ub.ap[:, mg * TC:(mg + 1) * TC], bank.ap[:, 0:TC]),
                    ins=[bank], outs=[ub])
            p.op("vector", lambda e: e.tensor_tensor(
                gs.ap, gs.ap, ub.ap, ALU.mult), ins=[gs, ub], outs=[gs])
            if not shared:
                bank6 = pbk[6]
                p.op("tensor", lambda e, e_i=e_i, c=c, bank6=bank6: e.matmul(
                    bank6.ap[:, 0:TC], sel.ap[0:16, e_i * 128:(e_i + 1) * 128],
                    rT.ap[0:16, c * TC:(c + 1) * TC], start=True, stop=True),
                    ins=[sel, rT], outs=[bank6])
                p.op("vector", lambda e, bank6=bank6: e.tensor_tensor(
                    gs.ap.rearrange("q (b t) -> q b t", t=TC),
                    gs.ap.rearrange("q (b t) -> q b t", t=TC),
                    bank6.ap[:, 0:TC].unsqueeze(1).broadcast_to((128, 4, TC)),
                    ALU.mult), ins=[gs, bank6], outs=[gs])
            for mo in range(FD):
                bank = pbk[2 + mo % 2]
                for k in range(4):
                    p.op("tensor", lambda e, k=k, mo=mo, bank=bank: e.matmul(
                        bank.ap[:, 0:TC],
                        wdb.ap[:, k * D + mo * 128:k * D + (mo + 1) * 128],
                        gs.ap[:, k * TC:(k + 1) * TC],
                        start=(k == 0), stop=(k == 3)),
                        ins=[wdb, gs], outs=[bank])
                p.op("scalar", lambda e, mo=mo, bank=bank: e.copy(
                    eo.ap[:, mo * TC:(mo + 1) * TC], bank.ap[:, 0:TC]),
                    ins=[bank], outs=[eo])
            d3 = dr["hacc_s"].rearrange("(b q) t -> q b t", q=128)[
                :, :, c * TC:(c + 1) * TC]
            s3 = eo.ap.rearrange("q (b t) -> q b t", t=TC)
            if e_i == 0:
                p.dma("gpsimd_q", lambda e, d3=d3, s3=s3: e.dma_start(
                    out=d3, in_=s3), ins=[eo], outs=[p.dbuf("hacc_s", c)])
            else:
                p.dma("gpsimd_q", lambda e, d3=d3, s3=s3: e.dma_start(
                    out=d3, in_=s3, accum_op=ALU.add),
                    ins=[eo], outs=[p.dbuf("hacc_s", c)])
    if "hacc" in dbg:
        hb = ar.alloc(FD * TC, "p5_hdbg")
        for c in range(NCHUNK):
            load_fm(p, dr["hacc_s"], hb, FD, c, name="hacc_s")
            store_fm(p, dr["dbg_hacc"], hb, FD, c)
        ar.freebuf(hb)
    for b in (wgb, wub, wdb, gs, ub, eo, xn_full, rT, sel):
        ar.freebuf(b)


def pass6a(p, ar, pbk, dr, dbg, ctx):
    tabs = ctx["tabs"]
    constk = ctx["constk"]
    ones_row = ctx["ones_row"]
    wz = [ar.alloc(DIN, f"p6_wz_{k}") for k in range(FD)]
    for k in range(FD):
        p.dma("sync_q", lambda e, k=k, b=wz[k]: e.dma_start(
            out=b.ap, in_=dr["w_in_z_T"][k * 128:(k + 1) * 128, :]), outs=[wz[k]])
    ones_col = ar.alloc(1, "p6_onec")
    p.op("vector", lambda e: e.memset(ones_col.ap, 1.0), outs=[ones_col])
    stat = ar.alloc(6 * TC, "p6_stat")
    sq = ar.alloc(TC, "p6_sq")
    for c in range(NCHUNK):
        xn_c = ar.alloc(FD * TC, f"p6_xn_{c}")
        load_fm(p, dr["xn_s"], xn_c, FD, c, name="xn_s")
        zs = ar.alloc(FDI * TC, f"p6_zs_{c}")
        for mi in range(FDI):
            bank = pbk[mi % 2]
            for k in range(FD):
                p.op("tensor", lambda e, k=k, mi=mi, bank=bank, xn_c=xn_c: e.matmul(
                    bank.ap[:, 0:TC], wz[k].ap[:, mi * 128:(mi + 1) * 128],
                    xn_c.ap[:, k * TC:(k + 1) * TC],
                    start=(k == 0), stop=(k == FD - 1)),
                    ins=[wz[k], xn_c], outs=[bank])
            p.op("scalar", lambda e, mi=mi, bank=bank: e.activation(
                zs.ap[:, mi * TC:(mi + 1) * TC], bank.ap[:, 0:TC], AF.Silu),
                ins=[bank], outs=[zs])
        ar.freebuf(xn_c)
        y = ar.alloc(FDI * TC, f"p6_y_{c}")
        xc_c = ar.alloc(FDI * TC, f"p6_xc_{c}")
        load_fm(p, dr["y_s"], y, FDI, c, name="y_s")
        load_fm(p, dr["xc_s"], xc_c, FDI, c, name="xc_s")
        bA, bB2 = pbk[2], pbk[3]
        for k in range(FDI):
            p.op("scalar", lambda e, k=k, y=y: e.activation(
                sq.ap, y.ap[:, k * TC:(k + 1) * TC], AF.Square),
                ins=[y], outs=[sq])
            p.op("tensor", lambda e, k=k, y=y, bA=bA: e.matmul(
                bA.ap[0:1, 0:TC], ones_col.ap, y.ap[:, k * TC:(k + 1) * TC],
                start=(k == 0), stop=(k == FDI - 1)),
                ins=[ones_col, y], outs=[bA])
            p.op("tensor", lambda e, k=k, bB2=bB2: e.matmul(
                bB2.ap[0:1, 0:TC], ones_col.ap, sq.ap,
                start=(k == 0), stop=(k == FDI - 1)),
                ins=[ones_col, sq], outs=[bB2])
        r0 = stat.ap[0:1, :]
        p.op("vector", lambda e, bA=bA: e.tensor_scalar(
            r0[:, 0:TC], bA.ap[0:1, 0:TC], 1.0 / DIN, None, ALU.mult),
            ins=[bA], outs=[stat])
        p.op("vector", lambda e, bB2=bB2: e.tensor_scalar(
            r0[:, TC:2 * TC], bB2.ap[0:1, 0:TC], 1.0 / DIN, None, ALU.mult),
            ins=[bB2], outs=[stat])
        p.op("vector", lambda e: e.tensor_tensor(
            r0[:, 2 * TC:3 * TC], r0[:, 0:TC], r0[:, 0:TC], ALU.mult),
            ins=[stat], outs=[stat])
        p.op("vector", lambda e: e.tensor_tensor(
            r0[:, 2 * TC:3 * TC], r0[:, TC:2 * TC], r0[:, 2 * TC:3 * TC],
            ALU.subtract), ins=[stat], outs=[stat])
        p.op("scalar", lambda e: e.activation(
            r0[:, 3 * TC:4 * TC], r0[:, 2 * TC:3 * TC], AF.Sqrt,
            bias=constk.ap[0:1, 0:1]), ins=[stat, constk], outs=[stat])
        p.op("vector", lambda e: e.reciprocal(
            r0[:, 4 * TC:5 * TC], r0[:, 3 * TC:4 * TC]), ins=[stat], outs=[stat])
        bMu, bRs = pbk[4], pbk[5]
        p.op("tensor", lambda e, bMu=bMu: e.matmul(
            bMu.ap[:, 0:TC], ones_row.ap[0:1, 0:128], r0[:, 0:TC],
            start=True, stop=True), ins=[ones_row, stat], outs=[bMu])
        p.op("tensor", lambda e, bRs=bRs: e.matmul(
            bRs.ap[:, 0:TC], ones_row.ap[0:1, 0:128], r0[:, 4 * TC:5 * TC],
            start=True, stop=True), ins=[ones_row, stat], outs=[bRs])
        for k in range(FDI):
            yk = y.ap[:, k * TC:(k + 1) * TC]
            p.op("vector", lambda e, yk=yk, bMu=bMu: e.tensor_tensor(
                yk, yk, bMu.ap[:, 0:TC], ALU.subtract), ins=[y, bMu], outs=[y])
            p.op("vector", lambda e, yk=yk, bRs=bRs: e.tensor_tensor(
                yk, yk, bRs.ap[:, 0:TC], ALU.mult), ins=[y, bRs], outs=[y])
            p.op("vector", lambda e, yk=yk, k=k: e.tensor_scalar(
                yk, yk, tabs["ln_g_pc"].ap[:, k:k + 1],
                tabs["ln_b_pc"].ap[:, k:k + 1], ALU.mult, ALU.add),
                ins=[y, tabs["ln_g_pc"], tabs["ln_b_pc"]], outs=[y])
            p.op("vector", lambda e, yk=yk, k=k, xc_c=xc_c: e.scalar_tensor_tensor(
                yk, xc_c.ap[:, k * TC:(k + 1) * TC],
                tabs["d_param_pc"].ap[:, k:k + 1], yk, ALU.mult, ALU.add),
                ins=[xc_c, tabs["d_param_pc"], y], outs=[y])
            p.op("vector", lambda e, yk=yk, k=k, zs=zs: e.tensor_tensor(
                yk, yk, zs.ap[:, k * TC:(k + 1) * TC], ALU.mult),
                ins=[y, zs], outs=[y])
        store_fm(p, dr["yg_s"], y, FDI, c, name="yg_s")
        if "yg" in dbg:
            store_fm(p, dr["dbg_yg"], y, FDI, c)
        for b in (zs, y, xc_c):
            ar.freebuf(b)
    for b in wz + [ones_col, stat, sq]:
        ar.freebuf(b)


def pass6b(p, ar, pbk, dr, dbg, ctx):
    ident = ctx["ident"]
    wo = [ar.alloc(D, f"p7_wo_{k}") for k in range(FDI)]
    for k in range(FDI):
        p.dma("sync_q", lambda e, k=k, b=wo[k]: e.dma_start(
            out=b.ap, in_=dr["w_out_T"][k * 128:(k + 1) * 128, :]), outs=[wo[k]])
    for c in range(NCHUNK):
        yg = ar.alloc(FDI * TC, f"p7_yg_{c}")
        load_fm(p, dr["yg_s"], yg, FDI, c, name="yg_s")
        gfm = ar.alloc(FD * TC, f"p7_g_{c}")
        load_fm(p, dr["hacc_s"], gfm, FD, c, name="hacc_s")
        s_tok = ar.alloc(D, f"p7_s_{c}")
        g_tok = ar.alloc(D, f"p7_gt_{c}")
        x_tok = ar.alloc(D, f"p7_x_{c}")
        b1 = ar.alloc(D, f"p7_b1_{c}")
        b2 = ar.alloc(D, f"p7_b2_{c}")
        moedep = ([ctx["moe_sc"][-1]] if "moe_sc" in ctx else [])
        for tt in range(4):
            t0 = c * TC + tt * 128
            p.dma("sync_q", lambda e, t0=t0, x_tok=x_tok: e.dma_start(
                out=x_tok.ap, in_=dr["x"][t0:t0 + 128, :]), outs=[x_tok])
            p.dma("sync_q", lambda e, t0=t0, b1=b1: e.dma_start(
                out=b1.ap, in_=dr["moebuf"][t0:t0 + 128, :]), outs=[b1],
                extra_deps=moedep)
            p.dma("sync_q", lambda e, t0=t0, b2=b2: e.dma_start(
                out=b2.ap, in_=dr["moebuf"][T + t0:T + t0 + 128, :]), outs=[b2],
                extra_deps=moedep)
            bL, bH = pbk[0], pbk[1]
            for k in range(FDI):
                p.op("tensor", lambda e, k=k, tt=tt, yg=yg, bL=bL: e.matmul(
                    bL.ap[:, 0:TC],
                    yg.ap[:, k * TC + tt * 128:k * TC + (tt + 1) * 128],
                    wo[k].ap[:, 0:512],
                    start=(k == 0), stop=(k == FDI - 1)),
                    ins=[yg, wo[k]], outs=[bL])
                p.op("tensor", lambda e, k=k, tt=tt, yg=yg, bH=bH: e.matmul(
                    bH.ap[:, 0:TC],
                    yg.ap[:, k * TC + tt * 128:k * TC + (tt + 1) * 128],
                    wo[k].ap[:, 512:1024],
                    start=(k == 0), stop=(k == FDI - 1)),
                    ins=[yg, wo[k]], outs=[bH])
            p.op("vector", lambda e, bL=bL, s_tok=s_tok: e.tensor_copy(
                s_tok.ap[:, 0:512], bL.ap[:, 0:TC]), ins=[bL], outs=[s_tok])
            p.op("vector", lambda e, bH=bH, s_tok=s_tok: e.tensor_copy(
                s_tok.ap[:, 512:1024], bH.ap[:, 0:TC]), ins=[bH], outs=[s_tok])
            bT0, bT1 = pbk[2], pbk[3]
            for f in range(FD):
                bk = bT0 if f < 4 else bT1
                p.op("tensor", lambda e, f=f, tt=tt, gfm=gfm, bk=bk: e.transpose(
                    bk.ap[:, (f % 4) * 128:(f % 4 + 1) * 128],
                    gfm.ap[:, f * TC + tt * 128:f * TC + (tt + 1) * 128],
                    ident.ap), ins=[gfm, ident], outs=[bk])
            p.op("vector", lambda e, bT0=bT0, g_tok=g_tok: e.tensor_copy(
                g_tok.ap[:, 0:512], bT0.ap[:, 0:512]), ins=[bT0], outs=[g_tok])
            p.op("vector", lambda e, bT1=bT1, g_tok=g_tok: e.tensor_copy(
                g_tok.ap[:, 512:1024], bT1.ap[:, 0:512]), ins=[bT1], outs=[g_tok])
            p.op("vector", lambda e, g_tok=g_tok, b1=b1: e.tensor_tensor(
                g_tok.ap, g_tok.ap, b1.ap, ALU.add), ins=[g_tok, b1], outs=[g_tok])
            p.op("vector", lambda e, g_tok=g_tok, b2=b2: e.tensor_tensor(
                g_tok.ap, g_tok.ap, b2.ap, ALU.add), ins=[g_tok, b2], outs=[g_tok])
            p.op("scalar", lambda e, g_tok=g_tok: e.activation(
                g_tok.ap, g_tok.ap, AF.Silu), ins=[g_tok], outs=[g_tok])
            p.op("vector", lambda e, s_tok=s_tok, g_tok=g_tok: e.tensor_tensor(
                s_tok.ap, s_tok.ap, g_tok.ap, ALU.mult),
                ins=[s_tok, g_tok], outs=[s_tok])
            p.op("vector", lambda e, s_tok=s_tok, x_tok=x_tok: e.tensor_tensor(
                s_tok.ap, s_tok.ap, x_tok.ap, ALU.add),
                ins=[s_tok, x_tok], outs=[s_tok])
            p.dma("gpsimd_q", lambda e, t0=t0, s_tok=s_tok: e.dma_start(
                out=dr["out"][t0:t0 + 128, :], in_=s_tok.ap),
                ins=[s_tok], outs=[p.dbuf("out")])
        for b in (yg, gfm, s_tok, g_tok, x_tok, b1, b2):
            ar.freebuf(b)
    for b in wo:
        ar.freebuf(b)


def routing_sparse(p, ar, pbk, dr, ctx, xn_full):
    iota16 = ctx["iota16"]
    ident = ctx["ident"]
    gw = ar.alloc(FD * NE, "p5_gw")
    p.dma("sync_q", lambda e: e.dma_start(
        out=gw.ap.rearrange("q (b s) -> q b s", s=NE),
        in_=dr["gate_w_T"].rearrange("(b q) s -> q b s", q=128)), outs=[gw])
    sc_fm = ar.alloc(T, "p5_scfm")
    for c in range(NCHUNK):
        bank = pbk[0]
        for k in range(FD):
            p.op("tensor", lambda e, k=k, c=c, bank=bank: e.matmul(
                bank.ap[0:16, 0:TC], gw.ap[:, k * NE:(k + 1) * NE],
                xn_full.ap[:, k * T + c * TC:k * T + (c + 1) * TC],
                start=(k == 0), stop=(k == FD - 1)),
                ins=[gw, xn_full], outs=[bank])
        p.op("scalar", lambda e, c=c, bank=bank: e.activation(
            sc_fm.ap[0:16, c * TC:(c + 1) * TC], bank.ap[0:16, 0:TC],
            AF.Sigmoid), ins=[bank], outs=[sc_fm])
    tok0 = ar.alloc(NE, "p5_tok0")
    tok1 = ar.alloc(NE, "p5_tok1")
    p.op("gpsimd", lambda e: e.iota(tok0.ap.bitcast(U32), [[128, 16]],
                                    base=0, channel_multiplier=1), outs=[tok0])
    p.op("gpsimd", lambda e: e.iota(tok1.ap.bitcast(U32), [[128, 16]],
                                    base=T, channel_multiplier=1), outs=[tok1])
    NG = NE * CAP // 128
    zf = ar.alloc(NG, "p5_zf")
    p.op("vector", lambda e: e.memset(zf.ap, 0.0), outs=[zf])
    dumpi = ar.alloc(NG, "p5_dump")
    p.op("gpsimd", lambda e: e.iota(dumpi.ap.bitcast(U32), [[1, NG]],
                                    base=2 * T, channel_multiplier=NG), outs=[dumpi])
    for nm, buf in (("glist", zf), ("dlist", dumpi), ("wlist", zf)):
        s2 = (buf.ap[:, 0:NG].bitcast(U32) if nm != "wlist"
              else buf.ap[:, 0:NG])
        p.dma("gpsimd_q", lambda e, nm=nm, s2=s2: e.dma_start(
            out=dr[nm], in_=s2), ins=[buf], outs=[p.dbuf(nm)])
    sc_tok = ar.alloc(NE, "p5_sctok")
    oh1 = ar.alloc(16 * NE, "p5_oh1")
    oh2 = ar.alloc(16 * NE, "p5_oh2")
    idxf = ar.alloc(32, "p5_idxf")
    wp = ar.alloc(32, "p5_wp")
    msk_T = ar.alloc(T, "p5_mskT")
    mx8 = ar.alloc(8, "p5_mx8")
    idx = ar.alloc(8, "p5_idx")
    sm = ar.alloc(4, "p5_sm")
    t16 = ar.alloc(NE, "p5_t16")
    for tt in range(16):
        bank = pbk[4]
        p.op("tensor", lambda e, tt=tt, bank=bank: e.transpose(
            bank.ap[:, 0:16], sc_fm.ap[0:16, tt * 128:(tt + 1) * 128],
            ident.ap[0:16, 0:16]), ins=[sc_fm, ident], outs=[bank])
        p.op("vector", lambda e, bank=bank: e.tensor_copy(
            sc_tok.ap, bank.ap[:, 0:16]), ins=[bank], outs=[sc_tok])
        p.op("vector", lambda e: e.max(mx8.ap, sc_tok.ap),
             ins=[sc_tok], outs=[mx8], reducy=True)
        p.op("vector", lambda e: e.max_index(
            idx.ap.bitcast(U32), mx8.ap, sc_tok.ap),
            ins=[sc_tok, mx8], outs=[idx], reducy=True)
        p.op("vector", lambda e, tt=tt: e.tensor_copy(
            idxf.ap[:, 2 * tt:2 * tt + 2], idx.ap[:, 0:2].bitcast(U32)),
            ins=[idx], outs=[idxf])
        p.op("vector", lambda e: e.tensor_tensor(
            sm.ap[:, 0:1], mx8.ap[:, 0:1], mx8.ap[:, 1:2], ALU.add),
            ins=[mx8], outs=[sm])
        p.op("vector", lambda e: e.tensor_scalar(
            sm.ap[:, 0:1], sm.ap[:, 0:1], 1e-20, None, ALU.add),
            ins=[sm], outs=[sm])
        p.op("vector", lambda e: e.reciprocal(sm.ap[:, 1:2], sm.ap[:, 0:1]),
             ins=[sm], outs=[sm])
        p.op("vector", lambda e, tt=tt: e.tensor_scalar(
            wp.ap[:, 2 * tt:2 * tt + 2], mx8.ap[:, 0:2], sm.ap[:, 1:2], 2.5,
            ALU.mult, ALU.mult), ins=[mx8, sm], outs=[wp])
        o1 = oh1.ap[:, tt * NE:(tt + 1) * NE]
        o2 = oh2.ap[:, tt * NE:(tt + 1) * NE]
        p.op("vector", lambda e, o1=o1, tt=tt: e.tensor_scalar(
            o1, iota16.ap, idxf.ap[:, 2 * tt:2 * tt + 1], None, ALU.is_equal),
            ins=[iota16, idxf], outs=[oh1])
        p.op("vector", lambda e, o2=o2, tt=tt: e.tensor_scalar(
            o2, iota16.ap, idxf.ap[:, 2 * tt + 1:2 * tt + 2], None,
            ALU.is_equal), ins=[iota16, idxf], outs=[oh2])
        p.op("vector", lambda e, o1=o1, o2=o2: e.tensor_tensor(
            t16.ap, o1, o2, ALU.add), ins=[oh1, oh2], outs=[t16])
        yield
        bank3 = pbk[5]
        p.op("tensor", lambda e, bank3=bank3: e.transpose(
            bank3.ap[0:16, 0:128], t16.ap, ident.ap),
            ins=[t16, ident], outs=[bank3])
        p.op("vector", lambda e, tt=tt, bank3=bank3: e.tensor_copy(
            msk_T.ap[0:16, tt * 128:(tt + 1) * 128], bank3.ap[0:16, 0:128]),
            ins=[bank3], outs=[msk_T])
        yield
    zrow = ar.alloc(T, "p5_zrow")
    p.op("vector", lambda e: e.memset(zrow.ap[0:16, :], 0.0), outs=[zrow])
    pos_T = ar.alloc(T, "p5_posT")
    p.op("vector", lambda e: e.tensor_tensor_scan(
        pos_T.ap[0:16, :], msk_T.ap[0:16, :], zrow.ap[0:16, :], 0.0,
        ALU.add, ALU.add), ins=[msk_T, zrow], outs=[pos_T])
    pos_tok = ar.alloc(NE, "p5_postok")
    cols = ar.alloc(8, "p5_cols")
    du = ar.alloc(2, "p5_du")
    sc_toks = ctx.setdefault("sc_toks", [])
    for tt in range(16):
        bank = pbk[4]
        p.op("tensor", lambda e, tt=tt, bank=bank: e.transpose(
            bank.ap[:, 0:16], pos_T.ap[0:16, tt * 128:(tt + 1) * 128],
            ident.ap[0:16, 0:16]), ins=[pos_T, ident], outs=[bank])
        p.op("vector", lambda e, bank=bank: e.tensor_copy(
            pos_tok.ap, bank.ap[:, 0:16]), ins=[bank], outs=[pos_tok])
        for r, ohb, tokb in ((0, oh1, tok0), (1, oh2, tok1)):
            o = ohb.ap[:, tt * NE:(tt + 1) * NE]
            p.op("vector", lambda e, o=o: e.tensor_tensor(
                t16.ap, o, pos_tok.ap, ALU.mult),
                ins=[ohb, pos_tok], outs=[t16])
            p.op("vector", lambda e: e.tensor_reduce(
                cols.ap[:, 0:1], t16.ap, mybir.AxisListType.X, ALU.add),
                ins=[t16], outs=[cols], reducy=True)
            p.op("vector", lambda e, tt=tt, r=r: e.tensor_scalar(
                cols.ap[:, 1:2], idxf.ap[:, 2 * tt + r:2 * tt + r + 1],
                float(CAP), -1.0, ALU.mult, ALU.add),
                ins=[idxf], outs=[cols])
            p.op("vector", lambda e: e.tensor_tensor(
                cols.ap[:, 1:2], cols.ap[:, 1:2], cols.ap[:, 0:1], ALU.add),
                ins=[cols], outs=[cols])
            p.op("vector", lambda e: e.tensor_copy(
                du.ap[:, 0:1].bitcast(U32), cols.ap[:, 1:2]),
                ins=[cols], outs=[du])
            ioff = IndirectOffsetOnAxis(ap=du.ap[:, 0:1].bitcast(U32), axis=0)
            tk_g = tok0.ap[:, tt:tt + 1].bitcast(U32)
            tk_d = tokb.ap[:, tt:tt + 1].bitcast(U32)
            wsl = wp.ap[:, 2 * tt + r:2 * tt + r + 1]
            sc_toks.append(p.dma("gpsimd_q", lambda e, tk_g=tk_g, ioff=ioff:
                                 e.indirect_dma_start(out=dr["glist"],
                                                      out_offset=ioff,
                                                      in_=tk_g, in_offset=None),
                                 ins=[tok0, du],
                                 extra_deps=[p.dbuf("glist").wr]))
            sc_toks.append(p.dma("gpsimd_q", lambda e, tk_d=tk_d, ioff=ioff:
                                 e.indirect_dma_start(out=dr["dlist"],
                                                      out_offset=ioff,
                                                      in_=tk_d, in_offset=None),
                                 ins=[tokb, du],
                                 extra_deps=[p.dbuf("dlist").wr]))
            sc_toks.append(p.dma("gpsimd_q", lambda e, wsl=wsl, ioff=ioff:
                                 e.indirect_dma_start(out=dr["wlist"],
                                                      out_offset=ioff,
                                                      in_=wsl, in_offset=None),
                                 ins=[wp, du],
                                 extra_deps=[p.dbuf("wlist").wr]))
    for b in (gw, sc_fm, sc_tok, oh1, oh2, idxf, wp, msk_T, mx8, idx, sm,
              t16, zrow, pos_T, pos_tok, cols, du, tok0, tok1, zf, dumpi):
        ar.freebuf(b)
    return sc_toks


def pass5s(p, ar, pbk, dr, dbg, ctx):
    ones_row = ctx["ones_row"]
    ident = ctx["ident"]
    xn_full = ar.alloc(FD * T, "p5_xn")
    p.dma("sync_q", lambda e: e.dma_start(
        out=xn_full.ap.rearrange("q (b t) -> q b t", t=T),
        in_=dr["xn_s"].rearrange("(b q) t -> q b t", q=128)),
        outs=[xn_full],
        extra_deps=[tk for tk in (p.dbuf("xn_s", c).wr for c in range(NCHUNK))
                    if tk])
    sc_toks = routing_sparse(p, ar, pbk, dr, ctx, xn_full)
    wgb = ar.alloc(FD * M, "p5_wg")
    wub = ar.alloc(FD * M, "p5_wu")
    wdb = ar.alloc(4 * D, "p5_wd")
    gs = ar.alloc(4 * TC, "p5_gs")
    ub = ar.alloc(4 * TC, "p5_ub")
    eo = ar.alloc(FD * TC, "p5_eo")
    p.dma("sync_q", lambda e: e.dma_start(
        out=wgb.ap.rearrange("q (b m) -> q b m", m=M),
        in_=dr["ws_gate"].rearrange("(b q) m -> q b m", q=128)), outs=[wgb])
    p.dma("sync_q", lambda e: e.dma_start(
        out=wub.ap.rearrange("q (b m) -> q b m", m=M),
        in_=dr["ws_up"].rearrange("(b q) m -> q b m", q=128)), outs=[wub])
    p.dma("sync_q", lambda e: e.dma_start(
        out=wdb.ap.rearrange("q (b m) -> q b m", m=D),
        in_=dr["ws_down"].rearrange("(b q) m -> q b m", q=128)), outs=[wdb])
    for c in range(NCHUNK):
        for mg in range(4):
            bank = pbk[mg % 2]
            for k in range(FD):
                p.op("tensor", lambda e, k=k, mg=mg, c=c, bank=bank: e.matmul(
                    bank.ap[:, 0:TC],
                    wgb.ap[:, k * M + mg * 128:k * M + (mg + 1) * 128],
                    xn_full.ap[:, k * T + c * TC:k * T + (c + 1) * TC],
                    start=(k == 0), stop=(k == FD - 1)),
                    ins=[wgb, xn_full], outs=[bank])
            p.op("scalar", lambda e, mg=mg, bank=bank: e.activation(
                gs.ap[:, mg * TC:(mg + 1) * TC], bank.ap[:, 0:TC], AF.Silu),
                ins=[bank], outs=[gs])
        for mg in range(4):
            bank = pbk[mg % 2]
            for k in range(FD):
                p.op("tensor", lambda e, k=k, mg=mg, c=c, bank=bank: e.matmul(
                    bank.ap[:, 0:TC],
                    wub.ap[:, k * M + mg * 128:k * M + (mg + 1) * 128],
                    xn_full.ap[:, k * T + c * TC:k * T + (c + 1) * TC],
                    start=(k == 0), stop=(k == FD - 1)),
                    ins=[wub, xn_full], outs=[bank])
            p.op("scalar", lambda e, mg=mg, bank=bank: e.copy(
                ub.ap[:, mg * TC:(mg + 1) * TC], bank.ap[:, 0:TC]),
                ins=[bank], outs=[ub])
        p.op("vector", lambda e: e.tensor_tensor(
            gs.ap, gs.ap, ub.ap, ALU.mult), ins=[gs, ub], outs=[gs])
        for mo in range(FD):
            bank = pbk[2 + mo % 2]
            for k in range(4):
                p.op("tensor", lambda e, k=k, mo=mo, bank=bank: e.matmul(
                    bank.ap[:, 0:TC],
                    wdb.ap[:, k * D + mo * 128:k * D + (mo + 1) * 128],
                    gs.ap[:, k * TC:(k + 1) * TC],
                    start=(k == 0), stop=(k == 3)),
                    ins=[wdb, gs], outs=[bank])
            p.op("scalar", lambda e, mo=mo, bank=bank: e.copy(
                eo.ap[:, mo * TC:(mo + 1) * TC], bank.ap[:, 0:TC]),
                ins=[bank], outs=[eo])
        store_fm(p, dr["hacc_s"], eo, FD, c, name="hacc_s")
    ar.freebuf(xn_full)
    NP = CAP // 128
    xg_tok = [ar.alloc(D, f"p5_xg{i}") for i in range(NP)]
    xg_fm = ar.alloc(FD * CAP, "p5_xgfm")
    idxg = ar.alloc(2 * NP, "p5_idxg")
    wrow = ar.alloc(CAP, "p5_wrow")
    gs2 = ar.alloc(4 * CAP, "p5_gs2")
    ub2 = ar.alloc(4 * CAP, "p5_ub2")
    eo2 = ar.alloc(FD * CAP, "p5_eo2")
    eo_tok = [ar.alloc(D, f"p5_eot{i}") for i in range(NP)]
    gather_dep = [sc_toks[-1], p.dbuf("glist").wr, p.dbuf("dlist").wr,
                  p.dbuf("wlist").wr]
    gather_dep += [tk for tk in (p.dbuf("xn_tok_s", c).wr
                                 for c in range(NCHUNK)) if tk]
    first = True
    for e_i in range(NE):
        p.dma("sync_q", lambda e, e_i=e_i: e.dma_start(
            out=wgb.ap.rearrange("q (b m) -> q b m", m=M),
            in_=dr["w_gate"][e_i].rearrange("(b q) m -> q b m", q=128)),
            outs=[wgb])
        p.dma("sync_q", lambda e, e_i=e_i: e.dma_start(
            out=wub.ap.rearrange("q (b m) -> q b m", m=M),
            in_=dr["w_up"][e_i].rearrange("(b q) m -> q b m", q=128)),
            outs=[wub])
        p.dma("sync_q", lambda e, e_i=e_i: e.dma_start(
            out=wdb.ap.rearrange("q (b m) -> q b m", m=D),
            in_=dr["w_down"][e_i].rearrange("(b q) m -> q b m", q=128)),
            outs=[wdb])
        dep = gather_dep if first else []
        first = False
        for i in range(NP):
            r0 = e_i * CAP + i * 128
            p.dma("sync_q", lambda e, r0=r0, i=i: e.dma_start(
                out=idxg.ap[:, i:i + 1].bitcast(U32),
                in_=dr["glist"][r0:r0 + 128, :]), outs=[idxg], extra_deps=dep)
            p.dma("sync_q", lambda e, r0=r0, i=i: e.dma_start(
                out=idxg.ap[:, NP + i:NP + i + 1].bitcast(U32),
                in_=dr["dlist"][r0:r0 + 128, :]), outs=[idxg], extra_deps=dep)
        p.dma("sync_q", lambda e, e_i=e_i: e.dma_start(
            out=wrow.ap[0:1, :],
            in_=dr["wlist"][e_i * CAP:(e_i + 1) * CAP, 0:1].rearrange(
                "a one -> one a")), outs=[wrow], extra_deps=dep)
        for i in range(NP):
            ioff = IndirectOffsetOnAxis(ap=idxg.ap[:, i:i + 1].bitcast(U32),
                                        axis=0)
            p.dma("gpsimd_q", lambda e, i=i, ioff=ioff: e.indirect_dma_start(
                out=xg_tok[i].ap, out_offset=None,
                in_=dr["xn_tok_s"], in_offset=ioff),
                ins=[idxg], outs=[xg_tok[i]])
        for f in range(FD):
            bank = pbk[4 + f % 2]
            for i in range(NP):
                p.op("tensor", lambda e, f=f, i=i, bank=bank: e.transpose(
                    bank.ap[:, i * 128:(i + 1) * 128],
                    xg_tok[i].ap[:, f * 128:(f + 1) * 128], ident.ap),
                    ins=[xg_tok[i], ident], outs=[bank])
            p.op("vector", lambda e, f=f, bank=bank: e.tensor_copy(
                xg_fm.ap[:, f * CAP:(f + 1) * CAP], bank.ap[:, 0:CAP]),
                ins=[bank], outs=[xg_fm])
        for mg in range(4):
            bank = pbk[mg % 2]
            for k in range(FD):
                p.op("tensor", lambda e, k=k, mg=mg, bank=bank: e.matmul(
                    bank.ap[:, 0:CAP],
                    wgb.ap[:, k * M + mg * 128:k * M + (mg + 1) * 128],
                    xg_fm.ap[:, k * CAP:(k + 1) * CAP],
                    start=(k == 0), stop=(k == FD - 1)),
                    ins=[wgb, xg_fm], outs=[bank])
            p.op("scalar", lambda e, mg=mg, bank=bank: e.activation(
                gs2.ap[:, mg * CAP:(mg + 1) * CAP], bank.ap[:, 0:CAP], AF.Silu),
                ins=[bank], outs=[gs2])
        for mg in range(4):
            bank = pbk[mg % 2]
            for k in range(FD):
                p.op("tensor", lambda e, k=k, mg=mg, bank=bank: e.matmul(
                    bank.ap[:, 0:CAP],
                    wub.ap[:, k * M + mg * 128:k * M + (mg + 1) * 128],
                    xg_fm.ap[:, k * CAP:(k + 1) * CAP],
                    start=(k == 0), stop=(k == FD - 1)),
                    ins=[wub, xg_fm], outs=[bank])
            p.op("scalar", lambda e, mg=mg, bank=bank: e.copy(
                ub2.ap[:, mg * CAP:(mg + 1) * CAP], bank.ap[:, 0:CAP]),
                ins=[bank], outs=[ub2])
        p.op("vector", lambda e: e.tensor_tensor(
            gs2.ap, gs2.ap, ub2.ap, ALU.mult), ins=[gs2, ub2], outs=[gs2])
        bank6 = pbk[6]
        p.op("tensor", lambda e, bank6=bank6: e.matmul(
            bank6.ap[:, 0:CAP], ones_row.ap[0:1, 0:128], wrow.ap[0:1, :],
            start=True, stop=True), ins=[ones_row, wrow], outs=[bank6])
        p.op("vector", lambda e, bank6=bank6: e.tensor_tensor(
            gs2.ap.rearrange("q (b t) -> q b t", t=CAP),
            gs2.ap.rearrange("q (b t) -> q b t", t=CAP),
            bank6.ap[:, 0:CAP].unsqueeze(1).broadcast_to((128, 4, CAP)),
            ALU.mult), ins=[gs2, bank6], outs=[gs2])
        for mo in range(FD):
            bank = pbk[2 + mo % 2]
            for k in range(4):
                p.op("tensor", lambda e, k=k, mo=mo, bank=bank: e.matmul(
                    bank.ap[:, 0:CAP],
                    wdb.ap[:, k * D + mo * 128:k * D + (mo + 1) * 128],
                    gs2.ap[:, k * CAP:(k + 1) * CAP],
                    start=(k == 0), stop=(k == 3)),
                    ins=[wdb, gs2], outs=[bank])
            p.op("scalar", lambda e, mo=mo, bank=bank: e.copy(
                eo2.ap[:, mo * CAP:(mo + 1) * CAP], bank.ap[:, 0:CAP]),
                ins=[bank], outs=[eo2])
        for i in range(NP):
            bank = pbk[4 + i % 2]
            for f in range(FD):
                bk = bank if f < 4 else pbk[6]
                p.op("tensor", lambda e, f=f, i=i, bk=bk: e.transpose(
                    bk.ap[:, (f % 4) * 128:(f % 4 + 1) * 128],
                    eo2.ap[:, f * CAP + i * 128:f * CAP + (i + 1) * 128],
                    ident.ap), ins=[eo2, ident], outs=[bk])
                if f == 3:
                    p.op("vector", lambda e, i=i, bk=bk: e.tensor_copy(
                        eo_tok[i].ap[:, 0:512], bk.ap[:, 0:512]),
                        ins=[bk], outs=[eo_tok[i]])
                if f == 7:
                    p.op("vector", lambda e, i=i, bk=bk: e.tensor_copy(
                        eo_tok[i].ap[:, 512:1024], bk.ap[:, 0:512]),
                        ins=[bk], outs=[eo_tok[i]])
            soff = IndirectOffsetOnAxis(
                ap=idxg.ap[:, NP + i:NP + i + 1].bitcast(U32), axis=0)
            ctx.setdefault("moe_sc", []).append(p.dma(
                "gpsimd_q", lambda e, i=i, soff=soff: e.indirect_dma_start(
                    out=dr["moebuf"], out_offset=soff,
                    in_=eo_tok[i].ap, in_offset=None),
                ins=[eo_tok[i], idxg]))
    for b in xg_tok + eo_tok + [xg_fm, idxg, wrow, gs2, ub2, eo2,
                                wgb, wub, wdb, gs, ub, eo]:
        ar.freebuf(b)



def pass3_store_bpcp(p, ar, dr, bpcp):
    p.dma("gpsimd_q", lambda e: e.dma_start(out=dr["bpcp_s"], in_=bpcp.ap[0:32, :]),
          ins=[bpcp], outs=[p.dbuf("bpcp_s")])
    ar.freebuf(bpcp)
    p.qfence("gpsimd_q")


def routing_sparse_v3(p, ar, pbk, dr, ctx):
    xn_c = ar.alloc(FD * TC, "p5r_xn")
    gw = ar.alloc(FD * NE, "p5_gw")
    p.dma("sync_q", lambda e: e.dma_start(
        out=gw.ap.rearrange("q (b s) -> q b s", s=NE),
        in_=dr["gate_w_T"].rearrange("(b q) s -> q b s", q=128)), outs=[gw])
    sc_fm = ar.alloc(T, "p5_scfm")
    for c in range(NCHUNK):
        load_fm(p, dr["xn_s"], xn_c, FD, c, name="xn_s")
        bank = pbk[3]
        for k in range(FD):
            p.op("tensor", lambda e, k=k, c=c, bank=bank, xn_c=xn_c: e.matmul(
                bank.ap[0:16, 0:TC], gw.ap[:, k * NE:(k + 1) * NE],
                xn_c.ap[:, k * TC:(k + 1) * TC],
                start=(k == 0), stop=(k == FD - 1)),
                ins=[gw, xn_c], outs=[bank])
        p.op("scalar", lambda e, c=c, bank=bank: e.activation(
            sc_fm.ap[0:16, c * TC:(c + 1) * TC], bank.ap[0:16, 0:TC],
            AF.Sigmoid), ins=[bank], outs=[sc_fm])
        yield
    ar.freebuf(xn_c)
    ar.freebuf(gw)
    yield from routing_lists(p, ar, pbk, dr, ctx, sc_fm)


def routing_lists(p, ar, pbk, dr, ctx, sc_fm):
    iota16 = ctx["iota16"]
    ident = ctx["ident"]
    tok0 = ar.alloc(NE, "p5_tok0")
    tok1 = ar.alloc(NE, "p5_tok1")
    p.op("gpsimd", lambda e: e.iota(tok0.ap.bitcast(U32), [[128, 16]],
                                    base=0, channel_multiplier=1), outs=[tok0])
    p.op("gpsimd", lambda e: e.iota(tok1.ap.bitcast(U32), [[128, 16]],
                                    base=T, channel_multiplier=1), outs=[tok1])
    NG = NE * CAP // 128
    zf = ar.alloc(NG, "p5_zf")
    p.op("vector", lambda e: e.memset(zf.ap, 0.0), outs=[zf])
    dumpi = ar.alloc(NG, "p5_dump")
    p.op("gpsimd", lambda e: e.iota(dumpi.ap.bitcast(U32), [[1, NG]],
                                    base=2 * T, channel_multiplier=NG), outs=[dumpi])
    for nm, buf in (("glist", zf), ("dlist", dumpi), ("wlist", zf)):
        s2 = (buf.ap[:, 0:NG].bitcast(U32) if nm != "wlist"
              else buf.ap[:, 0:NG])
        p.dma("gpsimd_q", lambda e, nm=nm, s2=s2: e.dma_start(
            out=dr[nm], in_=s2), ins=[buf], outs=[p.dbuf(nm)])
    sc_tok = ar.alloc(NE, "p5_sctok")
    oh1 = ar.alloc(16 * NE, "p5_oh1")
    oh2 = ar.alloc(16 * NE, "p5_oh2")
    idxf = ar.alloc(32, "p5_idxf")
    wp = ar.alloc(32, "p5_wp")
    msk_T = ar.alloc(T, "p5_mskT")
    mx8 = ar.alloc(8, "p5_mx8")
    idx = ar.alloc(8, "p5_idx")
    sm = ar.alloc(4, "p5_sm")
    t16 = ar.alloc(NE, "p5_t16")
    for tt in range(16):
        bank = pbk[4]
        p.op("tensor", lambda e, tt=tt, bank=bank: e.transpose(
            bank.ap[:, 0:16], sc_fm.ap[0:16, tt * 128:(tt + 1) * 128],
            ident.ap[0:16, 0:16]), ins=[sc_fm, ident], outs=[bank])
        p.op("vector", lambda e, bank=bank: e.tensor_copy(
            sc_tok.ap, bank.ap[:, 0:16]), ins=[bank], outs=[sc_tok])
        p.op("vector", lambda e: e.max(mx8.ap, sc_tok.ap),
             ins=[sc_tok], outs=[mx8], reducy=True)
        p.op("vector", lambda e: e.max_index(
            idx.ap.bitcast(U32), mx8.ap, sc_tok.ap),
            ins=[sc_tok, mx8], outs=[idx], reducy=True)
        p.op("vector", lambda e, tt=tt: e.tensor_copy(
            idxf.ap[:, 2 * tt:2 * tt + 2], idx.ap[:, 0:2].bitcast(U32)),
            ins=[idx], outs=[idxf])
        p.op("vector", lambda e: e.tensor_tensor(
            sm.ap[:, 0:1], mx8.ap[:, 0:1], mx8.ap[:, 1:2], ALU.add),
            ins=[mx8], outs=[sm])
        p.op("vector", lambda e: e.tensor_scalar(
            sm.ap[:, 0:1], sm.ap[:, 0:1], 1e-20, None, ALU.add),
            ins=[sm], outs=[sm])
        p.op("vector", lambda e: e.reciprocal(sm.ap[:, 1:2], sm.ap[:, 0:1]),
             ins=[sm], outs=[sm])
        p.op("vector", lambda e, tt=tt: e.tensor_scalar(
            wp.ap[:, 2 * tt:2 * tt + 2], mx8.ap[:, 0:2], sm.ap[:, 1:2], 2.5,
            ALU.mult, ALU.mult), ins=[mx8, sm], outs=[wp])
        o1 = oh1.ap[:, tt * NE:(tt + 1) * NE]
        o2 = oh2.ap[:, tt * NE:(tt + 1) * NE]
        p.op("vector", lambda e, o1=o1, tt=tt: e.tensor_scalar(
            o1, iota16.ap, idxf.ap[:, 2 * tt:2 * tt + 1], None, ALU.is_equal),
            ins=[iota16, idxf], outs=[oh1])
        p.op("vector", lambda e, o2=o2, tt=tt: e.tensor_scalar(
            o2, iota16.ap, idxf.ap[:, 2 * tt + 1:2 * tt + 2], None,
            ALU.is_equal), ins=[iota16, idxf], outs=[oh2])
        p.op("vector", lambda e, o1=o1, o2=o2: e.tensor_tensor(
            t16.ap, o1, o2, ALU.add), ins=[oh1, oh2], outs=[t16])
        yield
        bank3 = pbk[5]
        p.op("tensor", lambda e, bank3=bank3: e.transpose(
            bank3.ap[0:16, 0:128], t16.ap, ident.ap),
            ins=[t16, ident], outs=[bank3])
        p.op("vector", lambda e, tt=tt, bank3=bank3: e.tensor_copy(
            msk_T.ap[0:16, tt * 128:(tt + 1) * 128], bank3.ap[0:16, 0:128]),
            ins=[bank3], outs=[msk_T])
        yield
    zrow = ar.alloc(T, "p5_zrow")
    p.op("vector", lambda e: e.memset(zrow.ap[0:16, :], 0.0), outs=[zrow])
    pos_T = ar.alloc(T, "p5_posT")
    p.op("vector", lambda e: e.tensor_tensor_scan(
        pos_T.ap[0:16, :], msk_T.ap[0:16, :], zrow.ap[0:16, :], 0.0,
        ALU.add, ALU.add), ins=[msk_T, zrow], outs=[pos_T])
    pos_tok = ar.alloc(NE, "p5_postok")
    cols = ar.alloc(8, "p5_cols")
    du = ar.alloc(2, "p5_du")
    sc_toks = ctx.setdefault("sc_toks", [])
    for tt in range(16):
        bank = pbk[4]
        p.op("tensor", lambda e, tt=tt, bank=bank: e.transpose(
            bank.ap[:, 0:16], pos_T.ap[0:16, tt * 128:(tt + 1) * 128],
            ident.ap[0:16, 0:16]), ins=[pos_T, ident], outs=[bank])
        p.op("vector", lambda e, bank=bank: e.tensor_copy(
            pos_tok.ap, bank.ap[:, 0:16]), ins=[bank], outs=[pos_tok])
        for r, ohb, tokb in ((0, oh1, tok0), (1, oh2, tok1)):
            o = ohb.ap[:, tt * NE:(tt + 1) * NE]
            p.op("vector", lambda e, o=o: e.tensor_tensor(
                t16.ap, o, pos_tok.ap, ALU.mult),
                ins=[ohb, pos_tok], outs=[t16])
            p.op("vector", lambda e: e.tensor_reduce(
                cols.ap[:, 0:1], t16.ap, mybir.AxisListType.X, ALU.add),
                ins=[t16], outs=[cols], reducy=True)
            p.op("vector", lambda e, tt=tt, r=r: e.tensor_scalar(
                cols.ap[:, 1:2], idxf.ap[:, 2 * tt + r:2 * tt + r + 1],
                float(CAP), -1.0, ALU.mult, ALU.add),
                ins=[idxf], outs=[cols])
            p.op("vector", lambda e: e.tensor_tensor(
                cols.ap[:, 1:2], cols.ap[:, 1:2], cols.ap[:, 0:1], ALU.add),
                ins=[cols], outs=[cols])
            p.op("vector", lambda e: e.tensor_copy(
                du.ap[:, 0:1].bitcast(U32), cols.ap[:, 1:2]),
                ins=[cols], outs=[du])
            ioff = IndirectOffsetOnAxis(ap=du.ap[:, 0:1].bitcast(U32), axis=0)
            tk_g = tok0.ap[:, tt:tt + 1].bitcast(U32)
            tk_d = tokb.ap[:, tt:tt + 1].bitcast(U32)
            wsl = wp.ap[:, 2 * tt + r:2 * tt + r + 1]
            sc_toks.append(p.dma("gpsimd_q", lambda e, tk_g=tk_g, ioff=ioff:
                                 e.indirect_dma_start(out=dr["glist"],
                                                      out_offset=ioff,
                                                      in_=tk_g, in_offset=None),
                                 ins=[tok0, du],
                                 extra_deps=[p.dbuf("glist").wr]))
            sc_toks.append(p.dma("gpsimd_q", lambda e, tk_d=tk_d, ioff=ioff:
                                 e.indirect_dma_start(out=dr["dlist"],
                                                      out_offset=ioff,
                                                      in_=tk_d, in_offset=None),
                                 ins=[tokb, du],
                                 extra_deps=[p.dbuf("dlist").wr]))
            sc_toks.append(p.dma("gpsimd_q", lambda e, wsl=wsl, ioff=ioff:
                                 e.indirect_dma_start(out=dr["wlist"],
                                                      out_offset=ioff,
                                                      in_=wsl, in_offset=None),
                                 ins=[wp, du],
                                 extra_deps=[p.dbuf("wlist").wr]))
    for b in (sc_fm, sc_tok, oh1, oh2, idxf, wp, msk_T, mx8, idx, sm,
              t16, zrow, pos_T, pos_tok, cols, du, tok0, tok1, zf, dumpi):
        ar.freebuf(b)
    p.qfence("gpsimd_q")
    return sc_toks


def moe_state(p, ar, dr, sc_toks):
    ms = {}
    ms["w"] = [ar.alloc(FD * M, "ms_w0"), ar.alloc(FD * M, "ms_w1"),
               ar.alloc(FD * M, "ms_w2")]
    ms["wi"] = 0
    ms["xe"] = ar.alloc(FD * TC, "ms_xe")
    ms["xg"] = [ar.alloc(D, f"ms_xg{i}") for i in range(CAP // 128)]
    ms["gs"] = ar.alloc(4 * TC, "ms_gs")
    ms["ub"] = ar.alloc(4 * TC, "ms_ub")
    ms["rb"] = ar.alloc(CAP, "ms_rb")
    ms["idxg"] = ar.alloc(2 * (CAP // 128), "ms_idxg")
    gd = [sc_toks[-1], p.dbuf("glist").wr, p.dbuf("dlist").wr,
          p.dbuf("wlist").wr]
    gd += [tk for tk in (p.dbuf("xn_tok_s", c).wr for c in range(NCHUNK)) if tk]
    ms["gather_dep"] = gd
    ms["first"] = True
    return ms


def next_w(p, ar, dr, ms, src_ap):
    b = ms["w"][ms["wi"] % 3]
    ms["wi"] += 1
    m2 = src_ap.shape[1]
    p.dma("sync_q", lambda e, b=b, src_ap=src_ap, m2=m2: e.dma_start(
        out=b.ap.rearrange("q (b m) -> q b m", m=m2),
        in_=src_ap.rearrange("(b q) m -> q b m", q=128)), outs=[b])
    return b


def expert_emit(p, ar, pbk, dr, ctx, ms, e_i):
    ident = ctx["ident"]
    NP = CAP // 128
    xg, xe, gs2, ub2 = ms["xg"], ms["xe"], ms["gs"], ms["ub"]
    idxg, rb = ms["idxg"], ms["rb"]
    wg = next_w(p, ar, dr, ms, dr["w_gate"][e_i])
    wu = next_w(p, ar, dr, ms, dr["w_up"][e_i])
    wd = next_w(p, ar, dr, ms, dr["w_down"][e_i])
    dep = ms["gather_dep"] if ms["first"] else []
    ms["first"] = False
    for i in range(NP):
        r0 = e_i * CAP + i * 128
        p.dma("gpsimd_q", lambda e, r0=r0, i=i: e.dma_start(
            out=idxg.ap[:, i:i + 1].bitcast(U32),
            in_=dr["glist"][r0:r0 + 128, :]), outs=[idxg], extra_deps=dep)
        p.dma("gpsimd_q", lambda e, r0=r0, i=i: e.dma_start(
            out=idxg.ap[:, NP + i:NP + i + 1].bitcast(U32),
            in_=dr["dlist"][r0:r0 + 128, :]), outs=[idxg], extra_deps=dep)
    p.dma("gpsimd_q", lambda e, e_i=e_i: e.dma_start(
        out=rb.ap, in_=dr["wlist"][e_i * CAP:(e_i + 1) * CAP, 0:1].rearrange(
            "a one -> one a").broadcast_to((128, CAP))), outs=[rb],
        extra_deps=dep)
    for i in range(NP):
        ioff = IndirectOffsetOnAxis(ap=idxg.ap[:, i:i + 1].bitcast(U32), axis=0)
        p.dma("gpsimd_q", lambda e, i=i, ioff=ioff: e.indirect_dma_start(
            out=xg[i].ap, out_offset=None,
            in_=dr["xn_tok_s"], in_offset=ioff),
            ins=[idxg], outs=[xg[i]])
    p.qfence("gpsimd_q")
    for f in range(FD):
        bank = pbk[4 + f % 2]
        for i in range(NP):
            p.op("tensor", lambda e, f=f, i=i, bank=bank: e.transpose(
                bank.ap[:, i * 128:(i + 1) * 128],
                xg[i].ap[:, f * 128:(f + 1) * 128], ident.ap),
                ins=[xg[i], ident], outs=[bank])
        p.op("scalar", lambda e, f=f, bank=bank: e.copy(
            xe.ap[:, f * CAP:(f + 1) * CAP], bank.ap[:, 0:CAP]),
            ins=[bank], outs=[xe])
    for wsrc, dst, act in ((wg, gs2, True), (wu, ub2, False)):
        for mg in range(4):
            bank = pbk[mg % 2]
            for k in range(FD):
                p.op("tensor", lambda e, k=k, mg=mg, bank=bank, wsrc=wsrc:
                     e.matmul(bank.ap[:, 0:CAP],
                              wsrc.ap[:, k * M + mg * 128:k * M + (mg + 1) * 128],
                              xe.ap[:, k * CAP:(k + 1) * CAP],
                              start=(k == 0), stop=(k == FD - 1)),
                     ins=[wsrc, xe], outs=[bank])
            if act:
                p.op("scalar", lambda e, mg=mg, bank=bank, dst=dst: e.activation(
                    dst.ap[:, mg * CAP:(mg + 1) * CAP], bank.ap[:, 0:CAP],
                    AF.Silu), ins=[bank], outs=[dst])
            else:
                p.op("scalar", lambda e, mg=mg, bank=bank, dst=dst: e.copy(
                    dst.ap[:, mg * CAP:(mg + 1) * CAP], bank.ap[:, 0:CAP]),
                    ins=[bank], outs=[dst])
    p.op("vector", lambda e: e.tensor_tensor(
        gs2.ap[:, 0:4 * CAP], gs2.ap[:, 0:4 * CAP], ub2.ap[:, 0:4 * CAP],
        ALU.mult), ins=[gs2, ub2], outs=[gs2])
    p.op("vector", lambda e: e.tensor_tensor(
        gs2.ap[:, 0:4 * CAP].rearrange("q (b t) -> q b t", t=CAP),
        gs2.ap[:, 0:4 * CAP].rearrange("q (b t) -> q b t", t=CAP),
        rb.ap.unsqueeze(1).broadcast_to((128, 4, CAP)),
        ALU.mult), ins=[gs2, rb], outs=[gs2])
    for mo in range(FD):
        bank = pbk[2 + mo % 2]
        for k in range(4):
            p.op("tensor", lambda e, k=k, mo=mo, bank=bank, wd=wd: e.matmul(
                bank.ap[:, 0:CAP],
                wd.ap[:, k * D + mo * 128:k * D + (mo + 1) * 128],
                gs2.ap[:, k * CAP:(k + 1) * CAP],
                start=(k == 0), stop=(k == 3)),
                ins=[wd, gs2], outs=[bank])
        p.op("scalar", lambda e, mo=mo, bank=bank: e.copy(
            xe.ap[:, mo * CAP:(mo + 1) * CAP], bank.ap[:, 0:CAP]),
            ins=[bank], outs=[xe])
        yield
    for i in range(NP):
        for fh in range(2):
            bk = pbk[4 + fh]
            for f4 in range(4):
                f = fh * 4 + f4
                p.op("tensor", lambda e, f=f, f4=f4, i=i, bk=bk: e.transpose(
                    bk.ap[:, f4 * 128:(f4 + 1) * 128],
                    xe.ap[:, f * CAP + i * 128:f * CAP + (i + 1) * 128],
                    ident.ap), ins=[xe, ident], outs=[bk])
            p.op("scalar", lambda e, i=i, fh=fh, bk=bk: e.copy(
                xg[i].ap[:, fh * 512:(fh + 1) * 512], bk.ap[:, 0:512]),
                ins=[bk], outs=[xg[i]])
            yield
        soff = IndirectOffsetOnAxis(
            ap=idxg.ap[:, NP + i:NP + i + 1].bitcast(U32), axis=0)
        ctx.setdefault("moe_sc", []).append(p.dma(
            "gpsimd_q", lambda e, i=i, soff=soff: e.indirect_dma_start(
                out=dr["moebuf"], out_offset=soff,
                in_=xg[i].ap, in_offset=None),
            ins=[xg[i], idxg]))


def shared_chunk_emit(p, ar, pbk, dr, ctx, ms, c, xn_c):
    gs2, ub2, xe = ms["gs"], ms["ub"], ms["xe"]
    wg = next_w(p, ar, dr, ms, dr["ws_gate"])
    wu = next_w(p, ar, dr, ms, dr["ws_up"])
    wd = next_w(p, ar, dr, ms, dr["ws_down"])
    for wsrc, dst, act in ((wg, gs2, True), (wu, ub2, False)):
        for mg in range(4):
            bank = pbk[mg % 2]
            for k in range(FD):
                p.op("tensor", lambda e, k=k, mg=mg, bank=bank, wsrc=wsrc,
                     xn_c=xn_c: e.matmul(
                         bank.ap[:, 0:TC],
                         wsrc.ap[:, k * M + mg * 128:k * M + (mg + 1) * 128],
                         xn_c.ap[:, k * TC:(k + 1) * TC],
                         start=(k == 0), stop=(k == FD - 1)),
                     ins=[wsrc, xn_c], outs=[bank])
            if act:
                p.op("scalar", lambda e, mg=mg, bank=bank, dst=dst: e.activation(
                    dst.ap[:, mg * TC:(mg + 1) * TC], bank.ap[:, 0:TC],
                    AF.Silu), ins=[bank], outs=[dst])
            else:
                p.op("scalar", lambda e, mg=mg, bank=bank, dst=dst: e.copy(
                    dst.ap[:, mg * TC:(mg + 1) * TC], bank.ap[:, 0:TC]),
                    ins=[bank], outs=[dst])
    p.op("vector", lambda e: e.tensor_tensor(
        gs2.ap, gs2.ap, ub2.ap, ALU.mult), ins=[gs2, ub2], outs=[gs2])
    for mo in range(FD):
        bank = pbk[2 + mo % 2]
        for k in range(4):
            p.op("tensor", lambda e, k=k, mo=mo, bank=bank, wd=wd: e.matmul(
                bank.ap[:, 0:TC],
                wd.ap[:, k * D + mo * 128:k * D + (mo + 1) * 128],
                gs2.ap[:, k * TC:(k + 1) * TC],
                start=(k == 0), stop=(k == 3)),
                ins=[wd, gs2], outs=[bank])
        p.op("scalar", lambda e, mo=mo, bank=bank: e.copy(
            xe.ap[:, mo * TC:(mo + 1) * TC], bank.ap[:, 0:TC]),
            ins=[bank], outs=[xe])
    store_fm(p, dr["hacc_s"], xe, FD, c, name="hacc_s")


def pass4v3(p, ar, pbk, dr, dbg, ctx, c, state):
    a_bc = ctx["a_bc"]
    HF = 8 * TC
    dt_h = ar.alloc(HF, f"s4dt_{c}")
    dtx = ar.alloc(HF, f"s4dtx_{c}")
    dA = ar.alloc(HF, f"s4dA_{c}")
    uh = ar.alloc(HF, f"s4uh_{c}")
    y_h = ar.alloc(HF, f"s4y_{c}")
    bB = [ar.alloc(TC, f"s4bB_{c}_{j}") for j in range(2)]
    bC = [ar.alloc(TC, f"s4bC_{c}_{j}") for j in range(2)]
    bp_dep = [p.dbuf("bpcp_s").wr] if p.dbuf("bpcp_s").wr else []
    for h in range(2):
        load_fm(p, dr["dt_s"], dt_h, 8, c, blk0=h * 8, name="dt_s")
        load_fm(p, dr["xc_s"], dA, 8, c, blk0=h * 8, name="xc_s")
        p.op("vector", lambda e: e.tensor_tensor(
            dtx.ap, dt_h.ap, dA.ap, ALU.mult), ins=[dt_h, dA], outs=[dtx])
        for s in range(S):
            jb = s % 2
            p.dma("sync_q", lambda e, s=s, jb=jb: e.dma_start(
                out=bB[jb].ap,
                in_=dr["bpcp_s"][s:s + 1, c * TC:(c + 1) * TC].broadcast_to(
                    (128, TC))), outs=[bB[jb]], extra_deps=bp_dep)
            p.dma("sync_q", lambda e, s=s, jb=jb: e.dma_start(
                out=bC[jb].ap,
                in_=dr["bpcp_s"][16 + s:17 + s, c * TC:(c + 1) * TC].broadcast_to(
                    (128, TC))), outs=[bC[jb]], extra_deps=bp_dep)
            p.op("scalar", lambda e, s=s: e.activation(
                dA.ap, dt_h.ap, AF.Exp, scale=a_bc.ap[:, s:s + 1]),
                ins=[dt_h, a_bc], outs=[dA])
            p.op("vector", lambda e, jb=jb: e.tensor_tensor(
                uh.ap.rearrange("q (b t) -> q b t", t=TC),
                dtx.ap.rearrange("q (b t) -> q b t", t=TC),
                bB[jb].ap.unsqueeze(1).broadcast_to((128, 8, TC)),
                ALU.mult), ins=[dtx, bB[jb]], outs=[uh])
            for dc in range(8):
                col = s * FDI + h * 8 + dc
                p.op("vector", lambda e, dc=dc, col=col:
                     e.tensor_tensor_scan(
                         uh.ap[:, dc * TC:(dc + 1) * TC],
                         dA.ap[:, dc * TC:(dc + 1) * TC],
                         uh.ap[:, dc * TC:(dc + 1) * TC],
                         state.ap[:, col:col + 1], ALU.mult, ALU.add),
                     ins=[dA, uh, state], outs=[uh])
            p.op("vector", lambda e, s=s, h=h: e.tensor_copy(
                state.ap[:, s * FDI + h * 8:s * FDI + h * 8 + 8],
                uh.ap.rearrange("q (b t) -> q b t", t=TC)[:, :, TC - 1:TC
                                                          ].squeeze(2)),
                ins=[uh], outs=[state])
            if s == 0:
                p.op("vector", lambda e, jb=jb: e.tensor_tensor(
                    y_h.ap.rearrange("q (b t) -> q b t", t=TC),
                    uh.ap.rearrange("q (b t) -> q b t", t=TC),
                    bC[jb].ap.unsqueeze(1).broadcast_to((128, 8, TC)),
                    ALU.mult), ins=[uh, bC[jb]], outs=[y_h])
            else:
                p.op("vector", lambda e, jb=jb: e.tensor_tensor(
                    dA.ap.rearrange("q (b t) -> q b t", t=TC),
                    uh.ap.rearrange("q (b t) -> q b t", t=TC),
                    bC[jb].ap.unsqueeze(1).broadcast_to((128, 8, TC)),
                    ALU.mult), ins=[uh, bC[jb]], outs=[dA])
                p.op("vector", lambda e: e.tensor_tensor(
                    y_h.ap, y_h.ap, dA.ap, ALU.add), ins=[dA, y_h], outs=[y_h])
        d3 = dr["y_s"].rearrange("(b q) t -> q b t", q=128)[
            :, h * 8:(h + 1) * 8, c * TC:(c + 1) * TC]
        s3 = y_h.ap.rearrange("q (b t) -> q b t", t=TC)
        p.dma("gpsimd_q", lambda e, d3=d3, s3=s3: e.dma_start(out=d3, in_=s3),
              ins=[y_h], outs=[p.dbuf("y_s", c)])
        if "y" in dbg:
            d3b = dr["dbg_y"].rearrange("(b q) t -> q b t", q=128)[
                :, h * 8:(h + 1) * 8, c * TC:(c + 1) * TC]
            p.dma("gpsimd_q", lambda e, d3b=d3b, s3=s3: e.dma_start(
                out=d3b, in_=s3), ins=[y_h])
    for b in [dt_h, dtx, uh, y_h] + dAb + bBC:
        ar.freebuf(b)
    p.qfence("gpsimd_q")


def pass6az_v3(p, ar, pbk, dr, dbg, ctx, c, xn_c):
    wzp = [ar.alloc(FD * 128, f"az_w_{c}_{j}") for j in range(2)]
    zt = ar.alloc(TC, f"az_z_{c}")
    for mi in range(FDI):
        wb = wzp[mi % 2]
        src = dr["w_in_z_T"][:, mi * 128:(mi + 1) * 128]
        p.dma("sync_q", lambda e, wb=wb, src=src: e.dma_start(
            out=wb.ap.rearrange("q (b m) -> q b m", m=128),
            in_=src.rearrange("(b q) m -> q b m", q=128)), outs=[wb])
        bank = pbk[6 + mi % 2]
        for k in range(FD):
            p.op("tensor", lambda e, k=k, bank=bank, wb=wb, xn_c=xn_c: e.matmul(
                bank.ap[:, 0:TC], wb.ap[:, k * 128:(k + 1) * 128],
                xn_c.ap[:, k * TC:(k + 1) * TC],
                start=(k == 0), stop=(k == FD - 1)),
                ins=[wb, xn_c], outs=[bank])
        p.op("scalar", lambda e, bank=bank, zt=zt: e.activation(
            zt.ap, bank.ap[:, 0:TC], AF.Silu), ins=[bank], outs=[zt])
        p.dma("gpsimd_q", lambda e, mi=mi, zt=zt: e.dma_start(
            out=dr["zs_s"][mi * 128:(mi + 1) * 128, c * TC:(c + 1) * TC],
            in_=zt.ap), ins=[zt], outs=[p.dbuf("zs_s", c)])
        yield
    for b in wzp + [zt]:
        ar.freebuf(b)
    p.qfence("gpsimd_q")


def pass6gate_v3(p, ar, pbk, dr, dbg, ctx, c):
    tabs = ctx["tabs"]
    constk = ctx["constk"]
    ones_row = ctx["ones_row"]
    ones_col = ctx["ones_col"]
    y0 = ar.alloc(8 * TC, f"g_y0_{c}")
    y1 = ar.alloc(8 * TC, f"g_y1_{c}")
    sq = ar.alloc(TC, f"g_sq_{c}")
    stat = ar.alloc(6 * TC, f"g_st_{c}")
    load_fm(p, dr["y_s"], y0, 8, c, blk0=0, name="y_s")
    load_fm(p, dr["y_s"], y1, 8, c, blk0=8, name="y_s")
    bA, bB2 = pbk[6], pbk[7]
    for k in range(FDI):
        yk = (y0 if k < 8 else y1).ap[:, (k % 8) * TC:(k % 8 + 1) * TC]
        ybuf = y0 if k < 8 else y1
        p.op("scalar", lambda e, yk=yk: e.activation(
            sq.ap, yk, AF.Square), ins=[ybuf], outs=[sq])
        p.op("tensor", lambda e, k=k, yk=yk, bA=bA: e.matmul(
            bA.ap[0:1, 0:TC], ones_col.ap, yk,
            start=(k == 0), stop=(k == FDI - 1)), ins=[ones_col, ybuf], outs=[bA])
        p.op("tensor", lambda e, k=k, bB2=bB2: e.matmul(
            bB2.ap[0:1, 0:TC], ones_col.ap, sq.ap,
            start=(k == 0), stop=(k == FDI - 1)), ins=[ones_col, sq], outs=[bB2])
    r0 = stat.ap[0:1, :]
    p.op("vector", lambda e, bA=bA: e.tensor_scalar(
        r0[:, 0:TC], bA.ap[0:1, 0:TC], 1.0 / DIN, None, ALU.mult),
        ins=[bA], outs=[stat])
    p.op("vector", lambda e, bB2=bB2: e.tensor_scalar(
        r0[:, TC:2 * TC], bB2.ap[0:1, 0:TC], 1.0 / DIN, None, ALU.mult),
        ins=[bB2], outs=[stat])
    p.op("vector", lambda e: e.tensor_tensor(
        r0[:, 2 * TC:3 * TC], r0[:, 0:TC], r0[:, 0:TC], ALU.mult),
        ins=[stat], outs=[stat])
    p.op("vector", lambda e: e.tensor_tensor(
        r0[:, TC:2 * TC], r0[:, TC:2 * TC], r0[:, 2 * TC:3 * TC],
        ALU.subtract), ins=[stat], outs=[stat])
    p.op("scalar", lambda e: e.activation(
        r0[:, 2 * TC:3 * TC], r0[:, TC:2 * TC], AF.Sqrt,
        bias=constk.ap[0:1, 0:1]), ins=[stat, constk], outs=[stat])
    p.op("vector", lambda e: e.reciprocal(
        r0[:, TC:2 * TC], r0[:, 2 * TC:3 * TC]), ins=[stat], outs=[stat])
    p.op("tensor", lambda e, bA=bA: e.matmul(
        bA.ap[:, 0:TC], ones_row.ap[0:1, 0:128], r0[:, 0:TC],
        start=True, stop=True), ins=[ones_row, stat], outs=[bA])
    p.op("tensor", lambda e, bB2=bB2: e.matmul(
        bB2.ap[:, 0:TC], ones_row.ap[0:1, 0:128], r0[:, TC:2 * TC],
        start=True, stop=True), ins=[ones_row, stat], outs=[bB2])
    xh = ar.alloc(8 * TC, f"g_xh_{c}")
    zh = ar.alloc(8 * TC, f"g_zh_{c}")
    for h in range(2):
        ybuf = y0 if h == 0 else y1
        load_fm(p, dr["xc_s"], xh, 8, c, blk0=h * 8, name="xc_s")
        load_fm(p, dr["zs_s"], zh, 8, c, blk0=h * 8, name="zs_s")
        for k8 in range(8):
            k = h * 8 + k8
            yk = ybuf.ap[:, k8 * TC:(k8 + 1) * TC]
            p.op("vector", lambda e, yk=yk, bA=bA: e.tensor_tensor(
                yk, yk, bA.ap[:, 0:TC], ALU.subtract), ins=[ybuf, bA], outs=[ybuf])
            p.op("vector", lambda e, yk=yk, bB2=bB2: e.tensor_tensor(
                yk, yk, bB2.ap[:, 0:TC], ALU.mult), ins=[ybuf, bB2], outs=[ybuf])
            p.op("vector", lambda e, yk=yk, k=k: e.tensor_scalar(
                yk, yk, tabs["ln_g_pc"].ap[:, k:k + 1],
                tabs["ln_b_pc"].ap[:, k:k + 1], ALU.mult, ALU.add),
                ins=[ybuf, tabs["ln_g_pc"], tabs["ln_b_pc"]], outs=[ybuf])
            p.op("vector", lambda e, yk=yk, k=k, k8=k8, xh=xh:
                 e.scalar_tensor_tensor(
                     yk, xh.ap[:, k8 * TC:(k8 + 1) * TC],
                     tabs["d_param_pc"].ap[:, k:k + 1], yk, ALU.mult, ALU.add),
                 ins=[xh, tabs["d_param_pc"], ybuf], outs=[ybuf])
            p.op("vector", lambda e, yk=yk, k8=k8, zh=zh: e.tensor_tensor(
                yk, yk, zh.ap[:, k8 * TC:(k8 + 1) * TC], ALU.mult),
                ins=[ybuf, zh], outs=[ybuf])
        d3 = dr["yg_s"].rearrange("(b q) t -> q b t", q=128)[
            :, h * 8:(h + 1) * 8, c * TC:(c + 1) * TC]
        s3 = ybuf.ap.rearrange("q (b t) -> q b t", t=TC)
        p.dma("gpsimd_q", lambda e, d3=d3, s3=s3: e.dma_start(out=d3, in_=s3),
              ins=[ybuf], outs=[p.dbuf("yg_s", c)])
        if "yg" in dbg:
            d3b = dr["dbg_yg"].rearrange("(b q) t -> q b t", q=128)[
                :, h * 8:(h + 1) * 8, c * TC:(c + 1) * TC]
            p.dma("gpsimd_q", lambda e, d3b=d3b, s3=s3: e.dma_start(
                out=d3b, in_=s3), ins=[ybuf])
    for b in (y0, y1, sq, stat, xh, zh):
        ar.freebuf(b)


def expert_emit_v4(p, ar, pbk, dr, ctx, ms, e_i):
    ident = ctx["ident"]
    NP = CAP // 128
    xg, xe, gs2, ub2 = ms["xg"], ms["xe"], ms["gs"], ms["ub"]
    idxg, rb = ms["idxg"], ms["rb"]
    wh = ms["wh"]

    def whalf(src_ap, klo):
        b = wh[ms["wi"] % 2]
        ms["wi"] += 1
        m2 = src_ap.shape[1]
        sl = src_ap[klo * 512:(klo + 4 * (m2 // 512) * 0 + 4) * 128 + klo * 0, :]
        return b

    dep = ms["gather_dep"] if ms["first"] else []
    ms["first"] = False
    for i in range(NP):
        r0 = e_i * CAP + i * 128
        p.dma("gpsimd_q", lambda e, r0=r0, i=i: e.dma_start(
            out=idxg.ap[:, i:i + 1].bitcast(U32),
            in_=dr["glist"][r0:r0 + 128, :]), outs=[idxg], extra_deps=dep)
        p.dma("gpsimd_q", lambda e, r0=r0, i=i: e.dma_start(
            out=idxg.ap[:, NP + i:NP + i + 1].bitcast(U32),
            in_=dr["dlist"][r0:r0 + 128, :]), outs=[idxg], extra_deps=dep)
    p.dma("gpsimd_q", lambda e, e_i=e_i: e.dma_start(
        out=rb.ap, in_=dr["wlist"][e_i * CAP:(e_i + 1) * CAP, 0:1].rearrange(
            "a one -> one a").broadcast_to((128, CAP))), outs=[rb],
        extra_deps=dep)
    for i in range(NP):
        ioff = IndirectOffsetOnAxis(ap=idxg.ap[:, i:i + 1].bitcast(U32), axis=0)
        p.dma("gpsimd_q", lambda e, i=i, ioff=ioff: e.indirect_dma_start(
            out=xg[i].ap, out_offset=None,
            in_=dr["xn_tok_s"], in_offset=ioff),
            ins=[idxg], outs=[xg[i]])
    p.qfence("gpsimd_q")
    for f in range(FD):
        bank = pbk[4 + f % 2]
        for i in range(NP):
            p.op("tensor", lambda e, f=f, i=i, bank=bank: e.transpose(
                bank.ap[:, i * 128:(i + 1) * 128],
                xg[i].ap[:, f * 128:(f + 1) * 128], ident.ap),
                ins=[xg[i], ident], outs=[bank])
        p.op("scalar", lambda e, f=f, bank=bank: e.copy(
            xe.ap[:, f * CAP:(f + 1) * CAP], bank.ap[:, 0:CAP]),
            ins=[bank], outs=[xe])
        yield
    for src_d, dst, act in ((dr["w_gate"][e_i], gs2, True),
                            (dr["w_up"][e_i], ub2, False)):
        halves = []
        for kh in range(2):
            b = wh[ms["wi"] % 2]
            ms["wi"] += 1
            sl = src_d[kh * 512:(kh + 1) * 512, :]
            p.dma("gpsimd_q", lambda e, b=b, sl=sl: e.dma_start(
                out=b.ap.rearrange("q (b m) -> q b m", m=M),
                in_=sl.rearrange("(b q) m -> q b m", q=128)), outs=[b])
            halves.append(b)
        for mg in range(4):
            bank = pbk[mg % 2]
            for k in range(FD):
                b = halves[k // 4]
                k4 = k % 4
                p.op("tensor", lambda e, k=k, k4=k4, mg=mg, bank=bank, b=b:
                     e.matmul(bank.ap[:, 0:CAP],
                              b.ap[:, k4 * M + mg * 128:k4 * M + (mg + 1) * 128],
                              xe.ap[:, k * CAP:(k + 1) * CAP],
                              start=(k == 0), stop=(k == FD - 1)),
                     ins=[b, xe], outs=[bank])
            if act:
                p.op("scalar", lambda e, mg=mg, bank=bank, dst=dst: e.activation(
                    dst.ap[:, mg * CAP:(mg + 1) * CAP], bank.ap[:, 0:CAP],
                    AF.Silu), ins=[bank], outs=[dst])
            else:
                p.op("scalar", lambda e, mg=mg, bank=bank, dst=dst: e.copy(
                    dst.ap[:, mg * CAP:(mg + 1) * CAP], bank.ap[:, 0:CAP]),
                    ins=[bank], outs=[dst])
            yield
    p.op("gpsimd", lambda e: e.tensor_tensor(
        gs2.ap[:, 0:4 * CAP], gs2.ap[:, 0:4 * CAP], ub2.ap[:, 0:4 * CAP],
        ALU.mult), ins=[gs2, ub2], outs=[gs2])
    p.op("gpsimd", lambda e: e.tensor_tensor(
        gs2.ap[:, 0:4 * CAP].rearrange("q (b t) -> q b t", t=CAP),
        gs2.ap[:, 0:4 * CAP].rearrange("q (b t) -> q b t", t=CAP),
        rb.ap.unsqueeze(1).broadcast_to((128, 4, CAP)),
        ALU.mult), ins=[gs2, rb], outs=[gs2])
    halves = []
    for kh in range(2):
        b = wh[ms["wi"] % 2]
        ms["wi"] += 1
        sl = dr["w_down"][e_i][kh * 256:(kh + 1) * 256, :]
        p.dma("gpsimd_q", lambda e, b=b, sl=sl: e.dma_start(
            out=b.ap.rearrange("q (b m) -> q b m", m=D),
            in_=sl.rearrange("(b q) m -> q b m", q=128)), outs=[b])
        halves.append(b)
    for mo in range(FD):
        bank = pbk[2 + mo % 2]
        for k in range(4):
            b = halves[k // 2]
            k2 = k % 2
            p.op("tensor", lambda e, k=k, k2=k2, mo=mo, bank=bank, b=b: e.matmul(
                bank.ap[:, 0:CAP],
                b.ap[:, k2 * D + mo * 128:k2 * D + (mo + 1) * 128],
                gs2.ap[:, k * CAP:(k + 1) * CAP],
                start=(k == 0), stop=(k == 3)),
                ins=[b, gs2], outs=[bank])
        p.op("scalar", lambda e, mo=mo, bank=bank: e.copy(
            xe.ap[:, mo * CAP:(mo + 1) * CAP], bank.ap[:, 0:CAP]),
            ins=[bank], outs=[xe])
        yield
    for i in range(NP):
        for fh in range(2):
            bk = pbk[4 + fh]
            for f4 in range(4):
                f = fh * 4 + f4
                p.op("tensor", lambda e, f=f, f4=f4, i=i, bk=bk: e.transpose(
                    bk.ap[:, f4 * 128:(f4 + 1) * 128],
                    xe.ap[:, f * CAP + i * 128:f * CAP + (i + 1) * 128],
                    ident.ap), ins=[xe, ident], outs=[bk])
            p.op("scalar", lambda e, i=i, fh=fh, bk=bk: e.copy(
                xg[i].ap[:, fh * 512:(fh + 1) * 512], bk.ap[:, 0:512]),
                ins=[bk], outs=[xg[i]])
            yield
        soff = IndirectOffsetOnAxis(
            ap=idxg.ap[:, NP + i:NP + i + 1].bitcast(U32), axis=0)
        ctx.setdefault("moe_sc", []).append(p.dma(
            "gpsimd_q", lambda e, i=i, soff=soff: e.indirect_dma_start(
                out=dr["moebuf"], out_offset=soff,
                in_=xg[i].ap, in_offset=None),
            ins=[xg[i], idxg]))
    p.qfence("gpsimd_q")


def shared_tail(p, ar, pbk, dr, ctx):
    wgb = ar.alloc(FD * M, "st_wg")
    wub = ar.alloc(FD * M, "st_wu")
    wdb = ar.alloc(4 * D, "st_wd")
    gs = ar.alloc(4 * TC, "st_gs")
    ub = ar.alloc(4 * TC, "st_ub")
    eo = ar.alloc(FD * TC, "st_eo")
    xn_c = ar.alloc(FD * TC, "st_xn")
    p.dma("sync_q", lambda e: e.dma_start(
        out=wgb.ap.rearrange("q (b m) -> q b m", m=M),
        in_=dr["ws_gate"].rearrange("(b q) m -> q b m", q=128)), outs=[wgb])
    p.dma("sync_q", lambda e: e.dma_start(
        out=wub.ap.rearrange("q (b m) -> q b m", m=M),
        in_=dr["ws_up"].rearrange("(b q) m -> q b m", q=128)), outs=[wub])
    p.dma("sync_q", lambda e: e.dma_start(
        out=wdb.ap.rearrange("q (b m) -> q b m", m=D),
        in_=dr["ws_down"].rearrange("(b q) m -> q b m", q=128)), outs=[wdb])
    for c in range(NCHUNK):
        load_fm(p, dr["xn_s"], xn_c, FD, c, name="xn_s")
        for mg in range(4):
            bank = pbk[mg % 2]
            for k in range(FD):
                p.op("tensor", lambda e, k=k, mg=mg, bank=bank, xn_c=xn_c:
                     e.matmul(bank.ap[:, 0:TC],
                              wgb.ap[:, k * M + mg * 128:k * M + (mg + 1) * 128],
                              xn_c.ap[:, k * TC:(k + 1) * TC],
                              start=(k == 0), stop=(k == FD - 1)),
                     ins=[wgb, xn_c], outs=[bank])
            p.op("scalar", lambda e, mg=mg, bank=bank: e.activation(
                gs.ap[:, mg * TC:(mg + 1) * TC], bank.ap[:, 0:TC], AF.Silu),
                ins=[bank], outs=[gs])
        for mg in range(4):
            bank = pbk[mg % 2]
            for k in range(FD):
                p.op("tensor", lambda e, k=k, mg=mg, bank=bank, xn_c=xn_c:
                     e.matmul(bank.ap[:, 0:TC],
                              wub.ap[:, k * M + mg * 128:k * M + (mg + 1) * 128],
                              xn_c.ap[:, k * TC:(k + 1) * TC],
                              start=(k == 0), stop=(k == FD - 1)),
                     ins=[wub, xn_c], outs=[bank])
            p.op("scalar", lambda e, mg=mg, bank=bank: e.copy(
                ub.ap[:, mg * TC:(mg + 1) * TC], bank.ap[:, 0:TC]),
                ins=[bank], outs=[ub])
        p.op("vector", lambda e: e.tensor_tensor(
            gs.ap, gs.ap, ub.ap, ALU.mult), ins=[gs, ub], outs=[gs])
        for mo in range(FD):
            bank = pbk[2 + mo % 2]
            for k in range(4):
                p.op("tensor", lambda e, k=k, mo=mo, bank=bank: e.matmul(
                    bank.ap[:, 0:TC],
                    wdb.ap[:, k * D + mo * 128:k * D + (mo + 1) * 128],
                    gs.ap[:, k * TC:(k + 1) * TC],
                    start=(k == 0), stop=(k == 3)),
                    ins=[wdb, gs], outs=[bank])
            p.op("scalar", lambda e, mo=mo, bank=bank: e.copy(
                eo.ap[:, mo * TC:(mo + 1) * TC], bank.ap[:, 0:TC]),
                ins=[bank], outs=[eo])
        store_fm(p, dr["hacc_s"], eo, FD, c, name="hacc_s")
    for b in (wgb, wub, wdb, gs, ub, eo, xn_c):
        ar.freebuf(b)
    p.qfence("gpsimd_q")


def shared_gen_v4(p, ar, pbk, dr, ctx, ms, c, xn_c):
    gs2, ub2, xe, wh = ms["gs"], ms["ub"], ms["xe"], ms["wh"]
    for src_d, dst, act in ((dr["ws_gate"], gs2, True), (dr["ws_up"], ub2, False)):
        halves = []
        for kh in range(2):
            b = wh[ms["wi"] % 2]
            ms["wi"] += 1
            sl = src_d[kh * 512:(kh + 1) * 512, :]
            p.dma("gpsimd_q", lambda e, b=b, sl=sl: e.dma_start(
                out=b.ap.rearrange("q (b m) -> q b m", m=M),
                in_=sl.rearrange("(b q) m -> q b m", q=128)), outs=[b])
            halves.append(b)
        for mg in range(4):
            bank = pbk[mg % 2]
            for k in range(FD):
                b = halves[k // 4]
                k4 = k % 4
                p.op("tensor", lambda e, k=k, k4=k4, mg=mg, bank=bank, b=b,
                     xn_c=xn_c: e.matmul(
                         bank.ap[:, 0:TC],
                         b.ap[:, k4 * M + mg * 128:k4 * M + (mg + 1) * 128],
                         xn_c.ap[:, k * TC:(k + 1) * TC],
                         start=(k == 0), stop=(k == FD - 1)),
                     ins=[b, xn_c], outs=[bank])
            if act:
                p.op("scalar", lambda e, mg=mg, bank=bank, dst=dst: e.activation(
                    dst.ap[:, mg * TC:(mg + 1) * TC], bank.ap[:, 0:TC],
                    AF.Silu), ins=[bank], outs=[dst])
            else:
                p.op("scalar", lambda e, mg=mg, bank=bank, dst=dst: e.copy(
                    dst.ap[:, mg * TC:(mg + 1) * TC], bank.ap[:, 0:TC]),
                    ins=[bank], outs=[dst])
            yield
    p.op("gpsimd", lambda e: e.tensor_tensor(
        gs2.ap, gs2.ap, ub2.ap, ALU.mult), ins=[gs2, ub2], outs=[gs2])
    halves = []
    for kh in range(2):
        b = wh[ms["wi"] % 2]
        ms["wi"] += 1
        sl = dr["ws_down"][kh * 256:(kh + 1) * 256, :]
        p.dma("gpsimd_q", lambda e, b=b, sl=sl: e.dma_start(
            out=b.ap.rearrange("q (b m) -> q b m", m=D),
            in_=sl.rearrange("(b q) m -> q b m", q=128)), outs=[b])
        halves.append(b)
    for mo in range(FD):
        bank = pbk[2 + mo % 2]
        for k in range(4):
            b = halves[k // 2]
            k2 = k % 2
            p.op("tensor", lambda e, k=k, k2=k2, mo=mo, bank=bank, b=b: e.matmul(
                bank.ap[:, 0:TC],
                b.ap[:, k2 * D + mo * 128:k2 * D + (mo + 1) * 128],
                gs2.ap[:, k * TC:(k + 1) * TC],
                start=(k == 0), stop=(k == 3)),
                ins=[b, gs2], outs=[bank])
        p.op("scalar", lambda e, mo=mo, bank=bank: e.copy(
            xe.ap[:, mo * TC:(mo + 1) * TC], bank.ap[:, 0:TC]),
            ins=[bank], outs=[xe])
        yield
    store_fm(p, dr["hacc_s"], xe, FD, c, name="hacc_s")
    p.qfence("gpsimd_q")


def pass4v4_alloc(p, ar, dr, c):
    HF = 8 * TC
    QF = 4 * TC
    bufs = dict(
        dt_h=ar.alloc(HF, f"s4dt_{c}"),
        dtx=ar.alloc(HF, f"s4dtx_{c}"),
        dAb=[ar.alloc(QF, f"s4dA_{c}_{j}") for j in range(2)],
        uh=ar.alloc(QF, f"s4uh_{c}"),
        y_h=ar.alloc(HF, f"s4y_{c}"),
        bBC=[ar.alloc(4 * TC, f"s4bBC_{c}_{j}") for j in range(2)])
    load_fm(p, dr["dt_s"], bufs["dt_h"], 8, c, blk0=0, name="dt_s")
    load_fm(p, dr["xc_s"], bufs["dtx"], 8, c, blk0=0, name="xc_s")
    return bufs


def pass4v4(p, ar, pbk, dr, dbg, ctx, c, state, bufs, feed=None):
    a_bc = ctx["a_bc"]
    HF = 8 * TC
    QF = 4 * TC
    dt_h, dtx, dAb, uh, y_h, bBC = (bufs["dt_h"], bufs["dtx"], bufs["dAb"],
                                    bufs["uh"], bufs["y_h"], bufs["bBC"])
    bp_dep = [p.dbuf("bpcp_s").wr] if p.dbuf("bpcp_s").wr else []
    for h in range(2):
        if h == 1:
            load_fm(p, dr["dt_s"], dt_h, 8, c, blk0=8, name="dt_s")
            load_fm(p, dr["xc_s"], dtx, 8, c, blk0=8, name="xc_s")
        p.op("vector", lambda e: e.tensor_tensor(
            dtx.ap, dt_h.ap, dtx.ap, ALU.mult), ins=[dt_h, dtx], outs=[dtx])
        qi = 0
        for s in range(S):
            sg, s2 = s // 2, s % 2
            grp = bBC[sg % 2]
            if s2 == 0:
                p.dma("sync_q", lambda e, sg=sg, grp=grp: e.dma_start(
                    out=grp.ap.rearrange("q (j t) -> q j t", t=TC),
                    in_=dr["bpcp_s"][4 * sg:4 * sg + 4, c * TC:(c + 1) * TC
                                     ].unsqueeze(0).broadcast_to((128, 4, TC))),
                    outs=[grp], extra_deps=bp_dep)
            bBs = grp.ap[:, (2 * s2) * TC:(2 * s2 + 1) * TC]
            bCs = grp.ap[:, (2 * s2 + 1) * TC:(2 * s2 + 2) * TC]
            for q in range(2):
                qs = q * QF
                dA = dAb[qi % 2]
                qi += 1
                p.op("scalar", lambda e, s=s, qs=qs, dA=dA: e.activation(
                    dA.ap, dt_h.ap[:, qs:qs + QF], AF.Exp,
                    scale=a_bc.ap[:, s:s + 1]), ins=[dt_h, a_bc], outs=[dA])
                p.op("vector", lambda e, bBs=bBs, qs=qs: e.tensor_tensor(
                    uh.ap.rearrange("q (b t) -> q b t", t=TC),
                    dtx.ap[:, qs:qs + QF].rearrange("q (b t) -> q b t", t=TC),
                    bBs.unsqueeze(1).broadcast_to((128, 4, TC)),
                    ALU.mult), ins=[dtx, grp], outs=[uh])
                for dc in range(4):
                    col = s * FDI + h * 8 + q * 4 + dc
                    p.op("vector", lambda e, dc=dc, col=col, dA=dA:
                         e.tensor_tensor_scan(
                             uh.ap[:, dc * TC:(dc + 1) * TC],
                             dA.ap[:, dc * TC:(dc + 1) * TC],
                             uh.ap[:, dc * TC:(dc + 1) * TC],
                             state.ap[:, col:col + 1], ALU.mult, ALU.add),
                         ins=[dA, uh, state], outs=[uh])
                p.op("vector", lambda e, s=s, h=h, q=q: e.tensor_copy(
                    state.ap[:, s * FDI + h * 8 + q * 4:s * FDI + h * 8 + q * 4 + 4],
                    uh.ap.rearrange("q (b t) -> q b t", t=TC)[:, :, TC - 1:TC
                                                              ].squeeze(2)),
                    ins=[uh], outs=[state])
                if s == 0:
                    p.op("vector", lambda e, bCs=bCs, qs=qs: e.tensor_tensor(
                        y_h.ap[:, qs:qs + QF].rearrange("q (b t) -> q b t", t=TC),
                        uh.ap.rearrange("q (b t) -> q b t", t=TC),
                        bCs.unsqueeze(1).broadcast_to((128, 4, TC)),
                        ALU.mult), ins=[uh, grp], outs=[y_h])
                else:
                    p.op("vector", lambda e, bCs=bCs, dA=dA: e.tensor_tensor(
                        dA.ap.rearrange("q (b t) -> q b t", t=TC),
                        uh.ap.rearrange("q (b t) -> q b t", t=TC),
                        bCs.unsqueeze(1).broadcast_to((128, 4, TC)),
                        ALU.mult), ins=[uh, grp], outs=[dA])
                    p.op("vector", lambda e, qs=qs, dA=dA: e.tensor_tensor(
                        y_h.ap[:, qs:qs + QF], y_h.ap[:, qs:qs + QF], dA.ap,
                        ALU.add), ins=[dA, y_h], outs=[y_h])
                if feed is not None:
                    feed(2)
        d3 = dr["y_s"].rearrange("(b q) t -> q b t", q=128)[
            :, h * 8:(h + 1) * 8, c * TC:(c + 1) * TC]
        s3 = y_h.ap.rearrange("q (b t) -> q b t", t=TC)
        p.dma("gpsimd_q", lambda e, d3=d3, s3=s3: e.dma_start(out=d3, in_=s3),
              ins=[y_h], outs=[p.dbuf("y_s", c)])
        if "y" in dbg:
            d3b = dr["dbg_y"].rearrange("(b q) t -> q b t", q=128)[
                :, h * 8:(h + 1) * 8, c * TC:(c + 1) * TC]
            p.dma("gpsimd_q", lambda e, d3b=d3b, s3=s3: e.dma_start(
                out=d3b, in_=s3), ins=[y_h])
    for b in [dt_h, dtx, uh, y_h] + dAb + bBC:
        ar.freebuf(b)
    p.qfence("gpsimd_q")


def pass6gate_v4(p, ar, pbk, dr, dbg, ctx, c):
    tabs = ctx["tabs"]
    constk = ctx["constk"]
    ones_row = ctx["ones_row"]
    ones_col = ctx["ones_col"]
    gb = ctx["gate_bufs"]
    yq, sq, stat, xk, zk = (gb["yq"], gb["sq"], gb["stat"], gb["xk"], gb["zk"])
    bA, bB2 = pbk[6], pbk[7]
    for qq in range(4):
        load_fm(p, dr["y_s"], yq, 4, c, blk0=qq * 4, name="y_s", q="scalar_q")
        p.qfence("scalar_q")
        for k4 in range(4):
            k = qq * 4 + k4
            yk = yq.ap[:, k4 * TC:(k4 + 1) * TC]
            p.op("scalar", lambda e, yk=yk: e.activation(
                sq.ap, yk, AF.Square), ins=[yq], outs=[sq])
            p.op("tensor", lambda e, k=k, yk=yk, bA=bA: e.matmul(
                bA.ap[0:1, 0:TC], ones_col.ap, yk,
                start=(k == 0), stop=(k == FDI - 1)),
                ins=[ones_col, yq], outs=[bA])
            p.op("tensor", lambda e, k=k, bB2=bB2: e.matmul(
                bB2.ap[0:1, 0:TC], ones_col.ap, sq.ap,
                start=(k == 0), stop=(k == FDI - 1)),
                ins=[ones_col, sq], outs=[bB2])
        yield
    r0 = stat.ap[0:1, :]
    p.op("vector", lambda e, bA=bA: e.tensor_scalar(
        r0[:, 0:TC], bA.ap[0:1, 0:TC], 1.0 / DIN, None, ALU.mult),
        ins=[bA], outs=[stat])
    p.op("vector", lambda e, bB2=bB2: e.tensor_scalar(
        r0[:, TC:2 * TC], bB2.ap[0:1, 0:TC], 1.0 / DIN, None, ALU.mult),
        ins=[bB2], outs=[stat])
    p.op("vector", lambda e: e.tensor_tensor(
        r0[:, 2 * TC:3 * TC], r0[:, 0:TC], r0[:, 0:TC], ALU.mult),
        ins=[stat], outs=[stat])
    p.op("vector", lambda e: e.tensor_tensor(
        r0[:, TC:2 * TC], r0[:, TC:2 * TC], r0[:, 2 * TC:3 * TC],
        ALU.subtract), ins=[stat], outs=[stat])
    p.op("scalar", lambda e: e.activation(
        r0[:, 2 * TC:3 * TC], r0[:, TC:2 * TC], AF.Sqrt,
        bias=constk.ap[0:1, 0:1]), ins=[stat, constk], outs=[stat])
    p.op("vector", lambda e: e.reciprocal(
        r0[:, TC:2 * TC], r0[:, 2 * TC:3 * TC]), ins=[stat], outs=[stat])
    p.op("tensor", lambda e, bA=bA: e.matmul(
        bA.ap[:, 0:TC], ones_row.ap[0:1, 0:128], r0[:, 0:TC],
        start=True, stop=True), ins=[ones_row, stat], outs=[bA])
    p.op("tensor", lambda e, bB2=bB2: e.matmul(
        bB2.ap[:, 0:TC], ones_row.ap[0:1, 0:128], r0[:, TC:2 * TC],
        start=True, stop=True), ins=[ones_row, stat], outs=[bB2])
    for qq in range(4):
        load_fm(p, dr["y_s"], yq, 4, c, blk0=qq * 4, name="y_s", q="scalar_q")
        p.qfence("scalar_q")
        for k2 in range(2):
            load_fm(p, dr["xc_s"], xk, 2, c, blk0=qq * 4 + k2 * 2,
                    name="xc_s", q="scalar_q")
            load_fm(p, dr["zs_s"], zk, 2, c, blk0=qq * 4 + k2 * 2,
                    name="zs_s", q="scalar_q")
            for kj in range(2):
                k4 = k2 * 2 + kj
                k = qq * 4 + k4
                yk = yq.ap[:, k4 * TC:(k4 + 1) * TC]
                p.op("vector", lambda e, yk=yk, bA=bA: e.tensor_tensor(
                    yk, yk, bA.ap[:, 0:TC], ALU.subtract), ins=[yq, bA], outs=[yq])
                p.op("vector", lambda e, yk=yk, bB2=bB2: e.tensor_tensor(
                    yk, yk, bB2.ap[:, 0:TC], ALU.mult), ins=[yq, bB2], outs=[yq])
                p.op("vector", lambda e, yk=yk, k=k: e.tensor_scalar(
                    yk, yk, tabs["ln_g_pc"].ap[:, k:k + 1],
                    tabs["ln_b_pc"].ap[:, k:k + 1], ALU.mult, ALU.add),
                    ins=[yq, tabs["ln_g_pc"], tabs["ln_b_pc"]], outs=[yq])
                p.op("vector", lambda e, yk=yk, k=k, kj=kj, xk=xk:
                     e.scalar_tensor_tensor(
                         yk, xk.ap[:, kj * TC:(kj + 1) * TC],
                         tabs["d_param_pc"].ap[:, k:k + 1], yk, ALU.mult,
                         ALU.add),
                     ins=[xk, tabs["d_param_pc"], yq], outs=[yq])
                p.op("vector", lambda e, yk=yk, kj=kj, zk=zk: e.tensor_tensor(
                    yk, yk, zk.ap[:, kj * TC:(kj + 1) * TC], ALU.mult),
                    ins=[yq, zk], outs=[yq])
                yield
        d3 = dr["yg_s"].rearrange("(b q) t -> q b t", q=128)[
            :, qq * 4:(qq + 1) * 4, c * TC:(c + 1) * TC]
        s3 = yq.ap.rearrange("q (b t) -> q b t", t=TC)
        p.dma("gpsimd_q", lambda e, d3=d3, s3=s3: e.dma_start(out=d3, in_=s3),
              ins=[yq], outs=[p.dbuf("yg_s", c)])
        if "yg" in dbg:
            d3b = dr["dbg_yg"].rearrange("(b q) t -> q b t", q=128)[
                :, qq * 4:(qq + 1) * 4, c * TC:(c + 1) * TC]
            p.dma("gpsimd_q", lambda e, d3b=d3b, s3=s3: e.dma_start(
                out=d3b, in_=s3), ins=[yq])
    p.qfence("gpsimd_q")
    p.qfence("scalar_q")


def kernel(**inputs):
    nc = build()
    in_maps = [host_inputs(inputs, core) for core in range(8)]
    res = run_bass_kernel_spmd(nc, in_maps, core_ids=list(range(8)))
    out = np.stack([res.results[i]["out"] for i in range(8)], axis=0)
    return out


# version 16
# speedup vs baseline: 1.0753x; 1.0284x over previous
import numpy as np
import concourse.bass as bass
import concourse.mybir as mybir
from concourse.bass import IndirectOffsetOnAxis
from concourse.bass_utils import run_bass_kernel_spmd

F32 = mybir.dt.float32
U32 = mybir.dt.uint32
AF = mybir.ActivationFunctionType
ALU = mybir.AluOpType

T = 2048
D = 1024
DIN = 2048
S = 16
NE = 16
M = 512
NCHUNK = 4
CAP = 384
TC = T // NCHUNK
FD = D // 128
FDI = DIN // 128
LN_EPS = 1e-5

ENGINES = ("tensor", "vector", "scalar", "gpsimd", "sync")
SEMS = ("tensor", "vector", "scalar", "gpsimd", "sync_q", "scalar_q", "gpsimd_q")
ENG_SEM = {"tensor": "tensor", "vector": "vector", "scalar": "scalar",
           "gpsimd": "gpsimd", "sync": None}


class Buf:
    __slots__ = ("ap", "wr", "rd", "name")

    def __init__(self, ap, name=""):
        self.ap = ap
        self.wr = None
        self.rd = []
        self.name = name


class Prog:
    def __init__(self, nc, sems):
        self.nc = nc
        self.sems = sems
        self.cnt = {s: 0 for s in SEMS}
        self.wm = {}
        self.streams = {e: [] for e in ENGINES}
        self.nops = 0
        self.reducy = set()
        self.dram = {}

    def dbuf(self, name, c=None):
        key = (name, c)
        if key not in self.dram:
            self.dram[key] = Buf(None, f"dram_{name}_{c}")
        return self.dram[key]

    def _emit(self, eng, fn, deps, sem_name, inc, soft_deps=()):
        waits = []
        own = ENG_SEM[eng]
        for hard, lst in ((True, deps), (False, soft_deps)):
            for s, v in lst:
                if v <= 0:
                    continue
                if not hard and s == own:
                    continue
                key = (eng, s)
                if self.wm.get(key, 0) >= v:
                    continue
                self.wm[key] = v
                waits.append((s, v))
        self.cnt[sem_name] += inc
        tok = (sem_name, self.cnt[sem_name])
        self.streams[eng].append((waits, fn, sem_name, inc))
        self.nops += 1
        return tok

    def op(self, eng, fn, ins=(), outs=(), extra_deps=(), reducy=False):
        deps = list(extra_deps)
        soft = []
        for b in ins:
            if b.wr is not None:
                deps.append(b.wr)
        for b in outs:
            if b.wr is not None:
                (deps if b.wr in self.reducy else soft).append(b.wr)
            soft.extend(b.rd)
        tok = self._emit(eng, fn, deps, ENG_SEM[eng], 1, soft_deps=soft)
        if reducy:
            self.reducy.add(tok)
        for b in ins:
            b.rd.append(tok)
        for b in outs:
            b.wr = tok
            b.rd = []
        return tok

    def dma(self, queue, fn, ins=(), outs=(), extra_deps=()):
        eng = {"sync_q": "sync", "scalar_q": "scalar", "gpsimd_q": "gpsimd"}[queue]
        deps = list(extra_deps)
        for b in ins:
            if b is not None and b.wr is not None:
                deps.append(b.wr)
        for b in outs:
            if b is None:
                continue
            if b.wr is not None:
                deps.append(b.wr)
            deps.extend(b.rd)
        tok = self._emit(eng, fn, deps, queue, 16)
        self._emit(eng, None, [tok], queue, 0)
        for b in ins:
            if b is not None:
                b.rd.append(tok)
        for b in outs:
            if b is not None:
                b.wr = tok
                b.rd = []
        return tok

    def qfence(self, queue):
        eng = {"sync_q": "sync", "scalar_q": "scalar", "gpsimd_q": "gpsimd"}[queue]
        self._emit(eng, None, [(queue, self.cnt[queue])], queue, 0)

    def wait_all(self, eng):
        deps = [(s, self.cnt[s]) for s in SEMS]
        self._emit(eng, None, deps, ENG_SEM[eng] or "sync_q", 0)

    def replay(self, block):
        sems = self.sems

        def mk(eng_name):
            def run(eng):
                for waits, fn, sem_name, inc in self.streams[eng_name]:
                    for s, v in waits:
                        eng.wait_ge(sems[s], v)
                    if fn is not None:
                        instr = fn(eng)
                        if inc:
                            instr.then_inc(sems[sem_name], inc)
            return run

        block.tensor(mk("tensor"))
        block.vector(mk("vector"))
        block.scalar(mk("scalar"))
        block.gpsimd(mk("gpsimd"))
        block.sync(mk("sync"))


class Arena:

    def __init__(self, handle, ncols):
        self.h = handle
        self.n = ncols
        self.free = [(0, ncols, [])]
        self.live = {}
        self.peak = 0

    def alloc(self, cols, name):
        cols = int(cols)
        for i, (st, ln, haz) in enumerate(self.free):
            if ln >= cols:
                rest = (st + cols, ln - cols, list(haz))
                self.free[i:i + 1] = [rest] if rest[1] > 0 else []
                self.live[name] = (st, cols)
                self.peak = max(self.peak, max((s + l for s, l, _ in ([(st, cols, 0)])), default=0),
                                st + cols)
                b = Buf(self.h[:, st:st + cols], name)
                if haz:
                    b.rd = list(haz)
                return b
        raise MemoryError(f"SBUF arena: cannot alloc {cols} cols for {name}; "
                          f"free={[(s, l) for s, l, _ in self.free]}")

    def freebuf(self, b):
        st, ln = self.live.pop(b.name)
        haz = list(b.rd)
        if b.wr is not None:
            haz.append(b.wr)
        self.free.append((st, ln, haz))
        self.free.sort(key=lambda x: x[0])
        merged = []
        for seg in self.free:
            if merged and merged[-1][0] + merged[-1][1] == seg[0]:
                s0, l0, h0 = merged.pop()
                merged.append((s0, l0 + seg[1], h0 + list(seg[2])))
            else:
                merged.append(list(seg) if isinstance(seg, tuple) else seg)
        self.free = [tuple(s) for s in merged]


def host_inputs(inputs, core):
    x = np.asarray(inputs["L0_attn_x"])[core]
    W_in = np.asarray(inputs["L0_attn_W_in"])
    m = {}
    m["x"] = np.ascontiguousarray(x)
    m["w_in_x_T"] = np.ascontiguousarray(W_in[:DIN].T)
    m["w_in_z_T"] = np.ascontiguousarray(W_in[DIN:].T)
    m["w_dt_T"] = np.ascontiguousarray(np.asarray(inputs["L0_attn_W_dt"]).T)
    wxT = np.asarray(inputs["L0_attn_W_x"]).T
    perm = np.empty(32, np.int64)
    perm[0::2] = np.arange(16)
    perm[1::2] = np.arange(16) + 16
    m["w_x_T"] = np.ascontiguousarray(wxT[:, perm])
    m["w_out_T"] = np.ascontiguousarray(np.asarray(inputs["L0_attn_W_out"]).T)
    m["gate_w_T"] = np.ascontiguousarray(np.asarray(inputs["L0_ffn_gate_w"]).T)
    m["w_gate"] = np.ascontiguousarray(np.asarray(inputs["L0_ffn_w_gate"]))
    m["w_up"] = np.ascontiguousarray(np.asarray(inputs["L0_ffn_w_up"]))
    m["w_down"] = np.ascontiguousarray(np.asarray(inputs["L0_ffn_w_down"]))
    m["ws_gate"] = np.ascontiguousarray(np.asarray(inputs["L0_ffn_ws_gate"]))
    m["ws_up"] = np.ascontiguousarray(np.asarray(inputs["L0_ffn_ws_up"]))
    m["ws_down"] = np.ascontiguousarray(np.asarray(inputs["L0_ffn_ws_down"]))

    def pc(v):
        return np.ascontiguousarray(np.asarray(v).reshape(FDI, 128).T)
    m["b_dt_pc"] = pc(inputs["L0_attn_b_dt"])
    m["conv_b_pc"] = pc(inputs["L0_attn_conv_b"])
    m["d_param_pc"] = pc(inputs["L0_attn_D_param"])
    m["ln_g_pc"] = pc(inputs["L0_attn_ln_g"])
    m["ln_b_pc"] = pc(inputs["L0_attn_ln_b"])
    cw = np.asarray(inputs["L0_attn_conv_w"]).reshape(FDI, 128, 4)
    m["conv_w_pc"] = np.ascontiguousarray(cw.transpose(1, 0, 2).reshape(128, FDI * 4))
    m["a_log"] = np.ascontiguousarray(np.asarray(inputs["L0_attn_A_log"]).reshape(1, S))
    m["identity"] = np.eye(128, dtype=np.float32)
    ck = np.zeros((128, 4), dtype=np.float32)
    ck[:, 0] = LN_EPS
    ck[:, 1] = 1.0
    m["constk"] = ck
    m["ones_row"] = np.ones((1, 128), dtype=np.float32)
    m["iota16"] = np.tile(np.arange(16, dtype=np.float32), (128, 1))
    sel = np.zeros((32, 32 * 128), dtype=np.float32)
    for j in range(32):
        sel[j, j * 128:(j + 1) * 128] = 1.0
    m["sel32"] = sel
    return m


DBG_SHAPES = {"xn": [D, T], "xc": [DIN, T], "dt": [DIN, T], "bp": [S, T],
              "cp": [S, T], "y": [DIN, T], "yg": [DIN, T], "s_out": [D, T],
              "hacc": [D, T]}


def build(debug_taps=(), stages=99):
    dbg = set(debug_taps)
    nc = bass.Bass("TRN2", target_bir_lowering=False)

    dr = {}
    dr["x"] = nc.dram_tensor("x", [T, D], F32, kind="ExternalInput").ap()
    for nm, shp in [("w_in_x_T", [D, DIN]), ("w_in_z_T", [D, DIN]),
                    ("w_dt_T", [DIN, DIN]), ("w_x_T", [DIN, 2 * S]),
                    ("w_out_T", [DIN, D]), ("gate_w_T", [D, NE]),
                    ("w_gate", [NE, D, M]), ("w_up", [NE, D, M]),
                    ("w_down", [NE, M, D]),
                    ("ws_gate", [D, M]), ("ws_up", [D, M]), ("ws_down", [M, D]),
                    ("b_dt_pc", [128, FDI]), ("conv_b_pc", [128, FDI]),
                    ("d_param_pc", [128, FDI]), ("ln_g_pc", [128, FDI]),
                    ("ln_b_pc", [128, FDI]), ("conv_w_pc", [128, FDI * 4]),
                    ("a_log", [1, S]), ("identity", [128, 128]),
                    ("ones_row", [1, 128]), ("iota16", [128, 16]),
                    ("constk", [128, 4]), ("sel32", [32, 32 * 128])]:
        dr[nm] = nc.dram_tensor(nm, shp, F32, kind="ExternalInput").ap()
    dr["out"] = nc.dram_tensor("out", [T, D], F32, kind="ExternalOutput").ap()
    dr["xn_s"] = nc.dram_tensor("xn_s", [D, T], F32, kind="Internal").ap()
    dr["xc_s"] = nc.dram_tensor("xc_s", [DIN, T], F32, kind="Internal").ap()
    dr["dt_s"] = nc.dram_tensor("dt_s", [DIN, T], F32, kind="Internal").ap()
    dr["y_s"] = nc.dram_tensor("y_s", [DIN, T], F32, kind="Internal").ap()
    dr["yg_s"] = nc.dram_tensor("yg_s", [DIN, T], F32, kind="Internal").ap()
    dr["hacc_s"] = nc.dram_tensor("hacc_s", [D, T], F32, kind="Internal").ap()
    dr["xn_tok_s"] = nc.dram_tensor("xn_tok_s", [T, D], F32, kind="Internal").ap()
    dr["bpcp_s"] = nc.dram_tensor("bpcp_s", [2 * S, T], F32, kind="Internal").ap()
    dr["zs_s"] = nc.dram_tensor("zs_s", [DIN, T], F32, kind="Internal").ap()
    dr["glist"] = nc.dram_tensor("glist", [NE * CAP, 1], U32, kind="Internal").ap()
    dr["dlist"] = nc.dram_tensor("dlist", [NE * CAP, 1], U32, kind="Internal").ap()
    dr["wlist"] = nc.dram_tensor("wlist", [NE * CAP, 1], F32, kind="Internal").ap()
    dr["moebuf"] = nc.dram_tensor("moebuf", [2 * T + NE * CAP, D], F32, kind="Internal").ap()
    for nm in dbg:
        if nm == "lists":
            dr["dbg_lists"] = nc.dram_tensor("dbg_lists", [128, 144], F32,
                                             kind="ExternalOutput").ap()
            continue
        if nm == "moebuf":
            dr["dbg_moebuf"] = nc.dram_tensor("dbg_moebuf", [2 * T, D], F32,
                                              kind="ExternalOutput").ap()
            continue
        dr["dbg_" + nm] = nc.dram_tensor("dbg_" + nm, DBG_SHAPES[nm], F32,
                                         kind="ExternalOutput").ap()

    import contextlib
    with contextlib.ExitStack() as stack:
        ncols = 53150
        sbh = stack.enter_context(nc.sbuf_tensor("arena", [128, ncols], F32))
        psh = [stack.enter_context(nc.psum_tensor(f"ps{i}", [128, 512], F32))
               for i in range(8)]
        sems = {s: stack.enter_context(nc.semaphore(s)) for s in SEMS}
        p = Prog(nc, sems)
        ar = Arena(sbh, ncols)
        pbk = [Buf(psh[i], f"ps{i}") for i in range(8)]
        build_body(p, ar, pbk, dr, dbg, stages)
        with nc.Block() as block:
            p.replay(block)
    return nc


def build_body(p, ar, pbk, dr, dbg, stages):
    nc = p.nc

    ident = ar.alloc(128, "ident")
    ones_row = ar.alloc(128, "ones_row")
    tabs = {}
    for nm in ("b_dt_pc", "conv_b_pc", "d_param_pc", "ln_g_pc", "ln_b_pc"):
        tabs[nm] = ar.alloc(FDI, nm)
        p.dma("sync_q", lambda e, nm=nm, b=tabs[nm]: e.dma_start(out=b.ap, in_=dr[nm]),
              outs=[tabs[nm]])
    conv_w = ar.alloc(FDI * 4, "conv_w_pc")
    a_log = ar.alloc(S, "a_log")
    iota16 = ar.alloc(S, "iota16")
    constk = ar.alloc(4, "constk")
    p.dma("sync_q", lambda e: e.dma_start(out=constk.ap, in_=dr["constk"]), outs=[constk])
    p.dma("sync_q", lambda e: e.dma_start(out=ident.ap, in_=dr["identity"]), outs=[ident])
    p.dma("sync_q", lambda e: e.dma_start(out=ones_row.ap[0:1, :], in_=dr["ones_row"]),
          outs=[ones_row])
    p.dma("sync_q", lambda e: e.dma_start(out=conv_w.ap, in_=dr["conv_w_pc"]), outs=[conv_w])
    p.dma("sync_q", lambda e: e.dma_start(out=a_log.ap[0:1, :], in_=dr["a_log"]), outs=[a_log])
    p.dma("sync_q", lambda e: e.dma_start(out=iota16.ap, in_=dr["iota16"]), outs=[iota16])

    a_row = ar.alloc(S, "a_row")
    p.op("scalar", lambda e: e.activation(a_row.ap[0:1, :], a_log.ap[0:1, :], AF.Exp),
         ins=[a_log], outs=[a_row])
    p.op("vector", lambda e: e.tensor_scalar(a_row.ap[0:1, :], a_row.ap[0:1, :],
                                             -1.0, None, ALU.mult),
         ins=[a_row], outs=[a_row])
    a_bc = ar.alloc(S, "a_bc")
    p.op("tensor", lambda e: e.matmul(pbk[7].ap[:, 0:S], ones_row.ap[0:1, :],
                                      a_row.ap[0:1, :], start=True, stop=True),
         ins=[ones_row, a_row], outs=[pbk[7]])
    p.op("vector", lambda e: e.tensor_copy(a_bc.ap, pbk[7].ap[:, 0:S]),
         ins=[pbk[7]], outs=[a_bc])

    ctx = dict(ident=ident, ones_row=ones_row, tabs=tabs, conv_w=conv_w,
               a_bc=a_bc, iota16=iota16, constk=constk)

    ones_col = ar.alloc(1, "onec")
    p.op("vector", lambda e: e.memset(ones_col.ap, 1.0), outs=[ones_col])
    ctx["ones_col"] = ones_col
    pass1(p, ar, pbk, dr, dbg, ctx)
    pass2(p, ar, pbk, dr, dbg, ctx)
    rgen = routing_sparse_v3(p, ar, pbk, dr, ctx)

    def rfeed(n):
        for _ in range(n):
            if next(rgen, "END") == "END":
                break
    bpcp = pass3(p, ar, pbk, dr, dbg, ctx, feed=rfeed)
    pass3_store_bpcp(p, ar, dr, bpcp)
    rfeed(10 ** 9)
    sc_toks = ctx["sc_toks"]
    ms = {}
    ms["wh"] = [ar.alloc(4 * M, "ms_wh0"), ar.alloc(4 * M, "ms_wh1")]
    ms["wi"] = 0
    ms["xe"] = ar.alloc(FD * TC, "ms_xe")
    ms["xg"] = [ar.alloc(D, f"ms_xg{i}") for i in range(CAP // 128)]
    ms["gs"] = ar.alloc(4 * TC, "ms_gs")
    ms["ub"] = ar.alloc(4 * TC, "ms_ub")
    ms["rb"] = ar.alloc(CAP, "ms_rb")
    ms["idxg"] = ar.alloc(2 * (CAP // 128), "ms_idxg")
    gd = [sc_toks[-1], p.dbuf("glist").wr, p.dbuf("dlist").wr,
          p.dbuf("wlist").wr]
    gd += [tk for tk in (p.dbuf("xn_tok_s", cc).wr for cc in range(NCHUNK)) if tk]
    ms["gather_dep"] = gd
    ms["first"] = True
    state = ar.alloc(S * FDI, "scan_state")
    p.op("vector", lambda e: e.memset(state.ap, 0.0), outs=[state])
    ctx["gate_bufs"] = dict(
        yq=ar.alloc(4 * TC, "g_yq"), sq=ar.alloc(TC, "g_sq"),
        stat=ar.alloc(3 * TC, "g_st"), xk=ar.alloc(2 * TC, "g_xk"),
        zk=ar.alloc(2 * TC, "g_zk"))
    xn_c = ar.alloc(FD * TC, "blk_xn")
    import itertools
    EXP_SCHED = [list(range(0, 8)), list(range(8, 12)), list(range(12, 16)), []]
    for c in range(NCHUNK):
        sb4 = pass4v4_alloc(p, ar, dr, c)
        load_fm(p, dr["xn_s"], xn_c, FD, c, name="xn_s")
        gens = [pass6az_v3(p, ar, pbk, dr, dbg, ctx, c, xn_c),
                 shared_gen_v4(p, ar, pbk, dr, ctx, ms, c, xn_c)]
        for e_i in EXP_SCHED[c]:
            gens.append(expert_emit_v4(p, ar, pbk, dr, ctx, ms, e_i))
        chain = itertools.chain(*gens)

        def feed(n, chain=chain):
            for _ in range(n):
                if next(chain, "END") == "END":
                    break
        pass4v4(p, ar, pbk, dr, dbg, ctx, c, state, sb4, feed)
        feed(10 ** 9)
        if c > 0:
            for _ in pass6gate_v4(p, ar, pbk, dr, dbg, ctx, c - 1):
                pass
    for _ in pass6gate_v4(p, ar, pbk, dr, dbg, ctx, NCHUNK - 1):
        pass
    for b in ms["wh"] + ms["xg"] + [ms["xe"], ms["gs"], ms["ub"], ms["rb"],
                                    ms["idxg"], state, xn_c] + list(
                                        ctx["gate_bufs"].values()):
        ar.freebuf(b)
    if "lists" in dbg:
        lb = ar.alloc(3 * 48, "dbg_lb")
        NG = NE * CAP // 128
        mt = [ctx["moe_sc"][-1]] if "moe_sc" in ctx else []
        for j, nm in enumerate(("glist", "dlist", "wlist")):
            src_ap = dr[nm]
            if nm != "wlist":
                p.dma("sync_q", lambda e, j=j, src_ap=src_ap, lb=lb, NG=NG:
                      e.dma_start(out=lb.ap[:, j * NG:(j + 1) * NG].bitcast(U32),
                                  in_=src_ap), outs=[lb], extra_deps=mt)
            else:
                p.dma("sync_q", lambda e, j=j, src_ap=src_ap, lb=lb, NG=NG:
                      e.dma_start(out=lb.ap[:, j * NG:(j + 1) * NG],
                                  in_=src_ap), outs=[lb], extra_deps=mt)
        cv = ar.alloc(3 * 48, "dbg_cv")
        NGc = NE * CAP // 128
        for j in range(2):
            p.op("vector", lambda e, j=j, NGc=NGc: e.tensor_copy(
                cv.ap[:, j * NGc:(j + 1) * NGc],
                lb.ap[:, j * NGc:(j + 1) * NGc].bitcast(U32)),
                ins=[lb], outs=[cv])
        p.op("vector", lambda e, NGc=NGc: e.tensor_copy(
            cv.ap[:, 2 * NGc:3 * NGc], lb.ap[:, 2 * NGc:3 * NGc]),
            ins=[lb], outs=[cv])
        p.dma("gpsimd_q", lambda e, cv=cv: e.dma_start(
            out=dr["dbg_lists"], in_=cv.ap), ins=[cv])
    if "moebuf" in dbg:
        mb = ar.alloc(FD * TC, "dbg_mb")
        mt = [ctx["moe_sc"][-1]] if "moe_sc" in ctx else []
        for cc in range(8):
            p.dma("sync_q", lambda e, cc=cc, mb=mb: e.dma_start(
                out=mb.ap, in_=dr["moebuf"][cc * 512:(cc + 1) * 512, :].rearrange(
                    "(b q) d -> q b d", q=128)), outs=[mb], extra_deps=mt)
            p.dma("gpsimd_q", lambda e, cc=cc, mb=mb: e.dma_start(
                out=dr["dbg_moebuf"][cc * 512:(cc + 1) * 512, :].rearrange(
                    "(b q) d -> q b d", q=128), in_=mb.ap), ins=[mb])
        ar.freebuf(mb)
    pass6b(p, ar, pbk, dr, dbg, ctx)

    p.wait_all("sync")


def pass1(p, ar, pbk, dr, dbg, ctx):
    ident = ctx["ident"]
    constk = ctx["constk"]
    sq = ar.alloc(D, "p1_sq")
    for c in range(NCHUNK):
        xts = [ar.alloc(D, f"p1_x_{c}_{i}") for i in range(4)]
        mv = ar.alloc(8, f"p1_mv_{c}")
        xn_cb = ar.alloc(FD * TC, f"p1_xn_{c}")
        for pt in range(4):
            xt = xts[pt]
            t0 = c * TC + pt * 128
            p.dma("sync_q", lambda e, xt=xt, t0=t0: e.dma_start(
                out=xt.ap, in_=dr["x"][t0:t0 + 128, :]), outs=[xt])
            p.op("vector", lambda e, xt=xt, mv=mv: e.tensor_reduce(
                mv.ap[:, 0:1], xt.ap, mybir.AxisListType.X, ALU.add),
                ins=[xt], outs=[mv])
            p.op("scalar", lambda e, xt=xt, mv=mv: e.activation(
                sq.ap, xt.ap, AF.Square, accum_out=mv.ap[:, 1:2]),
                ins=[xt], outs=[sq, mv])
            p.op("vector", lambda e, mv=mv: e.tensor_scalar(
                mv.ap[:, 2:4], mv.ap[:, 0:2], 1.0 / D, None, ALU.mult),
                ins=[mv], outs=[mv])
            p.op("vector", lambda e, mv=mv: e.tensor_tensor(
                mv.ap[:, 4:5], mv.ap[:, 2:3], mv.ap[:, 2:3], ALU.mult),
                ins=[mv], outs=[mv])
            p.op("vector", lambda e, mv=mv: e.tensor_tensor(
                mv.ap[:, 5:6], mv.ap[:, 3:4], mv.ap[:, 4:5], ALU.subtract),
                ins=[mv], outs=[mv])
            p.op("scalar", lambda e, mv=mv: e.activation(
                mv.ap[:, 6:7], mv.ap[:, 5:6], AF.Sqrt, bias=constk.ap[:, 0:1]),
                ins=[mv, constk], outs=[mv])
            p.op("vector", lambda e, mv=mv: e.reciprocal(
                mv.ap[:, 7:8], mv.ap[:, 6:7]), ins=[mv], outs=[mv])
            p.op("vector", lambda e, xt=xt, mv=mv: e.tensor_scalar(
                xt.ap, xt.ap, mv.ap[:, 2:3], mv.ap[:, 7:8],
                ALU.subtract, ALU.mult), ins=[xt, mv], outs=[xt])
        for pt in range(4):
            t0 = c * TC + pt * 128
            p.dma("gpsimd_q", lambda e, xt=xts[pt], t0=t0: e.dma_start(
                out=dr["xn_tok_s"][t0:t0 + 128, :], in_=xt.ap),
                ins=[xts[pt]], outs=[p.dbuf("xn_tok_s", c)])
        for f in range(FD):
            bank = pbk[f % 4]
            for pt in range(4):
                xt = xts[pt]
                p.op("tensor", lambda e, xt=xt, f=f, pt=pt, bank=bank: e.transpose(
                    bank.ap[:, pt * 128:(pt + 1) * 128],
                    xt.ap[:, f * 128:(f + 1) * 128], ident.ap),
                    ins=[xt, ident], outs=[bank])
            p.op("vector", lambda e, bank=bank, d=xn_cb, f=f: e.tensor_copy(
                d.ap[:, f * TC:(f + 1) * TC], bank.ap[:, 0:512]),
                ins=[bank], outs=[xn_cb])
        store_fm(p, dr["xn_s"], xn_cb, FD, c, name="xn_s")
        if "xn" in dbg:
            store_fm(p, dr["dbg_xn"], xn_cb, FD, c)
        for b in xts + [mv, xn_cb]:
            ar.freebuf(b)
    ar.freebuf(sq)
    p.qfence("gpsimd_q")


def store_fm(p, dram_ap, buf, nblk, c, q="gpsimd_q", rows=128, name=None):
    d3 = dram_ap.rearrange("(b q) t -> q b t", q=128)[0:rows, :, c * TC:(c + 1) * TC]
    s3 = buf.ap[0:rows, 0:nblk * TC].rearrange("q (b t) -> q b t", t=TC)
    outs = [p.dbuf(name, c)] if name else []
    return p.dma(q, lambda e: e.dma_start(out=d3, in_=s3), ins=[buf], outs=outs)


def load_fm(p, dram_ap, buf, nblk, c, q="sync_q", rows=128, blk0=0, name=None):
    d3 = dram_ap.rearrange("(b q) t -> q b t", q=128)[0:rows, blk0:blk0 + nblk,
                                                      c * TC:(c + 1) * TC]
    s3 = buf.ap[0:rows, 0:nblk * TC].rearrange("q (b t) -> q b t", t=TC)
    ins = [p.dbuf(name, c)] if name else []
    return p.dma(q, lambda e: e.dma_start(out=s3, in_=d3), outs=[buf], ins=ins)


def mm_acc(p, bank, w_tiles, x_buf, nk, n=TC, bank_cols=None):
    ins_bufs = None
    for k in range(nk):
        lhsT = w_tiles(k)
        rhs = x_buf(k)
        p.op("tensor", lambda e, lhsT=lhsT, rhs=rhs, k=k: e.matmul(
            bank.ap[:, 0:n] if bank_cols is None else bank.ap[:, bank_cols[0]:bank_cols[1]],
            lhsT, rhs, start=(k == 0), stop=(k == nk - 1)),
            ins=[], outs=[bank], extra_deps=[])


def pass2(p, ar, pbk, dr, dbg, ctx):
    conv_w = ctx["conv_w"]
    tabs = ctx["tabs"]
    wb = [ar.alloc(DIN, f"p2_w_{k}") for k in range(FD)]
    for k in range(FD):
        p.dma("sync_q", lambda e, k=k, b=wb[k]: e.dma_start(
            out=b.ap, in_=dr["w_in_x_T"][k * 128:(k + 1) * 128, :]), outs=[wb[k]])
    tail = ar.alloc(3 * FDI, "p2_tail")
    p.op("vector", lambda e: e.memset(tail.ap, 0.0), outs=[tail])
    for c in range(NCHUNK):
        xn_c = ar.alloc(FD * TC, f"p2_xn_{c}")
        load_fm(p, dr["xn_s"], xn_c, FD, c, name="xn_s")
        xp = [ar.alloc(TC + 3, f"p2_xp_{c}_{i}") for i in range(FDI)]
        xc = ar.alloc(FDI * TC, f"p2_xc_{c}")
        acc = ar.alloc(TC, f"p2_acc_{c}")
        for mi in range(FDI):
            bank = pbk[mi % 2]
            p.op("vector", lambda e, mi=mi, b=xp[mi]: e.tensor_copy(
                b.ap[:, 0:3], tail.ap[:, mi * 3:mi * 3 + 3]),
                ins=[tail], outs=[xp[mi]])
            for k in range(FD):
                p.op("tensor", lambda e, k=k, mi=mi, bank=bank, xn_c=xn_c: e.matmul(
                    bank.ap[:, 0:TC], wb[k].ap[:, mi * 128:(mi + 1) * 128],
                    xn_c.ap[:, k * TC:(k + 1) * TC],
                    start=(k == 0), stop=(k == FD - 1)),
                    ins=[wb[k], xn_c], outs=[bank])
            p.op("scalar", lambda e, mi=mi, bank=bank, b=xp[mi]: e.copy(
                b.ap[:, 3:TC + 3], bank.ap[:, 0:TC]), ins=[bank], outs=[xp[mi]])
            p.op("vector", lambda e, mi=mi, b=xp[mi]: e.tensor_copy(
                tail.ap[:, mi * 3:mi * 3 + 3], b.ap[:, TC:TC + 3]),
                ins=[xp[mi]], outs=[tail])
            p.op("vector", lambda e, mi=mi, b=xp[mi], acc=acc: e.tensor_scalar(
                acc.ap, b.ap[:, 0:TC], conv_w.ap[:, mi * 4:mi * 4 + 1], None,
                ALU.mult), ins=[xp[mi], conv_w], outs=[acc])
            for kk in range(1, 4):
                p.op("vector", lambda e, mi=mi, kk=kk, b=xp[mi], acc=acc:
                     e.scalar_tensor_tensor(
                         acc.ap, b.ap[:, kk:kk + TC],
                         conv_w.ap[:, mi * 4 + kk:mi * 4 + kk + 1], acc.ap,
                         ALU.mult, ALU.add),
                     ins=[xp[mi], conv_w, acc], outs=[acc])
            p.op("scalar", lambda e, mi=mi, acc=acc, xc=xc: e.activation(
                xc.ap[:, mi * TC:(mi + 1) * TC], acc.ap, AF.Silu,
                bias=tabs["conv_b_pc"].ap[:, mi:mi + 1]),
                ins=[acc, tabs["conv_b_pc"]], outs=[xc])
        store_fm(p, dr["xc_s"], xc, FDI, c, name="xc_s")
        if "xc" in dbg:
            store_fm(p, dr["dbg_xc"], xc, FDI, c)
        for b in [xn_c, xc, acc] + xp:
            ar.freebuf(b)
    for b in wb + [tail]:
        ar.freebuf(b)
    p.qfence("gpsimd_q")


def pass3(p, ar, pbk, dr, dbg, ctx, feed=None):
    tabs = ctx["tabs"]
    constk = ctx["constk"]
    bpcp = ar.alloc(T, "bpcp")
    wx = ar.alloc(FDI * 2 * S, "p3_wx")
    p.dma("sync_q", lambda e: e.dma_start(
        out=wx.ap.rearrange("q (b s) -> q b s", s=2 * S),
        in_=dr["w_x_T"].rearrange("(b q) s -> q b s", q=128)), outs=[wx])
    tmp = [ar.alloc(TC, f"p3_t{i}") for i in range(4)]
    for mh in range(2):
        wdt = [ar.alloc(1024, f"p3_w_{mh}_{k}") for k in range(FDI)]
        for k in range(FDI):
            p.dma("sync_q", lambda e, k=k, mh=mh, b=wdt[k]: e.dma_start(
                out=b.ap, in_=dr["w_dt_T"][k * 128:(k + 1) * 128,
                                           mh * 1024:(mh + 1) * 1024]),
                outs=[wdt[k]])
        for c in range(NCHUNK):
            xc_c = ar.alloc(FDI * TC, f"p3_xc_{mh}_{c}")
            load_fm(p, dr["xc_s"], xc_c, FDI, c, name="xc_s")
            dtb = ar.alloc(8 * TC, f"p3_dt_{mh}_{c}")
            for mo in range(8):
                mi = mh * 8 + mo
                bank = pbk[mo % 2]
                for k in range(FDI):
                    p.op("tensor", lambda e, k=k, mo=mo, bank=bank, xc_c=xc_c,
                         wdt=wdt: e.matmul(
                             bank.ap[:, 0:TC], wdt[k].ap[:, mo * 128:(mo + 1) * 128],
                             xc_c.ap[:, k * TC:(k + 1) * TC],
                             start=(k == 0), stop=(k == FDI - 1)),
                         ins=[wdt[k], xc_c], outs=[bank])
                v, r, m, l = tmp
                p.op("scalar", lambda e, bank=bank, mi=mi: e.activation(
                    v.ap, bank.ap[:, 0:TC], AF.Identity,
                    bias=tabs["b_dt_pc"].ap[:, mi:mi + 1]),
                    ins=[bank, tabs["b_dt_pc"]], outs=[v])
                p.op("scalar", lambda e: e.activation(r.ap, v.ap, AF.Relu),
                     ins=[v], outs=[r])
                p.op("vector", lambda e: e.scalar_tensor_tensor(
                    m.ap, r.ap, -2.0, v.ap, ALU.mult, ALU.add),
                    ins=[r, v], outs=[m])
                p.op("scalar", lambda e: e.activation(m.ap, m.ap, AF.Exp),
                     ins=[m], outs=[m])
                p.op("scalar", lambda e: e.activation(
                    l.ap, m.ap, AF.Ln, bias=constk.ap[:, 1:2]),
                    ins=[m, constk], outs=[l])
                p.op("vector", lambda e, mo=mo, dtb=dtb: e.tensor_tensor(
                    dtb.ap[:, mo * TC:(mo + 1) * TC], r.ap, l.ap, ALU.add),
                    ins=[r, l], outs=[dtb])
                if feed is not None:
                    feed(1)
            d3 = dr["dt_s"].rearrange("(b q) t -> q b t", q=128)[
                :, mh * 8:(mh + 1) * 8, c * TC:(c + 1) * TC]
            s3 = dtb.ap.rearrange("q (b t) -> q b t", t=TC)
            p.dma("gpsimd_q", lambda e, d3=d3, s3=s3: e.dma_start(out=d3, in_=s3),
                  ins=[dtb], outs=[p.dbuf("dt_s", c)])
            if "dt" in dbg:
                d3b = dr["dbg_dt"].rearrange("(b q) t -> q b t", q=128)[
                    :, mh * 8:(mh + 1) * 8, c * TC:(c + 1) * TC]
                p.dma("gpsimd_q", lambda e, d3b=d3b, s3=s3: e.dma_start(
                    out=d3b, in_=s3), ins=[dtb])
            if mh == 0:
                bank = pbk[2]
                for k in range(FDI):
                    p.op("tensor", lambda e, k=k, bank=bank, xc_c=xc_c: e.matmul(
                        bank.ap[0:32, 0:TC], wx.ap[:, k * 2 * S:(k + 1) * 2 * S],
                        xc_c.ap[:, k * TC:(k + 1) * TC],
                        start=(k == 0), stop=(k == FDI - 1)),
                        ins=[wx, xc_c], outs=[bank])
                p.op("vector", lambda e, c=c, bank=bank: e.tensor_copy(
                    bpcp.ap[0:32, c * TC:(c + 1) * TC], bank.ap[0:32, 0:TC]),
                    ins=[bank], outs=[bpcp])
            ar.freebuf(xc_c)
            ar.freebuf(dtb)
        for b in wdt:
            ar.freebuf(b)
    for b in tmp + [wx]:
        ar.freebuf(b)
    if "bp" in dbg:
        p.dma("gpsimd_q", lambda e: e.dma_start(
            out=dr["dbg_bp"], in_=bpcp.ap[0:S, :]), ins=[bpcp])
    if "cp" in dbg:
        p.dma("gpsimd_q", lambda e: e.dma_start(
            out=dr["dbg_cp"], in_=bpcp.ap[S:2 * S, :]), ins=[bpcp])
    return bpcp


def pass4(p, ar, pbk, dr, dbg, ctx):
    a_bc = ctx["a_bc"]
    bpcp = ctx["bpcp"]
    sel = ar.alloc(32 * 128, "p4_sel")
    p.dma("sync_q", lambda e: e.dma_start(out=sel.ap[0:32, :], in_=dr["sel32"]),
          outs=[sel])
    state = ar.alloc(S * FDI, "p4_state")
    p.op("vector", lambda e: e.memset(state.ap, 0.0), outs=[state])
    W = FDI * TC
    for c in range(NCHUNK):
        dt_c = ar.alloc(W, f"p4_dt_{c}")
        xc_c = ar.alloc(W, f"p4_xc_{c}")
        load_fm(p, dr["dt_s"], dt_c, FDI, c, name="dt_s")
        load_fm(p, dr["xc_s"], xc_c, FDI, c, name="xc_s")
        dtx = ar.alloc(W, f"p4_dtx_{c}")
        p.op("vector", lambda e, a=dt_c, b=xc_c, o=dtx: e.tensor_tensor(
            o.ap, a.ap, b.ap, ALU.mult), ins=[dt_c, xc_c], outs=[dtx])
        ar.freebuf(xc_c)
        dA = ar.alloc(W, f"p4_dA_{c}")
        uh = ar.alloc(W, f"p4_uh_{c}")
        y = ar.alloc(W, f"p4_y_{c}")
        for s in range(S):
            bB, bC = pbk[4], pbk[5]
            p.op("tensor", lambda e, s=s, c=c, bB=bB: e.matmul(
                bB.ap[:, 0:TC], sel.ap[0:32, s * 128:(s + 1) * 128],
                bpcp.ap[0:32, c * TC:(c + 1) * TC], start=True, stop=True),
                ins=[sel, bpcp], outs=[bB])
            p.op("tensor", lambda e, s=s, c=c, bC=bC: e.matmul(
                bC.ap[:, 0:TC], sel.ap[0:32, (16 + s) * 128:(17 + s) * 128],
                bpcp.ap[0:32, c * TC:(c + 1) * TC], start=True, stop=True),
                ins=[sel, bpcp], outs=[bC])
            p.op("scalar", lambda e, s=s, dt_c=dt_c, dA=dA: e.activation(
                dA.ap, dt_c.ap, AF.Exp, scale=a_bc.ap[:, s:s + 1]),
                ins=[dt_c, a_bc], outs=[dA])
            p.op("vector", lambda e, dtx=dtx, bB=bB, uh=uh: e.tensor_tensor(
                uh.ap.rearrange("q (b t) -> q b t", t=TC), dtx.ap.rearrange(
                    "q (b t) -> q b t", t=TC),
                bB.ap[:, 0:TC].unsqueeze(1).broadcast_to((128, FDI, TC)),
                ALU.mult), ins=[dtx, bB], outs=[uh])
            for dc in range(FDI):
                col = s * FDI + dc
                p.op("vector", lambda e, dc=dc, col=col, dA=dA, uh=uh, st=state:
                     e.tensor_tensor_scan(
                         uh.ap[:, dc * TC:(dc + 1) * TC],
                         dA.ap[:, dc * TC:(dc + 1) * TC],
                         uh.ap[:, dc * TC:(dc + 1) * TC],
                         st.ap[:, col:col + 1], ALU.mult, ALU.add),
                     ins=[dA, uh, state], outs=[uh])
            p.op("vector", lambda e, s=s, uh=uh, st=state: e.tensor_copy(
                st.ap[:, s * FDI:(s + 1) * FDI],
                uh.ap.rearrange("q (b t) -> q b t", t=TC)[:, :, TC - 1:TC].squeeze(2)),
                ins=[uh], outs=[state])
            if s == 0:
                p.op("vector", lambda e, uh=uh, bC=bC, y=y: e.tensor_tensor(
                    y.ap.rearrange("q (b t) -> q b t", t=TC),
                    uh.ap.rearrange("q (b t) -> q b t", t=TC),
                    bC.ap[:, 0:TC].unsqueeze(1).broadcast_to((128, FDI, TC)),
                    ALU.mult), ins=[uh, bC], outs=[y])
            else:
                p.op("vector", lambda e, uh=uh, bC=bC, dA=dA: e.tensor_tensor(
                    dA.ap.rearrange("q (b t) -> q b t", t=TC),
                    uh.ap.rearrange("q (b t) -> q b t", t=TC),
                    bC.ap[:, 0:TC].unsqueeze(1).broadcast_to((128, FDI, TC)),
                    ALU.mult), ins=[uh, bC], outs=[dA])
                p.op("vector", lambda e, y=y, dA=dA: e.tensor_tensor(
                    y.ap, y.ap, dA.ap, ALU.add), ins=[dA, y], outs=[y])
        store_fm(p, dr["y_s"], y, FDI, c, name="y_s")
        if "y" in dbg:
            store_fm(p, dr["dbg_y"], y, FDI, c)
        for b in (dt_c, dtx, dA, uh, y):
            ar.freebuf(b)
    for b in (sel, state):
        ar.freebuf(b)


def routing(p, ar, pbk, dr, ctx, xn_full):
    iota16 = ctx["iota16"]
    sel = ctx["sel"]
    gw = ar.alloc(FD * NE, "p5_gw")
    p.dma("sync_q", lambda e: e.dma_start(
        out=gw.ap.rearrange("q (b s) -> q b s", s=NE),
        in_=dr["gate_w_T"].rearrange("(b q) s -> q b s", q=128)), outs=[gw])
    sc_fm = ar.alloc(T, "p5_scfm")
    for c in range(NCHUNK):
        bank = pbk[0]
        for k in range(FD):
            p.op("tensor", lambda e, k=k, c=c, bank=bank: e.matmul(
                bank.ap[0:16, 0:TC], gw.ap[:, k * NE:(k + 1) * NE],
                xn_full.ap[:, k * T + c * TC:k * T + (c + 1) * TC],
                start=(k == 0), stop=(k == FD - 1)),
                ins=[gw, xn_full], outs=[bank])
        p.op("scalar", lambda e, c=c, bank=bank: e.activation(
            sc_fm.ap[0:16, c * TC:(c + 1) * TC], bank.ap[0:16, 0:TC],
            AF.Sigmoid), ins=[bank], outs=[sc_fm])
    sc_tok = ar.alloc(16 * NE, "p5_sctok")
    rt_tok = ar.alloc(16 * NE, "p5_rttok")
    mx8 = ar.alloc(8, "p5_mx8")
    idx = ar.alloc(8, "p5_idx")
    w12 = ar.alloc(6, "p5_w12")
    oh = ar.alloc(NE, "p5_oh")
    rT = ar.alloc(T, "p5_rT")
    for tt in range(16):
        bank = pbk[1]
        p.op("tensor", lambda e, tt=tt, bank=bank: e.transpose(
            bank.ap[:, 0:16], sc_fm.ap[0:16, tt * 128:(tt + 1) * 128],
            ctx["ident"].ap[0:16, 0:16]), ins=[sc_fm, ctx["ident"]], outs=[bank])
        p.op("vector", lambda e, tt=tt, bank=bank: e.tensor_copy(
            sc_tok.ap[:, tt * NE:(tt + 1) * NE], bank.ap[:, 0:16]),
            ins=[bank], outs=[sc_tok])
        st = sc_tok.ap[:, tt * NE:(tt + 1) * NE]
        p.op("vector", lambda e, st=st: e.max(mx8.ap, st),
             ins=[sc_tok], outs=[mx8], reducy=True)
        p.op("vector", lambda e, st=st: e.max_index(
            idx.ap.bitcast(U32), mx8.ap, st),
            ins=[sc_tok, mx8], outs=[idx], reducy=True)
        p.op("vector", lambda e: e.tensor_copy(
            w12.ap[:, 0:2], idx.ap[:, 0:2].bitcast(U32)),
            ins=[idx], outs=[w12])
        p.op("vector", lambda e: e.tensor_tensor(
            w12.ap[:, 2:3], mx8.ap[:, 0:1], mx8.ap[:, 1:2], ALU.add),
            ins=[mx8], outs=[w12])
        p.op("vector", lambda e: e.tensor_scalar(
            w12.ap[:, 2:3], w12.ap[:, 2:3], 1e-20, None, ALU.add),
            ins=[w12], outs=[w12])
        p.op("vector", lambda e: e.reciprocal(w12.ap[:, 3:4], w12.ap[:, 2:3]),
             ins=[w12], outs=[w12])
        p.op("vector", lambda e: e.tensor_scalar(
            w12.ap[:, 4:5], mx8.ap[:, 0:1], w12.ap[:, 3:4], 2.5,
            ALU.mult, ALU.mult), ins=[mx8, w12], outs=[w12])
        p.op("vector", lambda e: e.tensor_scalar(
            w12.ap[:, 5:6], mx8.ap[:, 1:2], w12.ap[:, 3:4], 2.5,
            ALU.mult, ALU.mult), ins=[mx8, w12], outs=[w12])
        rts = rt_tok.ap[:, tt * NE:(tt + 1) * NE]
        p.op("vector", lambda e: e.tensor_scalar(
            oh.ap, iota16.ap, w12.ap[:, 0:1], None, ALU.is_equal),
            ins=[iota16, w12], outs=[oh])
        p.op("vector", lambda e, rts=rts: e.tensor_scalar(
            rts, oh.ap, w12.ap[:, 4:5], None, ALU.mult),
            ins=[oh, w12], outs=[rt_tok])
        p.op("vector", lambda e: e.tensor_scalar(
            oh.ap, iota16.ap, w12.ap[:, 1:2], None, ALU.is_equal),
            ins=[iota16, w12], outs=[oh])
        p.op("vector", lambda e, rts=rts: e.scalar_tensor_tensor(
            rts, oh.ap, w12.ap[:, 5:6], rts, ALU.mult, ALU.add),
            ins=[oh, w12, rt_tok], outs=[rt_tok])
        bank2 = pbk[2]
        p.op("tensor", lambda e, rts=rts, bank2=bank2: e.transpose(
            bank2.ap[0:16, 0:128], rts, ctx["ident"].ap),
            ins=[rt_tok, ctx["ident"]], outs=[bank2])
        p.op("vector", lambda e, tt=tt, bank2=bank2: e.tensor_copy(
            rT.ap[0:16, tt * 128:(tt + 1) * 128], bank2.ap[0:16, 0:128]),
            ins=[bank2], outs=[rT])
    for b in (gw, sc_fm, sc_tok, rt_tok, mx8, idx, w12, oh):
        ar.freebuf(b)
    return rT


def pass5(p, ar, pbk, dr, dbg, ctx):
    sel = ctx["sel"] = ar.alloc(32 * 128, "p5_sel")
    p.dma("sync_q", lambda e: e.dma_start(out=sel.ap[0:32, :], in_=dr["sel32"]),
          outs=[sel])
    xn_full = ar.alloc(FD * T, "p5_xn")
    p.dma("sync_q", lambda e: e.dma_start(
        out=xn_full.ap.rearrange("q (b t) -> q b t", t=T),
        in_=dr["xn_s"].rearrange("(b q) t -> q b t", q=128)),
        outs=[xn_full], extra_deps=[p.dbuf("xn_s", c).wr for c in range(NCHUNK)
                                    if p.dbuf("xn_s", c).wr])
    rT = routing(p, ar, pbk, dr, ctx, xn_full)
    wgb = ar.alloc(FD * M, "p5_wg")
    wub = ar.alloc(FD * M, "p5_wu")
    wdb = ar.alloc(4 * D, "p5_wd")
    gs = ar.alloc(4 * TC, "p5_gs")
    ub = ar.alloc(4 * TC, "p5_ub")
    eo = ar.alloc(FD * TC, "p5_eo")
    for e_i in range(NE + 1):
        shared = e_i == NE
        wg_src = dr["ws_gate"] if shared else dr["w_gate"][e_i]
        wu_src = dr["ws_up"] if shared else dr["w_up"][e_i]
        wd_src = dr["ws_down"] if shared else dr["w_down"][e_i]
        p.dma("sync_q", lambda e, s=wg_src: e.dma_start(
            out=wgb.ap.rearrange("q (b m) -> q b m", m=M),
            in_=s.rearrange("(b q) m -> q b m", q=128)), outs=[wgb])
        p.dma("sync_q", lambda e, s=wu_src: e.dma_start(
            out=wub.ap.rearrange("q (b m) -> q b m", m=M),
            in_=s.rearrange("(b q) m -> q b m", q=128)), outs=[wub])
        p.dma("sync_q", lambda e, s=wd_src: e.dma_start(
            out=wdb.ap.rearrange("q (b m) -> q b m", m=D),
            in_=s.rearrange("(b q) m -> q b m", q=128)), outs=[wdb])
        for c in range(NCHUNK):
            for mg in range(4):
                bank = pbk[mg % 2]
                for k in range(FD):
                    p.op("tensor", lambda e, k=k, mg=mg, c=c, bank=bank: e.matmul(
                        bank.ap[:, 0:TC],
                        wgb.ap[:, k * M + mg * 128:k * M + (mg + 1) * 128],
                        xn_full.ap[:, k * T + c * TC:k * T + (c + 1) * TC],
                        start=(k == 0), stop=(k == FD - 1)),
                        ins=[wgb, xn_full], outs=[bank])
                p.op("scalar", lambda e, mg=mg, bank=bank: e.activation(
                    gs.ap[:, mg * TC:(mg + 1) * TC], bank.ap[:, 0:TC], AF.Silu),
                    ins=[bank], outs=[gs])
            for mg in range(4):
                bank = pbk[mg % 2]
                for k in range(FD):
                    p.op("tensor", lambda e, k=k, mg=mg, c=c, bank=bank: e.matmul(
                        bank.ap[:, 0:TC],
                        wub.ap[:, k * M + mg * 128:k * M + (mg + 1) * 128],
                        xn_full.ap[:, k * T + c * TC:k * T + (c + 1) * TC],
                        start=(k == 0), stop=(k == FD - 1)),
                        ins=[wub, xn_full], outs=[bank])
                p.op("scalar", lambda e, mg=mg, bank=bank: e.copy(
                    ub.ap[:, mg * TC:(mg + 1) * TC], bank.ap[:, 0:TC]),
                    ins=[bank], outs=[ub])
            p.op("vector", lambda e: e.tensor_tensor(
                gs.ap, gs.ap, ub.ap, ALU.mult), ins=[gs, ub], outs=[gs])
            if not shared:
                bank6 = pbk[6]
                p.op("tensor", lambda e, e_i=e_i, c=c, bank6=bank6: e.matmul(
                    bank6.ap[:, 0:TC], sel.ap[0:16, e_i * 128:(e_i + 1) * 128],
                    rT.ap[0:16, c * TC:(c + 1) * TC], start=True, stop=True),
                    ins=[sel, rT], outs=[bank6])
                p.op("vector", lambda e, bank6=bank6: e.tensor_tensor(
                    gs.ap.rearrange("q (b t) -> q b t", t=TC),
                    gs.ap.rearrange("q (b t) -> q b t", t=TC),
                    bank6.ap[:, 0:TC].unsqueeze(1).broadcast_to((128, 4, TC)),
                    ALU.mult), ins=[gs, bank6], outs=[gs])
            for mo in range(FD):
                bank = pbk[2 + mo % 2]
                for k in range(4):
                    p.op("tensor", lambda e, k=k, mo=mo, bank=bank: e.matmul(
                        bank.ap[:, 0:TC],
                        wdb.ap[:, k * D + mo * 128:k * D + (mo + 1) * 128],
                        gs.ap[:, k * TC:(k + 1) * TC],
                        start=(k == 0), stop=(k == 3)),
                        ins=[wdb, gs], outs=[bank])
                p.op("scalar", lambda e, mo=mo, bank=bank: e.copy(
                    eo.ap[:, mo * TC:(mo + 1) * TC], bank.ap[:, 0:TC]),
                    ins=[bank], outs=[eo])
            d3 = dr["hacc_s"].rearrange("(b q) t -> q b t", q=128)[
                :, :, c * TC:(c + 1) * TC]
            s3 = eo.ap.rearrange("q (b t) -> q b t", t=TC)
            if e_i == 0:
                p.dma("gpsimd_q", lambda e, d3=d3, s3=s3: e.dma_start(
                    out=d3, in_=s3), ins=[eo], outs=[p.dbuf("hacc_s", c)])
            else:
                p.dma("gpsimd_q", lambda e, d3=d3, s3=s3: e.dma_start(
                    out=d3, in_=s3, accum_op=ALU.add),
                    ins=[eo], outs=[p.dbuf("hacc_s", c)])
    if "hacc" in dbg:
        hb = ar.alloc(FD * TC, "p5_hdbg")
        for c in range(NCHUNK):
            load_fm(p, dr["hacc_s"], hb, FD, c, name="hacc_s")
            store_fm(p, dr["dbg_hacc"], hb, FD, c)
        ar.freebuf(hb)
    for b in (wgb, wub, wdb, gs, ub, eo, xn_full, rT, sel):
        ar.freebuf(b)


def pass6a(p, ar, pbk, dr, dbg, ctx):
    tabs = ctx["tabs"]
    constk = ctx["constk"]
    ones_row = ctx["ones_row"]
    wz = [ar.alloc(DIN, f"p6_wz_{k}") for k in range(FD)]
    for k in range(FD):
        p.dma("sync_q", lambda e, k=k, b=wz[k]: e.dma_start(
            out=b.ap, in_=dr["w_in_z_T"][k * 128:(k + 1) * 128, :]), outs=[wz[k]])
    ones_col = ar.alloc(1, "p6_onec")
    p.op("vector", lambda e: e.memset(ones_col.ap, 1.0), outs=[ones_col])
    stat = ar.alloc(6 * TC, "p6_stat")
    sq = ar.alloc(TC, "p6_sq")
    for c in range(NCHUNK):
        xn_c = ar.alloc(FD * TC, f"p6_xn_{c}")
        load_fm(p, dr["xn_s"], xn_c, FD, c, name="xn_s")
        zs = ar.alloc(FDI * TC, f"p6_zs_{c}")
        for mi in range(FDI):
            bank = pbk[mi % 2]
            for k in range(FD):
                p.op("tensor", lambda e, k=k, mi=mi, bank=bank, xn_c=xn_c: e.matmul(
                    bank.ap[:, 0:TC], wz[k].ap[:, mi * 128:(mi + 1) * 128],
                    xn_c.ap[:, k * TC:(k + 1) * TC],
                    start=(k == 0), stop=(k == FD - 1)),
                    ins=[wz[k], xn_c], outs=[bank])
            p.op("scalar", lambda e, mi=mi, bank=bank: e.activation(
                zs.ap[:, mi * TC:(mi + 1) * TC], bank.ap[:, 0:TC], AF.Silu),
                ins=[bank], outs=[zs])
        ar.freebuf(xn_c)
        y = ar.alloc(FDI * TC, f"p6_y_{c}")
        xc_c = ar.alloc(FDI * TC, f"p6_xc_{c}")
        load_fm(p, dr["y_s"], y, FDI, c, name="y_s")
        load_fm(p, dr["xc_s"], xc_c, FDI, c, name="xc_s")
        bA, bB2 = pbk[2], pbk[3]
        for k in range(FDI):
            p.op("scalar", lambda e, k=k, y=y: e.activation(
                sq.ap, y.ap[:, k * TC:(k + 1) * TC], AF.Square),
                ins=[y], outs=[sq])
            p.op("tensor", lambda e, k=k, y=y, bA=bA: e.matmul(
                bA.ap[0:1, 0:TC], ones_col.ap, y.ap[:, k * TC:(k + 1) * TC],
                start=(k == 0), stop=(k == FDI - 1)),
                ins=[ones_col, y], outs=[bA])
            p.op("tensor", lambda e, k=k, bB2=bB2: e.matmul(
                bB2.ap[0:1, 0:TC], ones_col.ap, sq.ap,
                start=(k == 0), stop=(k == FDI - 1)),
                ins=[ones_col, sq], outs=[bB2])
        r0 = stat.ap[0:1, :]
        p.op("vector", lambda e, bA=bA: e.tensor_scalar(
            r0[:, 0:TC], bA.ap[0:1, 0:TC], 1.0 / DIN, None, ALU.mult),
            ins=[bA], outs=[stat])
        p.op("vector", lambda e, bB2=bB2: e.tensor_scalar(
            r0[:, TC:2 * TC], bB2.ap[0:1, 0:TC], 1.0 / DIN, None, ALU.mult),
            ins=[bB2], outs=[stat])
        p.op("vector", lambda e: e.tensor_tensor(
            r0[:, 2 * TC:3 * TC], r0[:, 0:TC], r0[:, 0:TC], ALU.mult),
            ins=[stat], outs=[stat])
        p.op("vector", lambda e: e.tensor_tensor(
            r0[:, 2 * TC:3 * TC], r0[:, TC:2 * TC], r0[:, 2 * TC:3 * TC],
            ALU.subtract), ins=[stat], outs=[stat])
        p.op("scalar", lambda e: e.activation(
            r0[:, 3 * TC:4 * TC], r0[:, 2 * TC:3 * TC], AF.Sqrt,
            bias=constk.ap[0:1, 0:1]), ins=[stat, constk], outs=[stat])
        p.op("vector", lambda e: e.reciprocal(
            r0[:, 4 * TC:5 * TC], r0[:, 3 * TC:4 * TC]), ins=[stat], outs=[stat])
        bMu, bRs = pbk[4], pbk[5]
        p.op("tensor", lambda e, bMu=bMu: e.matmul(
            bMu.ap[:, 0:TC], ones_row.ap[0:1, 0:128], r0[:, 0:TC],
            start=True, stop=True), ins=[ones_row, stat], outs=[bMu])
        p.op("tensor", lambda e, bRs=bRs: e.matmul(
            bRs.ap[:, 0:TC], ones_row.ap[0:1, 0:128], r0[:, 4 * TC:5 * TC],
            start=True, stop=True), ins=[ones_row, stat], outs=[bRs])
        for k in range(FDI):
            yk = y.ap[:, k * TC:(k + 1) * TC]
            p.op("vector", lambda e, yk=yk, bMu=bMu: e.tensor_tensor(
                yk, yk, bMu.ap[:, 0:TC], ALU.subtract), ins=[y, bMu], outs=[y])
            p.op("vector", lambda e, yk=yk, bRs=bRs: e.tensor_tensor(
                yk, yk, bRs.ap[:, 0:TC], ALU.mult), ins=[y, bRs], outs=[y])
            p.op("vector", lambda e, yk=yk, k=k: e.tensor_scalar(
                yk, yk, tabs["ln_g_pc"].ap[:, k:k + 1],
                tabs["ln_b_pc"].ap[:, k:k + 1], ALU.mult, ALU.add),
                ins=[y, tabs["ln_g_pc"], tabs["ln_b_pc"]], outs=[y])
            p.op("vector", lambda e, yk=yk, k=k, xc_c=xc_c: e.scalar_tensor_tensor(
                yk, xc_c.ap[:, k * TC:(k + 1) * TC],
                tabs["d_param_pc"].ap[:, k:k + 1], yk, ALU.mult, ALU.add),
                ins=[xc_c, tabs["d_param_pc"], y], outs=[y])
            p.op("vector", lambda e, yk=yk, k=k, zs=zs: e.tensor_tensor(
                yk, yk, zs.ap[:, k * TC:(k + 1) * TC], ALU.mult),
                ins=[y, zs], outs=[y])
        store_fm(p, dr["yg_s"], y, FDI, c, name="yg_s")
        if "yg" in dbg:
            store_fm(p, dr["dbg_yg"], y, FDI, c)
        for b in (zs, y, xc_c):
            ar.freebuf(b)
    for b in wz + [ones_col, stat, sq]:
        ar.freebuf(b)


def pass6b(p, ar, pbk, dr, dbg, ctx):
    ident = ctx["ident"]
    wo = [ar.alloc(D, f"p7_wo_{k}") for k in range(FDI)]
    for k in range(FDI):
        p.dma("sync_q", lambda e, k=k, b=wo[k]: e.dma_start(
            out=b.ap, in_=dr["w_out_T"][k * 128:(k + 1) * 128, :]), outs=[wo[k]])
    for c in range(NCHUNK):
        yg = ar.alloc(FDI * TC, f"p7_yg_{c}")
        load_fm(p, dr["yg_s"], yg, FDI, c, name="yg_s")
        gfm = ar.alloc(FD * TC, f"p7_g_{c}")
        load_fm(p, dr["hacc_s"], gfm, FD, c, name="hacc_s")
        s_tok = ar.alloc(D, f"p7_s_{c}")
        g_tok = ar.alloc(D, f"p7_gt_{c}")
        x_tok = ar.alloc(D, f"p7_x_{c}")
        b1 = ar.alloc(D, f"p7_b1_{c}")
        b2 = ar.alloc(D, f"p7_b2_{c}")
        moedep = ([ctx["moe_sc"][-1]] if "moe_sc" in ctx else [])
        for tt in range(4):
            t0 = c * TC + tt * 128
            p.dma("sync_q", lambda e, t0=t0, x_tok=x_tok: e.dma_start(
                out=x_tok.ap, in_=dr["x"][t0:t0 + 128, :]), outs=[x_tok])
            p.dma("sync_q", lambda e, t0=t0, b1=b1: e.dma_start(
                out=b1.ap, in_=dr["moebuf"][t0:t0 + 128, :]), outs=[b1],
                extra_deps=moedep)
            p.dma("sync_q", lambda e, t0=t0, b2=b2: e.dma_start(
                out=b2.ap, in_=dr["moebuf"][T + t0:T + t0 + 128, :]), outs=[b2],
                extra_deps=moedep)
            bL, bH = pbk[0], pbk[1]
            for k in range(FDI):
                p.op("tensor", lambda e, k=k, tt=tt, yg=yg, bL=bL: e.matmul(
                    bL.ap[:, 0:TC],
                    yg.ap[:, k * TC + tt * 128:k * TC + (tt + 1) * 128],
                    wo[k].ap[:, 0:512],
                    start=(k == 0), stop=(k == FDI - 1)),
                    ins=[yg, wo[k]], outs=[bL])
                p.op("tensor", lambda e, k=k, tt=tt, yg=yg, bH=bH: e.matmul(
                    bH.ap[:, 0:TC],
                    yg.ap[:, k * TC + tt * 128:k * TC + (tt + 1) * 128],
                    wo[k].ap[:, 512:1024],
                    start=(k == 0), stop=(k == FDI - 1)),
                    ins=[yg, wo[k]], outs=[bH])
            p.op("vector", lambda e, bL=bL, s_tok=s_tok: e.tensor_copy(
                s_tok.ap[:, 0:512], bL.ap[:, 0:TC]), ins=[bL], outs=[s_tok])
            p.op("vector", lambda e, bH=bH, s_tok=s_tok: e.tensor_copy(
                s_tok.ap[:, 512:1024], bH.ap[:, 0:TC]), ins=[bH], outs=[s_tok])
            bT0, bT1 = pbk[2], pbk[3]
            for f in range(FD):
                bk = bT0 if f < 4 else bT1
                p.op("tensor", lambda e, f=f, tt=tt, gfm=gfm, bk=bk: e.transpose(
                    bk.ap[:, (f % 4) * 128:(f % 4 + 1) * 128],
                    gfm.ap[:, f * TC + tt * 128:f * TC + (tt + 1) * 128],
                    ident.ap), ins=[gfm, ident], outs=[bk])
            p.op("vector", lambda e, bT0=bT0, g_tok=g_tok: e.tensor_copy(
                g_tok.ap[:, 0:512], bT0.ap[:, 0:512]), ins=[bT0], outs=[g_tok])
            p.op("vector", lambda e, bT1=bT1, g_tok=g_tok: e.tensor_copy(
                g_tok.ap[:, 512:1024], bT1.ap[:, 0:512]), ins=[bT1], outs=[g_tok])
            p.op("vector", lambda e, g_tok=g_tok, b1=b1: e.tensor_tensor(
                g_tok.ap, g_tok.ap, b1.ap, ALU.add), ins=[g_tok, b1], outs=[g_tok])
            p.op("vector", lambda e, g_tok=g_tok, b2=b2: e.tensor_tensor(
                g_tok.ap, g_tok.ap, b2.ap, ALU.add), ins=[g_tok, b2], outs=[g_tok])
            p.op("scalar", lambda e, g_tok=g_tok: e.activation(
                g_tok.ap, g_tok.ap, AF.Silu), ins=[g_tok], outs=[g_tok])
            p.op("vector", lambda e, s_tok=s_tok, g_tok=g_tok: e.tensor_tensor(
                s_tok.ap, s_tok.ap, g_tok.ap, ALU.mult),
                ins=[s_tok, g_tok], outs=[s_tok])
            p.op("vector", lambda e, s_tok=s_tok, x_tok=x_tok: e.tensor_tensor(
                s_tok.ap, s_tok.ap, x_tok.ap, ALU.add),
                ins=[s_tok, x_tok], outs=[s_tok])
            p.dma("gpsimd_q", lambda e, t0=t0, s_tok=s_tok: e.dma_start(
                out=dr["out"][t0:t0 + 128, :], in_=s_tok.ap),
                ins=[s_tok], outs=[p.dbuf("out")])
        for b in (yg, gfm, s_tok, g_tok, x_tok, b1, b2):
            ar.freebuf(b)
    for b in wo:
        ar.freebuf(b)


def routing_sparse(p, ar, pbk, dr, ctx, xn_full):
    iota16 = ctx["iota16"]
    ident = ctx["ident"]
    gw = ar.alloc(FD * NE, "p5_gw")
    p.dma("sync_q", lambda e: e.dma_start(
        out=gw.ap.rearrange("q (b s) -> q b s", s=NE),
        in_=dr["gate_w_T"].rearrange("(b q) s -> q b s", q=128)), outs=[gw])
    sc_fm = ar.alloc(T, "p5_scfm")
    for c in range(NCHUNK):
        bank = pbk[0]
        for k in range(FD):
            p.op("tensor", lambda e, k=k, c=c, bank=bank: e.matmul(
                bank.ap[0:16, 0:TC], gw.ap[:, k * NE:(k + 1) * NE],
                xn_full.ap[:, k * T + c * TC:k * T + (c + 1) * TC],
                start=(k == 0), stop=(k == FD - 1)),
                ins=[gw, xn_full], outs=[bank])
        p.op("scalar", lambda e, c=c, bank=bank: e.activation(
            sc_fm.ap[0:16, c * TC:(c + 1) * TC], bank.ap[0:16, 0:TC],
            AF.Sigmoid), ins=[bank], outs=[sc_fm])
    tok0 = ar.alloc(NE, "p5_tok0")
    tok1 = ar.alloc(NE, "p5_tok1")
    p.op("gpsimd", lambda e: e.iota(tok0.ap.bitcast(U32), [[128, 16]],
                                    base=0, channel_multiplier=1), outs=[tok0])
    p.op("gpsimd", lambda e: e.iota(tok1.ap.bitcast(U32), [[128, 16]],
                                    base=T, channel_multiplier=1), outs=[tok1])
    NG = NE * CAP // 128
    zf = ar.alloc(NG, "p5_zf")
    p.op("vector", lambda e: e.memset(zf.ap, 0.0), outs=[zf])
    dumpi = ar.alloc(NG, "p5_dump")
    p.op("gpsimd", lambda e: e.iota(dumpi.ap.bitcast(U32), [[1, NG]],
                                    base=2 * T, channel_multiplier=NG), outs=[dumpi])
    for nm, buf in (("glist", zf), ("dlist", dumpi), ("wlist", zf)):
        s2 = (buf.ap[:, 0:NG].bitcast(U32) if nm != "wlist"
              else buf.ap[:, 0:NG])
        p.dma("gpsimd_q", lambda e, nm=nm, s2=s2: e.dma_start(
            out=dr[nm], in_=s2), ins=[buf], outs=[p.dbuf(nm)])
    sc_tok = ar.alloc(NE, "p5_sctok")
    oh1 = ar.alloc(16 * NE, "p5_oh1")
    oh2 = ar.alloc(16 * NE, "p5_oh2")
    idxf = ar.alloc(32, "p5_idxf")
    wp = ar.alloc(32, "p5_wp")
    msk_T = ar.alloc(T, "p5_mskT")
    mx8 = ar.alloc(8, "p5_mx8")
    idx = ar.alloc(8, "p5_idx")
    sm = ar.alloc(4, "p5_sm")
    t16 = ar.alloc(NE, "p5_t16")
    for tt in range(16):
        bank = pbk[4]
        p.op("tensor", lambda e, tt=tt, bank=bank: e.transpose(
            bank.ap[:, 0:16], sc_fm.ap[0:16, tt * 128:(tt + 1) * 128],
            ident.ap[0:16, 0:16]), ins=[sc_fm, ident], outs=[bank])
        p.op("vector", lambda e, bank=bank: e.tensor_copy(
            sc_tok.ap, bank.ap[:, 0:16]), ins=[bank], outs=[sc_tok])
        p.op("vector", lambda e: e.max(mx8.ap, sc_tok.ap),
             ins=[sc_tok], outs=[mx8], reducy=True)
        p.op("vector", lambda e: e.max_index(
            idx.ap.bitcast(U32), mx8.ap, sc_tok.ap),
            ins=[sc_tok, mx8], outs=[idx], reducy=True)
        p.op("vector", lambda e, tt=tt: e.tensor_copy(
            idxf.ap[:, 2 * tt:2 * tt + 2], idx.ap[:, 0:2].bitcast(U32)),
            ins=[idx], outs=[idxf])
        p.op("vector", lambda e: e.tensor_tensor(
            sm.ap[:, 0:1], mx8.ap[:, 0:1], mx8.ap[:, 1:2], ALU.add),
            ins=[mx8], outs=[sm])
        p.op("vector", lambda e: e.tensor_scalar(
            sm.ap[:, 0:1], sm.ap[:, 0:1], 1e-20, None, ALU.add),
            ins=[sm], outs=[sm])
        p.op("vector", lambda e: e.reciprocal(sm.ap[:, 1:2], sm.ap[:, 0:1]),
             ins=[sm], outs=[sm])
        p.op("vector", lambda e, tt=tt: e.tensor_scalar(
            wp.ap[:, 2 * tt:2 * tt + 2], mx8.ap[:, 0:2], sm.ap[:, 1:2], 2.5,
            ALU.mult, ALU.mult), ins=[mx8, sm], outs=[wp])
        o1 = oh1.ap[:, tt * NE:(tt + 1) * NE]
        o2 = oh2.ap[:, tt * NE:(tt + 1) * NE]
        p.op("vector", lambda e, o1=o1, tt=tt: e.tensor_scalar(
            o1, iota16.ap, idxf.ap[:, 2 * tt:2 * tt + 1], None, ALU.is_equal),
            ins=[iota16, idxf], outs=[oh1])
        p.op("vector", lambda e, o2=o2, tt=tt: e.tensor_scalar(
            o2, iota16.ap, idxf.ap[:, 2 * tt + 1:2 * tt + 2], None,
            ALU.is_equal), ins=[iota16, idxf], outs=[oh2])
        p.op("vector", lambda e, o1=o1, o2=o2: e.tensor_tensor(
            t16.ap, o1, o2, ALU.add), ins=[oh1, oh2], outs=[t16])
        yield
        bank3 = pbk[5]
        p.op("tensor", lambda e, bank3=bank3: e.transpose(
            bank3.ap[0:16, 0:128], t16.ap, ident.ap),
            ins=[t16, ident], outs=[bank3])
        p.op("vector", lambda e, tt=tt, bank3=bank3: e.tensor_copy(
            msk_T.ap[0:16, tt * 128:(tt + 1) * 128], bank3.ap[0:16, 0:128]),
            ins=[bank3], outs=[msk_T])
        yield
    zrow = ar.alloc(T, "p5_zrow")
    p.op("vector", lambda e: e.memset(zrow.ap[0:16, :], 0.0), outs=[zrow])
    pos_T = ar.alloc(T, "p5_posT")
    p.op("vector", lambda e: e.tensor_tensor_scan(
        pos_T.ap[0:16, :], msk_T.ap[0:16, :], zrow.ap[0:16, :], 0.0,
        ALU.add, ALU.add), ins=[msk_T, zrow], outs=[pos_T])
    pos_tok = ar.alloc(NE, "p5_postok")
    cols = ar.alloc(8, "p5_cols")
    du = ar.alloc(2, "p5_du")
    sc_toks = ctx.setdefault("sc_toks", [])
    for tt in range(16):
        bank = pbk[4]
        p.op("tensor", lambda e, tt=tt, bank=bank: e.transpose(
            bank.ap[:, 0:16], pos_T.ap[0:16, tt * 128:(tt + 1) * 128],
            ident.ap[0:16, 0:16]), ins=[pos_T, ident], outs=[bank])
        p.op("vector", lambda e, bank=bank: e.tensor_copy(
            pos_tok.ap, bank.ap[:, 0:16]), ins=[bank], outs=[pos_tok])
        for r, ohb, tokb in ((0, oh1, tok0), (1, oh2, tok1)):
            o = ohb.ap[:, tt * NE:(tt + 1) * NE]
            p.op("vector", lambda e, o=o: e.tensor_tensor(
                t16.ap, o, pos_tok.ap, ALU.mult),
                ins=[ohb, pos_tok], outs=[t16])
            p.op("vector", lambda e: e.tensor_reduce(
                cols.ap[:, 0:1], t16.ap, mybir.AxisListType.X, ALU.add),
                ins=[t16], outs=[cols], reducy=True)
            p.op("vector", lambda e, tt=tt, r=r: e.tensor_scalar(
                cols.ap[:, 1:2], idxf.ap[:, 2 * tt + r:2 * tt + r + 1],
                float(CAP), -1.0, ALU.mult, ALU.add),
                ins=[idxf], outs=[cols])
            p.op("vector", lambda e: e.tensor_tensor(
                cols.ap[:, 1:2], cols.ap[:, 1:2], cols.ap[:, 0:1], ALU.add),
                ins=[cols], outs=[cols])
            p.op("vector", lambda e: e.tensor_copy(
                du.ap[:, 0:1].bitcast(U32), cols.ap[:, 1:2]),
                ins=[cols], outs=[du])
            ioff = IndirectOffsetOnAxis(ap=du.ap[:, 0:1].bitcast(U32), axis=0)
            tk_g = tok0.ap[:, tt:tt + 1].bitcast(U32)
            tk_d = tokb.ap[:, tt:tt + 1].bitcast(U32)
            wsl = wp.ap[:, 2 * tt + r:2 * tt + r + 1]
            sc_toks.append(p.dma("gpsimd_q", lambda e, tk_g=tk_g, ioff=ioff:
                                 e.indirect_dma_start(out=dr["glist"],
                                                      out_offset=ioff,
                                                      in_=tk_g, in_offset=None),
                                 ins=[tok0, du],
                                 extra_deps=[p.dbuf("glist").wr]))
            sc_toks.append(p.dma("gpsimd_q", lambda e, tk_d=tk_d, ioff=ioff:
                                 e.indirect_dma_start(out=dr["dlist"],
                                                      out_offset=ioff,
                                                      in_=tk_d, in_offset=None),
                                 ins=[tokb, du],
                                 extra_deps=[p.dbuf("dlist").wr]))
            sc_toks.append(p.dma("gpsimd_q", lambda e, wsl=wsl, ioff=ioff:
                                 e.indirect_dma_start(out=dr["wlist"],
                                                      out_offset=ioff,
                                                      in_=wsl, in_offset=None),
                                 ins=[wp, du],
                                 extra_deps=[p.dbuf("wlist").wr]))
    for b in (gw, sc_fm, sc_tok, oh1, oh2, idxf, wp, msk_T, mx8, idx, sm,
              t16, zrow, pos_T, pos_tok, cols, du, tok0, tok1, zf, dumpi):
        ar.freebuf(b)
    return sc_toks


def pass5s(p, ar, pbk, dr, dbg, ctx):
    ones_row = ctx["ones_row"]
    ident = ctx["ident"]
    xn_full = ar.alloc(FD * T, "p5_xn")
    p.dma("sync_q", lambda e: e.dma_start(
        out=xn_full.ap.rearrange("q (b t) -> q b t", t=T),
        in_=dr["xn_s"].rearrange("(b q) t -> q b t", q=128)),
        outs=[xn_full],
        extra_deps=[tk for tk in (p.dbuf("xn_s", c).wr for c in range(NCHUNK))
                    if tk])
    sc_toks = routing_sparse(p, ar, pbk, dr, ctx, xn_full)
    wgb = ar.alloc(FD * M, "p5_wg")
    wub = ar.alloc(FD * M, "p5_wu")
    wdb = ar.alloc(4 * D, "p5_wd")
    gs = ar.alloc(4 * TC, "p5_gs")
    ub = ar.alloc(4 * TC, "p5_ub")
    eo = ar.alloc(FD * TC, "p5_eo")
    p.dma("sync_q", lambda e: e.dma_start(
        out=wgb.ap.rearrange("q (b m) -> q b m", m=M),
        in_=dr["ws_gate"].rearrange("(b q) m -> q b m", q=128)), outs=[wgb])
    p.dma("sync_q", lambda e: e.dma_start(
        out=wub.ap.rearrange("q (b m) -> q b m", m=M),
        in_=dr["ws_up"].rearrange("(b q) m -> q b m", q=128)), outs=[wub])
    p.dma("sync_q", lambda e: e.dma_start(
        out=wdb.ap.rearrange("q (b m) -> q b m", m=D),
        in_=dr["ws_down"].rearrange("(b q) m -> q b m", q=128)), outs=[wdb])
    for c in range(NCHUNK):
        for mg in range(4):
            bank = pbk[mg % 2]
            for k in range(FD):
                p.op("tensor", lambda e, k=k, mg=mg, c=c, bank=bank: e.matmul(
                    bank.ap[:, 0:TC],
                    wgb.ap[:, k * M + mg * 128:k * M + (mg + 1) * 128],
                    xn_full.ap[:, k * T + c * TC:k * T + (c + 1) * TC],
                    start=(k == 0), stop=(k == FD - 1)),
                    ins=[wgb, xn_full], outs=[bank])
            p.op("scalar", lambda e, mg=mg, bank=bank: e.activation(
                gs.ap[:, mg * TC:(mg + 1) * TC], bank.ap[:, 0:TC], AF.Silu),
                ins=[bank], outs=[gs])
        for mg in range(4):
            bank = pbk[mg % 2]
            for k in range(FD):
                p.op("tensor", lambda e, k=k, mg=mg, c=c, bank=bank: e.matmul(
                    bank.ap[:, 0:TC],
                    wub.ap[:, k * M + mg * 128:k * M + (mg + 1) * 128],
                    xn_full.ap[:, k * T + c * TC:k * T + (c + 1) * TC],
                    start=(k == 0), stop=(k == FD - 1)),
                    ins=[wub, xn_full], outs=[bank])
            p.op("scalar", lambda e, mg=mg, bank=bank: e.copy(
                ub.ap[:, mg * TC:(mg + 1) * TC], bank.ap[:, 0:TC]),
                ins=[bank], outs=[ub])
        p.op("vector", lambda e: e.tensor_tensor(
            gs.ap, gs.ap, ub.ap, ALU.mult), ins=[gs, ub], outs=[gs])
        for mo in range(FD):
            bank = pbk[2 + mo % 2]
            for k in range(4):
                p.op("tensor", lambda e, k=k, mo=mo, bank=bank: e.matmul(
                    bank.ap[:, 0:TC],
                    wdb.ap[:, k * D + mo * 128:k * D + (mo + 1) * 128],
                    gs.ap[:, k * TC:(k + 1) * TC],
                    start=(k == 0), stop=(k == 3)),
                    ins=[wdb, gs], outs=[bank])
            p.op("scalar", lambda e, mo=mo, bank=bank: e.copy(
                eo.ap[:, mo * TC:(mo + 1) * TC], bank.ap[:, 0:TC]),
                ins=[bank], outs=[eo])
        store_fm(p, dr["hacc_s"], eo, FD, c, name="hacc_s")
    ar.freebuf(xn_full)
    NP = CAP // 128
    xg_tok = [ar.alloc(D, f"p5_xg{i}") for i in range(NP)]
    xg_fm = ar.alloc(FD * CAP, "p5_xgfm")
    idxg = ar.alloc(2 * NP, "p5_idxg")
    wrow = ar.alloc(CAP, "p5_wrow")
    gs2 = ar.alloc(4 * CAP, "p5_gs2")
    ub2 = ar.alloc(4 * CAP, "p5_ub2")
    eo2 = ar.alloc(FD * CAP, "p5_eo2")
    eo_tok = [ar.alloc(D, f"p5_eot{i}") for i in range(NP)]
    gather_dep = [sc_toks[-1], p.dbuf("glist").wr, p.dbuf("dlist").wr,
                  p.dbuf("wlist").wr]
    gather_dep += [tk for tk in (p.dbuf("xn_tok_s", c).wr
                                 for c in range(NCHUNK)) if tk]
    first = True
    for e_i in range(NE):
        p.dma("sync_q", lambda e, e_i=e_i: e.dma_start(
            out=wgb.ap.rearrange("q (b m) -> q b m", m=M),
            in_=dr["w_gate"][e_i].rearrange("(b q) m -> q b m", q=128)),
            outs=[wgb])
        p.dma("sync_q", lambda e, e_i=e_i: e.dma_start(
            out=wub.ap.rearrange("q (b m) -> q b m", m=M),
            in_=dr["w_up"][e_i].rearrange("(b q) m -> q b m", q=128)),
            outs=[wub])
        p.dma("sync_q", lambda e, e_i=e_i: e.dma_start(
            out=wdb.ap.rearrange("q (b m) -> q b m", m=D),
            in_=dr["w_down"][e_i].rearrange("(b q) m -> q b m", q=128)),
            outs=[wdb])
        dep = gather_dep if first else []
        first = False
        for i in range(NP):
            r0 = e_i * CAP + i * 128
            p.dma("sync_q", lambda e, r0=r0, i=i: e.dma_start(
                out=idxg.ap[:, i:i + 1].bitcast(U32),
                in_=dr["glist"][r0:r0 + 128, :]), outs=[idxg], extra_deps=dep)
            p.dma("sync_q", lambda e, r0=r0, i=i: e.dma_start(
                out=idxg.ap[:, NP + i:NP + i + 1].bitcast(U32),
                in_=dr["dlist"][r0:r0 + 128, :]), outs=[idxg], extra_deps=dep)
        p.dma("sync_q", lambda e, e_i=e_i: e.dma_start(
            out=wrow.ap[0:1, :],
            in_=dr["wlist"][e_i * CAP:(e_i + 1) * CAP, 0:1].rearrange(
                "a one -> one a")), outs=[wrow], extra_deps=dep)
        for i in range(NP):
            ioff = IndirectOffsetOnAxis(ap=idxg.ap[:, i:i + 1].bitcast(U32),
                                        axis=0)
            p.dma("gpsimd_q", lambda e, i=i, ioff=ioff: e.indirect_dma_start(
                out=xg_tok[i].ap, out_offset=None,
                in_=dr["xn_tok_s"], in_offset=ioff),
                ins=[idxg], outs=[xg_tok[i]])
        for f in range(FD):
            bank = pbk[4 + f % 2]
            for i in range(NP):
                p.op("tensor", lambda e, f=f, i=i, bank=bank: e.transpose(
                    bank.ap[:, i * 128:(i + 1) * 128],
                    xg_tok[i].ap[:, f * 128:(f + 1) * 128], ident.ap),
                    ins=[xg_tok[i], ident], outs=[bank])
            p.op("vector", lambda e, f=f, bank=bank: e.tensor_copy(
                xg_fm.ap[:, f * CAP:(f + 1) * CAP], bank.ap[:, 0:CAP]),
                ins=[bank], outs=[xg_fm])
        for mg in range(4):
            bank = pbk[mg % 2]
            for k in range(FD):
                p.op("tensor", lambda e, k=k, mg=mg, bank=bank: e.matmul(
                    bank.ap[:, 0:CAP],
                    wgb.ap[:, k * M + mg * 128:k * M + (mg + 1) * 128],
                    xg_fm.ap[:, k * CAP:(k + 1) * CAP],
                    start=(k == 0), stop=(k == FD - 1)),
                    ins=[wgb, xg_fm], outs=[bank])
            p.op("scalar", lambda e, mg=mg, bank=bank: e.activation(
                gs2.ap[:, mg * CAP:(mg + 1) * CAP], bank.ap[:, 0:CAP], AF.Silu),
                ins=[bank], outs=[gs2])
        for mg in range(4):
            bank = pbk[mg % 2]
            for k in range(FD):
                p.op("tensor", lambda e, k=k, mg=mg, bank=bank: e.matmul(
                    bank.ap[:, 0:CAP],
                    wub.ap[:, k * M + mg * 128:k * M + (mg + 1) * 128],
                    xg_fm.ap[:, k * CAP:(k + 1) * CAP],
                    start=(k == 0), stop=(k == FD - 1)),
                    ins=[wub, xg_fm], outs=[bank])
            p.op("scalar", lambda e, mg=mg, bank=bank: e.copy(
                ub2.ap[:, mg * CAP:(mg + 1) * CAP], bank.ap[:, 0:CAP]),
                ins=[bank], outs=[ub2])
        p.op("vector", lambda e: e.tensor_tensor(
            gs2.ap, gs2.ap, ub2.ap, ALU.mult), ins=[gs2, ub2], outs=[gs2])
        bank6 = pbk[6]
        p.op("tensor", lambda e, bank6=bank6: e.matmul(
            bank6.ap[:, 0:CAP], ones_row.ap[0:1, 0:128], wrow.ap[0:1, :],
            start=True, stop=True), ins=[ones_row, wrow], outs=[bank6])
        p.op("vector", lambda e, bank6=bank6: e.tensor_tensor(
            gs2.ap.rearrange("q (b t) -> q b t", t=CAP),
            gs2.ap.rearrange("q (b t) -> q b t", t=CAP),
            bank6.ap[:, 0:CAP].unsqueeze(1).broadcast_to((128, 4, CAP)),
            ALU.mult), ins=[gs2, bank6], outs=[gs2])
        for mo in range(FD):
            bank = pbk[2 + mo % 2]
            for k in range(4):
                p.op("tensor", lambda e, k=k, mo=mo, bank=bank: e.matmul(
                    bank.ap[:, 0:CAP],
                    wdb.ap[:, k * D + mo * 128:k * D + (mo + 1) * 128],
                    gs2.ap[:, k * CAP:(k + 1) * CAP],
                    start=(k == 0), stop=(k == 3)),
                    ins=[wdb, gs2], outs=[bank])
            p.op("scalar", lambda e, mo=mo, bank=bank: e.copy(
                eo2.ap[:, mo * CAP:(mo + 1) * CAP], bank.ap[:, 0:CAP]),
                ins=[bank], outs=[eo2])
        for i in range(NP):
            bank = pbk[4 + i % 2]
            for f in range(FD):
                bk = bank if f < 4 else pbk[6]
                p.op("tensor", lambda e, f=f, i=i, bk=bk: e.transpose(
                    bk.ap[:, (f % 4) * 128:(f % 4 + 1) * 128],
                    eo2.ap[:, f * CAP + i * 128:f * CAP + (i + 1) * 128],
                    ident.ap), ins=[eo2, ident], outs=[bk])
                if f == 3:
                    p.op("vector", lambda e, i=i, bk=bk: e.tensor_copy(
                        eo_tok[i].ap[:, 0:512], bk.ap[:, 0:512]),
                        ins=[bk], outs=[eo_tok[i]])
                if f == 7:
                    p.op("vector", lambda e, i=i, bk=bk: e.tensor_copy(
                        eo_tok[i].ap[:, 512:1024], bk.ap[:, 0:512]),
                        ins=[bk], outs=[eo_tok[i]])
            soff = IndirectOffsetOnAxis(
                ap=idxg.ap[:, NP + i:NP + i + 1].bitcast(U32), axis=0)
            ctx.setdefault("moe_sc", []).append(p.dma(
                "gpsimd_q", lambda e, i=i, soff=soff: e.indirect_dma_start(
                    out=dr["moebuf"], out_offset=soff,
                    in_=eo_tok[i].ap, in_offset=None),
                ins=[eo_tok[i], idxg]))
    for b in xg_tok + eo_tok + [xg_fm, idxg, wrow, gs2, ub2, eo2,
                                wgb, wub, wdb, gs, ub, eo]:
        ar.freebuf(b)



def pass3_store_bpcp(p, ar, dr, bpcp):
    p.dma("gpsimd_q", lambda e: e.dma_start(out=dr["bpcp_s"], in_=bpcp.ap[0:32, :]),
          ins=[bpcp], outs=[p.dbuf("bpcp_s")])
    ar.freebuf(bpcp)
    p.qfence("gpsimd_q")


def routing_sparse_v3(p, ar, pbk, dr, ctx):
    xn_c = ar.alloc(FD * TC, "p5r_xn")
    gw = ar.alloc(FD * NE, "p5_gw")
    p.dma("sync_q", lambda e: e.dma_start(
        out=gw.ap.rearrange("q (b s) -> q b s", s=NE),
        in_=dr["gate_w_T"].rearrange("(b q) s -> q b s", q=128)), outs=[gw])
    sc_fm = ar.alloc(T, "p5_scfm")
    for c in range(NCHUNK):
        load_fm(p, dr["xn_s"], xn_c, FD, c, name="xn_s")
        bank = pbk[3]
        for k in range(FD):
            p.op("tensor", lambda e, k=k, c=c, bank=bank, xn_c=xn_c: e.matmul(
                bank.ap[0:16, 0:TC], gw.ap[:, k * NE:(k + 1) * NE],
                xn_c.ap[:, k * TC:(k + 1) * TC],
                start=(k == 0), stop=(k == FD - 1)),
                ins=[gw, xn_c], outs=[bank])
        p.op("scalar", lambda e, c=c, bank=bank: e.activation(
            sc_fm.ap[0:16, c * TC:(c + 1) * TC], bank.ap[0:16, 0:TC],
            AF.Sigmoid), ins=[bank], outs=[sc_fm])
        yield
    ar.freebuf(xn_c)
    ar.freebuf(gw)
    yield from routing_lists(p, ar, pbk, dr, ctx, sc_fm)


def routing_lists(p, ar, pbk, dr, ctx, sc_fm):
    iota16 = ctx["iota16"]
    ident = ctx["ident"]
    tok0 = ar.alloc(NE, "p5_tok0")
    tok1 = ar.alloc(NE, "p5_tok1")
    p.op("gpsimd", lambda e: e.iota(tok0.ap.bitcast(U32), [[128, 16]],
                                    base=0, channel_multiplier=1), outs=[tok0])
    p.op("gpsimd", lambda e: e.iota(tok1.ap.bitcast(U32), [[128, 16]],
                                    base=T, channel_multiplier=1), outs=[tok1])
    NG = NE * CAP // 128
    zf = ar.alloc(NG, "p5_zf")
    p.op("vector", lambda e: e.memset(zf.ap, 0.0), outs=[zf])
    dumpi = ar.alloc(NG, "p5_dump")
    p.op("gpsimd", lambda e: e.iota(dumpi.ap.bitcast(U32), [[1, NG]],
                                    base=2 * T, channel_multiplier=NG), outs=[dumpi])
    for nm, buf in (("glist", zf), ("dlist", dumpi), ("wlist", zf)):
        s2 = (buf.ap[:, 0:NG].bitcast(U32) if nm != "wlist"
              else buf.ap[:, 0:NG])
        p.dma("gpsimd_q", lambda e, nm=nm, s2=s2: e.dma_start(
            out=dr[nm], in_=s2), ins=[buf], outs=[p.dbuf(nm)])
    sc_tok = ar.alloc(NE, "p5_sctok")
    oh1 = ar.alloc(16 * NE, "p5_oh1")
    oh2 = ar.alloc(16 * NE, "p5_oh2")
    idxf = ar.alloc(32, "p5_idxf")
    wp = ar.alloc(32, "p5_wp")
    msk_T = ar.alloc(T, "p5_mskT")
    mx8 = ar.alloc(8, "p5_mx8")
    idx = ar.alloc(8, "p5_idx")
    sm = ar.alloc(4, "p5_sm")
    t16 = ar.alloc(NE, "p5_t16")
    for tt in range(16):
        bank = pbk[4]
        p.op("tensor", lambda e, tt=tt, bank=bank: e.transpose(
            bank.ap[:, 0:16], sc_fm.ap[0:16, tt * 128:(tt + 1) * 128],
            ident.ap[0:16, 0:16]), ins=[sc_fm, ident], outs=[bank])
        p.op("vector", lambda e, bank=bank: e.tensor_copy(
            sc_tok.ap, bank.ap[:, 0:16]), ins=[bank], outs=[sc_tok])
        p.op("vector", lambda e: e.max(mx8.ap, sc_tok.ap),
             ins=[sc_tok], outs=[mx8], reducy=True)
        p.op("vector", lambda e: e.max_index(
            idx.ap.bitcast(U32), mx8.ap, sc_tok.ap),
            ins=[sc_tok, mx8], outs=[idx], reducy=True)
        p.op("vector", lambda e, tt=tt: e.tensor_copy(
            idxf.ap[:, 2 * tt:2 * tt + 2], idx.ap[:, 0:2].bitcast(U32)),
            ins=[idx], outs=[idxf])
        p.op("vector", lambda e: e.tensor_tensor(
            sm.ap[:, 0:1], mx8.ap[:, 0:1], mx8.ap[:, 1:2], ALU.add),
            ins=[mx8], outs=[sm])
        p.op("vector", lambda e: e.tensor_scalar(
            sm.ap[:, 0:1], sm.ap[:, 0:1], 1e-20, None, ALU.add),
            ins=[sm], outs=[sm])
        p.op("vector", lambda e: e.reciprocal(sm.ap[:, 1:2], sm.ap[:, 0:1]),
             ins=[sm], outs=[sm])
        p.op("vector", lambda e, tt=tt: e.tensor_scalar(
            wp.ap[:, 2 * tt:2 * tt + 2], mx8.ap[:, 0:2], sm.ap[:, 1:2], 2.5,
            ALU.mult, ALU.mult), ins=[mx8, sm], outs=[wp])
        o1 = oh1.ap[:, tt * NE:(tt + 1) * NE]
        o2 = oh2.ap[:, tt * NE:(tt + 1) * NE]
        p.op("vector", lambda e, o1=o1, tt=tt: e.tensor_scalar(
            o1, iota16.ap, idxf.ap[:, 2 * tt:2 * tt + 1], None, ALU.is_equal),
            ins=[iota16, idxf], outs=[oh1])
        p.op("vector", lambda e, o2=o2, tt=tt: e.tensor_scalar(
            o2, iota16.ap, idxf.ap[:, 2 * tt + 1:2 * tt + 2], None,
            ALU.is_equal), ins=[iota16, idxf], outs=[oh2])
        p.op("vector", lambda e, o1=o1, o2=o2: e.tensor_tensor(
            t16.ap, o1, o2, ALU.add), ins=[oh1, oh2], outs=[t16])
        yield
        bank3 = pbk[5]
        p.op("tensor", lambda e, bank3=bank3: e.transpose(
            bank3.ap[0:16, 0:128], t16.ap, ident.ap),
            ins=[t16, ident], outs=[bank3])
        p.op("vector", lambda e, tt=tt, bank3=bank3: e.tensor_copy(
            msk_T.ap[0:16, tt * 128:(tt + 1) * 128], bank3.ap[0:16, 0:128]),
            ins=[bank3], outs=[msk_T])
        yield
    zrow = ar.alloc(T, "p5_zrow")
    p.op("vector", lambda e: e.memset(zrow.ap[0:16, :], 0.0), outs=[zrow])
    pos_T = ar.alloc(T, "p5_posT")
    p.op("vector", lambda e: e.tensor_tensor_scan(
        pos_T.ap[0:16, :], msk_T.ap[0:16, :], zrow.ap[0:16, :], 0.0,
        ALU.add, ALU.add), ins=[msk_T, zrow], outs=[pos_T])
    pos_tok = ar.alloc(NE, "p5_postok")
    cols = ar.alloc(8, "p5_cols")
    du = ar.alloc(2, "p5_du")
    sc_toks = ctx.setdefault("sc_toks", [])
    for tt in range(16):
        bank = pbk[4]
        p.op("tensor", lambda e, tt=tt, bank=bank: e.transpose(
            bank.ap[:, 0:16], pos_T.ap[0:16, tt * 128:(tt + 1) * 128],
            ident.ap[0:16, 0:16]), ins=[pos_T, ident], outs=[bank])
        p.op("vector", lambda e, bank=bank: e.tensor_copy(
            pos_tok.ap, bank.ap[:, 0:16]), ins=[bank], outs=[pos_tok])
        for r, ohb, tokb in ((0, oh1, tok0), (1, oh2, tok1)):
            o = ohb.ap[:, tt * NE:(tt + 1) * NE]
            p.op("vector", lambda e, o=o: e.tensor_tensor(
                t16.ap, o, pos_tok.ap, ALU.mult),
                ins=[ohb, pos_tok], outs=[t16])
            p.op("vector", lambda e: e.tensor_reduce(
                cols.ap[:, 0:1], t16.ap, mybir.AxisListType.X, ALU.add),
                ins=[t16], outs=[cols], reducy=True)
            p.op("vector", lambda e, tt=tt, r=r: e.tensor_scalar(
                cols.ap[:, 1:2], idxf.ap[:, 2 * tt + r:2 * tt + r + 1],
                float(CAP), -1.0, ALU.mult, ALU.add),
                ins=[idxf], outs=[cols])
            p.op("vector", lambda e: e.tensor_tensor(
                cols.ap[:, 1:2], cols.ap[:, 1:2], cols.ap[:, 0:1], ALU.add),
                ins=[cols], outs=[cols])
            p.op("vector", lambda e: e.tensor_copy(
                du.ap[:, 0:1].bitcast(U32), cols.ap[:, 1:2]),
                ins=[cols], outs=[du])
            ioff = IndirectOffsetOnAxis(ap=du.ap[:, 0:1].bitcast(U32), axis=0)
            tk_g = tok0.ap[:, tt:tt + 1].bitcast(U32)
            tk_d = tokb.ap[:, tt:tt + 1].bitcast(U32)
            wsl = wp.ap[:, 2 * tt + r:2 * tt + r + 1]
            sc_toks.append(p.dma("gpsimd_q", lambda e, tk_g=tk_g, ioff=ioff:
                                 e.indirect_dma_start(out=dr["glist"],
                                                      out_offset=ioff,
                                                      in_=tk_g, in_offset=None),
                                 ins=[tok0, du],
                                 extra_deps=[p.dbuf("glist").wr]))
            sc_toks.append(p.dma("gpsimd_q", lambda e, tk_d=tk_d, ioff=ioff:
                                 e.indirect_dma_start(out=dr["dlist"],
                                                      out_offset=ioff,
                                                      in_=tk_d, in_offset=None),
                                 ins=[tokb, du],
                                 extra_deps=[p.dbuf("dlist").wr]))
            sc_toks.append(p.dma("gpsimd_q", lambda e, wsl=wsl, ioff=ioff:
                                 e.indirect_dma_start(out=dr["wlist"],
                                                      out_offset=ioff,
                                                      in_=wsl, in_offset=None),
                                 ins=[wp, du],
                                 extra_deps=[p.dbuf("wlist").wr]))
    for b in (sc_fm, sc_tok, oh1, oh2, idxf, wp, msk_T, mx8, idx, sm,
              t16, zrow, pos_T, pos_tok, cols, du, tok0, tok1, zf, dumpi):
        ar.freebuf(b)
    p.qfence("gpsimd_q")
    return sc_toks


def moe_state(p, ar, dr, sc_toks):
    ms = {}
    ms["w"] = [ar.alloc(FD * M, "ms_w0"), ar.alloc(FD * M, "ms_w1"),
               ar.alloc(FD * M, "ms_w2")]
    ms["wi"] = 0
    ms["xe"] = ar.alloc(FD * TC, "ms_xe")
    ms["xg"] = [ar.alloc(D, f"ms_xg{i}") for i in range(CAP // 128)]
    ms["gs"] = ar.alloc(4 * TC, "ms_gs")
    ms["ub"] = ar.alloc(4 * TC, "ms_ub")
    ms["rb"] = ar.alloc(CAP, "ms_rb")
    ms["idxg"] = ar.alloc(2 * (CAP // 128), "ms_idxg")
    gd = [sc_toks[-1], p.dbuf("glist").wr, p.dbuf("dlist").wr,
          p.dbuf("wlist").wr]
    gd += [tk for tk in (p.dbuf("xn_tok_s", c).wr for c in range(NCHUNK)) if tk]
    ms["gather_dep"] = gd
    ms["first"] = True
    return ms


def next_w(p, ar, dr, ms, src_ap):
    b = ms["w"][ms["wi"] % 3]
    ms["wi"] += 1
    m2 = src_ap.shape[1]
    p.dma("sync_q", lambda e, b=b, src_ap=src_ap, m2=m2: e.dma_start(
        out=b.ap.rearrange("q (b m) -> q b m", m=m2),
        in_=src_ap.rearrange("(b q) m -> q b m", q=128)), outs=[b])
    return b


def expert_emit(p, ar, pbk, dr, ctx, ms, e_i):
    ident = ctx["ident"]
    NP = CAP // 128
    xg, xe, gs2, ub2 = ms["xg"], ms["xe"], ms["gs"], ms["ub"]
    idxg, rb = ms["idxg"], ms["rb"]
    wg = next_w(p, ar, dr, ms, dr["w_gate"][e_i])
    wu = next_w(p, ar, dr, ms, dr["w_up"][e_i])
    wd = next_w(p, ar, dr, ms, dr["w_down"][e_i])
    dep = ms["gather_dep"] if ms["first"] else []
    ms["first"] = False
    for i in range(NP):
        r0 = e_i * CAP + i * 128
        p.dma("gpsimd_q", lambda e, r0=r0, i=i: e.dma_start(
            out=idxg.ap[:, i:i + 1].bitcast(U32),
            in_=dr["glist"][r0:r0 + 128, :]), outs=[idxg], extra_deps=dep)
        p.dma("gpsimd_q", lambda e, r0=r0, i=i: e.dma_start(
            out=idxg.ap[:, NP + i:NP + i + 1].bitcast(U32),
            in_=dr["dlist"][r0:r0 + 128, :]), outs=[idxg], extra_deps=dep)
    p.dma("gpsimd_q", lambda e, e_i=e_i: e.dma_start(
        out=rb.ap, in_=dr["wlist"][e_i * CAP:(e_i + 1) * CAP, 0:1].rearrange(
            "a one -> one a").broadcast_to((128, CAP))), outs=[rb],
        extra_deps=dep)
    for i in range(NP):
        ioff = IndirectOffsetOnAxis(ap=idxg.ap[:, i:i + 1].bitcast(U32), axis=0)
        p.dma("gpsimd_q", lambda e, i=i, ioff=ioff: e.indirect_dma_start(
            out=xg[i].ap, out_offset=None,
            in_=dr["xn_tok_s"], in_offset=ioff),
            ins=[idxg], outs=[xg[i]])
    p.qfence("gpsimd_q")
    for f in range(FD):
        bank = pbk[4 + f % 2]
        for i in range(NP):
            p.op("tensor", lambda e, f=f, i=i, bank=bank: e.transpose(
                bank.ap[:, i * 128:(i + 1) * 128],
                xg[i].ap[:, f * 128:(f + 1) * 128], ident.ap),
                ins=[xg[i], ident], outs=[bank])
        p.op("scalar", lambda e, f=f, bank=bank: e.copy(
            xe.ap[:, f * CAP:(f + 1) * CAP], bank.ap[:, 0:CAP]),
            ins=[bank], outs=[xe])
    for wsrc, dst, act in ((wg, gs2, True), (wu, ub2, False)):
        for mg in range(4):
            bank = pbk[mg % 2]
            for k in range(FD):
                p.op("tensor", lambda e, k=k, mg=mg, bank=bank, wsrc=wsrc:
                     e.matmul(bank.ap[:, 0:CAP],
                              wsrc.ap[:, k * M + mg * 128:k * M + (mg + 1) * 128],
                              xe.ap[:, k * CAP:(k + 1) * CAP],
                              start=(k == 0), stop=(k == FD - 1)),
                     ins=[wsrc, xe], outs=[bank])
            if act:
                p.op("scalar", lambda e, mg=mg, bank=bank, dst=dst: e.activation(
                    dst.ap[:, mg * CAP:(mg + 1) * CAP], bank.ap[:, 0:CAP],
                    AF.Silu), ins=[bank], outs=[dst])
            else:
                p.op("scalar", lambda e, mg=mg, bank=bank, dst=dst: e.copy(
                    dst.ap[:, mg * CAP:(mg + 1) * CAP], bank.ap[:, 0:CAP]),
                    ins=[bank], outs=[dst])
    p.op("vector", lambda e: e.tensor_tensor(
        gs2.ap[:, 0:4 * CAP], gs2.ap[:, 0:4 * CAP], ub2.ap[:, 0:4 * CAP],
        ALU.mult), ins=[gs2, ub2], outs=[gs2])
    p.op("vector", lambda e: e.tensor_tensor(
        gs2.ap[:, 0:4 * CAP].rearrange("q (b t) -> q b t", t=CAP),
        gs2.ap[:, 0:4 * CAP].rearrange("q (b t) -> q b t", t=CAP),
        rb.ap.unsqueeze(1).broadcast_to((128, 4, CAP)),
        ALU.mult), ins=[gs2, rb], outs=[gs2])
    for mo in range(FD):
        bank = pbk[2 + mo % 2]
        for k in range(4):
            p.op("tensor", lambda e, k=k, mo=mo, bank=bank, wd=wd: e.matmul(
                bank.ap[:, 0:CAP],
                wd.ap[:, k * D + mo * 128:k * D + (mo + 1) * 128],
                gs2.ap[:, k * CAP:(k + 1) * CAP],
                start=(k == 0), stop=(k == 3)),
                ins=[wd, gs2], outs=[bank])
        p.op("scalar", lambda e, mo=mo, bank=bank: e.copy(
            xe.ap[:, mo * CAP:(mo + 1) * CAP], bank.ap[:, 0:CAP]),
            ins=[bank], outs=[xe])
        yield
    for i in range(NP):
        for fh in range(2):
            bk = pbk[4 + fh]
            for f4 in range(4):
                f = fh * 4 + f4
                p.op("tensor", lambda e, f=f, f4=f4, i=i, bk=bk: e.transpose(
                    bk.ap[:, f4 * 128:(f4 + 1) * 128],
                    xe.ap[:, f * CAP + i * 128:f * CAP + (i + 1) * 128],
                    ident.ap), ins=[xe, ident], outs=[bk])
            p.op("scalar", lambda e, i=i, fh=fh, bk=bk: e.copy(
                xg[i].ap[:, fh * 512:(fh + 1) * 512], bk.ap[:, 0:512]),
                ins=[bk], outs=[xg[i]])
            yield
        soff = IndirectOffsetOnAxis(
            ap=idxg.ap[:, NP + i:NP + i + 1].bitcast(U32), axis=0)
        ctx.setdefault("moe_sc", []).append(p.dma(
            "gpsimd_q", lambda e, i=i, soff=soff: e.indirect_dma_start(
                out=dr["moebuf"], out_offset=soff,
                in_=xg[i].ap, in_offset=None),
            ins=[xg[i], idxg]))


def shared_chunk_emit(p, ar, pbk, dr, ctx, ms, c, xn_c):
    gs2, ub2, xe = ms["gs"], ms["ub"], ms["xe"]
    wg = next_w(p, ar, dr, ms, dr["ws_gate"])
    wu = next_w(p, ar, dr, ms, dr["ws_up"])
    wd = next_w(p, ar, dr, ms, dr["ws_down"])
    for wsrc, dst, act in ((wg, gs2, True), (wu, ub2, False)):
        for mg in range(4):
            bank = pbk[mg % 2]
            for k in range(FD):
                p.op("tensor", lambda e, k=k, mg=mg, bank=bank, wsrc=wsrc,
                     xn_c=xn_c: e.matmul(
                         bank.ap[:, 0:TC],
                         wsrc.ap[:, k * M + mg * 128:k * M + (mg + 1) * 128],
                         xn_c.ap[:, k * TC:(k + 1) * TC],
                         start=(k == 0), stop=(k == FD - 1)),
                     ins=[wsrc, xn_c], outs=[bank])
            if act:
                p.op("scalar", lambda e, mg=mg, bank=bank, dst=dst: e.activation(
                    dst.ap[:, mg * TC:(mg + 1) * TC], bank.ap[:, 0:TC],
                    AF.Silu), ins=[bank], outs=[dst])
            else:
                p.op("scalar", lambda e, mg=mg, bank=bank, dst=dst: e.copy(
                    dst.ap[:, mg * TC:(mg + 1) * TC], bank.ap[:, 0:TC]),
                    ins=[bank], outs=[dst])
    p.op("vector", lambda e: e.tensor_tensor(
        gs2.ap, gs2.ap, ub2.ap, ALU.mult), ins=[gs2, ub2], outs=[gs2])
    for mo in range(FD):
        bank = pbk[2 + mo % 2]
        for k in range(4):
            p.op("tensor", lambda e, k=k, mo=mo, bank=bank, wd=wd: e.matmul(
                bank.ap[:, 0:TC],
                wd.ap[:, k * D + mo * 128:k * D + (mo + 1) * 128],
                gs2.ap[:, k * TC:(k + 1) * TC],
                start=(k == 0), stop=(k == 3)),
                ins=[wd, gs2], outs=[bank])
        p.op("scalar", lambda e, mo=mo, bank=bank: e.copy(
            xe.ap[:, mo * TC:(mo + 1) * TC], bank.ap[:, 0:TC]),
            ins=[bank], outs=[xe])
    store_fm(p, dr["hacc_s"], xe, FD, c, name="hacc_s")


def pass4v3(p, ar, pbk, dr, dbg, ctx, c, state):
    a_bc = ctx["a_bc"]
    HF = 8 * TC
    dt_h = ar.alloc(HF, f"s4dt_{c}")
    dtx = ar.alloc(HF, f"s4dtx_{c}")
    dA = ar.alloc(HF, f"s4dA_{c}")
    uh = ar.alloc(HF, f"s4uh_{c}")
    y_h = ar.alloc(HF, f"s4y_{c}")
    bB = [ar.alloc(TC, f"s4bB_{c}_{j}") for j in range(2)]
    bC = [ar.alloc(TC, f"s4bC_{c}_{j}") for j in range(2)]
    bp_dep = [p.dbuf("bpcp_s").wr] if p.dbuf("bpcp_s").wr else []
    for h in range(2):
        load_fm(p, dr["dt_s"], dt_h, 8, c, blk0=h * 8, name="dt_s")
        load_fm(p, dr["xc_s"], dA, 8, c, blk0=h * 8, name="xc_s")
        p.op("vector", lambda e: e.tensor_tensor(
            dtx.ap, dt_h.ap, dA.ap, ALU.mult), ins=[dt_h, dA], outs=[dtx])
        for s in range(S):
            jb = s % 2
            p.dma("sync_q", lambda e, s=s, jb=jb: e.dma_start(
                out=bB[jb].ap,
                in_=dr["bpcp_s"][s:s + 1, c * TC:(c + 1) * TC].broadcast_to(
                    (128, TC))), outs=[bB[jb]], extra_deps=bp_dep)
            p.dma("sync_q", lambda e, s=s, jb=jb: e.dma_start(
                out=bC[jb].ap,
                in_=dr["bpcp_s"][16 + s:17 + s, c * TC:(c + 1) * TC].broadcast_to(
                    (128, TC))), outs=[bC[jb]], extra_deps=bp_dep)
            p.op("scalar", lambda e, s=s: e.activation(
                dA.ap, dt_h.ap, AF.Exp, scale=a_bc.ap[:, s:s + 1]),
                ins=[dt_h, a_bc], outs=[dA])
            p.op("vector", lambda e, jb=jb: e.tensor_tensor(
                uh.ap.rearrange("q (b t) -> q b t", t=TC),
                dtx.ap.rearrange("q (b t) -> q b t", t=TC),
                bB[jb].ap.unsqueeze(1).broadcast_to((128, 8, TC)),
                ALU.mult), ins=[dtx, bB[jb]], outs=[uh])
            for dc in range(8):
                col = s * FDI + h * 8 + dc
                p.op("vector", lambda e, dc=dc, col=col:
                     e.tensor_tensor_scan(
                         uh.ap[:, dc * TC:(dc + 1) * TC],
                         dA.ap[:, dc * TC:(dc + 1) * TC],
                         uh.ap[:, dc * TC:(dc + 1) * TC],
                         state.ap[:, col:col + 1], ALU.mult, ALU.add),
                     ins=[dA, uh, state], outs=[uh])
            p.op("vector", lambda e, s=s, h=h: e.tensor_copy(
                state.ap[:, s * FDI + h * 8:s * FDI + h * 8 + 8],
                uh.ap.rearrange("q (b t) -> q b t", t=TC)[:, :, TC - 1:TC
                                                          ].squeeze(2)),
                ins=[uh], outs=[state])
            if s == 0:
                p.op("vector", lambda e, jb=jb: e.tensor_tensor(
                    y_h.ap.rearrange("q (b t) -> q b t", t=TC),
                    uh.ap.rearrange("q (b t) -> q b t", t=TC),
                    bC[jb].ap.unsqueeze(1).broadcast_to((128, 8, TC)),
                    ALU.mult), ins=[uh, bC[jb]], outs=[y_h])
            else:
                p.op("vector", lambda e, jb=jb: e.tensor_tensor(
                    dA.ap.rearrange("q (b t) -> q b t", t=TC),
                    uh.ap.rearrange("q (b t) -> q b t", t=TC),
                    bC[jb].ap.unsqueeze(1).broadcast_to((128, 8, TC)),
                    ALU.mult), ins=[uh, bC[jb]], outs=[dA])
                p.op("vector", lambda e: e.tensor_tensor(
                    y_h.ap, y_h.ap, dA.ap, ALU.add), ins=[dA, y_h], outs=[y_h])
        d3 = dr["y_s"].rearrange("(b q) t -> q b t", q=128)[
            :, h * 8:(h + 1) * 8, c * TC:(c + 1) * TC]
        s3 = y_h.ap.rearrange("q (b t) -> q b t", t=TC)
        p.dma("gpsimd_q", lambda e, d3=d3, s3=s3: e.dma_start(out=d3, in_=s3),
              ins=[y_h], outs=[p.dbuf("y_s", c)])
        if "y" in dbg:
            d3b = dr["dbg_y"].rearrange("(b q) t -> q b t", q=128)[
                :, h * 8:(h + 1) * 8, c * TC:(c + 1) * TC]
            p.dma("gpsimd_q", lambda e, d3b=d3b, s3=s3: e.dma_start(
                out=d3b, in_=s3), ins=[y_h])
    for b in [dt_h, dtx, uh, y_h] + dAb + bBC:
        ar.freebuf(b)
    p.qfence("gpsimd_q")


def pass6az_v3(p, ar, pbk, dr, dbg, ctx, c, xn_c):
    wzp = [ar.alloc(FD * 128, f"az_w_{c}_{j}") for j in range(2)]
    zt = ar.alloc(TC, f"az_z_{c}")
    for mi in range(FDI):
        wb = wzp[mi % 2]
        src = dr["w_in_z_T"][:, mi * 128:(mi + 1) * 128]
        p.dma("sync_q", lambda e, wb=wb, src=src: e.dma_start(
            out=wb.ap.rearrange("q (b m) -> q b m", m=128),
            in_=src.rearrange("(b q) m -> q b m", q=128)), outs=[wb])
        bank = pbk[6 + mi % 2]
        for k in range(FD):
            p.op("tensor", lambda e, k=k, bank=bank, wb=wb, xn_c=xn_c: e.matmul(
                bank.ap[:, 0:TC], wb.ap[:, k * 128:(k + 1) * 128],
                xn_c.ap[:, k * TC:(k + 1) * TC],
                start=(k == 0), stop=(k == FD - 1)),
                ins=[wb, xn_c], outs=[bank])
        p.op("scalar", lambda e, bank=bank, zt=zt: e.activation(
            zt.ap, bank.ap[:, 0:TC], AF.Silu), ins=[bank], outs=[zt])
        p.dma("gpsimd_q", lambda e, mi=mi, zt=zt: e.dma_start(
            out=dr["zs_s"][mi * 128:(mi + 1) * 128, c * TC:(c + 1) * TC],
            in_=zt.ap), ins=[zt], outs=[p.dbuf("zs_s", c)])
        yield
    for b in wzp + [zt]:
        ar.freebuf(b)
    p.qfence("gpsimd_q")


def pass6gate_v3(p, ar, pbk, dr, dbg, ctx, c):
    tabs = ctx["tabs"]
    constk = ctx["constk"]
    ones_row = ctx["ones_row"]
    ones_col = ctx["ones_col"]
    y0 = ar.alloc(8 * TC, f"g_y0_{c}")
    y1 = ar.alloc(8 * TC, f"g_y1_{c}")
    sq = ar.alloc(TC, f"g_sq_{c}")
    stat = ar.alloc(6 * TC, f"g_st_{c}")
    load_fm(p, dr["y_s"], y0, 8, c, blk0=0, name="y_s")
    load_fm(p, dr["y_s"], y1, 8, c, blk0=8, name="y_s")
    bA, bB2 = pbk[6], pbk[7]
    for k in range(FDI):
        yk = (y0 if k < 8 else y1).ap[:, (k % 8) * TC:(k % 8 + 1) * TC]
        ybuf = y0 if k < 8 else y1
        p.op("scalar", lambda e, yk=yk: e.activation(
            sq.ap, yk, AF.Square), ins=[ybuf], outs=[sq])
        p.op("tensor", lambda e, k=k, yk=yk, bA=bA: e.matmul(
            bA.ap[0:1, 0:TC], ones_col.ap, yk,
            start=(k == 0), stop=(k == FDI - 1)), ins=[ones_col, ybuf], outs=[bA])
        p.op("tensor", lambda e, k=k, bB2=bB2: e.matmul(
            bB2.ap[0:1, 0:TC], ones_col.ap, sq.ap,
            start=(k == 0), stop=(k == FDI - 1)), ins=[ones_col, sq], outs=[bB2])
    r0 = stat.ap[0:1, :]
    p.op("vector", lambda e, bA=bA: e.tensor_scalar(
        r0[:, 0:TC], bA.ap[0:1, 0:TC], 1.0 / DIN, None, ALU.mult),
        ins=[bA], outs=[stat])
    p.op("vector", lambda e, bB2=bB2: e.tensor_scalar(
        r0[:, TC:2 * TC], bB2.ap[0:1, 0:TC], 1.0 / DIN, None, ALU.mult),
        ins=[bB2], outs=[stat])
    p.op("vector", lambda e: e.tensor_tensor(
        r0[:, 2 * TC:3 * TC], r0[:, 0:TC], r0[:, 0:TC], ALU.mult),
        ins=[stat], outs=[stat])
    p.op("vector", lambda e: e.tensor_tensor(
        r0[:, TC:2 * TC], r0[:, TC:2 * TC], r0[:, 2 * TC:3 * TC],
        ALU.subtract), ins=[stat], outs=[stat])
    p.op("scalar", lambda e: e.activation(
        r0[:, 2 * TC:3 * TC], r0[:, TC:2 * TC], AF.Sqrt,
        bias=constk.ap[0:1, 0:1]), ins=[stat, constk], outs=[stat])
    p.op("vector", lambda e: e.reciprocal(
        r0[:, TC:2 * TC], r0[:, 2 * TC:3 * TC]), ins=[stat], outs=[stat])
    p.op("tensor", lambda e, bA=bA: e.matmul(
        bA.ap[:, 0:TC], ones_row.ap[0:1, 0:128], r0[:, 0:TC],
        start=True, stop=True), ins=[ones_row, stat], outs=[bA])
    p.op("tensor", lambda e, bB2=bB2: e.matmul(
        bB2.ap[:, 0:TC], ones_row.ap[0:1, 0:128], r0[:, TC:2 * TC],
        start=True, stop=True), ins=[ones_row, stat], outs=[bB2])
    xh = ar.alloc(8 * TC, f"g_xh_{c}")
    zh = ar.alloc(8 * TC, f"g_zh_{c}")
    for h in range(2):
        ybuf = y0 if h == 0 else y1
        load_fm(p, dr["xc_s"], xh, 8, c, blk0=h * 8, name="xc_s")
        load_fm(p, dr["zs_s"], zh, 8, c, blk0=h * 8, name="zs_s")
        for k8 in range(8):
            k = h * 8 + k8
            yk = ybuf.ap[:, k8 * TC:(k8 + 1) * TC]
            p.op("vector", lambda e, yk=yk, bA=bA: e.tensor_tensor(
                yk, yk, bA.ap[:, 0:TC], ALU.subtract), ins=[ybuf, bA], outs=[ybuf])
            p.op("vector", lambda e, yk=yk, bB2=bB2: e.tensor_tensor(
                yk, yk, bB2.ap[:, 0:TC], ALU.mult), ins=[ybuf, bB2], outs=[ybuf])
            p.op("vector", lambda e, yk=yk, k=k: e.tensor_scalar(
                yk, yk, tabs["ln_g_pc"].ap[:, k:k + 1],
                tabs["ln_b_pc"].ap[:, k:k + 1], ALU.mult, ALU.add),
                ins=[ybuf, tabs["ln_g_pc"], tabs["ln_b_pc"]], outs=[ybuf])
            p.op("vector", lambda e, yk=yk, k=k, k8=k8, xh=xh:
                 e.scalar_tensor_tensor(
                     yk, xh.ap[:, k8 * TC:(k8 + 1) * TC],
                     tabs["d_param_pc"].ap[:, k:k + 1], yk, ALU.mult, ALU.add),
                 ins=[xh, tabs["d_param_pc"], ybuf], outs=[ybuf])
            p.op("vector", lambda e, yk=yk, k8=k8, zh=zh: e.tensor_tensor(
                yk, yk, zh.ap[:, k8 * TC:(k8 + 1) * TC], ALU.mult),
                ins=[ybuf, zh], outs=[ybuf])
        d3 = dr["yg_s"].rearrange("(b q) t -> q b t", q=128)[
            :, h * 8:(h + 1) * 8, c * TC:(c + 1) * TC]
        s3 = ybuf.ap.rearrange("q (b t) -> q b t", t=TC)
        p.dma("gpsimd_q", lambda e, d3=d3, s3=s3: e.dma_start(out=d3, in_=s3),
              ins=[ybuf], outs=[p.dbuf("yg_s", c)])
        if "yg" in dbg:
            d3b = dr["dbg_yg"].rearrange("(b q) t -> q b t", q=128)[
                :, h * 8:(h + 1) * 8, c * TC:(c + 1) * TC]
            p.dma("gpsimd_q", lambda e, d3b=d3b, s3=s3: e.dma_start(
                out=d3b, in_=s3), ins=[ybuf])
    for b in (y0, y1, sq, stat, xh, zh):
        ar.freebuf(b)


def expert_emit_v4(p, ar, pbk, dr, ctx, ms, e_i):
    ident = ctx["ident"]
    NP = CAP // 128
    xg, xe, gs2, ub2 = ms["xg"], ms["xe"], ms["gs"], ms["ub"]
    idxg, rb = ms["idxg"], ms["rb"]
    wh = ms["wh"]

    def whalf(src_ap, klo):
        b = wh[ms["wi"] % 2]
        ms["wi"] += 1
        m2 = src_ap.shape[1]
        sl = src_ap[klo * 512:(klo + 4 * (m2 // 512) * 0 + 4) * 128 + klo * 0, :]
        return b

    dep = ms["gather_dep"] if ms["first"] else []
    ms["first"] = False
    for i in range(NP):
        r0 = e_i * CAP + i * 128
        p.dma("gpsimd_q", lambda e, r0=r0, i=i: e.dma_start(
            out=idxg.ap[:, i:i + 1].bitcast(U32),
            in_=dr["glist"][r0:r0 + 128, :]), outs=[idxg], extra_deps=dep)
        p.dma("gpsimd_q", lambda e, r0=r0, i=i: e.dma_start(
            out=idxg.ap[:, NP + i:NP + i + 1].bitcast(U32),
            in_=dr["dlist"][r0:r0 + 128, :]), outs=[idxg], extra_deps=dep)
    p.dma("gpsimd_q", lambda e, e_i=e_i: e.dma_start(
        out=rb.ap, in_=dr["wlist"][e_i * CAP:(e_i + 1) * CAP, 0:1].rearrange(
            "a one -> one a").broadcast_to((128, CAP))), outs=[rb],
        extra_deps=dep)
    for i in range(NP):
        ioff = IndirectOffsetOnAxis(ap=idxg.ap[:, i:i + 1].bitcast(U32), axis=0)
        p.dma("gpsimd_q", lambda e, i=i, ioff=ioff: e.indirect_dma_start(
            out=xg[i].ap, out_offset=None,
            in_=dr["xn_tok_s"], in_offset=ioff),
            ins=[idxg], outs=[xg[i]])
    p.qfence("gpsimd_q")
    for f in range(FD):
        bank = pbk[4 + f % 2]
        for i in range(NP):
            p.op("tensor", lambda e, f=f, i=i, bank=bank: e.transpose(
                bank.ap[:, i * 128:(i + 1) * 128],
                xg[i].ap[:, f * 128:(f + 1) * 128], ident.ap),
                ins=[xg[i], ident], outs=[bank])
        p.op("scalar", lambda e, f=f, bank=bank: e.copy(
            xe.ap[:, f * CAP:(f + 1) * CAP], bank.ap[:, 0:CAP]),
            ins=[bank], outs=[xe])
        yield
    for src_d, dst, act in ((dr["w_gate"][e_i], gs2, True),
                            (dr["w_up"][e_i], ub2, False)):
        halves = []
        for kh in range(2):
            b = wh[ms["wi"] % 2]
            ms["wi"] += 1
            sl = src_d[kh * 512:(kh + 1) * 512, :]
            p.dma("gpsimd_q", lambda e, b=b, sl=sl: e.dma_start(
                out=b.ap.rearrange("q (b m) -> q b m", m=M),
                in_=sl.rearrange("(b q) m -> q b m", q=128)), outs=[b])
            halves.append(b)
        for mg in range(4):
            bank = pbk[mg % 2]
            for k in range(FD):
                b = halves[k // 4]
                k4 = k % 4
                p.op("tensor", lambda e, k=k, k4=k4, mg=mg, bank=bank, b=b:
                     e.matmul(bank.ap[:, 0:CAP],
                              b.ap[:, k4 * M + mg * 128:k4 * M + (mg + 1) * 128],
                              xe.ap[:, k * CAP:(k + 1) * CAP],
                              start=(k == 0), stop=(k == FD - 1)),
                     ins=[b, xe], outs=[bank])
            if act:
                p.op("scalar", lambda e, mg=mg, bank=bank, dst=dst: e.activation(
                    dst.ap[:, mg * CAP:(mg + 1) * CAP], bank.ap[:, 0:CAP],
                    AF.Silu), ins=[bank], outs=[dst])
            else:
                p.op("scalar", lambda e, mg=mg, bank=bank, dst=dst: e.copy(
                    dst.ap[:, mg * CAP:(mg + 1) * CAP], bank.ap[:, 0:CAP]),
                    ins=[bank], outs=[dst])
            yield
    p.op("gpsimd", lambda e: e.tensor_tensor(
        gs2.ap[:, 0:4 * CAP], gs2.ap[:, 0:4 * CAP], ub2.ap[:, 0:4 * CAP],
        ALU.mult), ins=[gs2, ub2], outs=[gs2])
    p.op("gpsimd", lambda e: e.tensor_tensor(
        gs2.ap[:, 0:4 * CAP].rearrange("q (b t) -> q b t", t=CAP),
        gs2.ap[:, 0:4 * CAP].rearrange("q (b t) -> q b t", t=CAP),
        rb.ap.unsqueeze(1).broadcast_to((128, 4, CAP)),
        ALU.mult), ins=[gs2, rb], outs=[gs2])
    halves = []
    for kh in range(2):
        b = wh[ms["wi"] % 2]
        ms["wi"] += 1
        sl = dr["w_down"][e_i][kh * 256:(kh + 1) * 256, :]
        p.dma("gpsimd_q", lambda e, b=b, sl=sl: e.dma_start(
            out=b.ap.rearrange("q (b m) -> q b m", m=D),
            in_=sl.rearrange("(b q) m -> q b m", q=128)), outs=[b])
        halves.append(b)
    for mo in range(FD):
        bank = pbk[2 + mo % 2]
        for k in range(4):
            b = halves[k // 2]
            k2 = k % 2
            p.op("tensor", lambda e, k=k, k2=k2, mo=mo, bank=bank, b=b: e.matmul(
                bank.ap[:, 0:CAP],
                b.ap[:, k2 * D + mo * 128:k2 * D + (mo + 1) * 128],
                gs2.ap[:, k * CAP:(k + 1) * CAP],
                start=(k == 0), stop=(k == 3)),
                ins=[b, gs2], outs=[bank])
        p.op("scalar", lambda e, mo=mo, bank=bank: e.copy(
            xe.ap[:, mo * CAP:(mo + 1) * CAP], bank.ap[:, 0:CAP]),
            ins=[bank], outs=[xe])
        yield
    for i in range(NP):
        for fh in range(2):
            bk = pbk[4 + fh]
            for f4 in range(4):
                f = fh * 4 + f4
                p.op("tensor", lambda e, f=f, f4=f4, i=i, bk=bk: e.transpose(
                    bk.ap[:, f4 * 128:(f4 + 1) * 128],
                    xe.ap[:, f * CAP + i * 128:f * CAP + (i + 1) * 128],
                    ident.ap), ins=[xe, ident], outs=[bk])
            p.op("scalar", lambda e, i=i, fh=fh, bk=bk: e.copy(
                xg[i].ap[:, fh * 512:(fh + 1) * 512], bk.ap[:, 0:512]),
                ins=[bk], outs=[xg[i]])
            yield
        soff = IndirectOffsetOnAxis(
            ap=idxg.ap[:, NP + i:NP + i + 1].bitcast(U32), axis=0)
        ctx.setdefault("moe_sc", []).append(p.dma(
            "gpsimd_q", lambda e, i=i, soff=soff: e.indirect_dma_start(
                out=dr["moebuf"], out_offset=soff,
                in_=xg[i].ap, in_offset=None),
            ins=[xg[i], idxg]))
    p.qfence("gpsimd_q")


def shared_tail(p, ar, pbk, dr, ctx):
    wgb = ar.alloc(FD * M, "st_wg")
    wub = ar.alloc(FD * M, "st_wu")
    wdb = ar.alloc(4 * D, "st_wd")
    gs = ar.alloc(4 * TC, "st_gs")
    ub = ar.alloc(4 * TC, "st_ub")
    eo = ar.alloc(FD * TC, "st_eo")
    xn_c = ar.alloc(FD * TC, "st_xn")
    p.dma("sync_q", lambda e: e.dma_start(
        out=wgb.ap.rearrange("q (b m) -> q b m", m=M),
        in_=dr["ws_gate"].rearrange("(b q) m -> q b m", q=128)), outs=[wgb])
    p.dma("sync_q", lambda e: e.dma_start(
        out=wub.ap.rearrange("q (b m) -> q b m", m=M),
        in_=dr["ws_up"].rearrange("(b q) m -> q b m", q=128)), outs=[wub])
    p.dma("sync_q", lambda e: e.dma_start(
        out=wdb.ap.rearrange("q (b m) -> q b m", m=D),
        in_=dr["ws_down"].rearrange("(b q) m -> q b m", q=128)), outs=[wdb])
    for c in range(NCHUNK):
        load_fm(p, dr["xn_s"], xn_c, FD, c, name="xn_s")
        for mg in range(4):
            bank = pbk[mg % 2]
            for k in range(FD):
                p.op("tensor", lambda e, k=k, mg=mg, bank=bank, xn_c=xn_c:
                     e.matmul(bank.ap[:, 0:TC],
                              wgb.ap[:, k * M + mg * 128:k * M + (mg + 1) * 128],
                              xn_c.ap[:, k * TC:(k + 1) * TC],
                              start=(k == 0), stop=(k == FD - 1)),
                     ins=[wgb, xn_c], outs=[bank])
            p.op("scalar", lambda e, mg=mg, bank=bank: e.activation(
                gs.ap[:, mg * TC:(mg + 1) * TC], bank.ap[:, 0:TC], AF.Silu),
                ins=[bank], outs=[gs])
        for mg in range(4):
            bank = pbk[mg % 2]
            for k in range(FD):
                p.op("tensor", lambda e, k=k, mg=mg, bank=bank, xn_c=xn_c:
                     e.matmul(bank.ap[:, 0:TC],
                              wub.ap[:, k * M + mg * 128:k * M + (mg + 1) * 128],
                              xn_c.ap[:, k * TC:(k + 1) * TC],
                              start=(k == 0), stop=(k == FD - 1)),
                     ins=[wub, xn_c], outs=[bank])
            p.op("scalar", lambda e, mg=mg, bank=bank: e.copy(
                ub.ap[:, mg * TC:(mg + 1) * TC], bank.ap[:, 0:TC]),
                ins=[bank], outs=[ub])
        p.op("vector", lambda e: e.tensor_tensor(
            gs.ap, gs.ap, ub.ap, ALU.mult), ins=[gs, ub], outs=[gs])
        for mo in range(FD):
            bank = pbk[2 + mo % 2]
            for k in range(4):
                p.op("tensor", lambda e, k=k, mo=mo, bank=bank: e.matmul(
                    bank.ap[:, 0:TC],
                    wdb.ap[:, k * D + mo * 128:k * D + (mo + 1) * 128],
                    gs.ap[:, k * TC:(k + 1) * TC],
                    start=(k == 0), stop=(k == 3)),
                    ins=[wdb, gs], outs=[bank])
            p.op("scalar", lambda e, mo=mo, bank=bank: e.copy(
                eo.ap[:, mo * TC:(mo + 1) * TC], bank.ap[:, 0:TC]),
                ins=[bank], outs=[eo])
        store_fm(p, dr["hacc_s"], eo, FD, c, name="hacc_s")
    for b in (wgb, wub, wdb, gs, ub, eo, xn_c):
        ar.freebuf(b)
    p.qfence("gpsimd_q")


def shared_gen_v4(p, ar, pbk, dr, ctx, ms, c, xn_c):
    gs2, ub2, xe, wh = ms["gs"], ms["ub"], ms["xe"], ms["wh"]
    for src_d, dst, act in ((dr["ws_gate"], gs2, True), (dr["ws_up"], ub2, False)):
        halves = []
        for kh in range(2):
            b = wh[ms["wi"] % 2]
            ms["wi"] += 1
            sl = src_d[kh * 512:(kh + 1) * 512, :]
            p.dma("gpsimd_q", lambda e, b=b, sl=sl: e.dma_start(
                out=b.ap.rearrange("q (b m) -> q b m", m=M),
                in_=sl.rearrange("(b q) m -> q b m", q=128)), outs=[b])
            halves.append(b)
        for mg in range(4):
            bank = pbk[mg % 2]
            for k in range(FD):
                b = halves[k // 4]
                k4 = k % 4
                p.op("tensor", lambda e, k=k, k4=k4, mg=mg, bank=bank, b=b,
                     xn_c=xn_c: e.matmul(
                         bank.ap[:, 0:TC],
                         b.ap[:, k4 * M + mg * 128:k4 * M + (mg + 1) * 128],
                         xn_c.ap[:, k * TC:(k + 1) * TC],
                         start=(k == 0), stop=(k == FD - 1)),
                     ins=[b, xn_c], outs=[bank])
            if act:
                p.op("scalar", lambda e, mg=mg, bank=bank, dst=dst: e.activation(
                    dst.ap[:, mg * TC:(mg + 1) * TC], bank.ap[:, 0:TC],
                    AF.Silu), ins=[bank], outs=[dst])
            else:
                p.op("scalar", lambda e, mg=mg, bank=bank, dst=dst: e.copy(
                    dst.ap[:, mg * TC:(mg + 1) * TC], bank.ap[:, 0:TC]),
                    ins=[bank], outs=[dst])
            yield
    p.op("gpsimd", lambda e: e.tensor_tensor(
        gs2.ap, gs2.ap, ub2.ap, ALU.mult), ins=[gs2, ub2], outs=[gs2])
    halves = []
    for kh in range(2):
        b = wh[ms["wi"] % 2]
        ms["wi"] += 1
        sl = dr["ws_down"][kh * 256:(kh + 1) * 256, :]
        p.dma("gpsimd_q", lambda e, b=b, sl=sl: e.dma_start(
            out=b.ap.rearrange("q (b m) -> q b m", m=D),
            in_=sl.rearrange("(b q) m -> q b m", q=128)), outs=[b])
        halves.append(b)
    for mo in range(FD):
        bank = pbk[2 + mo % 2]
        for k in range(4):
            b = halves[k // 2]
            k2 = k % 2
            p.op("tensor", lambda e, k=k, k2=k2, mo=mo, bank=bank, b=b: e.matmul(
                bank.ap[:, 0:TC],
                b.ap[:, k2 * D + mo * 128:k2 * D + (mo + 1) * 128],
                gs2.ap[:, k * TC:(k + 1) * TC],
                start=(k == 0), stop=(k == 3)),
                ins=[b, gs2], outs=[bank])
        p.op("scalar", lambda e, mo=mo, bank=bank: e.copy(
            xe.ap[:, mo * TC:(mo + 1) * TC], bank.ap[:, 0:TC]),
            ins=[bank], outs=[xe])
        yield
    store_fm(p, dr["hacc_s"], xe, FD, c, name="hacc_s")
    p.qfence("gpsimd_q")


def pass4v4_alloc(p, ar, dr, c):
    HF = 8 * TC
    QF = 4 * TC
    bufs = dict(
        dt_h=ar.alloc(HF, f"s4dt_{c}"),
        dtx=ar.alloc(HF, f"s4dtx_{c}"),
        dAb=[ar.alloc(QF, f"s4dA_{c}_{j}") for j in range(2)],
        uh=ar.alloc(QF, f"s4uh_{c}"),
        y_h=ar.alloc(HF, f"s4y_{c}"),
        bBC=[ar.alloc(4 * TC, f"s4bBC_{c}_{j}") for j in range(2)])
    load_fm(p, dr["dt_s"], bufs["dt_h"], 8, c, blk0=0, name="dt_s")
    load_fm(p, dr["xc_s"], bufs["dtx"], 8, c, blk0=0, name="xc_s")
    return bufs


def pass4v4(p, ar, pbk, dr, dbg, ctx, c, state, bufs, feed=None):
    a_bc = ctx["a_bc"]
    HF = 8 * TC
    QF = 4 * TC
    dt_h, dtx, dAb, uh, y_h, bBC = (bufs["dt_h"], bufs["dtx"], bufs["dAb"],
                                    bufs["uh"], bufs["y_h"], bufs["bBC"])
    bp_dep = [p.dbuf("bpcp_s").wr] if p.dbuf("bpcp_s").wr else []
    for h in range(2):
        if h == 1:
            load_fm(p, dr["dt_s"], dt_h, 8, c, blk0=8, name="dt_s")
            load_fm(p, dr["xc_s"], dtx, 8, c, blk0=8, name="xc_s")
        p.op("vector", lambda e: e.tensor_tensor(
            dtx.ap, dt_h.ap, dtx.ap, ALU.mult), ins=[dt_h, dtx], outs=[dtx])
        qi = 0
        for s in range(S):
            sg, s2 = s // 2, s % 2
            grp = bBC[sg % 2]
            if s2 == 0:
                p.dma("sync_q", lambda e, sg=sg, grp=grp: e.dma_start(
                    out=grp.ap.rearrange("q (j t) -> q j t", t=TC),
                    in_=dr["bpcp_s"][4 * sg:4 * sg + 4, c * TC:(c + 1) * TC
                                     ].unsqueeze(0).broadcast_to((128, 4, TC))),
                    outs=[grp], extra_deps=bp_dep)
            bBs = grp.ap[:, (2 * s2) * TC:(2 * s2 + 1) * TC]
            bCs = grp.ap[:, (2 * s2 + 1) * TC:(2 * s2 + 2) * TC]
            for q in range(2):
                qs = q * QF
                dA = dAb[qi % 2]
                qi += 1
                p.op("scalar", lambda e, s=s, qs=qs, dA=dA: e.activation(
                    dA.ap, dt_h.ap[:, qs:qs + QF], AF.Exp,
                    scale=a_bc.ap[:, s:s + 1]), ins=[dt_h, a_bc], outs=[dA])
                p.op("vector", lambda e, bBs=bBs, qs=qs: e.tensor_tensor(
                    uh.ap.rearrange("q (b t) -> q b t", t=TC),
                    dtx.ap[:, qs:qs + QF].rearrange("q (b t) -> q b t", t=TC),
                    bBs.unsqueeze(1).broadcast_to((128, 4, TC)),
                    ALU.mult), ins=[dtx, grp], outs=[uh])
                for dc in range(4):
                    col = s * FDI + h * 8 + q * 4 + dc
                    p.op("vector", lambda e, dc=dc, col=col, dA=dA:
                         e.tensor_tensor_scan(
                             uh.ap[:, dc * TC:(dc + 1) * TC],
                             dA.ap[:, dc * TC:(dc + 1) * TC],
                             uh.ap[:, dc * TC:(dc + 1) * TC],
                             state.ap[:, col:col + 1], ALU.mult, ALU.add),
                         ins=[dA, uh, state], outs=[uh])
                p.op("vector", lambda e, s=s, h=h, q=q: e.tensor_copy(
                    state.ap[:, s * FDI + h * 8 + q * 4:s * FDI + h * 8 + q * 4 + 4],
                    uh.ap.rearrange("q (b t) -> q b t", t=TC)[:, :, TC - 1:TC
                                                              ].squeeze(2)),
                    ins=[uh], outs=[state])
                if s == 0:
                    p.op("vector", lambda e, bCs=bCs, qs=qs: e.tensor_tensor(
                        y_h.ap[:, qs:qs + QF].rearrange("q (b t) -> q b t", t=TC),
                        uh.ap.rearrange("q (b t) -> q b t", t=TC),
                        bCs.unsqueeze(1).broadcast_to((128, 4, TC)),
                        ALU.mult), ins=[uh, grp], outs=[y_h])
                else:
                    p.op("vector", lambda e, bCs=bCs, dA=dA: e.tensor_tensor(
                        dA.ap.rearrange("q (b t) -> q b t", t=TC),
                        uh.ap.rearrange("q (b t) -> q b t", t=TC),
                        bCs.unsqueeze(1).broadcast_to((128, 4, TC)),
                        ALU.mult), ins=[uh, grp], outs=[dA])
                    p.op("vector", lambda e, qs=qs, dA=dA: e.tensor_tensor(
                        y_h.ap[:, qs:qs + QF], y_h.ap[:, qs:qs + QF], dA.ap,
                        ALU.add), ins=[dA, y_h], outs=[y_h])
                if feed is not None:
                    feed(2)
        d3 = dr["y_s"].rearrange("(b q) t -> q b t", q=128)[
            :, h * 8:(h + 1) * 8, c * TC:(c + 1) * TC]
        s3 = y_h.ap.rearrange("q (b t) -> q b t", t=TC)
        p.dma("gpsimd_q", lambda e, d3=d3, s3=s3: e.dma_start(out=d3, in_=s3),
              ins=[y_h], outs=[p.dbuf("y_s", c)])
        if "y" in dbg:
            d3b = dr["dbg_y"].rearrange("(b q) t -> q b t", q=128)[
                :, h * 8:(h + 1) * 8, c * TC:(c + 1) * TC]
            p.dma("gpsimd_q", lambda e, d3b=d3b, s3=s3: e.dma_start(
                out=d3b, in_=s3), ins=[y_h])
    for b in [dt_h, dtx, uh, y_h] + dAb + bBC:
        ar.freebuf(b)
    p.qfence("gpsimd_q")


def pass6gate_v4(p, ar, pbk, dr, dbg, ctx, c):
    tabs = ctx["tabs"]
    constk = ctx["constk"]
    ones_row = ctx["ones_row"]
    ones_col = ctx["ones_col"]
    gb = ctx["gate_bufs"]
    yq, sq, stat, xk, zk = (gb["yq"], gb["sq"], gb["stat"], gb["xk"], gb["zk"])
    bA, bB2 = pbk[6], pbk[7]
    for qq in range(4):
        load_fm(p, dr["y_s"], yq, 4, c, blk0=qq * 4, name="y_s", q="scalar_q")
        p.qfence("scalar_q")
        for k4 in range(4):
            k = qq * 4 + k4
            yk = yq.ap[:, k4 * TC:(k4 + 1) * TC]
            p.op("scalar", lambda e, yk=yk: e.activation(
                sq.ap, yk, AF.Square), ins=[yq], outs=[sq])
            p.op("tensor", lambda e, k=k, yk=yk, bA=bA: e.matmul(
                bA.ap[0:1, 0:TC], ones_col.ap, yk,
                start=(k == 0), stop=(k == FDI - 1)),
                ins=[ones_col, yq], outs=[bA])
            p.op("tensor", lambda e, k=k, bB2=bB2: e.matmul(
                bB2.ap[0:1, 0:TC], ones_col.ap, sq.ap,
                start=(k == 0), stop=(k == FDI - 1)),
                ins=[ones_col, sq], outs=[bB2])
        yield
    r0 = stat.ap[0:1, :]
    p.op("vector", lambda e, bA=bA: e.tensor_scalar(
        r0[:, 0:TC], bA.ap[0:1, 0:TC], 1.0 / DIN, None, ALU.mult),
        ins=[bA], outs=[stat])
    p.op("vector", lambda e, bB2=bB2: e.tensor_scalar(
        r0[:, TC:2 * TC], bB2.ap[0:1, 0:TC], 1.0 / DIN, None, ALU.mult),
        ins=[bB2], outs=[stat])
    p.op("vector", lambda e: e.tensor_tensor(
        r0[:, 2 * TC:3 * TC], r0[:, 0:TC], r0[:, 0:TC], ALU.mult),
        ins=[stat], outs=[stat])
    p.op("vector", lambda e: e.tensor_tensor(
        r0[:, TC:2 * TC], r0[:, TC:2 * TC], r0[:, 2 * TC:3 * TC],
        ALU.subtract), ins=[stat], outs=[stat])
    p.op("scalar", lambda e: e.activation(
        r0[:, 2 * TC:3 * TC], r0[:, TC:2 * TC], AF.Sqrt,
        bias=constk.ap[0:1, 0:1]), ins=[stat, constk], outs=[stat])
    p.op("vector", lambda e: e.reciprocal(
        r0[:, TC:2 * TC], r0[:, 2 * TC:3 * TC]), ins=[stat], outs=[stat])
    p.op("tensor", lambda e, bA=bA: e.matmul(
        bA.ap[:, 0:TC], ones_row.ap[0:1, 0:128], r0[:, 0:TC],
        start=True, stop=True), ins=[ones_row, stat], outs=[bA])
    p.op("tensor", lambda e, bB2=bB2: e.matmul(
        bB2.ap[:, 0:TC], ones_row.ap[0:1, 0:128], r0[:, TC:2 * TC],
        start=True, stop=True), ins=[ones_row, stat], outs=[bB2])
    for qq in range(4):
        load_fm(p, dr["y_s"], yq, 4, c, blk0=qq * 4, name="y_s", q="scalar_q")
        p.qfence("scalar_q")
        for k2 in range(2):
            load_fm(p, dr["xc_s"], xk, 2, c, blk0=qq * 4 + k2 * 2,
                    name="xc_s", q="scalar_q")
            load_fm(p, dr["zs_s"], zk, 2, c, blk0=qq * 4 + k2 * 2,
                    name="zs_s", q="scalar_q")
            for kj in range(2):
                k4 = k2 * 2 + kj
                k = qq * 4 + k4
                yk = yq.ap[:, k4 * TC:(k4 + 1) * TC]
                p.op("vector", lambda e, yk=yk, bA=bA: e.tensor_tensor(
                    yk, yk, bA.ap[:, 0:TC], ALU.subtract), ins=[yq, bA], outs=[yq])
                p.op("vector", lambda e, yk=yk, bB2=bB2: e.tensor_tensor(
                    yk, yk, bB2.ap[:, 0:TC], ALU.mult), ins=[yq, bB2], outs=[yq])
                p.op("vector", lambda e, yk=yk, k=k: e.tensor_scalar(
                    yk, yk, tabs["ln_g_pc"].ap[:, k:k + 1],
                    tabs["ln_b_pc"].ap[:, k:k + 1], ALU.mult, ALU.add),
                    ins=[yq, tabs["ln_g_pc"], tabs["ln_b_pc"]], outs=[yq])
                p.op("vector", lambda e, yk=yk, k=k, kj=kj, xk=xk:
                     e.scalar_tensor_tensor(
                         yk, xk.ap[:, kj * TC:(kj + 1) * TC],
                         tabs["d_param_pc"].ap[:, k:k + 1], yk, ALU.mult,
                         ALU.add),
                     ins=[xk, tabs["d_param_pc"], yq], outs=[yq])
                p.op("vector", lambda e, yk=yk, kj=kj, zk=zk: e.tensor_tensor(
                    yk, yk, zk.ap[:, kj * TC:(kj + 1) * TC], ALU.mult),
                    ins=[yq, zk], outs=[yq])
                yield
        d3 = dr["yg_s"].rearrange("(b q) t -> q b t", q=128)[
            :, qq * 4:(qq + 1) * 4, c * TC:(c + 1) * TC]
        s3 = yq.ap.rearrange("q (b t) -> q b t", t=TC)
        p.dma("gpsimd_q", lambda e, d3=d3, s3=s3: e.dma_start(out=d3, in_=s3),
              ins=[yq], outs=[p.dbuf("yg_s", c)])
        if "yg" in dbg:
            d3b = dr["dbg_yg"].rearrange("(b q) t -> q b t", q=128)[
                :, qq * 4:(qq + 1) * 4, c * TC:(c + 1) * TC]
            p.dma("gpsimd_q", lambda e, d3b=d3b, s3=s3: e.dma_start(
                out=d3b, in_=s3), ins=[yq])
    p.qfence("gpsimd_q")
    p.qfence("scalar_q")


def kernel(**inputs):
    nc = build()
    in_maps = [host_inputs(inputs, core) for core in range(8)]
    res = run_bass_kernel_spmd(nc, in_maps, core_ids=list(range(8)))
    out = np.stack([res.results[i]["out"] for i in range(8)], axis=0)
    return out
